# version 03
# baseline (speedup 1.0000x reference)
import numpy as np
import ml_dtypes
from contextlib import ExitStack

import concourse.bass as bass
import concourse.mybir as mybir
from concourse.bass_utils import run_bass_kernel_spmd

F32 = mybir.dt.float32
BF16 = mybir.dt.bfloat16
I32 = mybir.dt.int32
AF = mybir.ActivationFunctionType
ALU = mybir.AluOpType
AX = mybir.AxisListType

NCORE = 8
B, L, D = 2, 8192, 1024
R = 2048
NT = R // 128

bfnp = ml_dtypes.bfloat16


def f32c(x):
    return np.ascontiguousarray(np.asarray(x, np.float32))


class Q:
    def __init__(self, prog, name, step=1):
        self.prog = prog
        self.name = name
        self.step = step
        self.ops = []
        self.n = 0
        self._waited = {}

    def dep(self, token):
        if token is None:
            return
        pq, val = token
        if val <= 0 or (pq is self and val <= self.n):
            pass
        if val <= 0:
            return
        if self._waited.get(pq.name, 0) >= val:
            return
        self._waited[pq.name] = val
        sems = self.prog.sems

        def w(e, pn=pq.name, val=val):
            e.wait_ge(sems[pn], val)
        self.ops.append(w)

    def op(self, fn, inc=True, k=None, deps=()):
        for t in deps:
            self.dep(t)
        kk = (k if k is not None else self.step) if inc else 0
        sems = self.prog.sems
        if kk:
            def g(e, fn=fn, kk=kk, name=self.name):
                fn(e).then_inc(sems[name], kk)
            self.ops.append(g)
            self.n += kk
        else:
            self.ops.append(lambda e, fn=fn: fn(e))
        return (self, self.n)

    def tok(self):
        return (self, self.n)


class Prog:
    def __init__(self):
        self.nc = bass.Bass("TRN2", target_bir_lowering=False)
        self.st = ExitStack()
        self.sems = {}
        self.sy = Q(self, "sy", 16)
        self.gp = Q(self, "gp", 16)
        self.te = Q(self, "te", 1)
        self.ve = Q(self, "ve", 1)
        self.sc = Q(self, "sc", 1)
        self.queues = [self.sy, self.gp, self.te, self.ve, self.sc]

    def sb(self, name, shape, dtype):
        return self.st.enter_context(self.nc.sbuf_tensor(name, shape, dtype))

    def ps(self, name, shape, dtype=F32):
        return self.st.enter_context(self.nc.psum_tensor(name, shape, dtype))

    def barrier(self):
        toks = [(q, q.n) for q in self.queues]
        for q in self.queues:
            for t in toks:
                if t[0] is not q:
                    q.dep(t)

    def finish(self):
        nc = self.nc
        for q in self.queues:
            self.sems[q.name] = self.st.enter_context(nc.semaphore("sem_" + q.name))
        blk = self.st.enter_context(nc.Block())

        def run(qs):
            def body(e):
                for f in qs.ops:
                    f(e)
            return body
        blk.sync(run(self.sy))
        blk.gpsimd(run(self.gp))
        blk.tensor(run(self.te))
        blk.vector(run(self.ve))
        blk.scalar(run(self.sc))
        self.st.close()
        return nc


CPOOL_VALS = [1.0, -1.0, 0.5, 0.25, 1e-5, 1.0 / 16.0, 1.0 / 64.0, 1.0 / 1024.0,
              -1e30, 2.0, 1.0 / 512.0, 64.0, 1024.0, 0.125, -0.125, 1536.0]


def setup_consts(p, skip=()):
    nc = p.nc
    srcs = {
        "ident_bf": nc.inline_tensor(np.eye(128, dtype=bfnp), "c_ident_bf"),
        "ones1_bf": nc.inline_tensor(np.ones((1, 128), dtype=bfnp), "c_ones1_bf"),
        "ones1_f": nc.inline_tensor(np.ones((1, 128), np.float32), "c_ones1_f"),
        "ones128_bf": nc.inline_tensor(np.ones((128, 1), dtype=bfnp), "c_ones128_bf"),
        "ones128_f": nc.inline_tensor(np.ones((128, 1), np.float32), "c_ones128_f"),
        "ident_f": nc.inline_tensor(np.eye(128, dtype=np.float32), "c_ident_f"),
        "triu_f": nc.inline_tensor(np.triu(np.ones((128, 128), np.float32)), "c_triu_f"),
        "triu_bf": nc.inline_tensor(np.triu(np.ones((128, 128))).astype(bfnp), "c_triu_bf"),
        "cpool": nc.inline_tensor(np.tile(np.asarray(CPOOL_VALS, np.float32), (128, 1)), "c_cpool"),
    }
    shapes = {"ident_bf": ([128, 128], BF16), "ones1_bf": ([1, 128], BF16),
              "ones1_f": ([1, 128], F32), "ones128_bf": ([128, 1], BF16),
              "ones128_f": ([128, 1], F32),
              "ident_f": ([128, 128], F32),
              "triu_f": ([128, 128], F32), "triu_bf": ([128, 128], BF16),
              "cpool": ([128, len(CPOOL_VALS)], F32)}
    t = {}
    for nm, (sh, dt) in shapes.items():
        if nm in skip:
            continue
        t[nm] = p.sb("s_" + nm, sh, dt)
    for ci, cv in enumerate(CPOOL_VALS):
        nc.const_aps.register(F32, cv, t["cpool"][:, ci:ci + 1])
    for nm in shapes:
        if nm in skip:
            continue
        p.sy.op(lambda e, tt=t[nm], dd=srcs[nm]: e.dma_start(out=tt[:], in_=dd.ap()))
    t["loaded"] = p.sy.tok()
    return t


PAIR_PERM = np.concatenate([np.concatenate([np.arange(pr * 64, (pr + 1) * 64),
                                            np.arange((pr + 8) * 64, (pr + 9) * 64)])
                            for pr in range(8)])


def pair_view(ap2d, pr):
    return ap2d[:, pr * 128:(pr + 1) * 128]


def ppos(h):
    return (h % 8) * 128 + (h // 8) * 64


def proj_tm(p, psums, frees, xT, wsb, wtok, m, deps0=()):
    out = []
    for half in range(2):
        mm = None
        for kc in range(8):
            mm = p.te.op(lambda e, kc=kc, half=half, pst=psums[half]: e.matmul(
                pst[:], xT[:, kc * R + m * 128:kc * R + (m + 1) * 128],
                wsb[:, kc * D + half * 512: kc * D + (half + 1) * 512],
                start=(kc == 0), stop=(kc == 7)),
                inc=(kc == 7),
                deps=(list(deps0) + [wtok, frees[half]] if kc == 0 else []))
        out.append(mm)
    return out


def build_A():
    p = Prog()
    nc = p.nc
    x_d = nc.dram_tensor("x", [R, D], F32, kind="ExternalInput").ap()
    w_d = {n: nc.dram_tensor("w" + n, [D, D], F32, kind="ExternalInput").ap()
           for n in ("q", "k", "v", "g")}
    wgk1_d = nc.dram_tensor("wgk1", [D, 16], F32, kind="ExternalInput").ap()
    wgk2_d = nc.dram_tensor("wgk2", [16, D], F32, kind="ExternalInput").ap()
    bgk2_d = nc.dram_tensor("bgk2", [1, D], F32, kind="ExternalInput").ap()

    qf_d = nc.dram_tensor("qf", [8, 128, R], BF16, kind="ExternalOutput").ap()
    kf_d = nc.dram_tensor("kf", [8, 128, R], BF16, kind="ExternalOutput").ap()
    vt_d = nc.dram_tensor("vt", [R, D], BF16, kind="ExternalOutput").ap()
    e1_d = nc.dram_tensor("e1", [R, D], F32, kind="ExternalOutput").ap()
    gate_d = nc.dram_tensor("gate", [R, D], BF16, kind="ExternalOutput").ap()
    lc_d = nc.dram_tensor("lc", [NT, 128, 512], BF16, kind="ExternalOutput").ap()
    tseg_d = nc.dram_tensor("tseg", [128, 512], F32, kind="ExternalOutput").ap()
    dseg_d = nc.dram_tensor("dseg", [1, D], F32, kind="ExternalOutput").ap()
    dpre_d = nc.dram_tensor("dpre", [NT, 1, D], F32, kind="ExternalOutput").ap()

    C = setup_consts(p)

    xfp = p.sb("xfp", [128, D], F32)
    xbf = p.sb("xbf", [128, D], BF16)
    xT = p.sb("xT", [128, 8 * R], BF16)
    w_sb = {n: p.sb(f"w_{n}", [128, 8 * D], BF16) for n in ("q", "k", "v", "g")}
    wgk1_sb = p.sb("wgk1_sb", [128, 8 * 16], BF16)
    wgk2_sb = p.sb("wgk2_sb", [16, D], BF16)
    bgk2_row = p.sb("bgk2_row", [1, D], F32)
    bgk2_bc = p.sb("bgk2_bc", [128, D], F32)
    uf = p.sb("uf", [16, R], BF16)
    z_sb = p.sb("z_sb", [128, D], F32)
    tmp1 = p.sb("tmp1", [128, D], F32)
    hi = p.sb("hi", [128, D], BF16)
    lo = p.sb("lo", [128, D], BF16)
    e1_sb = p.sb("e1_sb", [128, D], F32)
    em_sb = p.sb("em_sb", [128, D], F32)
    vt_sb = p.sb("vt_sb", [128, D], BF16)
    ktm_sb = p.sb("ktm_sb", [128, D], BF16)
    gat_sb = p.sb("gat_sb", [128, D], BF16)
    pf_sb = p.sb("pf_sb", [128, 512], BF16)
    trb_sb = p.sb("trb_sb", [128, 128], BF16)
    egc_sb = p.sb("egc_sb", [1, D], F32)
    gc_sb = p.sb("gc_sb", [1, D], F32)
    e3_sb = p.sb("e3_sb", [128, 512], F32)
    s_sb = p.sb("s_sb", [128, 512], F32)
    sbf_sb = p.sb("sbf_sb", [128, 512], BF16)
    dacc = p.sb("dacc", [1, D], F32)

    ps_tr = p.ps("ps_tr", [128, 128], BF16)
    ps_tr2 = p.ps("ps_tr2", [128, 128], BF16)
    ps_a = p.ps("ps_a", [128, 512], F32)
    ps_b = p.ps("ps_b", [128, 512], F32)
    ps_g = p.ps("ps_g", [128, D], F32)
    ps_m = p.ps("ps_m", [128, 512], F32)
    ps_e3 = p.ps("ps_e3", [128, 512], F32)

    wtoks = {}
    for n in ("q", "k", "v", "g"):
        wtoks[n] = p.gp.op(lambda e, n=n: e.dma_start(
            out=w_sb[n][:].rearrange("p (c d) -> p c d", d=D),
            in_=w_d[n].rearrange("(c p) d -> p c d", p=128)))
    wtoks["gk1"] = p.gp.op(lambda e: e.dma_start(
        out=wgk1_sb[:].rearrange("p (c d) -> p c d", d=16),
        in_=wgk1_d.rearrange("(c p) d -> p c d", p=128)))
    wtoks["gk2"] = p.gp.op(lambda e: e.dma_start(out=wgk2_sb[:], in_=wgk2_d))
    t_bgk2 = p.sy.op(lambda e: e.dma_start(out=bgk2_row[:], in_=bgk2_d))

    mm0 = p.te.op(lambda e: e.matmul(ps_a[:], C["ones1_f"][:], bgk2_row[0:1, 0:512],
                                     start=True, stop=True), deps=[C["loaded"], t_bgk2])
    mm1 = p.te.op(lambda e: e.matmul(ps_b[:], C["ones1_f"][:], bgk2_row[0:1, 512:1024],
                                     start=True, stop=True))
    fa = p.ve.op(lambda e: e.tensor_copy(bgk2_bc[:, 0:512], ps_a[:]), deps=[mm0])
    fb = p.ve.op(lambda e: e.tensor_copy(bgk2_bc[:, 512:1024], ps_b[:]), deps=[mm1])
    p.ve.op(lambda e: e.memset(dacc[:], 0.0))
    p.ve.op(lambda e: e.memset(s_sb[:], 0.0))
    p.ve.op(lambda e: e.memset(sbf_sb[:], 0.0))

    for m in range(NT):
        p.barrier()
        t_x = p.sy.op(lambda e, m=m: e.dma_start(out=xfp[:], in_=x_d[m * 128:(m + 1) * 128, :]))
        t_cast = p.ve.op(lambda e: e.tensor_copy(xbf[:], xfp[:]), deps=[t_x])
        evac = [None, None]
        for kc in range(8):
            pst = ps_tr if kc % 2 == 0 else ps_tr2
            t_tr = p.te.op(lambda e, kc=kc, pst=pst: e.transpose(
                pst[:], xbf[:, kc * 128:(kc + 1) * 128], C["ident_bf"][:]),
                deps=[t_cast] + ([evac[kc % 2]] if evac[kc % 2] else []))
            evac[kc % 2] = p.ve.op(lambda e, kc=kc, m=m, pst=pst: e.tensor_copy(
                xT[:, kc * R + m * 128: kc * R + (m + 1) * 128], pst[:]), deps=[t_tr])

    p.barrier()
    fa, fb = p.te.tok(), p.te.tok()
    for tb4 in range(4):
        mm = None
        for kc in range(8):
            mm = p.te.op(lambda e, kc=kc, tb4=tb4: e.matmul(
                ps_a[0:16, :], wgk1_sb[:, kc * 16:(kc + 1) * 16],
                xT[:, kc * R + tb4 * 512:kc * R + (tb4 + 1) * 512],
                start=(kc == 0), stop=(kc == 7)),
                inc=(kc == 7),
                deps=([wtoks["gk1"], fa] if kc == 0 else []))
        fa = p.ve.op(lambda e, tb4=tb4: e.tensor_copy(uf[:, tb4 * 512:(tb4 + 1) * 512],
                                                      ps_a[0:16, :]), deps=[mm])
    uf_done = fa

    pf_free = None
    for pr in range(8):
        for tb4 in range(4):
            mm = None
            for kc in range(8):
                mm = p.te.op(lambda e, kc=kc, tb4=tb4, pr=pr: e.matmul(
                    ps_b[:], pair_view(w_sb["q"][:, kc * D:(kc + 1) * D], pr),
                    xT[:, kc * R + tb4 * 512:kc * R + (tb4 + 1) * 512],
                    start=(kc == 0), stop=(kc == 7)),
                    inc=(kc == 7),
                    deps=([wtoks["q"], fb] if kc == 0 else []))
            t_e = p.ve.op(lambda e: e.tensor_copy(pf_sb[:], ps_b[:]), deps=[mm, pf_free])
            fb = t_e
            t_s = p.sy.op(lambda e, pr=pr, tb4=tb4: e.dma_start(
                out=qf_d[pr, :, tb4 * 512:(tb4 + 1) * 512], in_=pf_sb[:]), deps=[t_e])
            pf_free = t_s

    for m in range(NT):
        p.barrier()
        mmz = []
        for half in range(2):
            pst = ps_a if half == 0 else ps_b
            mmz.append(p.te.op(lambda e, half=half, m=m, pst=pst: e.matmul(
                pst[:], uf[:, m * 128:(m + 1) * 128],
                wgk2_sb[:, half * 512:(half + 1) * 512], start=True, stop=True),
                deps=[uf_done, wtoks["gk2"]]))
        t_zs0 = p.ve.op(lambda e: e.tensor_add(z_sb[:, 0:512], ps_a[:], bgk2_bc[:, 0:512]),
                        deps=[mmz[0]])
        t_zs1 = p.ve.op(lambda e: e.tensor_add(z_sb[:, 512:1024], ps_b[:], bgk2_bc[:, 512:1024]),
                        deps=[mmz[1]])

        t_abs = p.sc.op(lambda e: e.activation(tmp1[:], z_sb[:], AF.Abs),
                        deps=[t_zs0, t_zs1])
        t_ex = p.sc.op(lambda e: e.activation(tmp1[:], tmp1[:], AF.Exp, scale=-1.0), deps=[t_abs])
        t_ln = p.sc.op(lambda e: e.activation(tmp1[:], tmp1[:], AF.Ln, bias=1.0), deps=[t_ex])
        t_mn = p.ve.op(lambda e: e.tensor_scalar(z_sb[:], z_sb[:], 0.0, None, ALU.min),
                       deps=[t_abs])
        t_sub = p.ve.op(lambda e: e.tensor_sub(z_sb[:], z_sb[:], tmp1[:]), deps=[t_ln, t_mn])
        t_g16 = p.ve.op(lambda e: e.tensor_scalar(z_sb[:], z_sb[:], 1.0 / 16.0, None, ALU.mult),
                        deps=[t_sub])
        t_hi = p.ve.op(lambda e: e.tensor_copy(hi[:], z_sb[:]), deps=[t_g16])
        t_lo = p.ve.op(lambda e: e.tensor_sub(lo[:], z_sb[:], hi[:]), deps=[t_hi])

        mmg = []
        for half in range(2):
            mmlast = None
            for j, src in enumerate((hi, lo)):
                mmlast = p.te.op(lambda e, half=half, src=src, j=j: e.matmul(
                    ps_g[:, half * 512:(half + 1) * 512], C["triu_bf"][:],
                    src[:, half * 512:(half + 1) * 512], start=(j == 0), stop=(j == 1)),
                    inc=(j == 1),
                    deps=([t_hi, t_lo] if j == 0 else []))
            mmg.append(mmlast)
        mmgc = []
        for half in range(2):
            pst = ps_a if half == 0 else ps_b
            mmlast = None
            for j, src in enumerate((hi, lo)):
                mmlast = p.te.op(lambda e, half=half, src=src, j=j, pst=pst: e.matmul(
                    pst[0:1, :], C["ones128_bf"][:],
                    src[:, half * 512:(half + 1) * 512], start=(j == 0), stop=(j == 1)),
                    inc=(j == 1),
                    deps=([t_hi, t_lo, t_zs0, t_zs1] if j == 0 else []))
            mmgc.append(mmlast)
        t_dpre = p.sy.op(lambda e, m=m: e.dma_start(out=dpre_d[m, :, :], in_=dacc[:]))
        t_gc0 = p.ve.op(lambda e: e.tensor_copy(gc_sb[0:1, 0:512], ps_a[0:1, :]), deps=[mmgc[0]])
        t_gc1 = p.ve.op(lambda e: e.tensor_copy(gc_sb[0:1, 512:1024], ps_b[0:1, :]), deps=[mmgc[1]])
        t_dacc = p.ve.op(lambda e: e.tensor_add(dacc[:], dacc[:], gc_sb[:]),
                         deps=[t_gc0, t_gc1, t_dpre])
        t_egc = p.sc.op(lambda e: e.activation(egc_sb[:], gc_sb[:], AF.Exp),
                        deps=[t_gc0, t_gc1])

        t_e1 = p.sc.op(lambda e: e.activation(e1_sb[:], ps_g[:], AF.Exp), deps=[mmg[0], mmg[1]])
        t_em = p.sc.op(lambda e: e.activation(em_sb[:], ps_g[:], AF.Exp, scale=-1.0), deps=[t_e1])
        t_e1s = p.sy.op(lambda e, m=m: e.dma_start(out=e1_d[m * 128:(m + 1) * 128, :], in_=e1_sb[:]),
                        deps=[t_e1])

        mmv = proj_tm(p, (ps_a, ps_b), (t_gc0, t_gc1), xT, w_sb["v"], wtoks["v"], m)
        t_vt0 = p.ve.op(lambda e: e.tensor_mul(vt_sb[:, 0:512], ps_a[:], em_sb[:, 0:512]),
                        deps=[mmv[0], t_em])
        t_vt1 = p.ve.op(lambda e: e.tensor_mul(vt_sb[:, 512:1024], ps_b[:], em_sb[:, 512:1024]),
                        deps=[mmv[1], t_em])
        t_vts = p.sy.op(lambda e, m=m: e.dma_start(out=vt_d[m * 128:(m + 1) * 128, :], in_=vt_sb[:]),
                        deps=[t_vt0, t_vt1])

        mmk = proj_tm(p, (ps_a, ps_b), (t_vt0, t_vt1), xT, w_sb["k"], wtoks["k"], m)
        t_k0 = p.ve.op(lambda e: e.tensor_copy(ktm_sb[:, 0:512], ps_a[:]), deps=[mmk[0]])
        t_k1 = p.ve.op(lambda e: e.tensor_copy(ktm_sb[:, 512:1024], ps_b[:]), deps=[mmk[1]])
        evac = [None, None]
        st_prev = [None, None]
        for pr in range(8):
            pst = ps_tr if pr % 2 == 0 else ps_tr2
            t_tr = p.te.op(lambda e, pr=pr, pst=pst: e.transpose(
                pst[:], pair_view(ktm_sb[:], pr), C["ident_bf"][:]),
                deps=[t_k0, t_k1] + ([evac[pr % 2]] if evac[pr % 2] else []))
            t_ke = p.ve.op(lambda e, pst=pst: e.tensor_copy(trb_sb[:], pst[:]),
                           deps=[t_tr] + ([st_prev[0]] if st_prev[0] else []))
            evac[pr % 2] = t_ke
            t_ks = p.sy.op(lambda e, pr=pr, m=m: e.dma_start(
                out=kf_d[pr, :, m * 128:(m + 1) * 128], in_=trb_sb[:]), deps=[t_ke])
            st_prev[0] = t_ks

        mmgat = proj_tm(p, (ps_a, ps_b), (t_k0, t_k1), xT, w_sb["g"], wtoks["g"], m)
        t_ga0 = p.sc.op(lambda e: e.activation(gat_sb[:, 0:512], ps_a[:], AF.Silu), deps=[mmgat[0]])
        t_ga1 = p.sc.op(lambda e: e.activation(gat_sb[:, 512:1024], ps_b[:], AF.Silu), deps=[mmgat[1]])
        t_gas = p.sy.op(lambda e, m=m: e.dma_start(out=gate_d[m * 128:(m + 1) * 128, :], in_=gat_sb[:]),
                        deps=[t_ga0, t_ga1])

        mmM = None
        for h in range(16):
            pr, half = h % 8, h // 8
            tp = (0, 0) if half == 0 else (0, 64)
            mmM = p.te.op(lambda e, h=h, pr=pr, half=half, tp=tp: e.matmul(
                ps_m[64 * half:64 * (half + 1), pr * 64:(pr + 1) * 64],
                ktm_sb[:, ppos(h):ppos(h) + 64], vt_sb[:, h * 64:(h + 1) * 64],
                start=True, stop=True, tile_position=tp),
                inc=(h == 15),
                deps=([t_k0, t_k1, t_vt0, t_vt1] if h == 0 else []))

        t_e3a = p.te.op(lambda e: e.matmul(ps_e3[0:64, :], C["ones1_f"][0:1, 0:64],
                                           egc_sb[0:1, 0:512], start=True, stop=True),
                        deps=[t_egc])
        t_e3b = p.te.op(lambda e: e.matmul(ps_e3[64:128, :], C["ones1_f"][0:1, 0:64],
                                           egc_sb[0:1, 512:1024], start=True, stop=True,
                                           tile_position=(0, 64)))
        t_e3c = p.ve.op(lambda e: e.tensor_copy(e3_sb[:], ps_e3[:]), deps=[t_e3b])

        t_lcs = p.sy.op(lambda e, m=m: e.dma_start(out=lc_d[m, :, :], in_=sbf_sb[:]))
        t_sadd = p.ve.op(lambda e: e.tensor_add(s_sb[:], s_sb[:], ps_m[:]), deps=[mmM, t_e3c])
        t_smul = p.ve.op(lambda e: e.tensor_mul(s_sb[:], s_sb[:], e3_sb[:]), deps=[t_sadd])
        p.ve.op(lambda e: e.tensor_copy(sbf_sb[:], s_sb[:]), deps=[t_smul, t_lcs])

    p.barrier()
    p.sy.op(lambda e: e.dma_start(out=tseg_d, in_=s_sb[:]))
    p.sy.op(lambda e: e.dma_start(out=dseg_d, in_=dacc[:]))
    p.sy.op(lambda e: e.nop(), inc=False, deps=[p.sy.tok(), p.gp.tok()])
    return p.finish()




def bcast_row(p, C, row_ap, ps, dst_sb, deps, n=1024):
    last = None
    nh = (n + 511) // 512
    for half in range(nh):
        w = min(512, n - half * 512)
        mm = p.te.op(lambda e, half=half, w=w: e.matmul(
            ps[:, 0:w], C["ones1_f"][:],
            row_ap[0:1, half * 512:half * 512 + w], start=True, stop=True),
            deps=(list(deps) if half == 0 else []) + ([last] if last else []))
        last = p.ve.op(lambda e, half=half, w=w: e.tensor_copy(
            dst_sb[:, half * 512:half * 512 + w], ps[:, 0:w]),
            deps=[mm])
    return last


def bcast_slayout(p, C, row_ap, ps_e3, e3_sb, deps):
    mm1 = p.te.op(lambda e: e.matmul(ps_e3[0:64, :], C["ones1_f"][0:1, 0:64],
                                     row_ap[0:1, 0:512], start=True, stop=True),
                  deps=deps)
    mm2 = p.te.op(lambda e: e.matmul(ps_e3[64:128, :], C["ones1_f"][0:1, 0:64],
                                     row_ap[0:1, 512:1024], start=True, stop=True,
                                     tile_position=(0, 64)))
    return p.ve.op(lambda e: e.tensor_copy(e3_sb[:], ps_e3[:]), deps=[mm1, mm2])


def ln_rowwise(p, x_ap, st1, st2, deps, eps=1e-5, n=1024):
    t_s = p.ve.op(lambda e: e.reduce_sum(st1[:], x_ap, axis=AX.X), deps=deps)
    t_m = p.ve.op(lambda e: e.tensor_scalar(st1[:], st1[:], 1.0 / n, None, ALU.mult),
                  deps=[t_s])
    t_c = p.ve.op(lambda e: e.tensor_scalar(x_ap, x_ap, st1[:, 0:1], None, ALU.subtract),
                  deps=[t_m])
    t_sq = p.sc.op(lambda e: e.activation(st2[:, 0:n], x_ap, AF.Square,
                                          accum_out=st1[:, 0:1]), deps=[t_c])
    t_v = p.ve.op(lambda e: e.tensor_scalar(st1[:], st1[:], 1.0 / n, eps, ALU.mult, ALU.add),
                  deps=[t_sq])
    t_sq2 = p.sc.op(lambda e: e.activation(st1[:], st1[:], AF.Sqrt), deps=[t_v])
    t_r = p.ve.op(lambda e: e.reciprocal(st1[:], st1[:]), deps=[t_sq2])
    t_n = p.ve.op(lambda e: e.tensor_scalar(x_ap, x_ap, st1[:, 0:1], None, ALU.mult),
                  deps=[t_r])
    return t_n


def xt_build(p, C, src_ap_fn, xT, xbf, ps_tr, ps_tr2, m, deps):
    t_cast = p.ve.op(lambda e, m=m: e.tensor_copy(xbf[:], src_ap_fn(m)), deps=deps)
    evac = [None, None]
    last = None
    for kc in range(8):
        pst = ps_tr if kc % 2 == 0 else ps_tr2
        t_tr = p.te.op(lambda e, kc=kc, pst=pst: e.transpose(
            pst[:], xbf[:, kc * 128:(kc + 1) * 128], C["ident_bf"][:]),
            deps=[t_cast] + ([evac[kc % 2]] if evac[kc % 2] else []))
        last = p.ve.op(lambda e, kc=kc, m=m, pst=pst: e.tensor_copy(
            xT[:, kc * R + m * 128: kc * R + (m + 1) * 128], pst[:]), deps=[t_tr])
        evac[kc % 2] = last
    return last


STOP_B = "full"
ATTN_TILES = 16
ATTN_SUB = 4


def build_B():
    stop = STOP_B
    def done():
        p.barrier()
        for m_ in range(NT):
            p.sy.op(lambda e, m_=m_: e.dma_start(out=x2_d[m_ * 128:(m_ + 1) * 128, :],
                                                 in_=X_sb[:, m_ * D:(m_ + 1) * D]))
        p.barrier()
        p.sy.op(lambda e: e.nop(), inc=False, deps=[p.sy.tok(), p.gp.tok()])
        return p.finish()
    p = Prog()
    nc = p.nc
    x_d = nc.dram_tensor("x", [R, D], F32, kind="ExternalInput").ap()
    qfi_d = nc.dram_tensor("qfi", [8, 128, R], BF16, kind="ExternalInput").ap()
    kfi_d = nc.dram_tensor("kfi", [8, 128, R], BF16, kind="ExternalInput").ap()
    vti_d = nc.dram_tensor("vti", [R, D], BF16, kind="ExternalInput").ap()
    e1i_d = nc.dram_tensor("e1i", [R, D], F32, kind="ExternalInput").ap()
    gati_d = nc.dram_tensor("gati", [R, D], BF16, kind="ExternalInput").ap()
    lci_d = nc.dram_tensor("lci", [NT, 128, 512], BF16, kind="ExternalInput").ap()
    dprei_d = nc.dram_tensor("dprei", [NT, 1, D], F32, kind="ExternalInput").ap()
    tcomp_d = nc.dram_tensor("tcomp", [3, 128, 512], F32, kind="ExternalInput").ap()
    dcomp_d = nc.dram_tensor("dcomp", [3, 1, D], F32, kind="ExternalInput").ap()
    wout_d = nc.dram_tensor("wout", [D, D], F32, kind="ExternalInput").ap()
    gamma_d = nc.dram_tensor("gamma", [1, D], F32, kind="ExternalInput").ap()
    beta_d = nc.dram_tensor("beta", [1, D], F32, kind="ExternalInput").ap()
    gatew_d = nc.dram_tensor("gatew", [D, 8], F32, kind="ExternalInput").ap()
    gateb_d = nc.dram_tensor("gateb", [1, 8], F32, kind="ExternalInput").ap()
    w1n_d = nc.dram_tensor("w1n", [8, D, D], F32, kind="ExternalInput").ap()
    w3n_d = nc.dram_tensor("w3n", [8, D, D], F32, kind="ExternalInput").ap()
    w2n_d = nc.dram_tensor("w2n", [8, D, D], F32, kind="ExternalInput").ap()
    sw1n_d = nc.dram_tensor("sw1n", [D, 2048], F32, kind="ExternalInput").ap()
    sw3n_d = nc.dram_tensor("sw3n", [D, 2048], F32, kind="ExternalInput").ap()
    sw2n_d = nc.dram_tensor("sw2n", [2048, D], F32, kind="ExternalInput").ap()
    wqn_d = nc.dram_tensor("wqn", [D, D], F32, kind="ExternalInput").ap()
    wkvn_d = nc.dram_tensor("wkvn", [D, D], F32, kind="ExternalInput").ap()
    lng_d = nc.dram_tensor("lng", [1, D], F32, kind="ExternalInput").ap()
    lnb_d = nc.dram_tensor("lnb", [1, D], F32, kind="ExternalInput").ap()
    cosf_d = nc.dram_tensor("cosf", [R, D], F32, kind="ExternalInput").ap()
    sinsf_d = nc.dram_tensor("sinsf", [R, D], F32, kind="ExternalInput").ap()
    x2_d = nc.dram_tensor("x2", [R, D], F32, kind="ExternalOutput").ap()
    qtm_d = nc.dram_tensor("qtm", [R, D], BF16, kind="ExternalOutput").ap()
    lkv_d = nc.dram_tensor("lkv", [R, D], BF16, kind="ExternalOutput").ap()

    C = setup_consts(p)

    X_sb = p.sb("X_sb", [128, NT * D], F32)
    xT = p.sb("xT", [128, 8 * R], BF16)
    qfb = p.sb("qfb", [128, 8 * 128], BF16)
    kfb = p.sb("kfb", [128, 8 * 128], BF16)
    vt_sb = p.sb("vt_sb", [128, D], BF16)
    e1_sb = p.sb("e1_sb", [128, D], F32)
    gat_sb = p.sb("gat_sb", [128, D], BF16)
    lc_sb = p.sb("lc_sb", [128, 512], BF16)
    dpre_sb = p.sb("dpre_sb", [1, D], F32)
    e3_sb = p.sb("e3_sb", [128, 512], F32)
    sin_sb = p.sb("sin_sb", [128, 512], F32)
    scw_sb = p.sb("scw_sb", [128, 512], F32)
    tc_sb = scw_sb
    scb_sb = p.sb("scb_sb", [128, 512], BF16)
    am_sb = p.sb("am_sb", [128, 2 * 128], BF16)
    o_sb = p.sb("o_sb", [128, D], F32)
    st1 = p.sb("st1", [128, 1], F32)
    st2 = p.sb("st2", [128, D], F32)
    row_sb = p.sb("row_sb", [1, D], F32)
    gamma_row = beta_row = row_sb
    gamma_bc = p.sb("gamma_bc", [128, D], F32)
    beta_bc = p.sb("beta_bc", [128, D], F32)
    z_sb = p.sb("z_sb", [128, D], BF16)
    xbf = z_sb
    trb_sb = p.sb("trb_sb", [128, 128], BF16)
    gatew_sb = p.sb("gatew_sb", [128, 8 * 8], BF16)
    gateb_row = p.sb("gateb_row", [1, 8], F32)
    gateb_bc = p.sb("gateb_bc", [128, 8], F32)
    gates_sb = p.sb("gates_sb", [128, NT * 8], F32)
    sg1 = p.sb("sg1", [128, 8], F32)
    sg2 = p.sb("sg2", [128, 8], F32)
    sg3 = p.sb("sg3", [128, 8], F32)
    w1_sb = p.sb("w1_sb", [128, 8 * D], BF16)
    w3_sb = p.sb("w3_sb", [128, 8 * D], BF16)
    w2_sb = p.sb("w2_sb", [128, 8 * D], BF16)
    hf_sb = p.sb("hf_sb", [128, 8 * 512], BF16)
    sl_sb = p.sb("sl_sb", [128, 512], BF16)
    wout_sb = w2_sb
    lng_bc = gamma_bc
    lnb_bc = beta_bc
    lng_row = lnb_row = row_sb
    cos_sb = e1_sb
    sins_sb = o_sb
    rp1 = None
    rp2 = st2
    ot_sb = p.sb("ot_sb", [128, D], BF16)
    stat16a = p.sb("stat16a", [128, 16], F32)
    stat16b = p.sb("stat16b", [128, 16], F32)

    rp1 = hf_sb[:].bitcast(F32)[:, 0:D]

    ps_tr = p.ps("ps_tr", [128, 128], BF16)
    ps_tr2 = p.ps("ps_tr2", [128, 128], BF16)
    ps_a = p.ps("ps_a", [128, 512], F32)
    ps_b = p.ps("ps_b", [128, 512], F32)
    ps_o = p.ps("ps_o", [128, D], F32)
    ps_e3 = p.ps("ps_e3", [128, 512], F32)

    t_x = None
    for m in range(NT):
        t_x = p.sy.op(lambda e, m=m: e.dma_start(out=X_sb[:, m * D:(m + 1) * D],
                                                 in_=x_d[m * 128:(m + 1) * 128, :]))
    pass
    t_gam = p.sy.op(lambda e: e.dma_start(out=gamma_row[:], in_=gamma_d))
    t_gbc = bcast_row(p, C, gamma_row, ps_e3, gamma_bc, [t_gam, C["loaded"]])
    t_bet = p.sy.op(lambda e: e.dma_start(out=beta_row[:], in_=beta_d), deps=[t_gbc])
    t_bbc = bcast_row(p, C, beta_row, ps_e3, beta_bc, [t_bet])
    t_wout = p.gp.op(lambda e: e.dma_start(
        out=wout_sb[:].rearrange("p (c d) -> p c d", d=D),
        in_=wout_d.rearrange("(c p) d -> p c d", p=128)))

    t_prev = p.ve.op(lambda e: e.memset(sin_sb[:], 0.0))
    for j in range(3):
        t_dc = p.sy.op(lambda e, j=j: e.dma_start(out=dpre_sb[:], in_=dcomp_d[j]),
                       deps=[t_prev])
        t_tc = p.sy.op(lambda e, j=j: e.dma_start(out=tc_sb[:], in_=tcomp_d[j]),
                       deps=[t_prev])
        t_ex = p.sc.op(lambda e: e.activation(dpre_sb[:], dpre_sb[:], AF.Exp),
                       deps=[t_dc, C["loaded"]])
        t_e3 = bcast_slayout(p, C, dpre_sb, ps_e3, e3_sb, [t_ex, t_gbc, t_bbc])
        t_m = p.ve.op(lambda e: e.tensor_mul(sin_sb[:], sin_sb[:], e3_sb[:]),
                      deps=[t_e3, t_prev])
        t_prev = p.ve.op(lambda e: e.tensor_add(sin_sb[:], sin_sb[:], tc_sb[:]),
                         deps=[t_m, t_tc])
    sin_done = t_prev
    if stop == "compose":
        return done()

    for m in range(min(NT, ATTN_TILES)):
        p.barrier()
        t_qf = p.sy.op(lambda e, m=m: e.dma_start(
            out=qfb[:].rearrange("p (r c) -> p r c", c=128),
            in_=qfi_d[:, :, m * 128:(m + 1) * 128].rearrange("r p c -> p r c")))
        t_kf = p.sy.op(lambda e, m=m: e.dma_start(
            out=kfb[:].rearrange("p (r c) -> p r c", c=128),
            in_=kfi_d[:, :, m * 128:(m + 1) * 128].rearrange("r p c -> p r c")))
        t_vt = p.sy.op(lambda e, m=m: e.dma_start(out=vt_sb[:], in_=vti_d[m * 128:(m + 1) * 128, :]))
        t_e1 = p.sy.op(lambda e, m=m: e.dma_start(out=e1_sb[:], in_=e1i_d[m * 128:(m + 1) * 128, :]))
        t_ga = p.sy.op(lambda e, m=m: e.dma_start(out=gat_sb[:], in_=gati_d[m * 128:(m + 1) * 128, :]))
        t_lc = p.sy.op(lambda e, m=m: e.dma_start(out=lc_sb[:], in_=lci_d[m]))
        t_dp = p.sy.op(lambda e, m=m: e.dma_start(out=dpre_sb[:], in_=dprei_d[m]))

        t_ex = p.sc.op(lambda e: e.activation(dpre_sb[:], dpre_sb[:], AF.Exp),
                       deps=[t_dp, sin_done])
        t_e3 = bcast_slayout(p, C, dpre_sb, ps_e3, e3_sb, [t_ex])
        t_sc1 = p.ve.op(lambda e: e.tensor_mul(scw_sb[:], e3_sb[:], sin_sb[:]), deps=[t_e3])
        t_sc2 = p.ve.op(lambda e: e.tensor_add(scw_sb[:], scw_sb[:], lc_sb[:]),
                        deps=[t_sc1, t_lc])
        t_scb = p.ve.op(lambda e: e.tensor_copy(scb_sb[:], scw_sb[:]), deps=[t_sc2])
        if ATTN_SUB < 1:
            continue

        mask_prev = None
        mm_last = None
        for pr in range(8):
            mmA = p.te.op(lambda e, pr=pr: e.matmul(
                ps_a[:, 0:128], kfb[0:64, pr * 128:(pr + 1) * 128],
                qfb[0:64, pr * 128:(pr + 1) * 128], start=True, stop=True,
                tile_position=(0, 0)),
                deps=([t_qf, t_kf, t_vt, t_scb] if pr == 0 else [mask_prev]))
            mmA2 = p.te.op(lambda e, pr=pr: e.matmul(
                ps_b[:, 0:128], kfb[64:128, pr * 128:(pr + 1) * 128],
                qfb[64:128, pr * 128:(pr + 1) * 128], start=True, stop=True,
                tile_position=(64, 0)))
            t_m1 = p.ve.op(lambda e: e.tensor_mul(am_sb[:, 0:128], ps_a[:, 0:128],
                                                  C["triu_f"][:]), deps=[mmA, mmA2])
            t_m2 = p.ve.op(lambda e: e.tensor_mul(am_sb[:, 128:256], ps_b[:, 0:128],
                                                  C["triu_f"][:]), deps=[mmA2])
            mask_prev = t_m2
            if ATTN_SUB < 2:
                mm_last = mmA2
                continue
            p.te.op(lambda e, pr=pr: e.matmul(
                ps_o[:, pr * 64:(pr + 1) * 64], am_sb[:, 0:128],
                vt_sb[:, pr * 64:(pr + 1) * 64], start=True, stop=False),
                inc=False, deps=[t_m1, t_m2])
            p.te.op(lambda e, pr=pr: e.matmul(
                ps_o[:, pr * 64:(pr + 1) * 64], qfb[0:64, pr * 128:(pr + 1) * 128],
                scb_sb[0:64, pr * 64:(pr + 1) * 64], start=False, stop=True,
                tile_position=(0, 0)), inc=False)
            p.te.op(lambda e, pr=pr: e.matmul(
                ps_o[:, 512 + pr * 64:512 + (pr + 1) * 64], am_sb[:, 128:256],
                vt_sb[:, 512 + pr * 64:512 + (pr + 1) * 64], start=True, stop=False),
                inc=False)
            mm_last = p.te.op(lambda e, pr=pr: e.matmul(
                ps_o[:, 512 + pr * 64:512 + (pr + 1) * 64],
                qfb[64:128, pr * 128:(pr + 1) * 128],
                scb_sb[64:128, pr * 64:(pr + 1) * 64], start=False, stop=True,
                tile_position=(64, 0)))

        if ATTN_SUB < 3:
            if ATTN_SUB >= 2:
                p.ve.op(lambda e: e.tensor_copy(o_sb[:], ps_o[:]), deps=[mm_last])
            continue
        t_o = p.ve.op(lambda e: e.tensor_mul(o_sb[:], ps_o[:], e1_sb[:]),
                      deps=[mm_last, t_e1])
        t_ln = ln_rowwise(p, o_sb[:], st1, st2, [t_o])
        t_a1 = p.ve.op(lambda e: e.tensor_mul(o_sb[:], o_sb[:], gamma_bc[:]),
                       deps=[t_ln, t_gbc])
        t_a2 = p.ve.op(lambda e: e.tensor_add(o_sb[:], o_sb[:], beta_bc[:]),
                       deps=[t_a1, t_bbc])
        t_zz = p.ve.op(lambda e: e.tensor_mul(z_sb[:], o_sb[:], gat_sb[:]),
                       deps=[t_a2, t_ga])

        if ATTN_SUB < 4:
            continue
        mml = None
        evac = [None, None]
        for kc in range(8):
            pst = ps_tr if kc % 2 == 0 else ps_tr2
            t_tr = p.te.op(lambda e, kc=kc, pst=pst: e.transpose(
                pst[:], z_sb[:, kc * 128:(kc + 1) * 128], C["ident_bf"][:]),
                deps=[t_zz] + ([evac[kc % 2]] if evac[kc % 2] else []))
            t_ev = p.ve.op(lambda e, pst=pst: e.tensor_copy(trb_sb[:], pst[:]),
                           deps=[t_tr] + ([mml] if mml else [t_o]))
            evac[kc % 2] = t_ev
            for half in range(2):
                mml = p.te.op(lambda e, kc=kc, half=half: e.matmul(
                    ps_o[:, half * 512:(half + 1) * 512], trb_sb[:],
                    wout_sb[:, kc * D + half * 512:kc * D + (half + 1) * 512],
                    start=(kc == 0), stop=(kc == 7)),
                    inc=True,
                    deps=([t_ev, t_wout] if half == 0 else [t_ev]))
        t_res = p.ve.op(lambda e, m=m: e.tensor_add(X_sb[:, m * D:(m + 1) * D],
                                                    X_sb[:, m * D:(m + 1) * D], ps_o[:]),
                        deps=[mml, t_x])
        ln_rowwise(p, X_sb[:, m * D:(m + 1) * D], st1, st2, [t_res])

    if stop == "attn":
        return done()
    p.barrier()
    for m in range(NT):
        p.barrier()
        xt_build(p, C, lambda mm: X_sb[:, mm * D:(mm + 1) * D], xT, xbf, ps_tr, ps_tr2, m, [])

    if stop == "xt1":
        return done()
    p.barrier()
    t_gw = p.gp.op(lambda e: e.dma_start(
        out=gatew_sb[:].rearrange("p (c d) -> p c d", d=8),
        in_=gatew_d.rearrange("(c p) d -> p c d", p=128)))
    t_gb = p.sy.op(lambda e: e.dma_start(out=gateb_row[:], in_=gateb_d))
    t_gbb = bcast_row(p, C, gateb_row, ps_e3, gateb_bc, [t_gb], n=8)
    for m in range(NT):
        mm = None
        for kc in range(8):
            mm = p.te.op(lambda e, kc=kc, m=m: e.matmul(
                ps_a[:, 0:8], xT[:, kc * R + m * 128:kc * R + (m + 1) * 128],
                gatew_sb[:, kc * 8:(kc + 1) * 8], start=(kc == 0), stop=(kc == 7)),
                inc=(kc == 7), deps=([t_gw, (p.ve, p.ve.n)] if kc == 0 else []))
        t_mx = p.ve.op(lambda e: e.reduce_max(st1[:], ps_a[:, 0:8], axis=AX.X, negate=True),
                       deps=[mm])
        t_ep = p.sc.op(lambda e: e.activation(sg1[:], ps_a[:, 0:8], AF.Exp,
                                              bias=st1[:, 0:1], accum_out=st2[:, 0:1]),
                       deps=[t_mx])
        t_rc = p.ve.op(lambda e: e.reciprocal(st2[:, 0:1], st2[:, 0:1]), deps=[t_ep])
        t_sc = p.ve.op(lambda e, m=m: e.tensor_scalar(gates_sb[:, m * 8:(m + 1) * 8],
                                                      sg1[:], st2[:, 0:1], None, ALU.mult),
                       deps=[t_rc])
        t_sb = p.ve.op(lambda e, m=m: e.tensor_add(sg1[:], gates_sb[:, m * 8:(m + 1) * 8],
                                                   gateb_bc[:]), deps=[t_sc, t_gbb])
        t_n1 = p.ve.op(lambda e: e.reduce_max(st1[:], sg1[:], axis=AX.X, negate=True),
                       deps=[t_sb])
        t_s1 = p.sc.op(lambda e: e.activation(sg2[:], sg1[:], AF.Sign, bias=st1[:, 0:1]),
                       deps=[t_n1])
        t_p1 = p.ve.op(lambda e: e.tensor_scalar(sg2[:], sg2[:], 1.0, None, ALU.add),
                       deps=[t_s1])
        t_mk = p.ve.op(lambda e: e.scalar_tensor_tensor(sg1[:], sg2[:], -1e30, sg1[:],
                                                        ALU.mult, ALU.add), deps=[t_p1])
        t_n2 = p.ve.op(lambda e: e.reduce_max(st1[:], sg1[:], axis=AX.X, negate=True),
                       deps=[t_mk])
        t_s2 = p.sc.op(lambda e: e.activation(sg3[:], sg1[:], AF.Sign, bias=st1[:, 0:1]),
                       deps=[t_n2])
        t_p2 = p.ve.op(lambda e: e.tensor_scalar(sg3[:], sg3[:], 1.0, None, ALU.add),
                       deps=[t_s2])
        t_ms = p.ve.op(lambda e: e.tensor_add(sg2[:], sg2[:], sg3[:]), deps=[t_p2])
        p.ve.op(lambda e, m=m: e.tensor_mul(gates_sb[:, m * 8:(m + 1) * 8],
                                            gates_sb[:, m * 8:(m + 1) * 8], sg2[:]),
                deps=[t_ms])

    if stop == "gating":
        return done()
    for ex in range(8):
        p.barrier()
        t_w1 = p.gp.op(lambda e, ex=ex: e.dma_start(
            out=w1_sb[:].rearrange("p (c d) -> p c d", d=D),
            in_=w1n_d[ex].rearrange("(c p) d -> p c d", p=128)))
        t_w3 = p.gp.op(lambda e, ex=ex: e.dma_start(
            out=w3_sb[:].rearrange("p (c d) -> p c d", d=D),
            in_=w3n_d[ex].rearrange("(c p) d -> p c d", p=128)))
        t_w2 = p.gp.op(lambda e, ex=ex: e.dma_start(
            out=w2_sb[:].rearrange("p (c d) -> p c d", d=D),
            in_=w2n_d[ex].rearrange("(c p) d -> p c d", p=128)))
        for tg in range(4):
            p.barrier()
            for i in range(8):
                mm1 = None
                for kc in range(8):
                    mm1 = p.te.op(lambda e, kc=kc, i=i, tg=tg: e.matmul(
                        ps_a[:], w1_sb[:, kc * D + i * 128:kc * D + (i + 1) * 128],
                        xT[:, kc * R + tg * 512:kc * R + (tg + 1) * 512],
                        start=(kc == 0), stop=(kc == 7)),
                        inc=(kc == 7),
                        deps=([t_w1, (p.ve, p.ve.n)] if kc == 0 else []))
                t_sl = p.sc.op(lambda e: e.activation(sl_sb[:], ps_a[:], AF.Silu),
                               deps=[mm1])
                mm3 = None
                for kc in range(8):
                    mm3 = p.te.op(lambda e, kc=kc, i=i, tg=tg: e.matmul(
                        ps_b[:], w3_sb[:, kc * D + i * 128:kc * D + (i + 1) * 128],
                        xT[:, kc * R + tg * 512:kc * R + (tg + 1) * 512],
                        start=(kc == 0), stop=(kc == 7)),
                        inc=(kc == 7),
                        deps=([t_w3, t_sl] if kc == 0 else []))
                p.ve.op(lambda e, i=i: e.tensor_mul(hf_sb[:, i * 512:(i + 1) * 512],
                                                    ps_b[:], sl_sb[:]),
                        deps=[mm3, t_sl])
            hf_done = (p.ve, p.ve.n)
            for tt in range(4):
                mty = m_ = tg * 4 + tt
                mm = None
                for dh in range(2):
                    for i in range(8):
                        mm = p.te.op(lambda e, i=i, dh=dh, tt=tt: e.matmul(
                            ps_o[:, dh * 512:(dh + 1) * 512],
                            hf_sb[:, i * 512 + tt * 128:i * 512 + (tt + 1) * 128],
                            w2_sb[:, i * D + dh * 512:i * D + (dh + 1) * 512],
                            start=(i == 0), stop=(i == 7)),
                            inc=(i == 7),
                            deps=([hf_done, t_w2, (p.ve, p.ve.n)] if (i == 0 and dh == 0) else []))
                p.ve.op(lambda e, m_=m_, ex=ex: e.scalar_tensor_tensor(
                    X_sb[:, m_ * D:(m_ + 1) * D], ps_o[:],
                    gates_sb[:, m_ * 8 + ex:m_ * 8 + ex + 1],
                    X_sb[:, m_ * D:(m_ + 1) * D], ALU.mult, ALU.add),
                    deps=[mm])

    if stop == "experts":
        return done()
    for ic in range(4):
        p.barrier()
        t_s1w = p.gp.op(lambda e, ic=ic: e.dma_start(
            out=w1_sb[:, 0:8 * 512].rearrange("p (c d) -> p c d", d=512),
            in_=sw1n_d[:, ic * 512:(ic + 1) * 512].rearrange("(c p) d -> p c d", p=128)))
        t_s3w = p.gp.op(lambda e, ic=ic: e.dma_start(
            out=w3_sb[:, 0:8 * 512].rearrange("p (c d) -> p c d", d=512),
            in_=sw3n_d[:, ic * 512:(ic + 1) * 512].rearrange("(c p) d -> p c d", p=128)))
        t_s2w = p.gp.op(lambda e, ic=ic: e.dma_start(
            out=w2_sb[:, 0:4 * D].rearrange("p (c d) -> p c d", d=D),
            in_=sw2n_d[ic * 512:(ic + 1) * 512, :].rearrange("(c p) d -> p c d", p=128)))
        for tg in range(4):
            p.barrier()
            for i in range(4):
                mm1 = None
                for kc in range(8):
                    mm1 = p.te.op(lambda e, kc=kc, i=i, tg=tg: e.matmul(
                        ps_a[:], w1_sb[:, kc * 512 + i * 128:kc * 512 + (i + 1) * 128],
                        xT[:, kc * R + tg * 512:kc * R + (tg + 1) * 512],
                        start=(kc == 0), stop=(kc == 7)),
                        inc=(kc == 7),
                        deps=([t_s1w, (p.ve, p.ve.n)] if kc == 0 else []))
                t_sl = p.sc.op(lambda e: e.activation(sl_sb[:], ps_a[:], AF.Silu),
                               deps=[mm1])
                mm3 = None
                for kc in range(8):
                    mm3 = p.te.op(lambda e, kc=kc, i=i, tg=tg: e.matmul(
                        ps_b[:], w3_sb[:, kc * 512 + i * 128:kc * 512 + (i + 1) * 128],
                        xT[:, kc * R + tg * 512:kc * R + (tg + 1) * 512],
                        start=(kc == 0), stop=(kc == 7)),
                        inc=(kc == 7),
                        deps=([t_s3w, t_sl] if kc == 0 else []))
                p.ve.op(lambda e, i=i: e.tensor_mul(hf_sb[:, i * 512:(i + 1) * 512],
                                                    ps_b[:], sl_sb[:]),
                        deps=[mm3, t_sl])
            hf_done = (p.ve, p.ve.n)
            for tt in range(4):
                m_ = tg * 4 + tt
                mm = None
                for dh in range(2):
                    for i in range(4):
                        mm = p.te.op(lambda e, i=i, dh=dh, tt=tt: e.matmul(
                            ps_o[:, dh * 512:(dh + 1) * 512],
                            hf_sb[:, i * 512 + tt * 128:i * 512 + (tt + 1) * 128],
                            w2_sb[:, i * D + dh * 512:i * D + (dh + 1) * 512],
                            start=(i == 0), stop=(i == 3)),
                            inc=(i == 3),
                            deps=([hf_done, t_s2w, (p.ve, p.ve.n)] if (i == 0 and dh == 0) else []))
                p.ve.op(lambda e, m_=m_: e.tensor_add(
                    X_sb[:, m_ * D:(m_ + 1) * D], X_sb[:, m_ * D:(m_ + 1) * D], ps_o[:]),
                    deps=[mm])

    if stop == "shared":
        return done()
    p.barrier()
    t_prevln = None
    for m in range(NT):
        t_n = ln_rowwise(p, X_sb[:, m * D:(m + 1) * D], st1, st2,
                         [t_prevln] if t_prevln else [])
        t_prevln = t_n
        p.sy.op(lambda e, m=m: e.dma_start(out=x2_d[m * 128:(m + 1) * 128, :],
                                           in_=X_sb[:, m * D:(m + 1) * D]), deps=[t_n])

    if stop == "ln2":
        return done()
    p.barrier()
    for m in range(NT):
        p.barrier()
        xt_build(p, C, lambda mm: X_sb[:, mm * D:(mm + 1) * D], xT, xbf, ps_tr, ps_tr2, m, [])
    p.barrier()
    wq_sb, wkv_sb = w1_sb, w3_sb
    t_wq = p.gp.op(lambda e: e.dma_start(
        out=wq_sb[:].rearrange("p (c d) -> p c d", d=D),
        in_=wqn_d.rearrange("(c p) d -> p c d", p=128)))
    t_wkv = p.gp.op(lambda e: e.dma_start(
        out=wkv_sb[:].rearrange("p (c d) -> p c d", d=D),
        in_=wkvn_d.rearrange("(c p) d -> p c d", p=128)))
    t_lg = p.sy.op(lambda e: e.dma_start(out=lng_row[:], in_=lng_d))
    t_lgb = bcast_row(p, C, lng_row, ps_e3, lng_bc, [t_lg])
    t_lb = p.sy.op(lambda e: e.dma_start(out=lnb_row[:], in_=lnb_d), deps=[t_lgb])
    t_lbb = bcast_row(p, C, lnb_row, ps_e3, lnb_bc, [t_lb])

    for m in range(NT):
        p.barrier()
        t_cos = p.sy.op(lambda e, m=m: e.dma_start(out=cos_sb[:], in_=cosf_d[m * 128:(m + 1) * 128, :]))
        t_sin = p.sy.op(lambda e, m=m: e.dma_start(out=sins_sb[:], in_=sinsf_d[m * 128:(m + 1) * 128, :]))
        t_store_prev = None
        for which, wsb, wtok, outd in (("q", wq_sb, t_wq, qtm_d), ("kv", wkv_sb, t_wkv, lkv_d)):
            mmp = proj_tm(p, (ps_a, ps_b), ((p.ve, p.ve.n), (p.ve, p.ve.n)), xT, wsb, wtok, m)
            t_r1a = p.ve.op(lambda e: e.tensor_mul(rp1[:, 0:512], ps_a[:], cos_sb[:, 0:512]),
                            deps=[mmp[0], t_cos])
            t_r1b = p.ve.op(lambda e: e.tensor_mul(rp1[:, 512:1024], ps_b[:], cos_sb[:, 512:1024]),
                            deps=[mmp[1], t_cos])
            t_r2a = p.ve.op(lambda e: e.tensor_copy(
                rp2[:, 0:512].rearrange("p (h s x) -> p h s x", h=8, s=2)[:, :, 0],
                ps_a[:].rearrange("p (h s x) -> p h s x", h=8, s=2)[:, :, 1]),
                deps=[mmp[0]])
            t_r2b = p.ve.op(lambda e: e.tensor_copy(
                rp2[:, 0:512].rearrange("p (h s x) -> p h s x", h=8, s=2)[:, :, 1],
                ps_a[:].rearrange("p (h s x) -> p h s x", h=8, s=2)[:, :, 0]))
            t_r2c = p.ve.op(lambda e: e.tensor_copy(
                rp2[:, 512:1024].rearrange("p (h s x) -> p h s x", h=8, s=2)[:, :, 0],
                ps_b[:].rearrange("p (h s x) -> p h s x", h=8, s=2)[:, :, 1]))
            t_r2d = p.ve.op(lambda e: e.tensor_copy(
                rp2[:, 512:1024].rearrange("p (h s x) -> p h s x", h=8, s=2)[:, :, 1],
                ps_b[:].rearrange("p (h s x) -> p h s x", h=8, s=2)[:, :, 0]))
            t_r2m = p.ve.op(lambda e: e.tensor_mul(rp2[:], rp2[:], sins_sb[:]),
                            deps=[t_r2a, t_r2b, t_r2c, t_r2d, t_sin])
            t_rs = p.ve.op(lambda e: e.tensor_add(rp1[:], rp1[:], rp2[:]),
                           deps=[t_r1a, t_r1b, t_r2m])
            if which == "q":
                t_f = p.ve.op(lambda e: e.tensor_scalar(ot_sb[:], rp1[:], 0.125, None, ALU.mult),
                              deps=[t_rs] + ([t_store_prev] if t_store_prev else []))
            else:
                t_mu = p.ve.op(lambda e: e.reduce_sum(
                    stat16a[:], rp1[:].rearrange("p (h x) -> p h x", x=64), axis=AX.X),
                    deps=[t_rs])
                t_mu2 = p.ve.op(lambda e: e.tensor_scalar(stat16a[:], stat16a[:], 1.0 / 64.0,
                                                          None, ALU.mult), deps=[t_mu])
                tsu = t_mu2
                for h in range(16):
                    tsu = p.ve.op(lambda e, h=h: e.tensor_scalar(
                        rp1[:, h * 64:(h + 1) * 64], rp1[:, h * 64:(h + 1) * 64],
                        stat16a[:, h:h + 1], None, ALU.subtract), deps=[tsu])
                t_sq = p.sc.op(lambda e: e.activation(st2[:], rp1[:], AF.Square), deps=[tsu])
                t_vv = p.ve.op(lambda e: e.reduce_sum(
                    stat16b[:], st2[:].rearrange("p (h x) -> p h x", x=64), axis=AX.X),
                    deps=[t_sq])
                t_vv2 = p.ve.op(lambda e: e.tensor_scalar(stat16b[:], stat16b[:], 1.0 / 64.0,
                                                          1e-5, ALU.mult, ALU.add), deps=[t_vv])
                t_sq2 = p.sc.op(lambda e: e.activation(stat16b[:], stat16b[:], AF.Sqrt),
                                deps=[t_vv2])
                t_rr = p.ve.op(lambda e: e.reciprocal(stat16b[:], stat16b[:]), deps=[t_sq2])
                tsu = t_rr
                for h in range(16):
                    tsu = p.ve.op(lambda e, h=h: e.tensor_scalar(
                        rp1[:, h * 64:(h + 1) * 64], rp1[:, h * 64:(h + 1) * 64],
                        stat16b[:, h:h + 1], None, ALU.mult), deps=[tsu])
                t_af1 = p.ve.op(lambda e: e.tensor_mul(rp1[:], rp1[:], lng_bc[:]),
                                deps=[tsu, t_lgb])
                t_af2 = p.ve.op(lambda e: e.tensor_add(rp1[:], rp1[:], lnb_bc[:]),
                                deps=[t_af1, t_lbb])
                t_f = p.ve.op(lambda e: e.tensor_copy(ot_sb[:], rp1[:]),
                              deps=[t_af2] + ([t_store_prev] if t_store_prev else []))
            t_store_prev = p.sy.op(lambda e, m=m, outd=outd: e.dma_start(
                out=outd[m * 128:(m + 1) * 128, :], in_=ot_sb[:]), deps=[t_f])

    p.barrier()
    p.sy.op(lambda e: e.nop(), inc=False, deps=[p.sy.tok(), p.gp.tok()])
    return p.finish()


_progs = {}


def get_prog(name):
    if name not in _progs:
        _progs[name] = {"A": build_A, "B": build_B, "C": build_C, "D": build_D, "E": build_E}[name]()
    return _progs[name]


def host_inputs_A(inputs, core_slices):
    maps = []
    for c in range(NCORE):
        maps.append({
            "x": core_slices[c],
            "wq": f32c(np.asarray(inputs["L0_attn_Wq"])[:, PAIR_PERM]),
            "wk": f32c(np.asarray(inputs["L0_attn_Wk"])[:, PAIR_PERM]),
            "wv": f32c(inputs["L0_attn_Wv"]),
            "wg": f32c(inputs["L0_attn_Wg"]),
            "wgk1": f32c(inputs["L0_attn_Wgk1"]),
            "wgk2": f32c(inputs["L0_attn_Wgk2"]),
            "bgk2": f32c(inputs["L0_attn_bgk2"]).reshape(1, D),
        })
    return maps


def run_A(inputs, core_slices):
    nc = get_prog("A")
    res = run_bass_kernel_spmd(nc, host_inputs_A(inputs, core_slices),
                               core_ids=list(range(NCORE)))
    return res.results


def rope_tables(positions):
    dh = 64
    inv = 1.0 / (10000.0 ** (np.arange(0, dh, 2, dtype=np.float64) / dh))
    fr = positions[:, None].astype(np.float64) * inv[None, :]
    emb = np.concatenate([fr, fr], axis=-1)
    cos = np.cos(emb).astype(np.float32)
    sin = np.sin(emb).astype(np.float32)
    sinS = sin.copy()
    sinS[:, 0:32] *= -1.0
    cosF = np.tile(cos, (1, 16))
    sinSF = np.tile(sinS, (1, 16))
    return f32c(cosF), f32c(sinSF)


def host_inputs_B(inputs, core_slices, resA):
    w1 = np.asarray(inputs["L0_ffn_w1"])
    w3 = np.asarray(inputs["L0_ffn_w3"])
    w2 = np.asarray(inputs["L0_ffn_w2"])
    w1n = f32c(np.transpose(w1, (0, 2, 1)))
    w3n = f32c(np.transpose(w3, (0, 2, 1)))
    w2n = f32c(np.transpose(w2, (0, 2, 1)))
    sw1n = f32c(np.asarray(inputs["L0_ffn_sw1"]).T)
    sw3n = f32c(np.asarray(inputs["L0_ffn_sw3"]).T)
    sw2n = f32c(np.asarray(inputs["L0_ffn_sw2"]).T)
    gatew = f32c(np.asarray(inputs["L0_ffn_gate_w"]).T)
    wqn = f32c(np.asarray(inputs["L1_attn_Wq"]).T)
    wkvn = f32c(np.asarray(inputs["L1_attn_Wkv"]).T)
    lng = f32c(np.tile(np.asarray(inputs["L1_attn_ln_g"]), 16)).reshape(1, D)
    lnb = f32c(np.tile(np.asarray(inputs["L1_attn_ln_b"]), 16)).reshape(1, D)
    maps = []
    for c in range(NCORE):
        b, s = c // 4, c % 4
        pos = np.arange(s * R, (s + 1) * R)
        cosF, sinSF = rope_tables(pos)
        tcomp = np.zeros((3, 128, 512), np.float32)
        dcomp = np.zeros((3, 1, D), np.float32)
        for j in range(s):
            pc = b * 4 + j
            tcomp[j] = np.asarray(resA[pc]["tseg"], np.float32)
            dcomp[j] = np.asarray(resA[pc]["dseg"], np.float32).reshape(1, D)
        maps.append({
            "x": core_slices[c],
            "qfi": np.asarray(resA[c]["qf"]),
            "kfi": np.asarray(resA[c]["kf"]),
            "vti": np.asarray(resA[c]["vt"]),
            "e1i": np.asarray(resA[c]["e1"]),
            "gati": np.asarray(resA[c]["gate"]),
            "lci": np.asarray(resA[c]["lc"]),
            "dprei": np.asarray(resA[c]["dpre"]),
            "tcomp": tcomp,
            "dcomp": dcomp,
            "wout": f32c(inputs["L0_attn_Wout"]),
            "gamma": f32c(inputs["L0_attn_gamma"]).reshape(1, D),
            "beta": f32c(inputs["L0_attn_beta"]).reshape(1, D),
            "gatew": gatew,
            "gateb": f32c(inputs["L0_ffn_gate_bias"]).reshape(1, 8),
            "w1n": w1n, "w3n": w3n, "w2n": w2n,
            "sw1n": sw1n, "sw3n": sw3n, "sw2n": sw2n,
            "wqn": wqn, "wkvn": wkvn,
            "lng": lng, "lnb": lnb,
            "cosf": cosF, "sinsf": sinSF,
        })
    return maps


def run_B(inputs, core_slices, resA):
    nc = get_prog("B")
    res = run_bass_kernel_spmd(nc, host_inputs_B(inputs, core_slices, resA),
                               core_ids=list(range(NCORE)))
    return res.results




C_HEADS = 4
C_TBS = 16
C_SUB = 3
C_NOMASK = False
C_NSCAP = 99
C_NORMCAP = 99


def build_C():
    p = Prog()
    nc = p.nc
    qsl_d = nc.dram_tensor("qsl", [L, 256], BF16, kind="ExternalInput").ap()
    ksl_d = nc.dram_tensor("ksl", [L, 256], BF16, kind="ExternalInput").ap()
    of_d = nc.dram_tensor("of", [256, L], BF16, kind="ExternalOutput").ap()

    C = setup_consts(p)
    masks_np = np.zeros((4, 128, 512), np.float32)
    for j in range(4):
        for t in range(512):
            masks_np[j, : min(128, max(0, t - 128 * j + 1)), t] = 1.0
    masks_d = nc.inline_tensor(masks_np, "c_masks")
    mask_sb = p.sb("mask_sb", [128, 4 * 512], F32)
    t_mk = p.sy.op(lambda e: e.dma_start(
        out=mask_sb[:].rearrange("p (j t) -> p j t", t=512),
        in_=masks_d.ap().rearrange("j p t -> p j t")))

    qsl_sb = p.sb("qsl_sb", [128, 64 * 256], BF16)
    ksl_sb = p.sb("ksl_sb", [128, 64 * 256], BF16)
    qf = p.sb("qf", [128, L], BF16)
    lkvf = p.sb("lkvf", [128, L], BF16)
    av65 = p.sb("av65", [128, 64 * 65], BF16)
    sq = p.sb("sq", [64, L], F32)
    pbf = p.sb("pbf", [128, 512], BF16)
    rl_sb = p.sb("rl_sb", [1, 512], F32)
    rlb_sb = p.sb("rlb_sb", [64, 512], F32)
    on_sb = p.sb("on_sb", [64, 512], BF16)
    kmax = p.sb("kmax", [1, 1], F32)
    kblk = p.sb("kblk", [1, 1], F32)
    mrow = p.sb("mrow", [1, 512], F32)

    ps_tr = p.ps("ps_tr", [64, 128], BF16)
    ps_tr2 = p.ps("ps_tr2", [64, 128], BF16)
    ps_s = p.ps("ps_s", [128, 512], F32)
    ps_oav = p.ps("ps_oav", [65, 512], F32)
    ps_r = p.ps("ps_r", [64, 512], F32)

    t_q = p.sy.op(lambda e: e.dma_start(
        out=qsl_sb[:].rearrange("p (r c) -> p r c", c=256),
        in_=qsl_d.rearrange("(r p) c -> p r c", p=128)))
    t_k = p.sy.op(lambda e: e.dma_start(
        out=ksl_sb[:].rearrange("p (r c) -> p r c", c=256),
        in_=ksl_d.rearrange("(r p) c -> p r c", p=128)))

    for hh in range(C_HEADS):
        p.barrier()
        t_neg = p.ve.op(lambda e: e.memset(lkvf[64:128, :], -1.0))
        t_av1 = p.ve.op(lambda e: e.memset(av65[:], 1.0))
        evac = [None, None]
        for st in range(64):
            pst = ps_tr if st % 2 == 0 else ps_tr2
            t_tr = p.te.op(lambda e, st=st, hh=hh, pst=pst: e.transpose(
                pst[:], ksl_sb[:, st * 256 + hh * 64: st * 256 + (hh + 1) * 64],
                C["ident_bf"][:]),
                deps=[t_k, C["loaded"]] + ([evac[st % 2]] if evac[st % 2] else []))
            evac[st % 2] = p.ve.op(lambda e, st=st, pst=pst: e.tensor_copy(
                lkvf[0:64, st * 128:(st + 1) * 128], pst[:]), deps=[t_tr])
            p.ve.op(lambda e, st=st, hh=hh: e.tensor_copy(
                av65[:, st * 65: st * 65 + 64],
                ksl_sb[:, st * 256 + hh * 64: st * 256 + (hh + 1) * 64]),
                deps=[t_av1])
        for st in range(64):
            pst = ps_tr if st % 2 == 0 else ps_tr2
            t_tr = p.te.op(lambda e, st=st, hh=hh, pst=pst: e.transpose(
                pst[:], qsl_sb[:, st * 256 + hh * 64: st * 256 + (hh + 1) * 64],
                C["ident_bf"][:]),
                deps=[t_q] + ([evac[st % 2]] if evac[st % 2] else []))
            evac[st % 2] = p.ve.op(lambda e, st=st, pst=pst: e.tensor_copy(
                qf[0:64, st * 128:(st + 1) * 128], pst[:]), deps=[t_tr])
        built = p.ve.tok()

        if C_SUB < 1:
            continue
        t_ks = p.sc.op(lambda e: e.activation(sq[:], lkvf[0:64, :], AF.Square),
                       deps=[built])
        t_km = p.ve.op(lambda e: e.memset(kmax[:], 0.0))
        for blk_ in range(16):
            mmk = p.te.op(lambda e, blk_=blk_: e.matmul(
                ps_r[0:1, :], C["ones128_f"][0:64, 0:1],
                sq[:, blk_ * 512:(blk_ + 1) * 512], start=True, stop=True),
                deps=[t_ks, p.ve.tok()])
            t_bm = p.ve.op(lambda e: e.reduce_max(kblk[:], ps_r[0:1, :], axis=AX.X),
                           deps=[mmk])
            t_km = p.ve.op(lambda e: e.tensor_max(kmax[:], kmax[:], kblk[:]),
                           deps=[t_bm, t_km])

        if C_SUB < 2:
            continue
        t_qs = p.sc.op(lambda e: e.activation(sq[:], qf[0:64, :], AF.Square),
                       deps=[built, t_km])
        for blk_ in range(16):
            mmq = p.te.op(lambda e, blk_=blk_: e.matmul(
                ps_r[0:1, :], C["ones128_f"][0:64, 0:1],
                sq[:, blk_ * 512:(blk_ + 1) * 512], start=True, stop=True),
                deps=[t_qs, p.ve.tok(), p.sc.tok()])
            t_mm = p.ve.op(lambda e: e.tensor_scalar(mrow[:], ps_r[0:1, :],
                                                     kmax[0:1, 0:1], None, ALU.mult),
                           deps=[mmq, t_km])
            t_sqr = p.sc.op(lambda e, blk_=blk_: e.activation(
                qf[64:65, blk_ * 512:(blk_ + 1) * 512], mrow[:], AF.Sqrt),
                deps=[t_mm])
        mhat_done = p.sc.tok()

        if C_SUB < 3:
            continue
        st_prev = None
        for tb in range(C_TBS):
            g0 = tb * 512
            kend = 1536 if g0 + 512 <= 1536 else g0 + 512
            ns = min(kend // 128, C_NSCAP)
            lat = g0 >= 1536
            mm_prev = None
            prev_read = None
            for st in range(ns):
                mm_s = p.te.op(lambda e, st=st, tb=tb: e.matmul(
                    ps_s[:], lkvf[0:65, st * 128:(st + 1) * 128],
                    qf[0:65, tb * 512:(tb + 1) * 512], start=True, stop=True),
                    deps=[mhat_done, (p.sc, p.sc.n)] + ([prev_read] if prev_read else []))
                t_e = p.sc.op(lambda e: e.activation(pbf[:], ps_s[:], AF.Exp),
                              deps=[mm_s] + ([mm_prev] if mm_prev else []))
                prev_read = t_e
                dep_av = [t_e]
                if lat and st >= ns - 4 and not C_NOMASK:
                    j = st - (ns - 4)
                    t_msk = p.ve.op(lambda e, j=j: e.tensor_mul(
                        pbf[:], pbf[:], mask_sb[:, j * 512:(j + 1) * 512]),
                        deps=[t_e, t_mk])
                    dep_av = [t_msk]
                mm_prev = p.te.op(lambda e, st=st, ns=ns: e.matmul(
                    ps_oav[:], av65[:, st * 65:(st + 1) * 65], pbf[:],
                    start=(st == 0), stop=(st == ns - 1)),
                    deps=dep_av + ([(p.ve, p.ve.n)] if st == 0 else []))
            if tb >= C_NORMCAP:
                p.ve.op(lambda e: e.tensor_copy(on_sb[:], ps_oav[0:64, :]),
                        deps=[mm_prev])
                continue
            t_rl = p.ve.op(lambda e: e.reciprocal(rl_sb[:], ps_oav[64:65, :]),
                           deps=[mm_prev])
            mm_rb = p.te.op(lambda e: e.matmul(
                ps_r[:], C["ones1_f"][0:1, 0:64], rl_sb[0:1, :],
                start=True, stop=True), deps=[t_rl])
            t_rc = p.ve.op(lambda e: e.tensor_copy(rlb_sb[:], ps_r[:]), deps=[mm_rb])
            t_on = p.ve.op(lambda e: e.tensor_mul(on_sb[:], ps_oav[0:64, :], rlb_sb[:]),
                           deps=[t_rc] + ([st_prev] if st_prev else []))
            st_prev = p.sy.op(lambda e, hh=hh, tb=tb: e.dma_start(
                out=of_d[hh * 64:(hh + 1) * 64, tb * 512:(tb + 1) * 512],
                in_=on_sb[:]), deps=[t_on])

    p.barrier()
    p.sy.op(lambda e: e.nop(), inc=False, deps=[p.sy.tok(), p.gp.tok()])
    return p.finish()


def host_inputs_C(resB):
    maps = []
    for c in range(NCORE):
        b, sp = c // 4, c % 4
        q_full = np.concatenate([np.asarray(resB[b * 4 + s]["qtm"]) for s in range(4)], axis=0)
        k_full = np.concatenate([np.asarray(resB[b * 4 + s]["lkv"]) for s in range(4)], axis=0)
        maps.append({
            "qsl": np.ascontiguousarray(q_full[:, sp * 256:(sp + 1) * 256]),
            "ksl": np.ascontiguousarray(k_full[:, sp * 256:(sp + 1) * 256]),
        })
    return maps


def run_C(resB):
    nc = get_prog("C")
    res = run_bass_kernel_spmd(nc, host_inputs_C(resB), core_ids=list(range(NCORE)))
    return res.results




PAIR8 = np.concatenate([np.concatenate([np.arange(h * 64, (h + 1) * 64),
                                        np.arange((h + 4) * 64, (h + 5) * 64)])
                        for h in range(4)])


D_STOP = "full"


def build_D():
    p = Prog()
    nc = p.nc
    ofall_d = nc.dram_tensor("ofall", [1024, R], BF16, kind="ExternalInput").ap()
    x2_d = nc.dram_tensor("x2", [R, D], F32, kind="ExternalInput").ap()
    won_d = nc.dram_tensor("won", [D, D], F32, kind="ExternalInput").ap()
    bo_d = nc.dram_tensor("bo", [1, D], F32, kind="ExternalInput").ap()
    wq_d = nc.dram_tensor("wq", [D, 512], F32, kind="ExternalInput").ap()
    wk_d = nc.dram_tensor("wk", [D, 512], F32, kind="ExternalInput").ap()
    wv_d = nc.dram_tensor("wv", [D, 512], F32, kind="ExternalInput").ap()
    wqa_d = nc.dram_tensor("wqa", [D, 64], F32, kind="ExternalInput").ap()
    wqb_d = nc.dram_tensor("wqb", [64, 512], F32, kind="ExternalInput").ap()
    wka_d = nc.dram_tensor("wka", [D, 64], F32, kind="ExternalInput").ap()
    wkb_d = nc.dram_tensor("wkb", [64, 512], F32, kind="ExternalInput").ap()
    wg1a_d = nc.dram_tensor("wg1a", [D, 16], F32, kind="ExternalInput").ap()
    wg1b_d = nc.dram_tensor("wg1b", [16, 512], F32, kind="ExternalInput").ap()
    bg1_d = nc.dram_tensor("bg1", [128, 4], F32, kind="ExternalInput").ap()
    wg2a_d = nc.dram_tensor("wg2a", [D, 16], F32, kind="ExternalInput").ap()
    wg2b_d = nc.dram_tensor("wg2b", [16, 512], F32, kind="ExternalInput").ap()
    bg2_d = nc.dram_tensor("bg2", [128, 4], F32, kind="ExternalInput").ap()
    we_d = nc.dram_tensor("we", [D, 4], F32, kind="ExternalInput").ap()
    wga_d = nc.dram_tensor("wga", [D, 64], F32, kind="ExternalInput").ap()
    wgb_d = nc.dram_tensor("wgb", [64, 512], F32, kind="ExternalInput").ap()

    x3_d = nc.dram_tensor("x3", [R, D], F32, kind="ExternalOutput").ap()
    opart_d = nc.dram_tensor("opart", [R, 512], F32, kind="ExternalOutput").ap()
    qgd_d = nc.dram_tensor("qgd", [4, 128, R], BF16, kind="ExternalOutput").ap()
    q2f_d = nc.dram_tensor("q2fo", [4, 128, R], BF16, kind="ExternalOutput").ap()
    gkhi_d = nc.dram_tensor("gkhi", [4, 128, R], BF16, kind="ExternalOutput").ap()
    gklo_d = nc.dram_tensor("gklo", [4, 128, R], BF16, kind="ExternalOutput").ap()
    rwf_d = nc.dram_tensor("rwf", [1, 4 * R], F32, kind="ExternalOutput").ap()
    gs_d = nc.dram_tensor("gs", [R, 512], BF16, kind="ExternalOutput").ap()
    vtm_d = nc.dram_tensor("vtm", [R, 512], BF16, kind="ExternalOutput").ap()
    lcs_d = nc.dram_tensor("lcs", [5, NT, 128, 256], BF16, kind="ExternalOutput").ap()
    tsg_d = nc.dram_tensor("tsg", [5, 128, 256], F32, kind="ExternalOutput").ap()
    dsg_d = nc.dram_tensor("dsg", [5, 128, 4], F32, kind="ExternalOutput").ap()
    cgc_d = nc.dram_tensor("cgc", [5, NT, 128, 4], F32, kind="ExternalOutput").ap()

    C = setup_consts(p, skip=("triu_bf", "ones1_bf"))

    X_sb = p.sb("X_sb", [128, NT * D], F32)
    xT = p.sb("xT", [128, 8 * R], BF16)
    st1 = p.sb("st1", [128, 1], F32)
    st16 = p.sb("st16", [128, 16], F32)
    w_small = {}
    wbig = p.sb("w_big", [128, 8 * 512], BF16)
    w_small["wq"] = w_small["wk"] = w_small["wv"] = wbig
    for nm, sh in [("wqa", [128, 8 * 64]), ("wqb", [64, 512]), ("wka", [128, 8 * 64]),
                   ("wkb", [64, 512]), ("wg1a", [128, 8 * 16]), ("wg1b", [16, 512]),
                   ("wg2a", [128, 8 * 16]), ("wg2b", [16, 512]), ("we", [128, 8 * 4]),
                   ("wga", [128, 8 * 64]), ("wgb", [64, 512])]:
        w_small[nm] = p.sb("w_" + nm, sh, BF16)
    bg1_sb = p.sb("bg1_sb", [128, 4], F32)
    bg2_sb = p.sb("bg2_sb", [128, 4], F32)
    q1f = p.sb("q1f", [128, 4 * R], BF16)
    won_sb = q1f
    v_tm = p.sb("v_tm", [128, NT * 512], BF16)
    k1f = p.sb("k1f", [128, 4 * R], BF16)
    q2f = p.sb("q2f", [128, 4 * R], BF16)
    k2f = p.sb("k2f", [128, 4 * R], BF16)
    _vv = k2f[:].bitcast(F32)
    st2 = _vv[:, 0:D]
    bo_bc = _vv[:, D:2 * D]
    ofb = k1f[:, 0:8 * 128]
    xbf = q2f[:, 0:D]
    row_sb = q2f[:].bitcast(F32)[0:1, 0:D]
    _xv = X_sb[:].bitcast(BF16)
    hi8 = _xv[:, 0 * 4 * R:1 * 4 * R]
    lo8 = _xv[:, 1 * 4 * R:2 * 4 * R]
    gk2hi = _xv[:, 2 * 4 * R:3 * 4 * R]
    gk2lo = _xv[:, 3 * 4 * R:4 * 4 * R]
    _tv = xT[:]
    qg = _tv[:, 0 * R:1 * R]
    kg = _tv[:, 1 * R:2 * R]
    kout = _tv[:, 2 * R:3 * R]
    rwbig = _tv[:, 3 * R:5 * R].bitcast(F32)
    mbig = _tv[:, 5 * R:7 * R].bitcast(F32)
    xaf = p.sb("xaf", [128, R], BF16)
    _af = xaf[:]
    zf = p.sb("zf", [128, R], F32)
    Gf = wbig[:].bitcast(F32)[:, 0:R]
    e2h = _af[:, 0:256].bitcast(F32)
    kout_tm = _af[:, 256:384]
    am2 = _af[:, 384:640]
    ob2 = _af[:, 640:896].bitcast(F32)
    ktm2 = _af[:, 896:1408]
    xab = p.sb("xab", [128, 64], BF16)
    S_sb = _af[:, 1408:1920].bitcast(F32)
    Sb_sb = _af[:, 1920:2048 + 128] if False else p.sb("Sb_sb", [128, 256], BF16)
    Stmp = e2h[:, 0:64]
    dec_sb = p.sb("dec_sb", [128, 4], F32)
    cgc_sb = p.sb("cgc_sb", [128, 4], F32)
    rw_sb = zf[:, 64:128]
    rwp = zf[0:1, 128:256]
    rwrow = None
    sg1 = p.sb("sg1", [128, 4], F32)
    sg2 = p.sb("sg2", [128, 4], F32)
    gsb = zf[:, 0:512]
    rwrow = zf[0:1, 0:R]
    ones128x128_f = p.sb("ones2d_f", [128, 128], F32)

    ps_tr = p.ps("ps_tr", [128, 128], BF16)
    ps_tr2 = p.ps("ps_tr2", [128, 128], BF16)
    ps_a = p.ps("ps_a", [128, 512], F32)
    ps_b = p.ps("ps_b", [128, 512], F32)
    ps_c = p.ps("ps_c", [128, 512], F32)
    ps_m = p.ps("ps_m", [128, 256], F32)
    p.ve.op(lambda e: e.memset(ones128x128_f[:], 1.0))

    t_x2l = None
    for m in range(NT):
        t_x2l = p.sy.op(lambda e, m=m: e.dma_start(out=X_sb[:, m * D:(m + 1) * D],
                                                   in_=x2_d[m * 128:(m + 1) * 128, :]))
    t_won = p.gp.op(lambda e: e.dma_start(
        out=won_sb[:].rearrange("p (c d) -> p c d", d=D),
        in_=won_d.rearrange("(c p) d -> p c d", p=128)))
    t_bo = p.sy.op(lambda e: e.dma_start(out=row_sb[:], in_=bo_d))
    t_bob = bcast_row(p, C, row_sb, ps_c, bo_bc, [t_bo, C["loaded"]])
    for m in range(NT):
        p.barrier()
        t_of = p.sy.op(lambda e, m=m: e.dma_start(
            out=ofb[:].rearrange("p (r c) -> p r c", c=128),
            in_=ofall_d[:, m * 128:(m + 1) * 128].rearrange("(r p) c -> p r c", p=128)))
        mml = None
        for kc in range(8):
            for half in range(2):
                mml = p.te.op(lambda e, kc=kc, half=half: e.matmul(
                    (ps_a if half == 0 else ps_b)[:], ofb[:, kc * 128:(kc + 1) * 128],
                    won_sb[:, kc * D + half * 512:kc * D + (half + 1) * 512],
                    start=(kc == 0), stop=(kc == 7)),
                    inc=(kc == 7),
                    deps=([t_of, t_won] if kc == 0 and half == 0 else []))
        t_r0 = p.ve.op(lambda e, m=m: e.tensor_add(X_sb[:, m * D:m * D + 512],
                                                   X_sb[:, m * D:m * D + 512], ps_a[:]),
                       deps=[mml, t_x2l, t_bob])
        t_r1 = p.ve.op(lambda e, m=m: e.tensor_add(X_sb[:, m * D + 512:(m + 1) * D],
                                                   X_sb[:, m * D + 512:(m + 1) * D], ps_b[:]),
                       deps=[t_r0])
        t_r2 = p.ve.op(lambda e, m=m: e.tensor_add(X_sb[:, m * D:(m + 1) * D],
                                                   X_sb[:, m * D:(m + 1) * D], bo_bc[:]),
                       deps=[t_r1])
        t_ln = ln_rowwise(p, X_sb[:, m * D:(m + 1) * D], st1, st2, [t_r2])
        p.sy.op(lambda e, m=m: e.dma_start(out=x3_d[m * 128:(m + 1) * 128, :],
                                           in_=X_sb[:, m * D:(m + 1) * D]), deps=[t_ln])

    for m in range(NT):
        p.barrier()
        xt_build(p, C, lambda mm: X_sb[:, mm * D:(mm + 1) * D], xT, xbf, ps_tr, ps_tr2, m, [])
    p.barrier()

    wt = {}
    for nm, dd, dlast in [("wqa", wqa_d, 64), ("wka", wka_d, 64),
                          ("wg1a", wg1a_d, 16), ("wg2a", wg2a_d, 16),
                          ("we", we_d, 4), ("wga", wga_d, 64)]:
        wt[nm] = p.gp.op(lambda e, nm=nm, dd=dd, dlast=dlast: e.dma_start(
            out=w_small[nm][:].rearrange("p (c d) -> p c d", d=dlast),
            in_=dd.rearrange("(c p) d -> p c d", p=128)))
    for nm, dd in [("wqb", wqb_d), ("wkb", wkb_d), ("wg1b", wg1b_d),
                   ("wg2b", wg2b_d), ("wgb", wgb_d)]:
        wt[nm] = p.gp.op(lambda e, nm=nm, dd=dd: e.dma_start(out=w_small[nm][:], in_=dd))
    t_bg1 = p.sy.op(lambda e: e.dma_start(out=bg1_sb[:], in_=bg1_d))
    t_bg2 = p.sy.op(lambda e: e.dma_start(out=bg2_sb[:], in_=bg2_d))

    def pair_proj(dst, wname, scale=None):
        last = None
        for rc in range(4):
            for tb4 in range(4):
                mm = None
                for kc in range(8):
                    mm = p.te.op(lambda e, rc=rc, tb4=tb4, kc=kc, wname=wname: e.matmul(
                        ps_c[:], w_small[wname][:, kc * 512 + rc * 128:kc * 512 + (rc + 1) * 128],
                        xT[:, kc * R + tb4 * 512:kc * R + (tb4 + 1) * 512],
                        start=(kc == 0), stop=(kc == 7)),
                        inc=(kc == 7),
                        deps=([wt[wname], (p.ve, p.ve.n)] if kc == 0 else []))
                if scale is None:
                    last = p.ve.op(lambda e, rc=rc, tb4=tb4, dst=dst: e.tensor_copy(
                        dst[:, rc * R + tb4 * 512:rc * R + (tb4 + 1) * 512], ps_c[:]),
                        deps=[mm])
                else:
                    last = p.ve.op(lambda e, rc=rc, tb4=tb4, dst=dst, scale=scale: e.tensor_scalar(
                        dst[:, rc * R + tb4 * 512:rc * R + (tb4 + 1) * 512], ps_c[:],
                        scale, None, ALU.mult), deps=[mm])
        return last

    p.barrier()
    wt["wq"] = p.gp.op(lambda e: e.dma_start(
        out=wbig[:].rearrange("p (c d) -> p c d", d=512),
        in_=wq_d.rearrange("(c p) d -> p c d", p=128)))
    pair_proj(q1f, "wq", scale=0.125)
    p.barrier()
    wt["wk"] = p.gp.op(lambda e: e.dma_start(
        out=wbig[:].rearrange("p (c d) -> p c d", d=512),
        in_=wk_d.rearrange("(c p) d -> p c d", p=128)))
    pair_proj(k1f, "wk")
    p.barrier()
    wt["wv"] = p.gp.op(lambda e: e.dma_start(
        out=wbig[:].rearrange("p (c d) -> p c d", d=512),
        in_=wv_d.rearrange("(c p) d -> p c d", p=128)))
    for m in range(NT):
        mm = None
        for kc in range(8):
            mm = p.te.op(lambda e, m=m, kc=kc: e.matmul(
                ps_c[:], xT[:, kc * R + m * 128:kc * R + (m + 1) * 128],
                w_small["wv"][:, kc * 512:(kc + 1) * 512],
                start=(kc == 0), stop=(kc == 7)),
                inc=(kc == 7), deps=([wt["wv"], (p.ve, p.ve.n)] if kc == 0 else []))
        t_v = p.ve.op(lambda e, m=m: e.tensor_copy(v_tm[:, m * 512:(m + 1) * 512], ps_c[:]),
                      deps=[mm])
        p.sy.op(lambda e, m=m: e.dma_start(out=vtm_d[m * 128:(m + 1) * 128, :],
                                           in_=v_tm[:, m * 512:(m + 1) * 512]), deps=[t_v])

    def logsig_chunk(sl, osl, hi_t, lo_t, dep):
        sc2 = st2[:, 0:512]
        t_ab = p.sc.op(lambda e: e.activation(sc2[:, 0:sl.stop - sl.start], zf[:, sl], AF.Abs), deps=[dep])
        t_ex = p.sc.op(lambda e: e.activation(sc2[:, 0:sl.stop - sl.start], sc2[:, 0:sl.stop - sl.start], AF.Exp, scale=-1.0),
                       deps=[t_ab])
        t_ln = p.sc.op(lambda e: e.activation(sc2[:, 0:sl.stop - sl.start], sc2[:, 0:sl.stop - sl.start], AF.Ln, bias=1.0),
                       deps=[t_ex])
        t_mn = p.ve.op(lambda e: e.tensor_scalar(zf[:, sl], zf[:, sl], 0.0, None, ALU.min),
                       deps=[t_ab])
        t_su = p.ve.op(lambda e: e.tensor_sub(zf[:, sl], zf[:, sl], sc2[:, 0:sl.stop - sl.start]),
                       deps=[t_ln, t_mn])
        t_sc = p.ve.op(lambda e: e.tensor_scalar(zf[:, sl], zf[:, sl], 1.0 / 16.0, None,
                                                 ALU.mult), deps=[t_su])
        t_hi = p.ve.op(lambda e: e.tensor_copy(hi_t[:, osl], zf[:, sl]), deps=[t_sc])
        return p.ve.op(lambda e: e.tensor_sub(lo_t[:, osl], zf[:, sl], hi_t[:, osl]),
                       deps=[t_hi])

    def lora_f(dst, wa, wb, ka, bias_sb=None, scale=None):
        for tb4 in range(4):
            mm = None
            for kc in range(8):
                mm = p.te.op(lambda e, tb4=tb4, kc=kc, wa=wa, ka=ka: e.matmul(
                    ps_c[0:ka, :], w_small[wa][:, kc * ka:(kc + 1) * ka],
                    xT[:, kc * R + tb4 * 512:kc * R + (tb4 + 1) * 512],
                    start=(kc == 0), stop=(kc == 7)),
                    inc=(kc == 7), deps=([wt[wa], (p.ve, p.ve.n)] if kc == 0 else []))
            p.ve.op(lambda e, tb4=tb4, ka=ka: e.tensor_copy(
                xaf[0:ka, tb4 * 512:(tb4 + 1) * 512], ps_c[0:ka, :]), deps=[mm])
        xaf_done = p.ve.tok()
        last = None
        for rc in range(4):
            for tb4 in range(4):
                mm = p.te.op(lambda e, rc=rc, tb4=tb4, wb=wb, ka=ka: e.matmul(
                    ps_c[:], w_small[wb][0:ka, rc * 128:(rc + 1) * 128],
                    xaf[0:ka, tb4 * 512:(tb4 + 1) * 512], start=True, stop=True),
                    deps=[xaf_done, wt[wb], (p.ve, p.ve.n), (p.sc, p.sc.n)])
                if bias_sb is None:
                    if scale is None:
                        last = p.ve.op(lambda e, rc=rc, tb4=tb4, dst=dst: e.tensor_copy(
                            dst[:, rc * R + tb4 * 512:rc * R + (tb4 + 1) * 512], ps_c[:]),
                            deps=[mm])
                    else:
                        last = p.ve.op(lambda e, rc=rc, tb4=tb4, dst=dst, scale=scale:
                                       e.tensor_scalar(
                            dst[:, rc * R + tb4 * 512:rc * R + (tb4 + 1) * 512], ps_c[:],
                            scale, None, ALU.mult), deps=[mm])
                else:
                    t_z = p.ve.op(lambda e, rc=rc, tb4=tb4, bias_sb=bias_sb: e.tensor_scalar(
                        zf[:, tb4 * 512:(tb4 + 1) * 512], ps_c[:],
                        bias_sb[:, rc:rc + 1], None, ALU.add), deps=[mm])
                    last = logsig_chunk(slice(tb4 * 512, (tb4 + 1) * 512),
                                        slice(rc * R + tb4 * 512, rc * R + (tb4 + 1) * 512),
                                        dst[0], dst[1], t_z)
        return last

    if D_STOP == "q1f":
        p.barrier()
        for rc in range(4):
            p.sy.op(lambda e, rc=rc: e.dma_start(out=qgd_d[rc],
                                                 in_=q1f[:, rc * R:(rc + 1) * R]))
        p.barrier()
        p.sy.op(lambda e: e.nop(), inc=False, deps=[p.sy.tok(), p.gp.tok()])
        return p.finish()
    p.barrier()
    lora_f(q2f, "wqa", "wqb", 64, scale=0.125)
    p.barrier()
    lora_f((hi8, lo8), "wg1a", "wg1b", 16, bias_sb=bg1_sb)
    p.barrier()
    t_gk2 = lora_f((gk2hi, gk2lo), "wg2a", "wg2b", 16, bias_sb=bg2_sb)
    p.barrier()
    for rc in range(4):
        p.sy.op(lambda e, rc=rc: e.dma_start(out=q2f_d[rc], in_=q2f[:, rc * R:(rc + 1) * R]))
        p.sy.op(lambda e, rc=rc: e.dma_start(out=gkhi_d[rc], in_=gk2hi[:, rc * R:(rc + 1) * R]))
        p.sy.op(lambda e, rc=rc: e.dma_start(out=gklo_d[rc], in_=gk2lo[:, rc * R:(rc + 1) * R]))

    for m in range(NT):
        p.barrier()
        mm = None
        for kc in range(8):
            mm = p.te.op(lambda e, m=m, kc=kc: e.matmul(
                ps_a[:, 0:64], xT[:, kc * R + m * 128:kc * R + (m + 1) * 128],
                w_small["wka"][:, kc * 64:(kc + 1) * 64],
                start=(kc == 0), stop=(kc == 7)),
                inc=(kc == 7), deps=([wt["wka"]] if kc == 0 else []))
        t_xc = p.ve.op(lambda e: e.tensor_copy(xab[:], ps_a[:, 0:64]), deps=[mm])
        t_tr = p.te.op(lambda e: e.transpose(ps_tr[0:64, :], xab[:], C["ident_bf"][:]),
                       deps=[t_xc])
        t_xf = p.ve.op(lambda e: e.tensor_copy(xaf[0:64, 0:128], ps_tr[0:64, 0:128]),
                       deps=[t_tr])
        mm2 = p.te.op(lambda e: e.matmul(ps_b[:], xaf[0:64, 0:128],
                                         w_small["wkb"][0:64, :], start=True, stop=True),
                      deps=[t_xf, wt["wkb"]])
        t_mx = p.ve.op(lambda e: e.reduce_max(
            st16[:, 0:8], ps_b[:].rearrange("p (h x) -> p h x", x=64), axis=AX.X,
            negate=True), deps=[mm2])
        t_eh = None
        for h in range(8):
            t_eh = p.sc.op(lambda e, h=h: e.activation(
                gsb[:, h * 64:(h + 1) * 64], ps_b[:, h * 64:(h + 1) * 64], AF.Exp,
                bias=st16[:, h:h + 1]), deps=[t_mx])
        t_sm = p.ve.op(lambda e: e.reduce_sum(
            st16[:, 8:16], gsb[:].rearrange("p (h x) -> p h x", x=64), axis=AX.X),
            deps=[t_eh])
        t_rc = p.ve.op(lambda e: e.reciprocal(st16[:, 8:16], st16[:, 8:16]), deps=[t_sm])
        t_k2 = None
        for h in range(8):
            t_k2 = p.ve.op(lambda e, h=h: e.tensor_scalar(
                ktm2[:, h * 64:(h + 1) * 64], gsb[:, h * 64:(h + 1) * 64],
                st16[:, 8 + h:9 + h], None, ALU.mult), deps=[t_rc])
        for rc in range(4):
            pst = ps_tr if rc % 2 == 0 else ps_tr2
            t_t2 = p.te.op(lambda e, rc=rc, pst=pst: e.transpose(
                pst[:], ktm2[:, rc * 128:(rc + 1) * 128], C["ident_bf"][:]),
                deps=[t_k2, (p.ve, p.ve.n)])
            p.ve.op(lambda e, rc=rc, m=m, pst=pst: e.tensor_copy(
                k2f[:, rc * R + m * 128:rc * R + (m + 1) * 128], pst[:]), deps=[t_t2])

    for m in range(NT):
        p.barrier()
        mm = None
        for kc in range(8):
            mm = p.te.op(lambda e, m=m, kc=kc: e.matmul(
                ps_a[:, 0:4], xT[:, kc * R + m * 128:kc * R + (m + 1) * 128],
                w_small["we"][:, kc * 4:(kc + 1) * 4],
                start=(kc == 0), stop=(kc == 7)),
                inc=(kc == 7), deps=([wt["we"]] if kc == 0 else []))
        t_mx = p.ve.op(lambda e: e.reduce_max(st1[:], ps_a[:, 0:4], axis=AX.X, negate=True),
                       deps=[mm])
        t_ep = p.sc.op(lambda e: e.activation(sg1[:], ps_a[:, 0:4], AF.Exp,
                                              bias=st1[:, 0:1], accum_out=st16[:, 0:1]),
                       deps=[t_mx])
        t_r = p.ve.op(lambda e: e.reciprocal(st16[:, 0:1], st16[:, 0:1]), deps=[t_ep])
        t_e = p.ve.op(lambda e, m=m: e.tensor_scalar(rw_sb[:, m * 4:(m + 1) * 4],
                                                     sg1[:], st16[:, 0:1], None, ALU.mult),
                      deps=[t_r])
        t_n1 = p.ve.op(lambda e, m=m: e.reduce_max(st1[:], rw_sb[:, m * 4:(m + 1) * 4],
                                                   axis=AX.X, negate=True), deps=[t_e])
        t_s1 = p.sc.op(lambda e, m=m: e.activation(sg2[:], rw_sb[:, m * 4:(m + 1) * 4],
                                                   AF.Sign, bias=st1[:, 0:1]), deps=[t_n1])
        t_p1 = p.ve.op(lambda e: e.tensor_scalar(sg2[:], sg2[:], 1.0, None, ALU.add),
                       deps=[t_s1])
        t_rw = p.ve.op(lambda e, m=m: e.tensor_mul(rw_sb[:, m * 4:(m + 1) * 4],
                                                   rw_sb[:, m * 4:(m + 1) * 4], sg2[:]),
                       deps=[t_p1])
        t_prevst = None
        for ex in range(4):
            mmt = p.te.op(lambda e, m=m, ex=ex: e.matmul(
                ps_m[0:1, 0:128], rw_sb[:, m * 4 + ex:m * 4 + ex + 1],
                C["ident_f"][:], start=True, stop=True),
                deps=[t_rw, (p.ve, p.ve.n)])
            t_cp = p.ve.op(lambda e: e.tensor_copy(rwp[:], ps_m[0:1, 0:128]),
                           deps=[mmt] + ([t_prevst] if t_prevst else []))
            t_prevst = p.sy.op(lambda e, m=m, ex=ex: e.dma_start(
                out=rwf_d[0:1, ex * R + m * 128:ex * R + (m + 1) * 128], in_=rwp[:]),
                deps=[t_cp])

    for m in range(NT):
        p.barrier()
        mm = None
        for kc in range(8):
            mm = p.te.op(lambda e, m=m, kc=kc: e.matmul(
                ps_a[:, 0:64], xT[:, kc * R + m * 128:kc * R + (m + 1) * 128],
                w_small["wga"][:, kc * 64:(kc + 1) * 64],
                start=(kc == 0), stop=(kc == 7)),
                inc=(kc == 7), deps=([wt["wga"]] if kc == 0 else []))
        t_xc = p.ve.op(lambda e: e.tensor_copy(xab[:], ps_a[:, 0:64]), deps=[mm])
        t_tr = p.te.op(lambda e: e.transpose(ps_tr[0:64, :], xab[:], C["ident_bf"][:]),
                       deps=[t_xc])
        t_xf = p.ve.op(lambda e: e.tensor_copy(xaf[0:64, 0:128], ps_tr[0:64, 0:128]),
                       deps=[t_tr])
        mm2 = p.te.op(lambda e: e.matmul(ps_b[:], xaf[0:64, 0:128],
                                         w_small["wgb"][0:64, :], start=True, stop=True),
                      deps=[t_xf, wt["wgb"]])
        t_sg = p.sc.op(lambda e: e.activation(gsb[:], ps_b[:], AF.Sigmoid), deps=[mm2])
        t_gs = p.ve.op(lambda e: e.tensor_mul(ktm2[:], ps_b[:], gsb[:]),
                       deps=[t_sg])
        p.sy.op(lambda e, m=m: e.dma_start(out=gs_d[m * 128:(m + 1) * 128, :],
                                           in_=ktm2[:]), deps=[t_gs])

    if D_STOP == "q1f2":
        p.barrier()
        for rc in range(4):
            p.sy.op(lambda e, rc=rc: e.dma_start(out=qgd_d[rc],
                                                 in_=q1f[:, rc * R:(rc + 1) * R]))
        p.barrier()
        p.sy.op(lambda e: e.nop(), inc=False, deps=[p.sy.tok(), p.gp.tok()])
        return p.finish()
    opart_tok = {}
    NINST = 2 if D_STOP == "inst2" else (1 if D_STOP == "inst1" else 5)
    for inst in range(NINST):
        p.barrier()
        if inst == 0:
            qsrc, ksrc, hsrc, lsrc = q1f, k1f, hi8, lo8
        else:
            qsrc, ksrc, hsrc, lsrc = q2f, k2f, gk2hi, gk2lo
            ex = inst - 1
            t_rwl = p.sy.op(lambda e, ex=ex: e.dma_start(
                out=rwrow[:], in_=rwf_d[0:1, ex * R:(ex + 1) * R]))
            last = None
            for tb4 in range(4):
                mmb = p.te.op(lambda e, tb4=tb4, ex=ex: e.matmul(
                    ps_c[:], C["ones1_f"][:],
                    rwrow[0:1, tb4 * 512:(tb4 + 1) * 512],
                    start=True, stop=True), deps=[(p.ve, p.ve.n), t_rwl])
                last = p.ve.op(lambda e, tb4=tb4: e.tensor_copy(
                    rwbig[:, tb4 * 512:(tb4 + 1) * 512], ps_c[:]), deps=[mmb])
            t_mb = p.sc.op(lambda e: e.activation(mbig[:], rwbig[:], AF.Sign), deps=[last])
            p.barrier()
        t_z0 = p.ve.op(lambda e: e.memset(S_sb[:], 0.0))
        t_z1 = p.ve.op(lambda e: e.memset(Sb_sb[:], 0.0))
        t_z2 = p.ve.op(lambda e: e.memset(cgc_sb[:], 0.0))
        for rc in range(4):
            p.barrier()
            sl = slice(rc * R, (rc + 1) * R)
            t_sc = None
            for m in range(NT):
                csl = slice(m * 128, (m + 1) * 128)
                osl = slice(rc * R + m * 128, rc * R + (m + 1) * 128)
                t_ad = p.ve.op(lambda e, osl=osl, hsrc=hsrc, lsrc=lsrc: e.tensor_add(
                    e2h[:], hsrc[:, osl], lsrc[:, osl]),
                    deps=[t_sc] if t_sc else [])
                if inst > 0:
                    t_ad = p.ve.op(lambda e, csl=csl: e.tensor_mul(e2h[:], e2h[:],
                                                                   mbig[:, csl]),
                                   deps=[t_ad])
                t_sc = p.ve.op(lambda e, csl=csl: e.tensor_tensor_scan(
                    Gf[:, csl], ones128x128_f[:], e2h[:], 0.0, ALU.mult, ALU.add),
                    deps=[t_ad])
            t_e1 = p.sc.op(lambda e: e.activation(zf[:], Gf[:], AF.Exp), deps=[t_sc])
            t_qg = p.ve.op(lambda e, sl=sl, qsrc=qsrc: e.tensor_mul(qg[:], qsrc[:, sl], zf[:]),
                           deps=[t_e1])
            if inst > 0:
                t_qg = p.ve.op(lambda e: e.tensor_mul(qg[:], qg[:], rwbig[:]),
                               deps=[t_qg])
            t_em = p.sc.op(lambda e: e.activation(zf[:], Gf[:], AF.Exp, scale=-1.0),
                           deps=[t_qg, t_e1])
            t_kg = p.ve.op(lambda e, sl=sl, ksrc=ksrc: e.tensor_mul(kg[:], ksrc[:, sl], zf[:]),
                           deps=[t_em])
            if inst > 0:
                t_kg = p.ve.op(lambda e: e.tensor_mul(kg[:], kg[:], rwbig[:]),
                               deps=[t_kg])
            t_ko = t_kg
            for m in range(NT):
                csl = slice(m * 128, (m + 1) * 128)
                t_ek = p.sc.op(lambda e, csl=csl, m=m: e.activation(
                    zf[:, csl], Gf[:, csl], AF.Exp, scale=-1.0,
                    bias=Gf[:, m * 128 + 127:m * 128 + 128]), deps=[t_ko, t_em])
                t_ko = p.ve.op(lambda e, csl=csl, m=m, rc=rc, ksrc=ksrc: e.tensor_mul(
                    kout[:, csl], ksrc[:, rc * R + m * 128:rc * R + (m + 1) * 128],
                    zf[:, csl]), deps=[t_ek])
            if inst > 0:
                t_ko = p.ve.op(lambda e: e.tensor_mul(kout[:], kout[:], rwbig[:]),
                               deps=[t_ko])
            if inst == 0:
                p.sy.op(lambda e, rc=rc: e.dma_start(out=qgd_d[rc], in_=qg[:]),
                        deps=[t_qg])
            if D_STOP == "qg0" and inst == 0 and rc == 0:
                p.sy.op(lambda e: e.dma_start(out=opart_d[0:128, :],
                                              in_=zf[:, 0:512]), deps=[t_e1, t_qg])
                p.sy.op(lambda e: e.dma_start(out=opart_d[128:256, :],
                                              in_=Gf[:, 0:512]), deps=[t_e1, t_qg])
                p.barrier()
                p.sy.op(lambda e: e.nop(), inc=False, deps=[p.sy.tok(), p.gp.tok()])
                return p.finish()
            t_m2 = None
            t_sm = None
            t_ob_prev = None
            t_os_prev = None
            for m in range(NT):
                csl = slice(m * 128, (m + 1) * 128)
                h0, h1 = rc, rc + 4
                t_ls = p.sy.op(lambda e, inst=inst, m=m, rc=rc: e.dma_start(
                    out=lcs_d[inst, m, :, rc * 64:(rc + 1) * 64],
                    in_=Sb_sb[:, rc * 64:(rc + 1) * 64]),
                    deps=[(p.ve, p.ve.n)])
                t_cs = p.sy.op(lambda e, inst=inst, m=m, rc=rc: e.dma_start(
                    out=cgc_d[inst, m, :, rc:rc + 1], in_=cgc_sb[:, rc:rc + 1],
                    allow_slow_non_contiguous=True))
                mmA = p.te.op(lambda e, csl=csl: e.matmul(
                    ps_a[:, 0:128], kg[0:64, csl], qg[0:64, csl],
                    start=True, stop=True, tile_position=(0, 0)),
                    deps=[t_ko] + ([t_m2] if t_m2 else []))
                mmA2 = p.te.op(lambda e, csl=csl: e.matmul(
                    ps_b[:, 0:128], kg[64:128, csl], qg[64:128, csl],
                    start=True, stop=True, tile_position=(64, 0)))
                t_m1 = p.ve.op(lambda e: e.tensor_mul(am2[:, 0:128], ps_a[:, 0:128],
                                                      C["triu_f"][:]), deps=[mmA, mmA2])
                t_m2 = p.ve.op(lambda e: e.tensor_mul(am2[:, 128:256], ps_b[:, 0:128],
                                                      C["triu_f"][:]), deps=[mmA2])
                mi0 = p.te.op(lambda e, m=m, h0=h0: e.matmul(
                    ps_c[:, 0:64], am2[:, 0:128],
                    v_tm[:, m * 512 + h0 * 64:m * 512 + (h0 + 1) * 64],
                    start=True, stop=True), inc=False,
                    deps=[t_m1, t_m2] + ([t_ob_prev] if t_ob_prev else []))
                mi1 = p.te.op(lambda e, m=m, h1=h1: e.matmul(
                    ps_c[:, 64:128], am2[:, 128:256],
                    v_tm[:, m * 512 + h1 * 64:m * 512 + (h1 + 1) * 64],
                    start=True, stop=True))
                t_ob = p.ve.op(lambda e: e.tensor_copy(ob2[:], ps_c[:, 0:128]),
                               deps=[mi1] + ([t_os_prev] if t_os_prev else []))
                t_ob_prev = t_ob
                key = (rc, m)
                t_os = p.gp.op(lambda e, m=m, rc=rc: e.dma_start(
                    out=opart_d[m * 128:(m + 1) * 128, :].rearrange(
                        "t (g h x) -> t g h x", g=2, h=4, x=64)[:, :, rc],
                    in_=ob2[:].rearrange("t (g x) -> t g x", x=64),
                    accum_op=ALU.add),
                    deps=[t_ob] + ([opart_tok[key]] if key in opart_tok else []))
                opart_tok[key] = t_os
                t_os_prev = t_os
                t_tr = p.te.op(lambda e, csl=csl: e.transpose(
                    ps_tr[:], kout[:, csl], C["ident_bf"][:]),
                    deps=[t_ob, (p.ve, p.ve.n)])
                t_ke = p.ve.op(lambda e: e.tensor_copy(kout_tm[:], ps_tr[:]),
                               deps=[t_tr])
                mmM = None
                for half in range(2):
                    h = rc + 4 * half
                    tp = (0, 0) if half == 0 else (0, 64)
                    mmM = p.te.op(lambda e, half=half, h=h, tp=tp, m=m, rc=rc: e.matmul(
                        ps_m[64 * half:64 * (half + 1), rc * 64:(rc + 1) * 64],
                        kout_tm[:, half * 64:(half + 1) * 64],
                        v_tm[:, m * 512 + h * 64:m * 512 + (h + 1) * 64],
                        start=True, stop=True, tile_position=tp),
                        inc=(half == 1), deps=([t_ke, t_sm] if half == 0 else []))
                t_dc = p.sc.op(lambda e, m=m, rc=rc: e.activation(
                    dec_sb[:, rc:rc + 1], Gf[:, m * 128 + 127:m * 128 + 128], AF.Exp),
                    deps=[mmM])
                t_cg = p.ve.op(lambda e, m=m, rc=rc: e.tensor_add(
                    cgc_sb[:, rc:rc + 1], cgc_sb[:, rc:rc + 1],
                    Gf[:, m * 128 + 127:m * 128 + 128]),
                    deps=[t_cs, t_dc])
                t_sa = p.ve.op(lambda e, rc=rc: e.tensor_scalar(
                    Stmp[:], S_sb[:, rc * 64:(rc + 1) * 64],
                    dec_sb[:, rc:rc + 1], None, ALU.mult), deps=[mmM, t_z0, t_cg, t_dc])
                t_sm = p.ve.op(lambda e, rc=rc: e.tensor_add(
                    S_sb[:, rc * 64:(rc + 1) * 64], Stmp[:],
                    ps_m[:, rc * 64:(rc + 1) * 64]), deps=[t_sa])
                p.ve.op(lambda e, rc=rc: e.tensor_copy(
                    Sb_sb[:, rc * 64:(rc + 1) * 64], S_sb[:, rc * 64:(rc + 1) * 64]),
                    deps=[t_sm, t_ls])
        p.barrier()
        p.sy.op(lambda e, inst=inst: e.dma_start(out=tsg_d[inst], in_=S_sb[:]))
        p.sy.op(lambda e, inst=inst: e.dma_start(out=dsg_d[inst], in_=cgc_sb[:]))

    p.barrier()
    p.sy.op(lambda e: e.nop(), inc=False, deps=[p.sy.tok(), p.gp.tok()])
    return p.finish()




def build_E():
    p = Prog()
    nc = p.nc
    x3_d = nc.dram_tensor("x3", [R, D], F32, kind="ExternalInput").ap()
    opart_d = nc.dram_tensor("opart", [R, 512], F32, kind="ExternalInput").ap()
    qgd_d = nc.dram_tensor("qgd", [4, 128, R], BF16, kind="ExternalInput").ap()
    q2f_d = nc.dram_tensor("q2f", [4, 128, R], BF16, kind="ExternalInput").ap()
    gkhi_d = nc.dram_tensor("gkhi", [4, 128, R], BF16, kind="ExternalInput").ap()
    gklo_d = nc.dram_tensor("gklo", [4, 128, R], BF16, kind="ExternalInput").ap()
    rwf_d = nc.dram_tensor("rwf", [1, 4 * R], F32, kind="ExternalInput").ap()
    gs_d = nc.dram_tensor("gs", [R, 512], BF16, kind="ExternalInput").ap()
    lcs_d = nc.dram_tensor("lcs", [5, NT, 128, 256], BF16, kind="ExternalInput").ap()
    cgc_d = nc.dram_tensor("cgc", [5, NT, 128, 4], F32, kind="ExternalInput").ap()
    tcomp_d = nc.dram_tensor("tcomp", [5, 3, 128, 256], F32, kind="ExternalInput").ap()
    dcomp_d = nc.dram_tensor("dcomp", [5, 3, 128, 4], F32, kind="ExternalInput").ap()
    nw_d = nc.dram_tensor("nw", [1, 512], F32, kind="ExternalInput").ap()
    won2_d = nc.dram_tensor("won2", [512, D], F32, kind="ExternalInput").ap()
    y_d = nc.dram_tensor("y", [R, D], F32, kind="ExternalOutput").ap()

    C = setup_consts(p, skip=("triu_bf", "triu_f"))

    qgd = p.sb("qgd_s", [128, 4 * R], BF16)
    q2f = p.sb("q2f_s", [128, 4 * R], BF16)
    gkhi = p.sb("gkhi_s", [128, 4 * R], BF16)
    gklo = p.sb("gklo_s", [128, 4 * R], BF16)
    qge = p.sb("qge", [128, 4 * R], BF16)
    o_acc = p.sb("o_acc", [128, NT * 512], F32)
    won2 = p.sb("won2_s", [128, 4 * D], BF16)
    gstile = p.sb("gstile", [128, 512], BF16)
    x3t = p.sb("x3t", [128, D], F32)
    zf = p.sb("zf", [128, R], F32)
    Gf = p.sb("Gf", [128, R], F32)
    e2h = p.sb("e2h", [128, 128], F32)
    rwbig = p.sb("rwbig", [128, R], F32)
    mbig = p.sb("mbig", [128, R], F32)
    rwrow = zf[0:1, 0:R]
    lc_sb = p.sb("lc_sb", [128, 256], BF16)
    cgr_sb = p.sb("cgr_sb", [128, 4], F32)
    dec_sb = p.sb("dec_sb", [128, 4], F32)
    Sin_sb = p.sb("Sin_sb", [128, 256], F32)
    Sc_sb = p.sb("Sc_sb", [128, 256], F32)
    Scb_sb = p.sb("Scb_sb", [128, 256], BF16)
    tc_sb = p.sb("tc_sb", [128, 256], F32)
    st1 = p.sb("st1", [128, 1], F32)
    st2 = p.sb("st2", [128, 1024], F32)
    s8 = p.sb("s8", [128, 8], F32)
    nw_bc = p.sb("nw_bc", [128, 512], F32)
    row_sb = p.sb("row_sb", [1, 512], F32)
    ob = p.sb("ob", [128, 512], BF16)
    obt = p.sb("obt", [128, 128], BF16)
    ones2d = p.sb("ones2d", [128, 128], F32)

    ps_tr = p.ps("ps_tr", [128, 128], BF16)
    ps_a = p.ps("ps_a", [128, 512], F32)
    ps_b = p.ps("ps_b", [128, 512], F32)
    ps_o = p.ps("ps_o", [128, D], F32)
    ps_bc = p.ps("ps_bc", [128, 512], F32)

    p.ve.op(lambda e: e.memset(ones2d[:], 1.0))
    t_in = {}
    for nm, dst, src in [("qgd", qgd, qgd_d), ("q2f", q2f, q2f_d),
                         ("gkhi", gkhi, gkhi_d), ("gklo", gklo, gklo_d)]:
        t_in[nm] = p.sy.op(lambda e, dst=dst, src=src: e.dma_start(
            out=dst[:].rearrange("p (r c) -> p r c", c=R),
            in_=src.rearrange("r p c -> p r c")))
    for m in range(NT):
        t_in[("op", m)] = p.sy.op(lambda e, m=m: e.dma_start(
            out=o_acc[:, m * 512:(m + 1) * 512], in_=opart_d[m * 128:(m + 1) * 128, :]))
    t_in["won2"] = p.gp.op(lambda e: e.dma_start(
        out=won2[:].rearrange("p (c d) -> p c d", d=D),
        in_=won2_d.rearrange("(c p) d -> p c d", p=128)))
    t_nw = p.sy.op(lambda e: e.dma_start(out=row_sb[:], in_=nw_d))
    t_nwb = bcast_row(p, C, row_sb, ps_bc, nw_bc, [t_nw, C["loaded"]], n=512)

    for inst in range(5):
        p.barrier()
        t_z = p.ve.op(lambda e: e.memset(Sin_sb[:], 0.0))
        for j in range(3):
            t_dc = p.sy.op(lambda e, inst=inst, j=j: e.dma_start(
                out=dec_sb[:], in_=dcomp_d[inst, j]), deps=[t_z, (p.sc, p.sc.n)])
            t_tc = p.sy.op(lambda e, inst=inst, j=j: e.dma_start(
                out=tc_sb[:], in_=tcomp_d[inst, j]))
            t_ex = p.sc.op(lambda e: e.activation(dec_sb[:], dec_sb[:], AF.Exp),
                           deps=[t_dc])
            t_sm = t_z
            for rc in range(4):
                t_sm = p.ve.op(lambda e, rc=rc: e.tensor_scalar(
                    Sin_sb[:, rc * 64:(rc + 1) * 64], Sin_sb[:, rc * 64:(rc + 1) * 64],
                    dec_sb[:, rc:rc + 1], None, ALU.mult), deps=[t_ex, t_sm])
            t_z = p.ve.op(lambda e: e.tensor_add(Sin_sb[:], Sin_sb[:], tc_sb[:]),
                          deps=[t_sm, t_tc])
        sin_done = t_z
        if inst == 0:
            qgsrc = qgd
            qg_done = t_in["qgd"]
        else:
            ex = inst - 1
            qgsrc = qge
            t_rwl = p.sy.op(lambda e, ex=ex: e.dma_start(
                out=rwrow[:], in_=rwf_d[0:1, ex * R:(ex + 1) * R]),
                deps=[(p.ve, p.ve.n), (p.te, p.te.n)])
            last = None
            for tb4 in range(4):
                mmb = p.te.op(lambda e, tb4=tb4: e.matmul(
                    ps_bc[:], C["ones1_f"][:],
                    rwrow[0:1, tb4 * 512:(tb4 + 1) * 512],
                    start=True, stop=True), deps=[t_rwl, (p.ve, p.ve.n)])
                last = p.ve.op(lambda e, tb4=tb4: e.tensor_copy(
                    rwbig[:, tb4 * 512:(tb4 + 1) * 512], ps_bc[:]), deps=[mmb])
            t_mb = p.sc.op(lambda e: e.activation(mbig[:], rwbig[:], AF.Sign),
                           deps=[last])
            qg_done = None
            for rc in range(4):
                t_sc = None
                for m in range(NT):
                    csl = slice(m * 128, (m + 1) * 128)
                    osl = slice(rc * R + m * 128, rc * R + (m + 1) * 128)
                    t_ad = p.ve.op(lambda e, osl=osl: e.tensor_add(
                        e2h[:], gkhi[:, osl], gklo[:, osl]),
                        deps=[t_in["gkhi"], t_in["gklo"]] + ([t_sc] if t_sc else []))
                    t_ad = p.ve.op(lambda e, csl=csl: e.tensor_mul(
                        e2h[:], e2h[:], mbig[:, csl]), deps=[t_ad, t_mb])
                    t_sc = p.ve.op(lambda e, csl=csl: e.tensor_tensor_scan(
                        Gf[:, csl], ones2d[:], e2h[:], 0.0, ALU.mult, ALU.add),
                        deps=[t_ad])
                t_e1 = p.sc.op(lambda e: e.activation(zf[:], Gf[:], AF.Exp),
                               deps=[t_sc, (p.ve, p.ve.n)])
                t_q1 = p.ve.op(lambda e, rc=rc: e.tensor_mul(
                    qge[:, rc * R:(rc + 1) * R], q2f[:, rc * R:(rc + 1) * R], zf[:]),
                    deps=[t_e1, t_in["q2f"]])
                qg_done = p.ve.op(lambda e, rc=rc: e.tensor_mul(
                    qge[:, rc * R:(rc + 1) * R], qge[:, rc * R:(rc + 1) * R], rwbig[:]),
                    deps=[t_q1])
        t_acc = None
        for m in range(NT):
            t_lc = p.sy.op(lambda e, inst=inst, m=m: e.dma_start(
                out=lc_sb[:], in_=lcs_d[inst, m]),
                deps=[(p.ve, p.ve.n)])
            t_cg = p.sy.op(lambda e, inst=inst, m=m: e.dma_start(
                out=cgr_sb[:], in_=cgc_d[inst, m]), deps=[(p.sc, p.sc.n)])
            t_dx = p.sc.op(lambda e: e.activation(dec_sb[:], cgr_sb[:], AF.Exp),
                           deps=[t_cg, sin_done])
            t_s = None
            for rc in range(4):
                t_s = p.ve.op(lambda e, rc=rc: e.tensor_scalar(
                    Sc_sb[:, rc * 64:(rc + 1) * 64], Sin_sb[:, rc * 64:(rc + 1) * 64],
                    dec_sb[:, rc:rc + 1], None, ALU.mult), deps=[t_dx, t_s])
            t_s = p.ve.op(lambda e: e.tensor_add(Sc_sb[:], Sc_sb[:], lc_sb[:]),
                          deps=[t_s, t_lc])
            t_sb = p.ve.op(lambda e: e.tensor_copy(Scb_sb[:], Sc_sb[:]), deps=[t_s])
            mmlast = None
            for rc in range(4):
                mmlast = p.te.op(lambda e, rc=rc, m=m, qgsrc=qgsrc: e.matmul(
                    ps_a[:, rc * 64:(rc + 1) * 64],
                    qgsrc[0:64, rc * R + m * 128:rc * R + (m + 1) * 128],
                    Scb_sb[0:64, rc * 64:(rc + 1) * 64],
                    start=True, stop=True, tile_position=(0, 0)),
                    deps=([t_sb, qg_done, t_acc] if rc == 0 else []))
                mmlast = p.te.op(lambda e, rc=rc, m=m, qgsrc=qgsrc: e.matmul(
                    ps_b[:, rc * 64:(rc + 1) * 64],
                    qgsrc[64:128, rc * R + m * 128:rc * R + (m + 1) * 128],
                    Scb_sb[64:128, rc * 64:(rc + 1) * 64],
                    start=True, stop=True, tile_position=(64, 0)))
            t_a1 = p.ve.op(lambda e, m=m: e.tensor_add(
                o_acc[:, m * 512:m * 512 + 256], o_acc[:, m * 512:m * 512 + 256],
                ps_a[:, 0:256]), deps=[mmlast, t_in[("op", m)]])
            t_acc = p.ve.op(lambda e, m=m: e.tensor_add(
                o_acc[:, m * 512 + 256:(m + 1) * 512],
                o_acc[:, m * 512 + 256:(m + 1) * 512], ps_b[:, 0:256]),
                deps=[t_a1])

    p.barrier()
    t_prev_ln = None
    t_x3 = None
    for m in range(NT):
        p.barrier()
        osl = slice(m * 512, (m + 1) * 512)
        t_gs = p.sy.op(lambda e, m=m: e.dma_start(out=gstile[:],
                                                  in_=gs_d[m * 128:(m + 1) * 128, :]))
        t_x3 = p.sy.op(lambda e, m=m: e.dma_start(out=x3t[:],
                                                  in_=x3_d[m * 128:(m + 1) * 128, :]))
        t_sq = p.sc.op(lambda e, osl=osl: e.activation(st2[:, 0:512], o_acc[:, osl],
                                                       AF.Square),
                       deps=[])
        t_rd = p.ve.op(lambda e: e.reduce_sum(
            s8[:], st2[:, 0:512].rearrange("p (h x) -> p h x", x=64), axis=AX.X), deps=[t_sq])
        t_v = p.ve.op(lambda e: e.tensor_scalar(s8[:], s8[:], 1.0 / 64.0, 1e-5,
                                                ALU.mult, ALU.add), deps=[t_rd])
        t_sq2 = p.sc.op(lambda e: e.activation(s8[:], s8[:], AF.Sqrt), deps=[t_v])
        t_r = p.ve.op(lambda e: e.reciprocal(s8[:], s8[:]), deps=[t_sq2])
        t_h = t_r
        for h in range(8):
            t_h = p.ve.op(lambda e, h=h, m=m: e.tensor_scalar(
                o_acc[:, m * 512 + h * 64:m * 512 + (h + 1) * 64],
                o_acc[:, m * 512 + h * 64:m * 512 + (h + 1) * 64],
                s8[:, h:h + 1], None, ALU.mult), deps=[t_h])
        t_nwm = p.ve.op(lambda e, osl=osl: e.tensor_mul(o_acc[:, osl], o_acc[:, osl],
                                                        nw_bc[:]), deps=[t_h, t_nwb])
        t_gsx = p.ve.op(lambda e, osl=osl: e.tensor_mul(ob[:], o_acc[:, osl], gstile[:]),
                        deps=[t_nwm, t_gs])
        mml = None
        evac = None
        for kc4 in range(4):
            t_tr = p.te.op(lambda e, kc4=kc4: e.transpose(
                ps_tr[:], ob[:, kc4 * 128:(kc4 + 1) * 128], C["ident_bf"][:]),
                deps=[t_gsx] + ([evac] if evac else []))
            evac = p.ve.op(lambda e: e.tensor_copy(obt[:], ps_tr[:]),
                           deps=[t_tr] + ([mml] if mml else []))
            for half in range(2):
                mml = p.te.op(lambda e, kc4=kc4, half=half: e.matmul(
                    ps_o[:, half * 512:(half + 1) * 512], obt[:],
                    won2[:, kc4 * D + half * 512:kc4 * D + (half + 1) * 512],
                    start=(kc4 == 0), stop=(kc4 == 3)),
                    inc=True,
                    deps=([evac, t_in["won2"]] if half == 0 else [evac]))
        t_y = p.ve.op(lambda e: e.tensor_add(x3t[:], x3t[:], ps_o[:]),
                      deps=[mml, t_x3])
        t_ln = ln_rowwise(p, x3t[:], st1, st2, [t_y, t_prev_ln] if t_prev_ln else [t_y],
                          n=1024)
        t_prev_ln = t_ln
        p.sy.op(lambda e, m=m: e.dma_start(out=y_d[m * 128:(m + 1) * 128, :],
                                           in_=x3t[:]), deps=[t_ln])

    p.barrier()
    p.sy.op(lambda e: e.nop(), inc=False, deps=[p.sy.tok(), p.gp.tok()])
    return p.finish()


def host_inputs_D(inputs, resB, resC):
    f = f32c
    wonT = f(np.asarray(inputs["L1_attn_Wo"]).T)
    wqP = f(np.asarray(inputs["L1_ffn_Wq"])[:, PAIR8])
    wkP = f(np.asarray(inputs["L1_ffn_Wk"])[:, PAIR8])
    wv = f(inputs["L1_ffn_Wv"])
    wqa = f(inputs["L1_ffn_Wqa"])
    wqbP = f(np.asarray(inputs["L1_ffn_Wqb"])[:, PAIR8])
    wka = f(inputs["L1_ffn_Wka"])
    wkbP = f(np.asarray(inputs["L1_ffn_Wkb"])[:, PAIR8])
    wg1a = f(inputs["L1_ffn_Wg1a"])
    wg1bP = f(np.asarray(inputs["L1_ffn_Wg1b"])[:, PAIR8])
    wg2a = f(inputs["L1_ffn_Wg2a"])
    wg2bP = f(np.asarray(inputs["L1_ffn_Wg2b"])[:, PAIR8])
    bg1P = f(np.asarray(inputs["L1_ffn_bg1"])[PAIR8].reshape(4, 128).T)
    bg2P = f(np.asarray(inputs["L1_ffn_bg2"])[PAIR8].reshape(4, 128).T)
    we = f(inputs["L1_ffn_We"])
    wga = f(inputs["L1_ffn_Wga"])
    wgb = f(inputs["L1_ffn_Wgb"])
    bo = f(inputs["L1_attn_bo"]).reshape(1, D)
    maps = []
    for c in range(NCORE):
        b, sq = c // 4, c % 4
        of_global = np.concatenate([np.asarray(resC[b * 4 + sp]["of"]) for sp in range(4)], axis=0)
        pr = np.arange(sq * R, (sq + 1) * R)
        gmap = np.where(pr < 6656, pr + 1536, pr - 6656)
        ofall = np.ascontiguousarray(of_global[:, gmap])
        maps.append({
            "ofall": ofall,
            "x2": np.asarray(resB[c]["x2"], np.float32),
            "won": wonT, "bo": bo,
            "wq": wqP, "wk": wkP, "wv": wv,
            "wqa": wqa, "wqb": wqbP, "wka": wka, "wkb": wkbP,
            "wg1a": wg1a, "wg1b": wg1bP, "bg1": bg1P,
            "wg2a": wg2a, "wg2b": wg2bP, "bg2": bg2P,
            "we": we, "wga": wga, "wgb": wgb,
        })
    return maps


def run_D(inputs, resB, resC):
    nc = get_prog("D")
    res = run_bass_kernel_spmd(nc, host_inputs_D(inputs, resB, resC),
                               core_ids=list(range(NCORE)))
    return res.results


def host_inputs_E(inputs, resD):
    nw = f32c(np.tile(np.asarray(inputs["L1_ffn_norm_w"]), 8)).reshape(1, 512)
    won2 = f32c(inputs["L1_ffn_Wo"])
    maps = []
    for c in range(NCORE):
        b, sq = c // 4, c % 4
        tcomp = np.zeros((5, 3, 128, 256), np.float32)
        dcomp = np.zeros((5, 3, 128, 4), np.float32)
        for j in range(sq):
            pc = b * 4 + j
            tcomp[:, j] = np.asarray(resD[pc]["tsg"], np.float32)
            dcomp[:, j] = np.asarray(resD[pc]["dsg"], np.float32)
        maps.append({
            "x3": np.asarray(resD[c]["x3"], np.float32),
            "opart": np.asarray(resD[c]["opart"], np.float32),
            "qgd": np.asarray(resD[c]["qgd"]),
            "q2f": np.asarray(resD[c]["q2fo"]),
            "gkhi": np.asarray(resD[c]["gkhi"]),
            "gklo": np.asarray(resD[c]["gklo"]),
            "rwf": np.asarray(resD[c]["rwf"], np.float32),
            "gs": np.asarray(resD[c]["gs"]),
            "lcs": np.asarray(resD[c]["lcs"]),
            "cgc": np.asarray(resD[c]["cgc"], np.float32),
            "tcomp": tcomp, "dcomp": dcomp,
            "nw": nw, "won2": won2,
        })
    return maps


def run_E(inputs, resD):
    nc = get_prog("E")
    res = run_bass_kernel_spmd(nc, host_inputs_E(inputs, resD),
                               core_ids=list(range(NCORE)))
    return res.results


def kernel(**inputs):
    x = np.asarray(inputs["L0_attn_x"], np.float32).reshape(B, L, D)
    slices = [np.ascontiguousarray(x[c // 4, (c % 4) * R:(c % 4 + 1) * R])
              for c in range(NCORE)]
    resA = run_A(inputs, slices)
    resB = run_B(inputs, slices, resA)
    resC = run_C(resB)
    resD = run_D(inputs, resB, resC)
    resE = run_E(inputs, resD)
    out = np.zeros((B, L, D), np.float32)
    for c in range(NCORE):
        b, sq = c // 4, c % 4
        out[b, sq * R:(sq + 1) * R] = np.asarray(resE[c]["y"], np.float32)
    return out


# version 04
# speedup vs baseline: 1.1110x; 1.1110x over previous
import numpy as np
import ml_dtypes
from contextlib import ExitStack

import concourse.bass as bass
import concourse.mybir as mybir
from concourse.bass_utils import run_bass_kernel_spmd

F32 = mybir.dt.float32
BF16 = mybir.dt.bfloat16
I32 = mybir.dt.int32
AF = mybir.ActivationFunctionType
ALU = mybir.AluOpType
AX = mybir.AxisListType

NCORE = 8
B, L, D = 2, 8192, 1024
R = 2048
NT = R // 128

bfnp = ml_dtypes.bfloat16


def f32c(x):
    return np.ascontiguousarray(np.asarray(x, np.float32))


class Q:
    def __init__(self, prog, name, step=1):
        self.prog = prog
        self.name = name
        self.step = step
        self.ops = []
        self.n = 0
        self._waited = {}

    def dep(self, token):
        if token is None:
            return
        pq, val = token
        if val <= 0 or (pq is self and val <= self.n):
            pass
        if val <= 0:
            return
        if self._waited.get(pq.name, 0) >= val:
            return
        self._waited[pq.name] = val
        sems = self.prog.sems

        def w(e, pn=pq.name, val=val):
            e.wait_ge(sems[pn], val)
        self.ops.append(w)

    def op(self, fn, inc=True, k=None, deps=()):
        for t in deps:
            self.dep(t)
        kk = (k if k is not None else self.step) if inc else 0
        sems = self.prog.sems
        if kk:
            def g(e, fn=fn, kk=kk, name=self.name):
                fn(e).then_inc(sems[name], kk)
            self.ops.append(g)
            self.n += kk
        else:
            self.ops.append(lambda e, fn=fn: fn(e))
        return (self, self.n)

    def tok(self):
        return (self, self.n)


class Prog:
    def __init__(self):
        self.nc = bass.Bass("TRN2", target_bir_lowering=False)
        self.st = ExitStack()
        self.sems = {}
        self.sy = Q(self, "sy", 16)
        self.gp = Q(self, "gp", 16)
        self.te = Q(self, "te", 1)
        self.ve = Q(self, "ve", 1)
        self.sc = Q(self, "sc", 1)
        self.queues = [self.sy, self.gp, self.te, self.ve, self.sc]

    def sb(self, name, shape, dtype):
        return self.st.enter_context(self.nc.sbuf_tensor(name, shape, dtype))

    def ps(self, name, shape, dtype=F32):
        return self.st.enter_context(self.nc.psum_tensor(name, shape, dtype))

    def barrier(self):
        toks = [(q, q.n) for q in self.queues]
        for q in self.queues:
            for t in toks:
                if t[0] is not q:
                    q.dep(t)

    def finish(self):
        nc = self.nc
        for q in self.queues:
            self.sems[q.name] = self.st.enter_context(nc.semaphore("sem_" + q.name))
        blk = self.st.enter_context(nc.Block())

        def run(qs):
            def body(e):
                for f in qs.ops:
                    f(e)
            return body
        blk.sync(run(self.sy))
        blk.gpsimd(run(self.gp))
        blk.tensor(run(self.te))
        blk.vector(run(self.ve))
        blk.scalar(run(self.sc))
        self.st.close()
        return nc


CPOOL_VALS = [1.0, -1.0, 0.5, 0.25, 1e-5, 1.0 / 16.0, 1.0 / 64.0, 1.0 / 1024.0,
              -1e30, 2.0, 1.0 / 512.0, 64.0, 1024.0, 0.125, -0.125, 1536.0]


def setup_consts(p, skip=()):
    nc = p.nc
    srcs = {
        "ident_bf": nc.inline_tensor(np.eye(128, dtype=bfnp), "c_ident_bf"),
        "ones1_bf": nc.inline_tensor(np.ones((1, 128), dtype=bfnp), "c_ones1_bf"),
        "ones1_f": nc.inline_tensor(np.ones((1, 128), np.float32), "c_ones1_f"),
        "ones128_bf": nc.inline_tensor(np.ones((128, 1), dtype=bfnp), "c_ones128_bf"),
        "ones128_f": nc.inline_tensor(np.ones((128, 1), np.float32), "c_ones128_f"),
        "ident_f": nc.inline_tensor(np.eye(128, dtype=np.float32), "c_ident_f"),
        "triu_f": nc.inline_tensor(np.triu(np.ones((128, 128), np.float32)), "c_triu_f"),
        "triu_bf": nc.inline_tensor(np.triu(np.ones((128, 128))).astype(bfnp), "c_triu_bf"),
        "cpool": nc.inline_tensor(np.tile(np.asarray(CPOOL_VALS, np.float32), (128, 1)), "c_cpool"),
    }
    shapes = {"ident_bf": ([128, 128], BF16), "ones1_bf": ([1, 128], BF16),
              "ones1_f": ([1, 128], F32), "ones128_bf": ([128, 1], BF16),
              "ones128_f": ([128, 1], F32),
              "ident_f": ([128, 128], F32),
              "triu_f": ([128, 128], F32), "triu_bf": ([128, 128], BF16),
              "cpool": ([128, len(CPOOL_VALS)], F32)}
    t = {}
    for nm, (sh, dt) in shapes.items():
        if nm in skip:
            continue
        t[nm] = p.sb("s_" + nm, sh, dt)
    for ci, cv in enumerate(CPOOL_VALS):
        nc.const_aps.register(F32, cv, t["cpool"][:, ci:ci + 1])
    for nm in shapes:
        if nm in skip:
            continue
        p.sy.op(lambda e, tt=t[nm], dd=srcs[nm]: e.dma_start(out=tt[:], in_=dd.ap()))
    t["loaded"] = p.sy.tok()
    return t


PAIR_PERM = np.concatenate([np.concatenate([np.arange(pr * 64, (pr + 1) * 64),
                                            np.arange((pr + 8) * 64, (pr + 9) * 64)])
                            for pr in range(8)])


def pair_view(ap2d, pr):
    return ap2d[:, pr * 128:(pr + 1) * 128]


def ppos(h):
    return (h % 8) * 128 + (h // 8) * 64


def proj_tm(p, psums, frees, xT, wsb, wtok, m, deps0=()):
    out = []
    for half in range(2):
        mm = None
        for kc in range(8):
            mm = p.te.op(lambda e, kc=kc, half=half, pst=psums[half]: e.matmul(
                pst[:], xT[:, kc * R + m * 128:kc * R + (m + 1) * 128],
                wsb[:, kc * D + half * 512: kc * D + (half + 1) * 512],
                start=(kc == 0), stop=(kc == 7)),
                inc=(kc == 7),
                deps=(list(deps0) + [wtok, frees[half]] if kc == 0 else []))
        out.append(mm)
    return out


def build_A():
    p = Prog()
    nc = p.nc
    x_d = nc.dram_tensor("x", [R, D], F32, kind="ExternalInput").ap()
    w_d = {n: nc.dram_tensor("w" + n, [D, D], F32, kind="ExternalInput").ap()
           for n in ("q", "k", "v", "g")}
    wgk1_d = nc.dram_tensor("wgk1", [D, 16], F32, kind="ExternalInput").ap()
    wgk2_d = nc.dram_tensor("wgk2", [16, D], F32, kind="ExternalInput").ap()
    bgk2_d = nc.dram_tensor("bgk2", [1, D], F32, kind="ExternalInput").ap()

    qf_d = nc.dram_tensor("qf", [8, 128, R], BF16, kind="ExternalOutput").ap()
    kf_d = nc.dram_tensor("kf", [8, 128, R], BF16, kind="ExternalOutput").ap()
    vt_d = nc.dram_tensor("vt", [R, D], BF16, kind="ExternalOutput").ap()
    e1_d = nc.dram_tensor("e1", [R, D], F32, kind="ExternalOutput").ap()
    gate_d = nc.dram_tensor("gate", [R, D], BF16, kind="ExternalOutput").ap()
    lc_d = nc.dram_tensor("lc", [NT, 128, 512], BF16, kind="ExternalOutput").ap()
    tseg_d = nc.dram_tensor("tseg", [128, 512], F32, kind="ExternalOutput").ap()
    dseg_d = nc.dram_tensor("dseg", [1, D], F32, kind="ExternalOutput").ap()
    dpre_d = nc.dram_tensor("dpre", [NT, 1, D], F32, kind="ExternalOutput").ap()

    C = setup_consts(p)

    xfp = p.sb("xfp", [128, D], F32)
    xbf = p.sb("xbf", [128, D], BF16)
    xT = p.sb("xT", [128, 8 * R], BF16)
    w_sb = {n: p.sb(f"w_{n}", [128, 8 * D], BF16) for n in ("q", "k", "v", "g")}
    wgk1_sb = p.sb("wgk1_sb", [128, 8 * 16], BF16)
    wgk2_sb = p.sb("wgk2_sb", [16, D], BF16)
    bgk2_row = p.sb("bgk2_row", [1, D], F32)
    bgk2_bc = p.sb("bgk2_bc", [128, D], F32)
    uf = p.sb("uf", [16, R], BF16)
    z_sb = p.sb("z_sb", [128, D], F32)
    tmp1 = p.sb("tmp1", [128, D], F32)
    hi = p.sb("hi", [128, D], BF16)
    lo = p.sb("lo", [128, D], BF16)
    e1_sb = p.sb("e1_sb", [128, D], F32)
    em_sb = p.sb("em_sb", [128, D], F32)
    vt_sb = p.sb("vt_sb", [128, D], BF16)
    ktm_sb = p.sb("ktm_sb", [128, D], BF16)
    gat_sb = p.sb("gat_sb", [128, D], BF16)
    pf_sb = p.sb("pf_sb", [128, 512], BF16)
    trb_sb = p.sb("trb_sb", [128, 128], BF16)
    egc_sb = p.sb("egc_sb", [1, D], F32)
    gc_sb = p.sb("gc_sb", [1, D], F32)
    e3_sb = p.sb("e3_sb", [128, 512], F32)
    s_sb = p.sb("s_sb", [128, 512], F32)
    sbf_sb = p.sb("sbf_sb", [128, 512], BF16)
    dacc = p.sb("dacc", [1, D], F32)

    ps_tr = p.ps("ps_tr", [128, 128], BF16)
    ps_tr2 = p.ps("ps_tr2", [128, 128], BF16)
    ps_a = p.ps("ps_a", [128, 512], F32)
    ps_b = p.ps("ps_b", [128, 512], F32)
    ps_g = p.ps("ps_g", [128, D], F32)
    ps_m = p.ps("ps_m", [128, 512], F32)
    ps_e3 = p.ps("ps_e3", [128, 512], F32)

    wtoks = {}
    for n in ("q", "k", "v", "g"):
        wtoks[n] = p.gp.op(lambda e, n=n: e.dma_start(
            out=w_sb[n][:].rearrange("p (c d) -> p c d", d=D),
            in_=w_d[n].rearrange("(c p) d -> p c d", p=128)))
    wtoks["gk1"] = p.gp.op(lambda e: e.dma_start(
        out=wgk1_sb[:].rearrange("p (c d) -> p c d", d=16),
        in_=wgk1_d.rearrange("(c p) d -> p c d", p=128)))
    wtoks["gk2"] = p.gp.op(lambda e: e.dma_start(out=wgk2_sb[:], in_=wgk2_d))
    t_bgk2 = p.sy.op(lambda e: e.dma_start(out=bgk2_row[:], in_=bgk2_d))

    mm0 = p.te.op(lambda e: e.matmul(ps_a[:], C["ones1_f"][:], bgk2_row[0:1, 0:512],
                                     start=True, stop=True), deps=[C["loaded"], t_bgk2])
    mm1 = p.te.op(lambda e: e.matmul(ps_b[:], C["ones1_f"][:], bgk2_row[0:1, 512:1024],
                                     start=True, stop=True))
    fa = p.ve.op(lambda e: e.tensor_copy(bgk2_bc[:, 0:512], ps_a[:]), deps=[mm0])
    fb = p.ve.op(lambda e: e.tensor_copy(bgk2_bc[:, 512:1024], ps_b[:]), deps=[mm1])
    p.ve.op(lambda e: e.memset(dacc[:], 0.0))
    p.ve.op(lambda e: e.memset(s_sb[:], 0.0))
    p.ve.op(lambda e: e.memset(sbf_sb[:], 0.0))

    for m in range(NT):
        p.barrier()
        t_x = p.sy.op(lambda e, m=m: e.dma_start(out=xfp[:], in_=x_d[m * 128:(m + 1) * 128, :]))
        t_cast = p.ve.op(lambda e: e.tensor_copy(xbf[:], xfp[:]), deps=[t_x])
        evac = [None, None]
        for kc in range(8):
            pst = ps_tr if kc % 2 == 0 else ps_tr2
            t_tr = p.te.op(lambda e, kc=kc, pst=pst: e.transpose(
                pst[:], xbf[:, kc * 128:(kc + 1) * 128], C["ident_bf"][:]),
                deps=[t_cast] + ([evac[kc % 2]] if evac[kc % 2] else []))
            evac[kc % 2] = p.ve.op(lambda e, kc=kc, m=m, pst=pst: e.tensor_copy(
                xT[:, kc * R + m * 128: kc * R + (m + 1) * 128], pst[:]), deps=[t_tr])

    p.barrier()
    fa, fb = p.te.tok(), p.te.tok()
    for tb4 in range(4):
        mm = None
        for kc in range(8):
            mm = p.te.op(lambda e, kc=kc, tb4=tb4: e.matmul(
                ps_a[0:16, :], wgk1_sb[:, kc * 16:(kc + 1) * 16],
                xT[:, kc * R + tb4 * 512:kc * R + (tb4 + 1) * 512],
                start=(kc == 0), stop=(kc == 7)),
                inc=(kc == 7),
                deps=([wtoks["gk1"], fa] if kc == 0 else []))
        fa = p.ve.op(lambda e, tb4=tb4: e.tensor_copy(uf[:, tb4 * 512:(tb4 + 1) * 512],
                                                      ps_a[0:16, :]), deps=[mm])
    uf_done = fa

    pf_free = None
    for pr in range(8):
        for tb4 in range(4):
            mm = None
            for kc in range(8):
                mm = p.te.op(lambda e, kc=kc, tb4=tb4, pr=pr: e.matmul(
                    ps_b[:], pair_view(w_sb["q"][:, kc * D:(kc + 1) * D], pr),
                    xT[:, kc * R + tb4 * 512:kc * R + (tb4 + 1) * 512],
                    start=(kc == 0), stop=(kc == 7)),
                    inc=(kc == 7),
                    deps=([wtoks["q"], fb] if kc == 0 else []))
            t_e = p.ve.op(lambda e: e.tensor_copy(pf_sb[:], ps_b[:]), deps=[mm, pf_free])
            fb = t_e
            t_s = p.sy.op(lambda e, pr=pr, tb4=tb4: e.dma_start(
                out=qf_d[pr, :, tb4 * 512:(tb4 + 1) * 512], in_=pf_sb[:]), deps=[t_e])
            pf_free = t_s

    for m in range(NT):
        p.barrier()
        mmz = []
        for half in range(2):
            pst = ps_a if half == 0 else ps_b
            mmz.append(p.te.op(lambda e, half=half, m=m, pst=pst: e.matmul(
                pst[:], uf[:, m * 128:(m + 1) * 128],
                wgk2_sb[:, half * 512:(half + 1) * 512], start=True, stop=True),
                deps=[uf_done, wtoks["gk2"]]))
        t_zs0 = p.ve.op(lambda e: e.tensor_add(z_sb[:, 0:512], ps_a[:], bgk2_bc[:, 0:512]),
                        deps=[mmz[0]])
        t_zs1 = p.ve.op(lambda e: e.tensor_add(z_sb[:, 512:1024], ps_b[:], bgk2_bc[:, 512:1024]),
                        deps=[mmz[1]])

        t_abs = p.sc.op(lambda e: e.activation(tmp1[:], z_sb[:], AF.Abs),
                        deps=[t_zs0, t_zs1])
        t_ex = p.sc.op(lambda e: e.activation(tmp1[:], tmp1[:], AF.Exp, scale=-1.0), deps=[t_abs])
        t_ln = p.sc.op(lambda e: e.activation(tmp1[:], tmp1[:], AF.Ln, bias=1.0), deps=[t_ex])
        t_mn = p.ve.op(lambda e: e.tensor_scalar(z_sb[:], z_sb[:], 0.0, None, ALU.min),
                       deps=[t_abs])
        t_sub = p.ve.op(lambda e: e.tensor_sub(z_sb[:], z_sb[:], tmp1[:]), deps=[t_ln, t_mn])
        t_g16 = p.ve.op(lambda e: e.tensor_scalar(z_sb[:], z_sb[:], 1.0 / 16.0, None, ALU.mult),
                        deps=[t_sub])
        t_hi = p.ve.op(lambda e: e.tensor_copy(hi[:], z_sb[:]), deps=[t_g16])
        t_lo = p.ve.op(lambda e: e.tensor_sub(lo[:], z_sb[:], hi[:]), deps=[t_hi])

        mmg = []
        for half in range(2):
            mmlast = None
            for j, src in enumerate((hi, lo)):
                mmlast = p.te.op(lambda e, half=half, src=src, j=j: e.matmul(
                    ps_g[:, half * 512:(half + 1) * 512], C["triu_bf"][:],
                    src[:, half * 512:(half + 1) * 512], start=(j == 0), stop=(j == 1)),
                    inc=(j == 1),
                    deps=([t_hi, t_lo] if j == 0 else []))
            mmg.append(mmlast)
        mmgc = []
        for half in range(2):
            pst = ps_a if half == 0 else ps_b
            mmlast = None
            for j, src in enumerate((hi, lo)):
                mmlast = p.te.op(lambda e, half=half, src=src, j=j, pst=pst: e.matmul(
                    pst[0:1, :], C["ones128_bf"][:],
                    src[:, half * 512:(half + 1) * 512], start=(j == 0), stop=(j == 1)),
                    inc=(j == 1),
                    deps=([t_hi, t_lo, t_zs0, t_zs1] if j == 0 else []))
            mmgc.append(mmlast)
        t_dpre = p.sy.op(lambda e, m=m: e.dma_start(out=dpre_d[m, :, :], in_=dacc[:]))
        t_gc0 = p.ve.op(lambda e: e.tensor_copy(gc_sb[0:1, 0:512], ps_a[0:1, :]), deps=[mmgc[0]])
        t_gc1 = p.ve.op(lambda e: e.tensor_copy(gc_sb[0:1, 512:1024], ps_b[0:1, :]), deps=[mmgc[1]])
        t_dacc = p.ve.op(lambda e: e.tensor_add(dacc[:], dacc[:], gc_sb[:]),
                         deps=[t_gc0, t_gc1, t_dpre])
        t_egc = p.sc.op(lambda e: e.activation(egc_sb[:], gc_sb[:], AF.Exp),
                        deps=[t_gc0, t_gc1])

        t_e1 = p.sc.op(lambda e: e.activation(e1_sb[:], ps_g[:], AF.Exp), deps=[mmg[0], mmg[1]])
        t_em = p.sc.op(lambda e: e.activation(em_sb[:], ps_g[:], AF.Exp, scale=-1.0), deps=[t_e1])
        t_e1s = p.sy.op(lambda e, m=m: e.dma_start(out=e1_d[m * 128:(m + 1) * 128, :], in_=e1_sb[:]),
                        deps=[t_e1])

        mmv = proj_tm(p, (ps_a, ps_b), (t_gc0, t_gc1), xT, w_sb["v"], wtoks["v"], m)
        t_vt0 = p.ve.op(lambda e: e.tensor_mul(vt_sb[:, 0:512], ps_a[:], em_sb[:, 0:512]),
                        deps=[mmv[0], t_em])
        t_vt1 = p.ve.op(lambda e: e.tensor_mul(vt_sb[:, 512:1024], ps_b[:], em_sb[:, 512:1024]),
                        deps=[mmv[1], t_em])
        t_vts = p.sy.op(lambda e, m=m: e.dma_start(out=vt_d[m * 128:(m + 1) * 128, :], in_=vt_sb[:]),
                        deps=[t_vt0, t_vt1])

        mmk = proj_tm(p, (ps_a, ps_b), (t_vt0, t_vt1), xT, w_sb["k"], wtoks["k"], m)
        t_k0 = p.ve.op(lambda e: e.tensor_copy(ktm_sb[:, 0:512], ps_a[:]), deps=[mmk[0]])
        t_k1 = p.ve.op(lambda e: e.tensor_copy(ktm_sb[:, 512:1024], ps_b[:]), deps=[mmk[1]])
        evac = [None, None]
        st_prev = [None, None]
        for pr in range(8):
            pst = ps_tr if pr % 2 == 0 else ps_tr2
            t_tr = p.te.op(lambda e, pr=pr, pst=pst: e.transpose(
                pst[:], pair_view(ktm_sb[:], pr), C["ident_bf"][:]),
                deps=[t_k0, t_k1] + ([evac[pr % 2]] if evac[pr % 2] else []))
            t_ke = p.ve.op(lambda e, pst=pst: e.tensor_copy(trb_sb[:], pst[:]),
                           deps=[t_tr] + ([st_prev[0]] if st_prev[0] else []))
            evac[pr % 2] = t_ke
            t_ks = p.sy.op(lambda e, pr=pr, m=m: e.dma_start(
                out=kf_d[pr, :, m * 128:(m + 1) * 128], in_=trb_sb[:]), deps=[t_ke])
            st_prev[0] = t_ks

        mmgat = proj_tm(p, (ps_a, ps_b), (t_k0, t_k1), xT, w_sb["g"], wtoks["g"], m)
        t_ga0 = p.sc.op(lambda e: e.activation(gat_sb[:, 0:512], ps_a[:], AF.Silu), deps=[mmgat[0]])
        t_ga1 = p.sc.op(lambda e: e.activation(gat_sb[:, 512:1024], ps_b[:], AF.Silu), deps=[mmgat[1]])
        t_gas = p.sy.op(lambda e, m=m: e.dma_start(out=gate_d[m * 128:(m + 1) * 128, :], in_=gat_sb[:]),
                        deps=[t_ga0, t_ga1])

        mmM = None
        for h in range(16):
            pr, half = h % 8, h // 8
            tp = (0, 0) if half == 0 else (0, 64)
            mmM = p.te.op(lambda e, h=h, pr=pr, half=half, tp=tp: e.matmul(
                ps_m[64 * half:64 * (half + 1), pr * 64:(pr + 1) * 64],
                ktm_sb[:, ppos(h):ppos(h) + 64], vt_sb[:, h * 64:(h + 1) * 64],
                start=True, stop=True, tile_position=tp),
                inc=(h == 15),
                deps=([t_k0, t_k1, t_vt0, t_vt1] if h == 0 else []))

        t_e3a = p.te.op(lambda e: e.matmul(ps_e3[0:64, :], C["ones1_f"][0:1, 0:64],
                                           egc_sb[0:1, 0:512], start=True, stop=True),
                        deps=[t_egc])
        t_e3b = p.te.op(lambda e: e.matmul(ps_e3[64:128, :], C["ones1_f"][0:1, 0:64],
                                           egc_sb[0:1, 512:1024], start=True, stop=True,
                                           tile_position=(0, 64)))
        t_e3c = p.ve.op(lambda e: e.tensor_copy(e3_sb[:], ps_e3[:]), deps=[t_e3b])

        t_lcs = p.sy.op(lambda e, m=m: e.dma_start(out=lc_d[m, :, :], in_=sbf_sb[:]))
        t_sadd = p.ve.op(lambda e: e.tensor_add(s_sb[:], s_sb[:], ps_m[:]), deps=[mmM, t_e3c])
        t_smul = p.ve.op(lambda e: e.tensor_mul(s_sb[:], s_sb[:], e3_sb[:]), deps=[t_sadd])
        p.ve.op(lambda e: e.tensor_copy(sbf_sb[:], s_sb[:]), deps=[t_smul, t_lcs])

    p.barrier()
    p.sy.op(lambda e: e.dma_start(out=tseg_d, in_=s_sb[:]))
    p.sy.op(lambda e: e.dma_start(out=dseg_d, in_=dacc[:]))
    p.sy.op(lambda e: e.nop(), inc=False, deps=[p.sy.tok(), p.gp.tok()])
    return p.finish()




def bcast_row(p, C, row_ap, ps, dst_sb, deps, n=1024):
    last = None
    nh = (n + 511) // 512
    for half in range(nh):
        w = min(512, n - half * 512)
        mm = p.te.op(lambda e, half=half, w=w: e.matmul(
            ps[:, 0:w], C["ones1_f"][:],
            row_ap[0:1, half * 512:half * 512 + w], start=True, stop=True),
            deps=(list(deps) if half == 0 else []) + ([last] if last else []))
        last = p.ve.op(lambda e, half=half, w=w: e.tensor_copy(
            dst_sb[:, half * 512:half * 512 + w], ps[:, 0:w]),
            deps=[mm])
    return last


def bcast_slayout(p, C, row_ap, ps_e3, e3_sb, deps):
    mm1 = p.te.op(lambda e: e.matmul(ps_e3[0:64, :], C["ones1_f"][0:1, 0:64],
                                     row_ap[0:1, 0:512], start=True, stop=True),
                  deps=deps)
    mm2 = p.te.op(lambda e: e.matmul(ps_e3[64:128, :], C["ones1_f"][0:1, 0:64],
                                     row_ap[0:1, 512:1024], start=True, stop=True,
                                     tile_position=(0, 64)))
    return p.ve.op(lambda e: e.tensor_copy(e3_sb[:], ps_e3[:]), deps=[mm1, mm2])


def ln_rowwise(p, x_ap, st1, st2, deps, eps=1e-5, n=1024):
    t_s = p.ve.op(lambda e: e.reduce_sum(st1[:], x_ap, axis=AX.X), deps=deps)
    t_m = p.ve.op(lambda e: e.tensor_scalar(st1[:], st1[:], 1.0 / n, None, ALU.mult),
                  deps=[t_s])
    t_c = p.ve.op(lambda e: e.tensor_scalar(x_ap, x_ap, st1[:, 0:1], None, ALU.subtract),
                  deps=[t_m])
    t_sq = p.sc.op(lambda e: e.activation(st2[:, 0:n], x_ap, AF.Square,
                                          accum_out=st1[:, 0:1]), deps=[t_c])
    t_v = p.ve.op(lambda e: e.tensor_scalar(st1[:], st1[:], 1.0 / n, eps, ALU.mult, ALU.add),
                  deps=[t_sq])
    t_sq2 = p.sc.op(lambda e: e.activation(st1[:], st1[:], AF.Sqrt), deps=[t_v])
    t_r = p.ve.op(lambda e: e.reciprocal(st1[:], st1[:]), deps=[t_sq2])
    t_n = p.ve.op(lambda e: e.tensor_scalar(x_ap, x_ap, st1[:, 0:1], None, ALU.mult),
                  deps=[t_r])
    return t_n


def xt_build(p, C, src_ap_fn, xT, xbf, ps_tr, ps_tr2, m, deps):
    t_cast = p.ve.op(lambda e, m=m: e.tensor_copy(xbf[:], src_ap_fn(m)), deps=deps)
    evac = [None, None]
    last = None
    for kc in range(8):
        pst = ps_tr if kc % 2 == 0 else ps_tr2
        t_tr = p.te.op(lambda e, kc=kc, pst=pst: e.transpose(
            pst[:], xbf[:, kc * 128:(kc + 1) * 128], C["ident_bf"][:]),
            deps=[t_cast] + ([evac[kc % 2]] if evac[kc % 2] else []))
        last = p.ve.op(lambda e, kc=kc, m=m, pst=pst: e.tensor_copy(
            xT[:, kc * R + m * 128: kc * R + (m + 1) * 128], pst[:]), deps=[t_tr])
        evac[kc % 2] = last
    return last


STOP_B = "full"
ATTN_TILES = 16
ATTN_SUB = 4


def build_B():
    stop = STOP_B
    def done():
        p.barrier()
        for m_ in range(NT):
            p.sy.op(lambda e, m_=m_: e.dma_start(out=x2_d[m_ * 128:(m_ + 1) * 128, :],
                                                 in_=X_sb[:, m_ * D:(m_ + 1) * D]))
        p.barrier()
        p.sy.op(lambda e: e.nop(), inc=False, deps=[p.sy.tok(), p.gp.tok()])
        return p.finish()
    p = Prog()
    nc = p.nc
    x_d = nc.dram_tensor("x", [R, D], F32, kind="ExternalInput").ap()
    qfi_d = nc.dram_tensor("qfi", [8, 128, R], BF16, kind="ExternalInput").ap()
    kfi_d = nc.dram_tensor("kfi", [8, 128, R], BF16, kind="ExternalInput").ap()
    vti_d = nc.dram_tensor("vti", [R, D], BF16, kind="ExternalInput").ap()
    e1i_d = nc.dram_tensor("e1i", [R, D], F32, kind="ExternalInput").ap()
    gati_d = nc.dram_tensor("gati", [R, D], BF16, kind="ExternalInput").ap()
    lci_d = nc.dram_tensor("lci", [NT, 128, 512], BF16, kind="ExternalInput").ap()
    dprei_d = nc.dram_tensor("dprei", [NT, 1, D], F32, kind="ExternalInput").ap()
    tcomp_d = nc.dram_tensor("tcomp", [3, 128, 512], F32, kind="ExternalInput").ap()
    dcomp_d = nc.dram_tensor("dcomp", [3, 1, D], F32, kind="ExternalInput").ap()
    wout_d = nc.dram_tensor("wout", [D, D], F32, kind="ExternalInput").ap()
    gamma_d = nc.dram_tensor("gamma", [1, D], F32, kind="ExternalInput").ap()
    beta_d = nc.dram_tensor("beta", [1, D], F32, kind="ExternalInput").ap()
    gatew_d = nc.dram_tensor("gatew", [D, 8], F32, kind="ExternalInput").ap()
    gateb_d = nc.dram_tensor("gateb", [1, 8], F32, kind="ExternalInput").ap()
    w1n_d = nc.dram_tensor("w1n", [8, D, D], F32, kind="ExternalInput").ap()
    w3n_d = nc.dram_tensor("w3n", [8, D, D], F32, kind="ExternalInput").ap()
    w2n_d = nc.dram_tensor("w2n", [8, D, D], F32, kind="ExternalInput").ap()
    sw1n_d = nc.dram_tensor("sw1n", [D, 2048], F32, kind="ExternalInput").ap()
    sw3n_d = nc.dram_tensor("sw3n", [D, 2048], F32, kind="ExternalInput").ap()
    sw2n_d = nc.dram_tensor("sw2n", [2048, D], F32, kind="ExternalInput").ap()
    wqn_d = nc.dram_tensor("wqn", [D, D], F32, kind="ExternalInput").ap()
    wkvn_d = nc.dram_tensor("wkvn", [D, D], F32, kind="ExternalInput").ap()
    lng_d = nc.dram_tensor("lng", [1, D], F32, kind="ExternalInput").ap()
    lnb_d = nc.dram_tensor("lnb", [1, D], F32, kind="ExternalInput").ap()
    cosf_d = nc.dram_tensor("cosf", [R, D], F32, kind="ExternalInput").ap()
    sinsf_d = nc.dram_tensor("sinsf", [R, D], F32, kind="ExternalInput").ap()
    x2_d = nc.dram_tensor("x2", [R, D], F32, kind="ExternalOutput").ap()
    qtm_d = nc.dram_tensor("qtm", [R, D], BF16, kind="ExternalOutput").ap()
    lkv_d = nc.dram_tensor("lkv", [R, D], BF16, kind="ExternalOutput").ap()

    C = setup_consts(p)

    X_sb = p.sb("X_sb", [128, NT * D], F32)
    xT = p.sb("xT", [128, 8 * R], BF16)
    qfb = p.sb("qfb", [128, 8 * 128], BF16)
    kfb = p.sb("kfb", [128, 8 * 128], BF16)
    vt_sb = p.sb("vt_sb", [128, D], BF16)
    e1_sb = p.sb("e1_sb", [128, D], F32)
    gat_sb = p.sb("gat_sb", [128, D], BF16)
    lc_sb = p.sb("lc_sb", [128, 512], BF16)
    dpre_sb = p.sb("dpre_sb", [1, D], F32)
    e3_sb = p.sb("e3_sb", [128, 512], F32)
    sin_sb = p.sb("sin_sb", [128, 512], F32)
    scw_sb = p.sb("scw_sb", [128, 512], F32)
    tc_sb = scw_sb
    scb_sb = p.sb("scb_sb", [128, 512], BF16)
    am_sb = p.sb("am_sb", [128, 2 * 128], BF16)
    o_sb = p.sb("o_sb", [128, D], F32)
    st1 = p.sb("st1", [128, 1], F32)
    st2 = p.sb("st2", [128, D], F32)
    row_sb = p.sb("row_sb", [1, D], F32)
    gamma_row = beta_row = row_sb
    gamma_bc = p.sb("gamma_bc", [128, D], F32)
    beta_bc = p.sb("beta_bc", [128, D], F32)
    z_sb = p.sb("z_sb", [128, D], BF16)
    xbf = z_sb
    trb_sb = p.sb("trb_sb", [128, 128], BF16)
    gatew_sb = p.sb("gatew_sb", [128, 8 * 8], BF16)
    gateb_row = p.sb("gateb_row", [1, 8], F32)
    gateb_bc = p.sb("gateb_bc", [128, 8], F32)
    gates_sb = p.sb("gates_sb", [128, NT * 8], F32)
    sg1 = p.sb("sg1", [128, 8], F32)
    sg2 = p.sb("sg2", [128, 8], F32)
    sg3 = p.sb("sg3", [128, 8], F32)
    w1_sb = p.sb("w1_sb", [128, 8 * D], BF16)
    w3_sb = p.sb("w3_sb", [128, 8 * D], BF16)
    w2_sb = p.sb("w2_sb", [128, 8 * D], BF16)
    hf_sb = p.sb("hf_sb", [128, 8 * 512], BF16)
    sl_sb = p.sb("sl_sb", [128, 512], BF16)
    wout_sb = w2_sb
    lng_bc = gamma_bc
    lnb_bc = beta_bc
    lng_row = lnb_row = row_sb
    cos_sb = e1_sb
    sins_sb = o_sb
    rp1 = None
    rp2 = st2
    ot_sb = p.sb("ot_sb", [128, D], BF16)
    stat16a = p.sb("stat16a", [128, 16], F32)
    stat16b = p.sb("stat16b", [128, 16], F32)

    rp1 = hf_sb[:].bitcast(F32)[:, 0:D]

    ps_tr = p.ps("ps_tr", [128, 128], BF16)
    ps_tr2 = p.ps("ps_tr2", [128, 128], BF16)
    ps_a = p.ps("ps_a", [128, 512], F32)
    ps_b = p.ps("ps_b", [128, 512], F32)
    ps_o = p.ps("ps_o", [128, D], F32)
    ps_e3 = p.ps("ps_e3", [128, 512], F32)

    t_x = None
    for m in range(NT):
        t_x = p.sy.op(lambda e, m=m: e.dma_start(out=X_sb[:, m * D:(m + 1) * D],
                                                 in_=x_d[m * 128:(m + 1) * 128, :]))
    pass
    t_gam = p.sy.op(lambda e: e.dma_start(out=gamma_row[:], in_=gamma_d))
    t_gbc = bcast_row(p, C, gamma_row, ps_e3, gamma_bc, [t_gam, C["loaded"]])
    t_bet = p.sy.op(lambda e: e.dma_start(out=beta_row[:], in_=beta_d), deps=[t_gbc])
    t_bbc = bcast_row(p, C, beta_row, ps_e3, beta_bc, [t_bet])
    t_wout = p.gp.op(lambda e: e.dma_start(
        out=wout_sb[:].rearrange("p (c d) -> p c d", d=D),
        in_=wout_d.rearrange("(c p) d -> p c d", p=128)))

    t_prev = p.ve.op(lambda e: e.memset(sin_sb[:], 0.0))
    for j in range(3):
        t_dc = p.sy.op(lambda e, j=j: e.dma_start(out=dpre_sb[:], in_=dcomp_d[j]),
                       deps=[t_prev])
        t_tc = p.sy.op(lambda e, j=j: e.dma_start(out=tc_sb[:], in_=tcomp_d[j]),
                       deps=[t_prev])
        t_ex = p.sc.op(lambda e: e.activation(dpre_sb[:], dpre_sb[:], AF.Exp),
                       deps=[t_dc, C["loaded"]])
        t_e3 = bcast_slayout(p, C, dpre_sb, ps_e3, e3_sb, [t_ex, t_gbc, t_bbc])
        t_m = p.ve.op(lambda e: e.tensor_mul(sin_sb[:], sin_sb[:], e3_sb[:]),
                      deps=[t_e3, t_prev])
        t_prev = p.ve.op(lambda e: e.tensor_add(sin_sb[:], sin_sb[:], tc_sb[:]),
                         deps=[t_m, t_tc])
    sin_done = t_prev
    if stop == "compose":
        return done()

    for m in range(min(NT, ATTN_TILES)):
        p.barrier()
        t_qf = p.sy.op(lambda e, m=m: e.dma_start(
            out=qfb[:].rearrange("p (r c) -> p r c", c=128),
            in_=qfi_d[:, :, m * 128:(m + 1) * 128].rearrange("r p c -> p r c")))
        t_kf = p.sy.op(lambda e, m=m: e.dma_start(
            out=kfb[:].rearrange("p (r c) -> p r c", c=128),
            in_=kfi_d[:, :, m * 128:(m + 1) * 128].rearrange("r p c -> p r c")))
        t_vt = p.sy.op(lambda e, m=m: e.dma_start(out=vt_sb[:], in_=vti_d[m * 128:(m + 1) * 128, :]))
        t_e1 = p.sy.op(lambda e, m=m: e.dma_start(out=e1_sb[:], in_=e1i_d[m * 128:(m + 1) * 128, :]))
        t_ga = p.sy.op(lambda e, m=m: e.dma_start(out=gat_sb[:], in_=gati_d[m * 128:(m + 1) * 128, :]))
        t_lc = p.sy.op(lambda e, m=m: e.dma_start(out=lc_sb[:], in_=lci_d[m]))
        t_dp = p.sy.op(lambda e, m=m: e.dma_start(out=dpre_sb[:], in_=dprei_d[m]))

        t_ex = p.sc.op(lambda e: e.activation(dpre_sb[:], dpre_sb[:], AF.Exp),
                       deps=[t_dp, sin_done])
        t_e3 = bcast_slayout(p, C, dpre_sb, ps_e3, e3_sb, [t_ex])
        t_sc1 = p.ve.op(lambda e: e.tensor_mul(scw_sb[:], e3_sb[:], sin_sb[:]), deps=[t_e3])
        t_sc2 = p.ve.op(lambda e: e.tensor_add(scw_sb[:], scw_sb[:], lc_sb[:]),
                        deps=[t_sc1, t_lc])
        t_scb = p.ve.op(lambda e: e.tensor_copy(scb_sb[:], scw_sb[:]), deps=[t_sc2])
        if ATTN_SUB < 1:
            continue

        mask_prev = None
        mm_last = None
        for pr in range(8):
            mmA = p.te.op(lambda e, pr=pr: e.matmul(
                ps_a[:, 0:128], kfb[0:64, pr * 128:(pr + 1) * 128],
                qfb[0:64, pr * 128:(pr + 1) * 128], start=True, stop=True,
                tile_position=(0, 0)),
                deps=([t_qf, t_kf, t_vt, t_scb] if pr == 0 else [mask_prev]))
            mmA2 = p.te.op(lambda e, pr=pr: e.matmul(
                ps_b[:, 0:128], kfb[64:128, pr * 128:(pr + 1) * 128],
                qfb[64:128, pr * 128:(pr + 1) * 128], start=True, stop=True,
                tile_position=(64, 0)))
            t_m1 = p.ve.op(lambda e: e.tensor_mul(am_sb[:, 0:128], ps_a[:, 0:128],
                                                  C["triu_f"][:]), deps=[mmA, mmA2])
            t_m2 = p.ve.op(lambda e: e.tensor_mul(am_sb[:, 128:256], ps_b[:, 0:128],
                                                  C["triu_f"][:]), deps=[mmA2])
            mask_prev = t_m2
            if ATTN_SUB < 2:
                mm_last = mmA2
                continue
            p.te.op(lambda e, pr=pr: e.matmul(
                ps_o[:, pr * 64:(pr + 1) * 64], am_sb[:, 0:128],
                vt_sb[:, pr * 64:(pr + 1) * 64], start=True, stop=False),
                inc=False, deps=[t_m1, t_m2])
            p.te.op(lambda e, pr=pr: e.matmul(
                ps_o[:, pr * 64:(pr + 1) * 64], qfb[0:64, pr * 128:(pr + 1) * 128],
                scb_sb[0:64, pr * 64:(pr + 1) * 64], start=False, stop=True,
                tile_position=(0, 0)), inc=False)
            p.te.op(lambda e, pr=pr: e.matmul(
                ps_o[:, 512 + pr * 64:512 + (pr + 1) * 64], am_sb[:, 128:256],
                vt_sb[:, 512 + pr * 64:512 + (pr + 1) * 64], start=True, stop=False),
                inc=False)
            mm_last = p.te.op(lambda e, pr=pr: e.matmul(
                ps_o[:, 512 + pr * 64:512 + (pr + 1) * 64],
                qfb[64:128, pr * 128:(pr + 1) * 128],
                scb_sb[64:128, pr * 64:(pr + 1) * 64], start=False, stop=True,
                tile_position=(64, 0)))

        if ATTN_SUB < 3:
            if ATTN_SUB >= 2:
                p.ve.op(lambda e: e.tensor_copy(o_sb[:], ps_o[:]), deps=[mm_last])
            continue
        t_o = p.ve.op(lambda e: e.tensor_mul(o_sb[:], ps_o[:], e1_sb[:]),
                      deps=[mm_last, t_e1])
        t_ln = ln_rowwise(p, o_sb[:], st1, st2, [t_o])
        t_a1 = p.ve.op(lambda e: e.tensor_mul(o_sb[:], o_sb[:], gamma_bc[:]),
                       deps=[t_ln, t_gbc])
        t_a2 = p.ve.op(lambda e: e.tensor_add(o_sb[:], o_sb[:], beta_bc[:]),
                       deps=[t_a1, t_bbc])
        t_zz = p.ve.op(lambda e: e.tensor_mul(z_sb[:], o_sb[:], gat_sb[:]),
                       deps=[t_a2, t_ga])

        if ATTN_SUB < 4:
            continue
        mml = None
        evac = [None, None]
        for kc in range(8):
            pst = ps_tr if kc % 2 == 0 else ps_tr2
            t_tr = p.te.op(lambda e, kc=kc, pst=pst: e.transpose(
                pst[:], z_sb[:, kc * 128:(kc + 1) * 128], C["ident_bf"][:]),
                deps=[t_zz] + ([evac[kc % 2]] if evac[kc % 2] else []))
            t_ev = p.ve.op(lambda e, pst=pst: e.tensor_copy(trb_sb[:], pst[:]),
                           deps=[t_tr] + ([mml] if mml else [t_o]))
            evac[kc % 2] = t_ev
            for half in range(2):
                mml = p.te.op(lambda e, kc=kc, half=half: e.matmul(
                    ps_o[:, half * 512:(half + 1) * 512], trb_sb[:],
                    wout_sb[:, kc * D + half * 512:kc * D + (half + 1) * 512],
                    start=(kc == 0), stop=(kc == 7)),
                    inc=True,
                    deps=([t_ev, t_wout] if half == 0 else [t_ev]))
        t_res = p.ve.op(lambda e, m=m: e.tensor_add(X_sb[:, m * D:(m + 1) * D],
                                                    X_sb[:, m * D:(m + 1) * D], ps_o[:]),
                        deps=[mml, t_x])
        ln_rowwise(p, X_sb[:, m * D:(m + 1) * D], st1, st2, [t_res])

    if stop == "attn":
        return done()
    p.barrier()
    for m in range(NT):
        p.barrier()
        xt_build(p, C, lambda mm: X_sb[:, mm * D:(mm + 1) * D], xT, xbf, ps_tr, ps_tr2, m, [])

    if stop == "xt1":
        return done()
    p.barrier()
    t_gw = p.gp.op(lambda e: e.dma_start(
        out=gatew_sb[:].rearrange("p (c d) -> p c d", d=8),
        in_=gatew_d.rearrange("(c p) d -> p c d", p=128)))
    t_gb = p.sy.op(lambda e: e.dma_start(out=gateb_row[:], in_=gateb_d))
    t_gbb = bcast_row(p, C, gateb_row, ps_e3, gateb_bc, [t_gb], n=8)
    for m in range(NT):
        mm = None
        for kc in range(8):
            mm = p.te.op(lambda e, kc=kc, m=m: e.matmul(
                ps_a[:, 0:8], xT[:, kc * R + m * 128:kc * R + (m + 1) * 128],
                gatew_sb[:, kc * 8:(kc + 1) * 8], start=(kc == 0), stop=(kc == 7)),
                inc=(kc == 7), deps=([t_gw, (p.ve, p.ve.n)] if kc == 0 else []))
        t_mx = p.ve.op(lambda e: e.reduce_max(st1[:], ps_a[:, 0:8], axis=AX.X, negate=True),
                       deps=[mm])
        t_ep = p.sc.op(lambda e: e.activation(sg1[:], ps_a[:, 0:8], AF.Exp,
                                              bias=st1[:, 0:1], accum_out=st2[:, 0:1]),
                       deps=[t_mx])
        t_rc = p.ve.op(lambda e: e.reciprocal(st2[:, 0:1], st2[:, 0:1]), deps=[t_ep])
        t_sc = p.ve.op(lambda e, m=m: e.tensor_scalar(gates_sb[:, m * 8:(m + 1) * 8],
                                                      sg1[:], st2[:, 0:1], None, ALU.mult),
                       deps=[t_rc])
        t_sb = p.ve.op(lambda e, m=m: e.tensor_add(sg1[:], gates_sb[:, m * 8:(m + 1) * 8],
                                                   gateb_bc[:]), deps=[t_sc, t_gbb])
        t_n1 = p.ve.op(lambda e: e.reduce_max(st1[:], sg1[:], axis=AX.X, negate=True),
                       deps=[t_sb])
        t_s1 = p.sc.op(lambda e: e.activation(sg2[:], sg1[:], AF.Sign, bias=st1[:, 0:1]),
                       deps=[t_n1])
        t_p1 = p.ve.op(lambda e: e.tensor_scalar(sg2[:], sg2[:], 1.0, None, ALU.add),
                       deps=[t_s1])
        t_mk = p.ve.op(lambda e: e.scalar_tensor_tensor(sg1[:], sg2[:], -1e30, sg1[:],
                                                        ALU.mult, ALU.add), deps=[t_p1])
        t_n2 = p.ve.op(lambda e: e.reduce_max(st1[:], sg1[:], axis=AX.X, negate=True),
                       deps=[t_mk])
        t_s2 = p.sc.op(lambda e: e.activation(sg3[:], sg1[:], AF.Sign, bias=st1[:, 0:1]),
                       deps=[t_n2])
        t_p2 = p.ve.op(lambda e: e.tensor_scalar(sg3[:], sg3[:], 1.0, None, ALU.add),
                       deps=[t_s2])
        t_ms = p.ve.op(lambda e: e.tensor_add(sg2[:], sg2[:], sg3[:]), deps=[t_p2])
        p.ve.op(lambda e, m=m: e.tensor_mul(gates_sb[:, m * 8:(m + 1) * 8],
                                            gates_sb[:, m * 8:(m + 1) * 8], sg2[:]),
                deps=[t_ms])

    if stop == "gating":
        return done()
    prev_sl = None
    prev_sto = None
    for ex in range(8):
        p.barrier()
        t_w1 = p.gp.op(lambda e, ex=ex: e.dma_start(
            out=w1_sb[:].rearrange("p (c d) -> p c d", d=D),
            in_=w1n_d[ex].rearrange("(c p) d -> p c d", p=128)))
        t_w3 = p.gp.op(lambda e, ex=ex: e.dma_start(
            out=w3_sb[:].rearrange("p (c d) -> p c d", d=D),
            in_=w3n_d[ex].rearrange("(c p) d -> p c d", p=128)))
        t_w2 = p.gp.op(lambda e, ex=ex: e.dma_start(
            out=w2_sb[:].rearrange("p (c d) -> p c d", d=D),
            in_=w2n_d[ex].rearrange("(c p) d -> p c d", p=128)))
        for tg in range(4):
            for i in range(8):
                mm1 = None
                for kc in range(8):
                    mm1 = p.te.op(lambda e, kc=kc, i=i, tg=tg: e.matmul(
                        ps_a[:], w1_sb[:, kc * D + i * 128:kc * D + (i + 1) * 128],
                        xT[:, kc * R + tg * 512:kc * R + (tg + 1) * 512],
                        start=(kc == 0), stop=(kc == 7)),
                        inc=(kc == 7),
                        deps=([t_w1] + ([prev_sl] if prev_sl else [(p.ve, p.ve.n)])
                              if kc == 0 else []))
                t_sl = p.sc.op(lambda e: e.activation(sl_sb[:], ps_a[:], AF.Silu),
                               deps=[mm1])
                prev_sl = t_sl
                mm3 = None
                for kc in range(8):
                    mm3 = p.te.op(lambda e, kc=kc, i=i, tg=tg: e.matmul(
                        ps_b[:], w3_sb[:, kc * D + i * 128:kc * D + (i + 1) * 128],
                        xT[:, kc * R + tg * 512:kc * R + (tg + 1) * 512],
                        start=(kc == 0), stop=(kc == 7)),
                        inc=(kc == 7),
                        deps=([t_w3, t_sl] if kc == 0 else []))
                p.ve.op(lambda e, i=i: e.tensor_mul(hf_sb[:, i * 512:(i + 1) * 512],
                                                    ps_b[:], sl_sb[:]),
                        deps=[mm3, t_sl])
            hf_done = (p.ve, p.ve.n)
            for tt in range(4):
                mty = m_ = tg * 4 + tt
                mm = None
                for dh in range(2):
                    for i in range(8):
                        mm = p.te.op(lambda e, i=i, dh=dh, tt=tt: e.matmul(
                            ps_o[:, dh * 512:(dh + 1) * 512],
                            hf_sb[:, i * 512 + tt * 128:i * 512 + (tt + 1) * 128],
                            w2_sb[:, i * D + dh * 512:i * D + (dh + 1) * 512],
                            start=(i == 0), stop=(i == 7)),
                            inc=(i == 7),
                            deps=([hf_done, t_w2]
                                  + ([prev_sto] if prev_sto else [(p.ve, p.ve.n)])
                                  if (i == 0 and dh == 0) else []))
                prev_sto = p.ve.op(lambda e, m_=m_, ex=ex: e.scalar_tensor_tensor(
                    X_sb[:, m_ * D:(m_ + 1) * D], ps_o[:],
                    gates_sb[:, m_ * 8 + ex:m_ * 8 + ex + 1],
                    X_sb[:, m_ * D:(m_ + 1) * D], ALU.mult, ALU.add),
                    deps=[mm])

    if stop == "experts":
        return done()
    for ic in range(4):
        p.barrier()
        t_s1w = p.gp.op(lambda e, ic=ic: e.dma_start(
            out=w1_sb[:, 0:8 * 512].rearrange("p (c d) -> p c d", d=512),
            in_=sw1n_d[:, ic * 512:(ic + 1) * 512].rearrange("(c p) d -> p c d", p=128)))
        t_s3w = p.gp.op(lambda e, ic=ic: e.dma_start(
            out=w3_sb[:, 0:8 * 512].rearrange("p (c d) -> p c d", d=512),
            in_=sw3n_d[:, ic * 512:(ic + 1) * 512].rearrange("(c p) d -> p c d", p=128)))
        t_s2w = p.gp.op(lambda e, ic=ic: e.dma_start(
            out=w2_sb[:, 0:4 * D].rearrange("p (c d) -> p c d", d=D),
            in_=sw2n_d[ic * 512:(ic + 1) * 512, :].rearrange("(c p) d -> p c d", p=128)))
        for tg in range(4):
            for i in range(4):
                mm1 = None
                for kc in range(8):
                    mm1 = p.te.op(lambda e, kc=kc, i=i, tg=tg: e.matmul(
                        ps_a[:], w1_sb[:, kc * 512 + i * 128:kc * 512 + (i + 1) * 128],
                        xT[:, kc * R + tg * 512:kc * R + (tg + 1) * 512],
                        start=(kc == 0), stop=(kc == 7)),
                        inc=(kc == 7),
                        deps=([t_s1w] + ([prev_sl] if prev_sl else [(p.ve, p.ve.n)])
                              if kc == 0 else []))
                t_sl = p.sc.op(lambda e: e.activation(sl_sb[:], ps_a[:], AF.Silu),
                               deps=[mm1])
                prev_sl = t_sl
                mm3 = None
                for kc in range(8):
                    mm3 = p.te.op(lambda e, kc=kc, i=i, tg=tg: e.matmul(
                        ps_b[:], w3_sb[:, kc * 512 + i * 128:kc * 512 + (i + 1) * 128],
                        xT[:, kc * R + tg * 512:kc * R + (tg + 1) * 512],
                        start=(kc == 0), stop=(kc == 7)),
                        inc=(kc == 7),
                        deps=([t_s3w, t_sl] if kc == 0 else []))
                p.ve.op(lambda e, i=i: e.tensor_mul(hf_sb[:, i * 512:(i + 1) * 512],
                                                    ps_b[:], sl_sb[:]),
                        deps=[mm3, t_sl])
            hf_done = (p.ve, p.ve.n)
            for tt in range(4):
                m_ = tg * 4 + tt
                mm = None
                for dh in range(2):
                    for i in range(4):
                        mm = p.te.op(lambda e, i=i, dh=dh, tt=tt: e.matmul(
                            ps_o[:, dh * 512:(dh + 1) * 512],
                            hf_sb[:, i * 512 + tt * 128:i * 512 + (tt + 1) * 128],
                            w2_sb[:, i * D + dh * 512:i * D + (dh + 1) * 512],
                            start=(i == 0), stop=(i == 3)),
                            inc=(i == 3),
                            deps=([hf_done, t_s2w]
                                  + ([prev_sto] if prev_sto else [(p.ve, p.ve.n)])
                                  if (i == 0 and dh == 0) else []))
                prev_sto = p.ve.op(lambda e, m_=m_: e.tensor_add(
                    X_sb[:, m_ * D:(m_ + 1) * D], X_sb[:, m_ * D:(m_ + 1) * D], ps_o[:]),
                    deps=[mm])

    if stop == "shared":
        return done()
    p.barrier()
    t_prevln = None
    for m in range(NT):
        t_n = ln_rowwise(p, X_sb[:, m * D:(m + 1) * D], st1, st2,
                         [t_prevln] if t_prevln else [])
        t_prevln = t_n
        p.sy.op(lambda e, m=m: e.dma_start(out=x2_d[m * 128:(m + 1) * 128, :],
                                           in_=X_sb[:, m * D:(m + 1) * D]), deps=[t_n])

    if stop == "ln2":
        return done()
    p.barrier()
    for m in range(NT):
        p.barrier()
        xt_build(p, C, lambda mm: X_sb[:, mm * D:(mm + 1) * D], xT, xbf, ps_tr, ps_tr2, m, [])
    p.barrier()
    wq_sb, wkv_sb = w1_sb, w3_sb
    t_wq = p.gp.op(lambda e: e.dma_start(
        out=wq_sb[:].rearrange("p (c d) -> p c d", d=D),
        in_=wqn_d.rearrange("(c p) d -> p c d", p=128)))
    t_wkv = p.gp.op(lambda e: e.dma_start(
        out=wkv_sb[:].rearrange("p (c d) -> p c d", d=D),
        in_=wkvn_d.rearrange("(c p) d -> p c d", p=128)))
    t_lg = p.sy.op(lambda e: e.dma_start(out=lng_row[:], in_=lng_d))
    t_lgb = bcast_row(p, C, lng_row, ps_e3, lng_bc, [t_lg])
    t_lb = p.sy.op(lambda e: e.dma_start(out=lnb_row[:], in_=lnb_d), deps=[t_lgb])
    t_lbb = bcast_row(p, C, lnb_row, ps_e3, lnb_bc, [t_lb])

    for m in range(NT):
        p.barrier()
        t_cos = p.sy.op(lambda e, m=m: e.dma_start(out=cos_sb[:], in_=cosf_d[m * 128:(m + 1) * 128, :]))
        t_sin = p.sy.op(lambda e, m=m: e.dma_start(out=sins_sb[:], in_=sinsf_d[m * 128:(m + 1) * 128, :]))
        t_store_prev = None
        for which, wsb, wtok, outd in (("q", wq_sb, t_wq, qtm_d), ("kv", wkv_sb, t_wkv, lkv_d)):
            mmp = proj_tm(p, (ps_a, ps_b), ((p.ve, p.ve.n), (p.ve, p.ve.n)), xT, wsb, wtok, m)
            t_r1a = p.ve.op(lambda e: e.tensor_mul(rp1[:, 0:512], ps_a[:], cos_sb[:, 0:512]),
                            deps=[mmp[0], t_cos])
            t_r1b = p.ve.op(lambda e: e.tensor_mul(rp1[:, 512:1024], ps_b[:], cos_sb[:, 512:1024]),
                            deps=[mmp[1], t_cos])
            t_r2a = p.ve.op(lambda e: e.tensor_copy(
                rp2[:, 0:512].rearrange("p (h s x) -> p h s x", h=8, s=2)[:, :, 0],
                ps_a[:].rearrange("p (h s x) -> p h s x", h=8, s=2)[:, :, 1]),
                deps=[mmp[0]])
            t_r2b = p.ve.op(lambda e: e.tensor_copy(
                rp2[:, 0:512].rearrange("p (h s x) -> p h s x", h=8, s=2)[:, :, 1],
                ps_a[:].rearrange("p (h s x) -> p h s x", h=8, s=2)[:, :, 0]))
            t_r2c = p.ve.op(lambda e: e.tensor_copy(
                rp2[:, 512:1024].rearrange("p (h s x) -> p h s x", h=8, s=2)[:, :, 0],
                ps_b[:].rearrange("p (h s x) -> p h s x", h=8, s=2)[:, :, 1]))
            t_r2d = p.ve.op(lambda e: e.tensor_copy(
                rp2[:, 512:1024].rearrange("p (h s x) -> p h s x", h=8, s=2)[:, :, 1],
                ps_b[:].rearrange("p (h s x) -> p h s x", h=8, s=2)[:, :, 0]))
            t_r2m = p.ve.op(lambda e: e.tensor_mul(rp2[:], rp2[:], sins_sb[:]),
                            deps=[t_r2a, t_r2b, t_r2c, t_r2d, t_sin])
            t_rs = p.ve.op(lambda e: e.tensor_add(rp1[:], rp1[:], rp2[:]),
                           deps=[t_r1a, t_r1b, t_r2m])
            if which == "q":
                t_f = p.ve.op(lambda e: e.tensor_scalar(ot_sb[:], rp1[:], 0.125, None, ALU.mult),
                              deps=[t_rs] + ([t_store_prev] if t_store_prev else []))
            else:
                t_mu = p.ve.op(lambda e: e.reduce_sum(
                    stat16a[:], rp1[:].rearrange("p (h x) -> p h x", x=64), axis=AX.X),
                    deps=[t_rs])
                t_mu2 = p.ve.op(lambda e: e.tensor_scalar(stat16a[:], stat16a[:], 1.0 / 64.0,
                                                          None, ALU.mult), deps=[t_mu])
                tsu = t_mu2
                for h in range(16):
                    tsu = p.ve.op(lambda e, h=h: e.tensor_scalar(
                        rp1[:, h * 64:(h + 1) * 64], rp1[:, h * 64:(h + 1) * 64],
                        stat16a[:, h:h + 1], None, ALU.subtract), deps=[tsu])
                t_sq = p.sc.op(lambda e: e.activation(st2[:], rp1[:], AF.Square), deps=[tsu])
                t_vv = p.ve.op(lambda e: e.reduce_sum(
                    stat16b[:], st2[:].rearrange("p (h x) -> p h x", x=64), axis=AX.X),
                    deps=[t_sq])
                t_vv2 = p.ve.op(lambda e: e.tensor_scalar(stat16b[:], stat16b[:], 1.0 / 64.0,
                                                          1e-5, ALU.mult, ALU.add), deps=[t_vv])
                t_sq2 = p.sc.op(lambda e: e.activation(stat16b[:], stat16b[:], AF.Sqrt),
                                deps=[t_vv2])
                t_rr = p.ve.op(lambda e: e.reciprocal(stat16b[:], stat16b[:]), deps=[t_sq2])
                tsu = t_rr
                for h in range(16):
                    tsu = p.ve.op(lambda e, h=h: e.tensor_scalar(
                        rp1[:, h * 64:(h + 1) * 64], rp1[:, h * 64:(h + 1) * 64],
                        stat16b[:, h:h + 1], None, ALU.mult), deps=[tsu])
                t_af1 = p.ve.op(lambda e: e.tensor_mul(rp1[:], rp1[:], lng_bc[:]),
                                deps=[tsu, t_lgb])
                t_af2 = p.ve.op(lambda e: e.tensor_add(rp1[:], rp1[:], lnb_bc[:]),
                                deps=[t_af1, t_lbb])
                t_f = p.ve.op(lambda e: e.tensor_copy(ot_sb[:], rp1[:]),
                              deps=[t_af2] + ([t_store_prev] if t_store_prev else []))
            t_store_prev = p.sy.op(lambda e, m=m, outd=outd: e.dma_start(
                out=outd[m * 128:(m + 1) * 128, :], in_=ot_sb[:]), deps=[t_f])

    p.barrier()
    p.sy.op(lambda e: e.nop(), inc=False, deps=[p.sy.tok(), p.gp.tok()])
    return p.finish()


_progs = {}


def get_prog(name):
    if name not in _progs:
        _progs[name] = {"A": build_A, "B": build_B, "C": build_C, "D": build_D, "E": build_E}[name]()
    return _progs[name]


def host_inputs_A(inputs, core_slices):
    maps = []
    for c in range(NCORE):
        maps.append({
            "x": core_slices[c],
            "wq": f32c(np.asarray(inputs["L0_attn_Wq"])[:, PAIR_PERM]),
            "wk": f32c(np.asarray(inputs["L0_attn_Wk"])[:, PAIR_PERM]),
            "wv": f32c(inputs["L0_attn_Wv"]),
            "wg": f32c(inputs["L0_attn_Wg"]),
            "wgk1": f32c(inputs["L0_attn_Wgk1"]),
            "wgk2": f32c(inputs["L0_attn_Wgk2"]),
            "bgk2": f32c(inputs["L0_attn_bgk2"]).reshape(1, D),
        })
    return maps


def run_A(inputs, core_slices):
    nc = get_prog("A")
    res = run_bass_kernel_spmd(nc, host_inputs_A(inputs, core_slices),
                               core_ids=list(range(NCORE)))
    return res.results


def rope_tables(positions):
    dh = 64
    inv = 1.0 / (10000.0 ** (np.arange(0, dh, 2, dtype=np.float64) / dh))
    fr = positions[:, None].astype(np.float64) * inv[None, :]
    emb = np.concatenate([fr, fr], axis=-1)
    cos = np.cos(emb).astype(np.float32)
    sin = np.sin(emb).astype(np.float32)
    sinS = sin.copy()
    sinS[:, 0:32] *= -1.0
    cosF = np.tile(cos, (1, 16))
    sinSF = np.tile(sinS, (1, 16))
    return f32c(cosF), f32c(sinSF)


def host_inputs_B(inputs, core_slices, resA):
    w1 = np.asarray(inputs["L0_ffn_w1"])
    w3 = np.asarray(inputs["L0_ffn_w3"])
    w2 = np.asarray(inputs["L0_ffn_w2"])
    w1n = f32c(np.transpose(w1, (0, 2, 1)))
    w3n = f32c(np.transpose(w3, (0, 2, 1)))
    w2n = f32c(np.transpose(w2, (0, 2, 1)))
    sw1n = f32c(np.asarray(inputs["L0_ffn_sw1"]).T)
    sw3n = f32c(np.asarray(inputs["L0_ffn_sw3"]).T)
    sw2n = f32c(np.asarray(inputs["L0_ffn_sw2"]).T)
    gatew = f32c(np.asarray(inputs["L0_ffn_gate_w"]).T)
    wqn = f32c(np.asarray(inputs["L1_attn_Wq"]).T)
    wkvn = f32c(np.asarray(inputs["L1_attn_Wkv"]).T)
    lng = f32c(np.tile(np.asarray(inputs["L1_attn_ln_g"]), 16)).reshape(1, D)
    lnb = f32c(np.tile(np.asarray(inputs["L1_attn_ln_b"]), 16)).reshape(1, D)
    maps = []
    for c in range(NCORE):
        b, s = c // 4, c % 4
        pos = np.arange(s * R, (s + 1) * R)
        cosF, sinSF = rope_tables(pos)
        tcomp = np.zeros((3, 128, 512), np.float32)
        dcomp = np.zeros((3, 1, D), np.float32)
        for j in range(s):
            pc = b * 4 + j
            tcomp[j] = np.asarray(resA[pc]["tseg"], np.float32)
            dcomp[j] = np.asarray(resA[pc]["dseg"], np.float32).reshape(1, D)
        maps.append({
            "x": core_slices[c],
            "qfi": np.asarray(resA[c]["qf"]),
            "kfi": np.asarray(resA[c]["kf"]),
            "vti": np.asarray(resA[c]["vt"]),
            "e1i": np.asarray(resA[c]["e1"]),
            "gati": np.asarray(resA[c]["gate"]),
            "lci": np.asarray(resA[c]["lc"]),
            "dprei": np.asarray(resA[c]["dpre"]),
            "tcomp": tcomp,
            "dcomp": dcomp,
            "wout": f32c(inputs["L0_attn_Wout"]),
            "gamma": f32c(inputs["L0_attn_gamma"]).reshape(1, D),
            "beta": f32c(inputs["L0_attn_beta"]).reshape(1, D),
            "gatew": gatew,
            "gateb": f32c(inputs["L0_ffn_gate_bias"]).reshape(1, 8),
            "w1n": w1n, "w3n": w3n, "w2n": w2n,
            "sw1n": sw1n, "sw3n": sw3n, "sw2n": sw2n,
            "wqn": wqn, "wkvn": wkvn,
            "lng": lng, "lnb": lnb,
            "cosf": cosF, "sinsf": sinSF,
        })
    return maps


def run_B(inputs, core_slices, resA):
    nc = get_prog("B")
    res = run_bass_kernel_spmd(nc, host_inputs_B(inputs, core_slices, resA),
                               core_ids=list(range(NCORE)))
    return res.results




C_HEADS = 4
C_TBS = 16
C_SUB = 3
C_NOMASK = False
C_NSCAP = 99
C_NORMCAP = 99


def build_C():
    p = Prog()
    nc = p.nc
    qsl_d = nc.dram_tensor("qsl", [L, 256], BF16, kind="ExternalInput").ap()
    ksl_d = nc.dram_tensor("ksl", [L, 256], BF16, kind="ExternalInput").ap()
    of_d = nc.dram_tensor("of", [256, L], BF16, kind="ExternalOutput").ap()

    C = setup_consts(p)
    masks_np = np.zeros((4, 128, 512), np.float32)
    for j in range(4):
        for t in range(512):
            masks_np[j, : min(128, max(0, t - 128 * j + 1)), t] = 1.0
    masks_d = nc.inline_tensor(masks_np, "c_masks")
    mask_sb = p.sb("mask_sb", [128, 4 * 512], F32)
    t_mk = p.sy.op(lambda e: e.dma_start(
        out=mask_sb[:].rearrange("p (j t) -> p j t", t=512),
        in_=masks_d.ap().rearrange("j p t -> p j t")))

    qsl_sb = p.sb("qsl_sb", [128, 64 * 256], BF16)
    ksl_sb = p.sb("ksl_sb", [128, 64 * 256], BF16)
    qf = p.sb("qf", [128, L], BF16)
    lkvf = p.sb("lkvf", [128, L], BF16)
    av65 = p.sb("av65", [128, 64 * 65], BF16)
    sq = p.sb("sq", [64, L], F32)
    pball = p.sb("pball", [128, L], BF16)
    o_acc = p.sb("o_acc", [65, L], F32)
    rl_sb = p.sb("rl_sb", [1, 512], F32)
    rlb_sb = p.sb("rlb_sb", [64, 512], F32)
    on_sb = p.sb("on_sb", [64, 512], BF16)
    kmax = p.sb("kmax", [1, 1], F32)
    kblk = p.sb("kblk", [1, 1], F32)
    mrow = p.sb("mrow", [1, 512], F32)

    ps_tr = p.ps("ps_tr", [64, 128], BF16)
    ps_tr2 = p.ps("ps_tr2", [64, 128], BF16)
    ps_s = p.ps("ps_s", [128, 512], F32)
    ps_s2 = p.ps("ps_s2", [128, 512], F32)
    ps_av = p.ps("ps_av", [65, 512], F32)
    ps_av2 = p.ps("ps_av2", [65, 512], F32)
    ps_r = p.ps("ps_r", [64, 512], F32)

    t_q = p.sy.op(lambda e: e.dma_start(
        out=qsl_sb[:].rearrange("p (r c) -> p r c", c=256),
        in_=qsl_d.rearrange("(r p) c -> p r c", p=128)))
    t_k = p.sy.op(lambda e: e.dma_start(
        out=ksl_sb[:].rearrange("p (r c) -> p r c", c=256),
        in_=ksl_d.rearrange("(r p) c -> p r c", p=128)))

    for hh in range(C_HEADS):
        p.barrier()
        t_neg = p.ve.op(lambda e: e.memset(lkvf[64:128, :], -1.0))
        t_av1 = p.ve.op(lambda e: e.memset(av65[:], 1.0))
        evac = [None, None]
        for st in range(64):
            pst = ps_tr if st % 2 == 0 else ps_tr2
            t_tr = p.te.op(lambda e, st=st, hh=hh, pst=pst: e.transpose(
                pst[:], ksl_sb[:, st * 256 + hh * 64: st * 256 + (hh + 1) * 64],
                C["ident_bf"][:]),
                deps=[t_k, C["loaded"]] + ([evac[st % 2]] if evac[st % 2] else []))
            evac[st % 2] = p.ve.op(lambda e, st=st, pst=pst: e.tensor_copy(
                lkvf[0:64, st * 128:(st + 1) * 128], pst[:]), deps=[t_tr])
            p.ve.op(lambda e, st=st, hh=hh: e.tensor_copy(
                av65[:, st * 65: st * 65 + 64],
                ksl_sb[:, st * 256 + hh * 64: st * 256 + (hh + 1) * 64]),
                deps=[t_av1])
        for st in range(64):
            pst = ps_tr if st % 2 == 0 else ps_tr2
            t_tr = p.te.op(lambda e, st=st, hh=hh, pst=pst: e.transpose(
                pst[:], qsl_sb[:, st * 256 + hh * 64: st * 256 + (hh + 1) * 64],
                C["ident_bf"][:]),
                deps=[t_q] + ([evac[st % 2]] if evac[st % 2] else []))
            evac[st % 2] = p.ve.op(lambda e, st=st, pst=pst: e.tensor_copy(
                qf[0:64, st * 128:(st + 1) * 128], pst[:]), deps=[t_tr])
        built = p.ve.tok()

        if C_SUB < 1:
            continue
        t_ks = p.sc.op(lambda e: e.activation(sq[:], lkvf[0:64, :], AF.Square),
                       deps=[built])
        t_km = p.ve.op(lambda e: e.memset(kmax[:], 0.0))
        for blk_ in range(16):
            mmk = p.te.op(lambda e, blk_=blk_: e.matmul(
                ps_r[0:1, :], C["ones128_f"][0:64, 0:1],
                sq[:, blk_ * 512:(blk_ + 1) * 512], start=True, stop=True),
                deps=[t_ks, p.ve.tok()])
            t_bm = p.ve.op(lambda e: e.reduce_max(kblk[:], ps_r[0:1, :], axis=AX.X),
                           deps=[mmk])
            t_km = p.ve.op(lambda e: e.tensor_max(kmax[:], kmax[:], kblk[:]),
                           deps=[t_bm, t_km])

        if C_SUB < 2:
            continue
        t_qs = p.sc.op(lambda e: e.activation(sq[:], qf[0:64, :], AF.Square),
                       deps=[built, t_km])
        for blk_ in range(16):
            mmq = p.te.op(lambda e, blk_=blk_: e.matmul(
                ps_r[0:1, :], C["ones128_f"][0:64, 0:1],
                sq[:, blk_ * 512:(blk_ + 1) * 512], start=True, stop=True),
                deps=[t_qs, p.ve.tok(), p.sc.tok()])
            t_mm = p.ve.op(lambda e: e.tensor_scalar(mrow[:], ps_r[0:1, :],
                                                     kmax[0:1, 0:1], None, ALU.mult),
                           deps=[mmq, t_km])
            t_sqr = p.sc.op(lambda e, blk_=blk_: e.activation(
                qf[64:65, blk_ * 512:(blk_ + 1) * 512], mrow[:], AF.Sqrt),
                deps=[t_mm])
        mhat_done = p.sc.tok()

        if C_SUB < 3:
            continue
        def kend_of(tb):
            g0 = tb * 512
            return 1536 if g0 + 512 <= 1536 else g0 + 512

        t_z = p.ve.op(lambda e: e.memset(o_acc[:], 0.0))
        prev_exp = {}
        prev_add = {}
        last_add = {}
        for st in range(64):
            tbs = [tb for tb in range(C_TBS) if st * 128 < kend_of(tb)]
            exps = {}
            for j, tb in enumerate(tbs):
                bank = j % 2
                pst = ps_s if bank == 0 else ps_s2
                mm_s = p.te.op(lambda e, st=st, tb=tb, pst=pst: e.matmul(
                    pst[:], lkvf[0:65, st * 128:(st + 1) * 128],
                    qf[0:65, tb * 512:(tb + 1) * 512], start=True, stop=True),
                    deps=[mhat_done] + ([prev_exp[bank]] if bank in prev_exp else []))
                t_e = p.sc.op(lambda e, tb=tb, pst=pst: e.activation(
                    pball[:, tb * 512:(tb + 1) * 512], pst[:], AF.Exp),
                    deps=[mm_s])
                prev_exp[bank] = t_e
                ns = kend_of(tb) // 128
                if tb * 512 >= 1536 and st >= ns - 4 and not C_NOMASK:
                    jj = st - (ns - 4)
                    t_e = p.ve.op(lambda e, jj=jj, tb=tb: e.tensor_mul(
                        pball[:, tb * 512:(tb + 1) * 512],
                        pball[:, tb * 512:(tb + 1) * 512],
                        mask_sb[:, jj * 512:(jj + 1) * 512]),
                        deps=[t_e, t_mk])
                exps[tb] = t_e
            for j, tb in enumerate(tbs):
                bank = j % 2
                pav = ps_av if bank == 0 else ps_av2
                mm_av = p.te.op(lambda e, st=st, tb=tb, pav=pav: e.matmul(
                    pav[:], av65[:, st * 65:(st + 1) * 65],
                    pball[:, tb * 512:(tb + 1) * 512], start=True, stop=True),
                    deps=[exps[tb], built] + ([prev_add[bank]] if bank in prev_add else []))
                t_add = p.ve.op(lambda e, tb=tb, pav=pav: e.tensor_add(
                    o_acc[:, tb * 512:(tb + 1) * 512],
                    o_acc[:, tb * 512:(tb + 1) * 512], pav[:]),
                    deps=[mm_av, t_z])
                prev_add[bank] = t_add
                last_add[tb] = t_add
        st_prev = None
        for tb in range(C_TBS):
            if tb not in last_add:
                continue
            t_rl = p.ve.op(lambda e, tb=tb: e.reciprocal(
                rl_sb[:], o_acc[64:65, tb * 512:(tb + 1) * 512]),
                deps=[last_add[tb]])
            mm_rb = p.te.op(lambda e: e.matmul(
                ps_r[:], C["ones1_f"][0:1, 0:64], rl_sb[0:1, :],
                start=True, stop=True), deps=[t_rl])
            t_rc = p.ve.op(lambda e: e.tensor_copy(rlb_sb[:], ps_r[:]), deps=[mm_rb])
            t_on = p.ve.op(lambda e, tb=tb: e.tensor_mul(
                on_sb[:], o_acc[0:64, tb * 512:(tb + 1) * 512], rlb_sb[:]),
                deps=[t_rc] + ([st_prev] if st_prev else []))
            st_prev = p.sy.op(lambda e, hh=hh, tb=tb: e.dma_start(
                out=of_d[hh * 64:(hh + 1) * 64, tb * 512:(tb + 1) * 512],
                in_=on_sb[:]), deps=[t_on])

    p.barrier()
    p.sy.op(lambda e: e.nop(), inc=False, deps=[p.sy.tok(), p.gp.tok()])
    return p.finish()


def host_inputs_C(resB):
    maps = []
    for c in range(NCORE):
        b, sp = c // 4, c % 4
        q_full = np.concatenate([np.asarray(resB[b * 4 + s]["qtm"]) for s in range(4)], axis=0)
        k_full = np.concatenate([np.asarray(resB[b * 4 + s]["lkv"]) for s in range(4)], axis=0)
        maps.append({
            "qsl": np.ascontiguousarray(q_full[:, sp * 256:(sp + 1) * 256]),
            "ksl": np.ascontiguousarray(k_full[:, sp * 256:(sp + 1) * 256]),
        })
    return maps


def run_C(resB):
    nc = get_prog("C")
    res = run_bass_kernel_spmd(nc, host_inputs_C(resB), core_ids=list(range(NCORE)))
    return res.results




PAIR8 = np.concatenate([np.concatenate([np.arange(h * 64, (h + 1) * 64),
                                        np.arange((h + 4) * 64, (h + 5) * 64)])
                        for h in range(4)])


D_STOP = "full"


def build_D():
    p = Prog()
    nc = p.nc
    ofall_d = nc.dram_tensor("ofall", [1024, R], BF16, kind="ExternalInput").ap()
    x2_d = nc.dram_tensor("x2", [R, D], F32, kind="ExternalInput").ap()
    won_d = nc.dram_tensor("won", [D, D], F32, kind="ExternalInput").ap()
    bo_d = nc.dram_tensor("bo", [1, D], F32, kind="ExternalInput").ap()
    wq_d = nc.dram_tensor("wq", [D, 512], F32, kind="ExternalInput").ap()
    wk_d = nc.dram_tensor("wk", [D, 512], F32, kind="ExternalInput").ap()
    wv_d = nc.dram_tensor("wv", [D, 512], F32, kind="ExternalInput").ap()
    wqa_d = nc.dram_tensor("wqa", [D, 64], F32, kind="ExternalInput").ap()
    wqb_d = nc.dram_tensor("wqb", [64, 512], F32, kind="ExternalInput").ap()
    wka_d = nc.dram_tensor("wka", [D, 64], F32, kind="ExternalInput").ap()
    wkb_d = nc.dram_tensor("wkb", [64, 512], F32, kind="ExternalInput").ap()
    wg1a_d = nc.dram_tensor("wg1a", [D, 16], F32, kind="ExternalInput").ap()
    wg1b_d = nc.dram_tensor("wg1b", [16, 512], F32, kind="ExternalInput").ap()
    bg1_d = nc.dram_tensor("bg1", [128, 4], F32, kind="ExternalInput").ap()
    wg2a_d = nc.dram_tensor("wg2a", [D, 16], F32, kind="ExternalInput").ap()
    wg2b_d = nc.dram_tensor("wg2b", [16, 512], F32, kind="ExternalInput").ap()
    bg2_d = nc.dram_tensor("bg2", [128, 4], F32, kind="ExternalInput").ap()
    we_d = nc.dram_tensor("we", [D, 4], F32, kind="ExternalInput").ap()
    wga_d = nc.dram_tensor("wga", [D, 64], F32, kind="ExternalInput").ap()
    wgb_d = nc.dram_tensor("wgb", [64, 512], F32, kind="ExternalInput").ap()

    x3_d = nc.dram_tensor("x3", [R, D], F32, kind="ExternalOutput").ap()
    opart_d = nc.dram_tensor("opart", [R, 512], F32, kind="ExternalOutput").ap()
    qgd_d = nc.dram_tensor("qgd", [4, 128, R], BF16, kind="ExternalOutput").ap()
    q2f_d = nc.dram_tensor("q2fo", [4, 128, R], BF16, kind="ExternalOutput").ap()
    gkhi_d = nc.dram_tensor("gkhi", [4, 128, R], BF16, kind="ExternalOutput").ap()
    gklo_d = nc.dram_tensor("gklo", [4, 128, R], BF16, kind="ExternalOutput").ap()
    rwf_d = nc.dram_tensor("rwf", [1, 4 * R], F32, kind="ExternalOutput").ap()
    gs_d = nc.dram_tensor("gs", [R, 512], BF16, kind="ExternalOutput").ap()
    vtm_d = nc.dram_tensor("vtm", [R, 512], BF16, kind="ExternalOutput").ap()
    lcs_d = nc.dram_tensor("lcs", [5, NT, 128, 256], BF16, kind="ExternalOutput").ap()
    tsg_d = nc.dram_tensor("tsg", [5, 128, 256], F32, kind="ExternalOutput").ap()
    dsg_d = nc.dram_tensor("dsg", [5, 128, 4], F32, kind="ExternalOutput").ap()
    cgc_d = nc.dram_tensor("cgc", [5, NT, 128, 4], F32, kind="ExternalOutput").ap()

    C = setup_consts(p, skip=("triu_bf", "ones1_bf"))

    X_sb = p.sb("X_sb", [128, NT * D], F32)
    xT = p.sb("xT", [128, 8 * R], BF16)
    st1 = p.sb("st1", [128, 1], F32)
    st16 = p.sb("st16", [128, 16], F32)
    w_small = {}
    wbig = p.sb("w_big", [128, 8 * 512], BF16)
    w_small["wq"] = w_small["wk"] = w_small["wv"] = wbig
    for nm, sh in [("wqa", [128, 8 * 64]), ("wqb", [64, 512]), ("wka", [128, 8 * 64]),
                   ("wkb", [64, 512]), ("wg1a", [128, 8 * 16]), ("wg1b", [16, 512]),
                   ("wg2a", [128, 8 * 16]), ("wg2b", [16, 512]), ("we", [128, 8 * 4]),
                   ("wga", [128, 8 * 64]), ("wgb", [64, 512])]:
        w_small[nm] = p.sb("w_" + nm, sh, BF16)
    bg1_sb = p.sb("bg1_sb", [128, 4], F32)
    bg2_sb = p.sb("bg2_sb", [128, 4], F32)
    q1f = p.sb("q1f", [128, 4 * R], BF16)
    won_sb = q1f
    v_tm = p.sb("v_tm", [128, NT * 512], BF16)
    k1f = p.sb("k1f", [128, 4 * R], BF16)
    q2f = p.sb("q2f", [128, 4 * R], BF16)
    k2f = p.sb("k2f", [128, 4 * R], BF16)
    _vv = k2f[:].bitcast(F32)
    st2 = _vv[:, 0:D]
    bo_bc = _vv[:, D:2 * D]
    ofb = k1f[:, 0:8 * 128]
    xbf = q2f[:, 0:D]
    row_sb = q2f[:].bitcast(F32)[0:1, 0:D]
    _xv = X_sb[:].bitcast(BF16)
    hi8 = _xv[:, 0 * 4 * R:1 * 4 * R]
    lo8 = _xv[:, 1 * 4 * R:2 * 4 * R]
    gk2hi = _xv[:, 2 * 4 * R:3 * 4 * R]
    gk2lo = _xv[:, 3 * 4 * R:4 * 4 * R]
    _tv = xT[:]
    qg = _tv[:, 0 * R:1 * R]
    kg = _tv[:, 1 * R:2 * R]
    kout = _tv[:, 2 * R:3 * R]
    rwbig = _tv[:, 3 * R:5 * R].bitcast(F32)
    mbig = _tv[:, 5 * R:7 * R].bitcast(F32)
    xaf = p.sb("xaf", [128, R], BF16)
    _af = xaf[:]
    zf = p.sb("zf", [128, R], F32)
    Gf = wbig[:].bitcast(F32)[:, 0:R]
    e2h = _af[:, 0:256].bitcast(F32)
    kout_tm = _af[:, 256:384]
    am2 = _af[:, 384:640]
    ob2 = _af[:, 640:896].bitcast(F32)
    ktm2 = _af[:, 896:1408]
    xab = p.sb("xab", [128, 64], BF16)
    S_sb = _af[:, 1408:1920].bitcast(F32)
    Sb_sb = _af[:, 1920:2048 + 128] if False else p.sb("Sb_sb", [128, 256], BF16)
    Stmp = e2h[:, 0:64]
    dec_sb = p.sb("dec_sb", [128, 4], F32)
    cgc_sb = p.sb("cgc_sb", [128, 4], F32)
    rw_sb = zf[:, 64:128]
    rwp = zf[0:1, 128:256]
    rwrow = None
    sg1 = p.sb("sg1", [128, 4], F32)
    sg2 = p.sb("sg2", [128, 4], F32)
    gsb = zf[:, 0:512]
    rwrow = zf[0:1, 0:R]
    ones128x128_f = p.sb("ones2d_f", [128, 128], F32)

    ps_tr = p.ps("ps_tr", [128, 128], BF16)
    ps_tr2 = p.ps("ps_tr2", [128, 128], BF16)
    ps_a = p.ps("ps_a", [128, 512], F32)
    ps_b = p.ps("ps_b", [128, 512], F32)
    ps_c = p.ps("ps_c", [128, 512], F32)
    ps_m = p.ps("ps_m", [128, 256], F32)
    p.ve.op(lambda e: e.memset(ones128x128_f[:], 1.0))

    t_x2l = None
    for m in range(NT):
        t_x2l = p.sy.op(lambda e, m=m: e.dma_start(out=X_sb[:, m * D:(m + 1) * D],
                                                   in_=x2_d[m * 128:(m + 1) * 128, :]))
    t_won = p.gp.op(lambda e: e.dma_start(
        out=won_sb[:].rearrange("p (c d) -> p c d", d=D),
        in_=won_d.rearrange("(c p) d -> p c d", p=128)))
    t_bo = p.sy.op(lambda e: e.dma_start(out=row_sb[:], in_=bo_d))
    t_bob = bcast_row(p, C, row_sb, ps_c, bo_bc, [t_bo, C["loaded"]])
    for m in range(NT):
        p.barrier()
        t_of = p.sy.op(lambda e, m=m: e.dma_start(
            out=ofb[:].rearrange("p (r c) -> p r c", c=128),
            in_=ofall_d[:, m * 128:(m + 1) * 128].rearrange("(r p) c -> p r c", p=128)))
        mml = None
        for kc in range(8):
            for half in range(2):
                mml = p.te.op(lambda e, kc=kc, half=half: e.matmul(
                    (ps_a if half == 0 else ps_b)[:], ofb[:, kc * 128:(kc + 1) * 128],
                    won_sb[:, kc * D + half * 512:kc * D + (half + 1) * 512],
                    start=(kc == 0), stop=(kc == 7)),
                    inc=(kc == 7),
                    deps=([t_of, t_won] if kc == 0 and half == 0 else []))
        t_r0 = p.ve.op(lambda e, m=m: e.tensor_add(X_sb[:, m * D:m * D + 512],
                                                   X_sb[:, m * D:m * D + 512], ps_a[:]),
                       deps=[mml, t_x2l, t_bob])
        t_r1 = p.ve.op(lambda e, m=m: e.tensor_add(X_sb[:, m * D + 512:(m + 1) * D],
                                                   X_sb[:, m * D + 512:(m + 1) * D], ps_b[:]),
                       deps=[t_r0])
        t_r2 = p.ve.op(lambda e, m=m: e.tensor_add(X_sb[:, m * D:(m + 1) * D],
                                                   X_sb[:, m * D:(m + 1) * D], bo_bc[:]),
                       deps=[t_r1])
        t_ln = ln_rowwise(p, X_sb[:, m * D:(m + 1) * D], st1, st2, [t_r2])
        p.sy.op(lambda e, m=m: e.dma_start(out=x3_d[m * 128:(m + 1) * 128, :],
                                           in_=X_sb[:, m * D:(m + 1) * D]), deps=[t_ln])

    for m in range(NT):
        p.barrier()
        xt_build(p, C, lambda mm: X_sb[:, mm * D:(mm + 1) * D], xT, xbf, ps_tr, ps_tr2, m, [])
    p.barrier()

    wt = {}
    for nm, dd, dlast in [("wqa", wqa_d, 64), ("wka", wka_d, 64),
                          ("wg1a", wg1a_d, 16), ("wg2a", wg2a_d, 16),
                          ("we", we_d, 4), ("wga", wga_d, 64)]:
        wt[nm] = p.gp.op(lambda e, nm=nm, dd=dd, dlast=dlast: e.dma_start(
            out=w_small[nm][:].rearrange("p (c d) -> p c d", d=dlast),
            in_=dd.rearrange("(c p) d -> p c d", p=128)))
    for nm, dd in [("wqb", wqb_d), ("wkb", wkb_d), ("wg1b", wg1b_d),
                   ("wg2b", wg2b_d), ("wgb", wgb_d)]:
        wt[nm] = p.gp.op(lambda e, nm=nm, dd=dd: e.dma_start(out=w_small[nm][:], in_=dd))
    t_bg1 = p.sy.op(lambda e: e.dma_start(out=bg1_sb[:], in_=bg1_d))
    t_bg2 = p.sy.op(lambda e: e.dma_start(out=bg2_sb[:], in_=bg2_d))

    def pair_proj(dst, wname, scale=None):
        last = None
        for rc in range(4):
            for tb4 in range(4):
                mm = None
                for kc in range(8):
                    mm = p.te.op(lambda e, rc=rc, tb4=tb4, kc=kc, wname=wname: e.matmul(
                        ps_c[:], w_small[wname][:, kc * 512 + rc * 128:kc * 512 + (rc + 1) * 128],
                        xT[:, kc * R + tb4 * 512:kc * R + (tb4 + 1) * 512],
                        start=(kc == 0), stop=(kc == 7)),
                        inc=(kc == 7),
                        deps=([wt[wname], (p.ve, p.ve.n)] if kc == 0 else []))
                if scale is None:
                    last = p.ve.op(lambda e, rc=rc, tb4=tb4, dst=dst: e.tensor_copy(
                        dst[:, rc * R + tb4 * 512:rc * R + (tb4 + 1) * 512], ps_c[:]),
                        deps=[mm])
                else:
                    last = p.ve.op(lambda e, rc=rc, tb4=tb4, dst=dst, scale=scale: e.tensor_scalar(
                        dst[:, rc * R + tb4 * 512:rc * R + (tb4 + 1) * 512], ps_c[:],
                        scale, None, ALU.mult), deps=[mm])
        return last

    p.barrier()
    wt["wq"] = p.gp.op(lambda e: e.dma_start(
        out=wbig[:].rearrange("p (c d) -> p c d", d=512),
        in_=wq_d.rearrange("(c p) d -> p c d", p=128)))
    pair_proj(q1f, "wq", scale=0.125)
    p.barrier()
    wt["wk"] = p.gp.op(lambda e: e.dma_start(
        out=wbig[:].rearrange("p (c d) -> p c d", d=512),
        in_=wk_d.rearrange("(c p) d -> p c d", p=128)))
    pair_proj(k1f, "wk")
    p.barrier()
    wt["wv"] = p.gp.op(lambda e: e.dma_start(
        out=wbig[:].rearrange("p (c d) -> p c d", d=512),
        in_=wv_d.rearrange("(c p) d -> p c d", p=128)))
    for m in range(NT):
        mm = None
        for kc in range(8):
            mm = p.te.op(lambda e, m=m, kc=kc: e.matmul(
                ps_c[:], xT[:, kc * R + m * 128:kc * R + (m + 1) * 128],
                w_small["wv"][:, kc * 512:(kc + 1) * 512],
                start=(kc == 0), stop=(kc == 7)),
                inc=(kc == 7), deps=([wt["wv"], (p.ve, p.ve.n)] if kc == 0 else []))
        t_v = p.ve.op(lambda e, m=m: e.tensor_copy(v_tm[:, m * 512:(m + 1) * 512], ps_c[:]),
                      deps=[mm])
        p.sy.op(lambda e, m=m: e.dma_start(out=vtm_d[m * 128:(m + 1) * 128, :],
                                           in_=v_tm[:, m * 512:(m + 1) * 512]), deps=[t_v])

    def logsig_chunk(sl, osl, hi_t, lo_t, dep):
        sc2 = st2[:, 0:512]
        t_ab = p.sc.op(lambda e: e.activation(sc2[:, 0:sl.stop - sl.start], zf[:, sl], AF.Abs), deps=[dep])
        t_ex = p.sc.op(lambda e: e.activation(sc2[:, 0:sl.stop - sl.start], sc2[:, 0:sl.stop - sl.start], AF.Exp, scale=-1.0),
                       deps=[t_ab])
        t_ln = p.sc.op(lambda e: e.activation(sc2[:, 0:sl.stop - sl.start], sc2[:, 0:sl.stop - sl.start], AF.Ln, bias=1.0),
                       deps=[t_ex])
        t_mn = p.ve.op(lambda e: e.tensor_scalar(zf[:, sl], zf[:, sl], 0.0, None, ALU.min),
                       deps=[t_ab])
        t_su = p.ve.op(lambda e: e.tensor_sub(zf[:, sl], zf[:, sl], sc2[:, 0:sl.stop - sl.start]),
                       deps=[t_ln, t_mn])
        t_sc = p.ve.op(lambda e: e.tensor_scalar(zf[:, sl], zf[:, sl], 1.0 / 16.0, None,
                                                 ALU.mult), deps=[t_su])
        t_hi = p.ve.op(lambda e: e.tensor_copy(hi_t[:, osl], zf[:, sl]), deps=[t_sc])
        return p.ve.op(lambda e: e.tensor_sub(lo_t[:, osl], zf[:, sl], hi_t[:, osl]),
                       deps=[t_hi])

    def lora_f(dst, wa, wb, ka, bias_sb=None, scale=None):
        for tb4 in range(4):
            mm = None
            for kc in range(8):
                mm = p.te.op(lambda e, tb4=tb4, kc=kc, wa=wa, ka=ka: e.matmul(
                    ps_c[0:ka, :], w_small[wa][:, kc * ka:(kc + 1) * ka],
                    xT[:, kc * R + tb4 * 512:kc * R + (tb4 + 1) * 512],
                    start=(kc == 0), stop=(kc == 7)),
                    inc=(kc == 7), deps=([wt[wa], (p.ve, p.ve.n)] if kc == 0 else []))
            p.ve.op(lambda e, tb4=tb4, ka=ka: e.tensor_copy(
                xaf[0:ka, tb4 * 512:(tb4 + 1) * 512], ps_c[0:ka, :]), deps=[mm])
        xaf_done = p.ve.tok()
        last = None
        for rc in range(4):
            for tb4 in range(4):
                mm = p.te.op(lambda e, rc=rc, tb4=tb4, wb=wb, ka=ka: e.matmul(
                    ps_c[:], w_small[wb][0:ka, rc * 128:(rc + 1) * 128],
                    xaf[0:ka, tb4 * 512:(tb4 + 1) * 512], start=True, stop=True),
                    deps=[xaf_done, wt[wb], (p.ve, p.ve.n), (p.sc, p.sc.n)])
                if bias_sb is None:
                    if scale is None:
                        last = p.ve.op(lambda e, rc=rc, tb4=tb4, dst=dst: e.tensor_copy(
                            dst[:, rc * R + tb4 * 512:rc * R + (tb4 + 1) * 512], ps_c[:]),
                            deps=[mm])
                    else:
                        last = p.ve.op(lambda e, rc=rc, tb4=tb4, dst=dst, scale=scale:
                                       e.tensor_scalar(
                            dst[:, rc * R + tb4 * 512:rc * R + (tb4 + 1) * 512], ps_c[:],
                            scale, None, ALU.mult), deps=[mm])
                else:
                    t_z = p.ve.op(lambda e, rc=rc, tb4=tb4, bias_sb=bias_sb: e.tensor_scalar(
                        zf[:, tb4 * 512:(tb4 + 1) * 512], ps_c[:],
                        bias_sb[:, rc:rc + 1], None, ALU.add), deps=[mm])
                    last = logsig_chunk(slice(tb4 * 512, (tb4 + 1) * 512),
                                        slice(rc * R + tb4 * 512, rc * R + (tb4 + 1) * 512),
                                        dst[0], dst[1], t_z)
        return last

    if D_STOP == "q1f":
        p.barrier()
        for rc in range(4):
            p.sy.op(lambda e, rc=rc: e.dma_start(out=qgd_d[rc],
                                                 in_=q1f[:, rc * R:(rc + 1) * R]))
        p.barrier()
        p.sy.op(lambda e: e.nop(), inc=False, deps=[p.sy.tok(), p.gp.tok()])
        return p.finish()
    p.barrier()
    lora_f(q2f, "wqa", "wqb", 64, scale=0.125)
    p.barrier()
    lora_f((hi8, lo8), "wg1a", "wg1b", 16, bias_sb=bg1_sb)
    p.barrier()
    t_gk2 = lora_f((gk2hi, gk2lo), "wg2a", "wg2b", 16, bias_sb=bg2_sb)
    p.barrier()
    for rc in range(4):
        p.sy.op(lambda e, rc=rc: e.dma_start(out=q2f_d[rc], in_=q2f[:, rc * R:(rc + 1) * R]))
        p.sy.op(lambda e, rc=rc: e.dma_start(out=gkhi_d[rc], in_=gk2hi[:, rc * R:(rc + 1) * R]))
        p.sy.op(lambda e, rc=rc: e.dma_start(out=gklo_d[rc], in_=gk2lo[:, rc * R:(rc + 1) * R]))

    for m in range(NT):
        p.barrier()
        mm = None
        for kc in range(8):
            mm = p.te.op(lambda e, m=m, kc=kc: e.matmul(
                ps_a[:, 0:64], xT[:, kc * R + m * 128:kc * R + (m + 1) * 128],
                w_small["wka"][:, kc * 64:(kc + 1) * 64],
                start=(kc == 0), stop=(kc == 7)),
                inc=(kc == 7), deps=([wt["wka"]] if kc == 0 else []))
        t_xc = p.ve.op(lambda e: e.tensor_copy(xab[:], ps_a[:, 0:64]), deps=[mm])
        t_tr = p.te.op(lambda e: e.transpose(ps_tr[0:64, :], xab[:], C["ident_bf"][:]),
                       deps=[t_xc])
        t_xf = p.ve.op(lambda e: e.tensor_copy(xaf[0:64, 0:128], ps_tr[0:64, 0:128]),
                       deps=[t_tr])
        mm2 = p.te.op(lambda e: e.matmul(ps_b[:], xaf[0:64, 0:128],
                                         w_small["wkb"][0:64, :], start=True, stop=True),
                      deps=[t_xf, wt["wkb"]])
        t_mx = p.ve.op(lambda e: e.reduce_max(
            st16[:, 0:8], ps_b[:].rearrange("p (h x) -> p h x", x=64), axis=AX.X,
            negate=True), deps=[mm2])
        t_eh = None
        for h in range(8):
            t_eh = p.sc.op(lambda e, h=h: e.activation(
                gsb[:, h * 64:(h + 1) * 64], ps_b[:, h * 64:(h + 1) * 64], AF.Exp,
                bias=st16[:, h:h + 1]), deps=[t_mx])
        t_sm = p.ve.op(lambda e: e.reduce_sum(
            st16[:, 8:16], gsb[:].rearrange("p (h x) -> p h x", x=64), axis=AX.X),
            deps=[t_eh])
        t_rc = p.ve.op(lambda e: e.reciprocal(st16[:, 8:16], st16[:, 8:16]), deps=[t_sm])
        t_k2 = None
        for h in range(8):
            t_k2 = p.ve.op(lambda e, h=h: e.tensor_scalar(
                ktm2[:, h * 64:(h + 1) * 64], gsb[:, h * 64:(h + 1) * 64],
                st16[:, 8 + h:9 + h], None, ALU.mult), deps=[t_rc])
        for rc in range(4):
            pst = ps_tr if rc % 2 == 0 else ps_tr2
            t_t2 = p.te.op(lambda e, rc=rc, pst=pst: e.transpose(
                pst[:], ktm2[:, rc * 128:(rc + 1) * 128], C["ident_bf"][:]),
                deps=[t_k2, (p.ve, p.ve.n)])
            p.ve.op(lambda e, rc=rc, m=m, pst=pst: e.tensor_copy(
                k2f[:, rc * R + m * 128:rc * R + (m + 1) * 128], pst[:]), deps=[t_t2])

    for m in range(NT):
        p.barrier()
        mm = None
        for kc in range(8):
            mm = p.te.op(lambda e, m=m, kc=kc: e.matmul(
                ps_a[:, 0:4], xT[:, kc * R + m * 128:kc * R + (m + 1) * 128],
                w_small["we"][:, kc * 4:(kc + 1) * 4],
                start=(kc == 0), stop=(kc == 7)),
                inc=(kc == 7), deps=([wt["we"]] if kc == 0 else []))
        t_mx = p.ve.op(lambda e: e.reduce_max(st1[:], ps_a[:, 0:4], axis=AX.X, negate=True),
                       deps=[mm])
        t_ep = p.sc.op(lambda e: e.activation(sg1[:], ps_a[:, 0:4], AF.Exp,
                                              bias=st1[:, 0:1], accum_out=st16[:, 0:1]),
                       deps=[t_mx])
        t_r = p.ve.op(lambda e: e.reciprocal(st16[:, 0:1], st16[:, 0:1]), deps=[t_ep])
        t_e = p.ve.op(lambda e, m=m: e.tensor_scalar(rw_sb[:, m * 4:(m + 1) * 4],
                                                     sg1[:], st16[:, 0:1], None, ALU.mult),
                      deps=[t_r])
        t_n1 = p.ve.op(lambda e, m=m: e.reduce_max(st1[:], rw_sb[:, m * 4:(m + 1) * 4],
                                                   axis=AX.X, negate=True), deps=[t_e])
        t_s1 = p.sc.op(lambda e, m=m: e.activation(sg2[:], rw_sb[:, m * 4:(m + 1) * 4],
                                                   AF.Sign, bias=st1[:, 0:1]), deps=[t_n1])
        t_p1 = p.ve.op(lambda e: e.tensor_scalar(sg2[:], sg2[:], 1.0, None, ALU.add),
                       deps=[t_s1])
        t_rw = p.ve.op(lambda e, m=m: e.tensor_mul(rw_sb[:, m * 4:(m + 1) * 4],
                                                   rw_sb[:, m * 4:(m + 1) * 4], sg2[:]),
                       deps=[t_p1])
        t_prevst = None
        for ex in range(4):
            mmt = p.te.op(lambda e, m=m, ex=ex: e.matmul(
                ps_m[0:1, 0:128], rw_sb[:, m * 4 + ex:m * 4 + ex + 1],
                C["ident_f"][:], start=True, stop=True),
                deps=[t_rw, (p.ve, p.ve.n)])
            t_cp = p.ve.op(lambda e: e.tensor_copy(rwp[:], ps_m[0:1, 0:128]),
                           deps=[mmt] + ([t_prevst] if t_prevst else []))
            t_prevst = p.sy.op(lambda e, m=m, ex=ex: e.dma_start(
                out=rwf_d[0:1, ex * R + m * 128:ex * R + (m + 1) * 128], in_=rwp[:]),
                deps=[t_cp])

    for m in range(NT):
        p.barrier()
        mm = None
        for kc in range(8):
            mm = p.te.op(lambda e, m=m, kc=kc: e.matmul(
                ps_a[:, 0:64], xT[:, kc * R + m * 128:kc * R + (m + 1) * 128],
                w_small["wga"][:, kc * 64:(kc + 1) * 64],
                start=(kc == 0), stop=(kc == 7)),
                inc=(kc == 7), deps=([wt["wga"]] if kc == 0 else []))
        t_xc = p.ve.op(lambda e: e.tensor_copy(xab[:], ps_a[:, 0:64]), deps=[mm])
        t_tr = p.te.op(lambda e: e.transpose(ps_tr[0:64, :], xab[:], C["ident_bf"][:]),
                       deps=[t_xc])
        t_xf = p.ve.op(lambda e: e.tensor_copy(xaf[0:64, 0:128], ps_tr[0:64, 0:128]),
                       deps=[t_tr])
        mm2 = p.te.op(lambda e: e.matmul(ps_b[:], xaf[0:64, 0:128],
                                         w_small["wgb"][0:64, :], start=True, stop=True),
                      deps=[t_xf, wt["wgb"]])
        t_sg = p.sc.op(lambda e: e.activation(gsb[:], ps_b[:], AF.Sigmoid), deps=[mm2])
        t_gs = p.ve.op(lambda e: e.tensor_mul(ktm2[:], ps_b[:], gsb[:]),
                       deps=[t_sg])
        p.sy.op(lambda e, m=m: e.dma_start(out=gs_d[m * 128:(m + 1) * 128, :],
                                           in_=ktm2[:]), deps=[t_gs])

    if D_STOP == "q1f2":
        p.barrier()
        for rc in range(4):
            p.sy.op(lambda e, rc=rc: e.dma_start(out=qgd_d[rc],
                                                 in_=q1f[:, rc * R:(rc + 1) * R]))
        p.barrier()
        p.sy.op(lambda e: e.nop(), inc=False, deps=[p.sy.tok(), p.gp.tok()])
        return p.finish()
    opart_tok = {}
    NINST = 2 if D_STOP == "inst2" else (1 if D_STOP == "inst1" else 5)
    for inst in range(NINST):
        p.barrier()
        if inst == 0:
            qsrc, ksrc, hsrc, lsrc = q1f, k1f, hi8, lo8
        else:
            qsrc, ksrc, hsrc, lsrc = q2f, k2f, gk2hi, gk2lo
            ex = inst - 1
            t_rwl = p.sy.op(lambda e, ex=ex: e.dma_start(
                out=rwrow[:], in_=rwf_d[0:1, ex * R:(ex + 1) * R]))
            last = None
            for tb4 in range(4):
                mmb = p.te.op(lambda e, tb4=tb4, ex=ex: e.matmul(
                    ps_c[:], C["ones1_f"][:],
                    rwrow[0:1, tb4 * 512:(tb4 + 1) * 512],
                    start=True, stop=True), deps=[(p.ve, p.ve.n), t_rwl])
                last = p.ve.op(lambda e, tb4=tb4: e.tensor_copy(
                    rwbig[:, tb4 * 512:(tb4 + 1) * 512], ps_c[:]), deps=[mmb])
            t_mb = p.sc.op(lambda e: e.activation(mbig[:], rwbig[:], AF.Sign), deps=[last])
            p.barrier()
        t_z0 = p.ve.op(lambda e: e.memset(S_sb[:], 0.0))
        t_z1 = p.ve.op(lambda e: e.memset(Sb_sb[:], 0.0))
        t_z2 = p.ve.op(lambda e: e.memset(cgc_sb[:], 0.0))
        for rc in range(4):
            p.barrier()
            sl = slice(rc * R, (rc + 1) * R)
            t_sc = None
            for m in range(NT):
                csl = slice(m * 128, (m + 1) * 128)
                osl = slice(rc * R + m * 128, rc * R + (m + 1) * 128)
                t_ad = p.ve.op(lambda e, osl=osl, hsrc=hsrc, lsrc=lsrc: e.tensor_add(
                    e2h[:], hsrc[:, osl], lsrc[:, osl]),
                    deps=[t_sc] if t_sc else [])
                if inst > 0:
                    t_ad = p.ve.op(lambda e, csl=csl: e.tensor_mul(e2h[:], e2h[:],
                                                                   mbig[:, csl]),
                                   deps=[t_ad])
                t_sc = p.ve.op(lambda e, csl=csl: e.tensor_tensor_scan(
                    Gf[:, csl], ones128x128_f[:], e2h[:], 0.0, ALU.mult, ALU.add),
                    deps=[t_ad])
            t_e1 = p.sc.op(lambda e: e.activation(zf[:], Gf[:], AF.Exp), deps=[t_sc])
            t_qg = p.ve.op(lambda e, sl=sl, qsrc=qsrc: e.tensor_mul(qg[:], qsrc[:, sl], zf[:]),
                           deps=[t_e1])
            if inst > 0:
                t_qg = p.ve.op(lambda e: e.tensor_mul(qg[:], qg[:], rwbig[:]),
                               deps=[t_qg])
            t_em = p.sc.op(lambda e: e.activation(zf[:], Gf[:], AF.Exp, scale=-1.0),
                           deps=[t_qg, t_e1])
            t_kg = p.ve.op(lambda e, sl=sl, ksrc=ksrc: e.tensor_mul(kg[:], ksrc[:, sl], zf[:]),
                           deps=[t_em])
            if inst > 0:
                t_kg = p.ve.op(lambda e: e.tensor_mul(kg[:], kg[:], rwbig[:]),
                               deps=[t_kg])
            t_ko = t_kg
            for m in range(NT):
                csl = slice(m * 128, (m + 1) * 128)
                t_ek = p.sc.op(lambda e, csl=csl, m=m: e.activation(
                    zf[:, csl], Gf[:, csl], AF.Exp, scale=-1.0,
                    bias=Gf[:, m * 128 + 127:m * 128 + 128]), deps=[t_ko, t_em])
                t_ko = p.ve.op(lambda e, csl=csl, m=m, rc=rc, ksrc=ksrc: e.tensor_mul(
                    kout[:, csl], ksrc[:, rc * R + m * 128:rc * R + (m + 1) * 128],
                    zf[:, csl]), deps=[t_ek])
            if inst > 0:
                t_ko = p.ve.op(lambda e: e.tensor_mul(kout[:], kout[:], rwbig[:]),
                               deps=[t_ko])
            if inst == 0:
                p.sy.op(lambda e, rc=rc: e.dma_start(out=qgd_d[rc], in_=qg[:]),
                        deps=[t_qg])
            if D_STOP == "qg0" and inst == 0 and rc == 0:
                p.sy.op(lambda e: e.dma_start(out=opart_d[0:128, :],
                                              in_=zf[:, 0:512]), deps=[t_e1, t_qg])
                p.sy.op(lambda e: e.dma_start(out=opart_d[128:256, :],
                                              in_=Gf[:, 0:512]), deps=[t_e1, t_qg])
                p.barrier()
                p.sy.op(lambda e: e.nop(), inc=False, deps=[p.sy.tok(), p.gp.tok()])
                return p.finish()
            t_m2 = None
            t_sm = None
            t_ob_prev = None
            t_os_prev = None
            for m in range(NT):
                csl = slice(m * 128, (m + 1) * 128)
                h0, h1 = rc, rc + 4
                t_ls = p.sy.op(lambda e, inst=inst, m=m, rc=rc: e.dma_start(
                    out=lcs_d[inst, m, :, rc * 64:(rc + 1) * 64],
                    in_=Sb_sb[:, rc * 64:(rc + 1) * 64]),
                    deps=[(p.ve, p.ve.n)])
                t_cs = p.sy.op(lambda e, inst=inst, m=m, rc=rc: e.dma_start(
                    out=cgc_d[inst, m, :, rc:rc + 1], in_=cgc_sb[:, rc:rc + 1],
                    allow_slow_non_contiguous=True))
                mmA = p.te.op(lambda e, csl=csl: e.matmul(
                    ps_a[:, 0:128], kg[0:64, csl], qg[0:64, csl],
                    start=True, stop=True, tile_position=(0, 0)),
                    deps=[t_ko] + ([t_m2] if t_m2 else []))
                mmA2 = p.te.op(lambda e, csl=csl: e.matmul(
                    ps_b[:, 0:128], kg[64:128, csl], qg[64:128, csl],
                    start=True, stop=True, tile_position=(64, 0)))
                t_m1 = p.ve.op(lambda e: e.tensor_mul(am2[:, 0:128], ps_a[:, 0:128],
                                                      C["triu_f"][:]), deps=[mmA, mmA2])
                t_m2 = p.ve.op(lambda e: e.tensor_mul(am2[:, 128:256], ps_b[:, 0:128],
                                                      C["triu_f"][:]), deps=[mmA2])
                mi0 = p.te.op(lambda e, m=m, h0=h0: e.matmul(
                    ps_c[:, 0:64], am2[:, 0:128],
                    v_tm[:, m * 512 + h0 * 64:m * 512 + (h0 + 1) * 64],
                    start=True, stop=True), inc=False,
                    deps=[t_m1, t_m2] + ([t_ob_prev] if t_ob_prev else []))
                mi1 = p.te.op(lambda e, m=m, h1=h1: e.matmul(
                    ps_c[:, 64:128], am2[:, 128:256],
                    v_tm[:, m * 512 + h1 * 64:m * 512 + (h1 + 1) * 64],
                    start=True, stop=True))
                t_ob = p.ve.op(lambda e: e.tensor_copy(ob2[:], ps_c[:, 0:128]),
                               deps=[mi1] + ([t_os_prev] if t_os_prev else []))
                t_ob_prev = t_ob
                key = (rc, m)
                t_os = p.gp.op(lambda e, m=m, rc=rc: e.dma_start(
                    out=opart_d[m * 128:(m + 1) * 128, :].rearrange(
                        "t (g h x) -> t g h x", g=2, h=4, x=64)[:, :, rc],
                    in_=ob2[:].rearrange("t (g x) -> t g x", x=64),
                    accum_op=ALU.add),
                    deps=[t_ob] + ([opart_tok[key]] if key in opart_tok else []))
                opart_tok[key] = t_os
                t_os_prev = t_os
                t_tr = p.te.op(lambda e, csl=csl: e.transpose(
                    ps_tr[:], kout[:, csl], C["ident_bf"][:]),
                    deps=[t_ob, (p.ve, p.ve.n)])
                t_ke = p.ve.op(lambda e: e.tensor_copy(kout_tm[:], ps_tr[:]),
                               deps=[t_tr])
                mmM = None
                for half in range(2):
                    h = rc + 4 * half
                    tp = (0, 0) if half == 0 else (0, 64)
                    mmM = p.te.op(lambda e, half=half, h=h, tp=tp, m=m, rc=rc: e.matmul(
                        ps_m[64 * half:64 * (half + 1), rc * 64:(rc + 1) * 64],
                        kout_tm[:, half * 64:(half + 1) * 64],
                        v_tm[:, m * 512 + h * 64:m * 512 + (h + 1) * 64],
                        start=True, stop=True, tile_position=tp),
                        inc=(half == 1), deps=([t_ke, t_sm] if half == 0 else []))
                t_dc = p.sc.op(lambda e, m=m, rc=rc: e.activation(
                    dec_sb[:, rc:rc + 1], Gf[:, m * 128 + 127:m * 128 + 128], AF.Exp),
                    deps=[mmM])
                t_cg = p.ve.op(lambda e, m=m, rc=rc: e.tensor_add(
                    cgc_sb[:, rc:rc + 1], cgc_sb[:, rc:rc + 1],
                    Gf[:, m * 128 + 127:m * 128 + 128]),
                    deps=[t_cs, t_dc])
                t_sa = p.ve.op(lambda e, rc=rc: e.tensor_scalar(
                    Stmp[:], S_sb[:, rc * 64:(rc + 1) * 64],
                    dec_sb[:, rc:rc + 1], None, ALU.mult), deps=[mmM, t_z0, t_cg, t_dc])
                t_sm = p.ve.op(lambda e, rc=rc: e.tensor_add(
                    S_sb[:, rc * 64:(rc + 1) * 64], Stmp[:],
                    ps_m[:, rc * 64:(rc + 1) * 64]), deps=[t_sa])
                p.ve.op(lambda e, rc=rc: e.tensor_copy(
                    Sb_sb[:, rc * 64:(rc + 1) * 64], S_sb[:, rc * 64:(rc + 1) * 64]),
                    deps=[t_sm, t_ls])
        p.barrier()
        p.sy.op(lambda e, inst=inst: e.dma_start(out=tsg_d[inst], in_=S_sb[:]))
        p.sy.op(lambda e, inst=inst: e.dma_start(out=dsg_d[inst], in_=cgc_sb[:]))

    p.barrier()
    p.sy.op(lambda e: e.nop(), inc=False, deps=[p.sy.tok(), p.gp.tok()])
    return p.finish()




def build_E():
    p = Prog()
    nc = p.nc
    x3_d = nc.dram_tensor("x3", [R, D], F32, kind="ExternalInput").ap()
    opart_d = nc.dram_tensor("opart", [R, 512], F32, kind="ExternalInput").ap()
    qgd_d = nc.dram_tensor("qgd", [4, 128, R], BF16, kind="ExternalInput").ap()
    q2f_d = nc.dram_tensor("q2f", [4, 128, R], BF16, kind="ExternalInput").ap()
    gkhi_d = nc.dram_tensor("gkhi", [4, 128, R], BF16, kind="ExternalInput").ap()
    gklo_d = nc.dram_tensor("gklo", [4, 128, R], BF16, kind="ExternalInput").ap()
    rwf_d = nc.dram_tensor("rwf", [1, 4 * R], F32, kind="ExternalInput").ap()
    gs_d = nc.dram_tensor("gs", [R, 512], BF16, kind="ExternalInput").ap()
    lcs_d = nc.dram_tensor("lcs", [5, NT, 128, 256], BF16, kind="ExternalInput").ap()
    cgc_d = nc.dram_tensor("cgc", [5, NT, 128, 4], F32, kind="ExternalInput").ap()
    tcomp_d = nc.dram_tensor("tcomp", [5, 3, 128, 256], F32, kind="ExternalInput").ap()
    dcomp_d = nc.dram_tensor("dcomp", [5, 3, 128, 4], F32, kind="ExternalInput").ap()
    nw_d = nc.dram_tensor("nw", [1, 512], F32, kind="ExternalInput").ap()
    won2_d = nc.dram_tensor("won2", [512, D], F32, kind="ExternalInput").ap()
    y_d = nc.dram_tensor("y", [R, D], F32, kind="ExternalOutput").ap()

    C = setup_consts(p, skip=("triu_bf", "triu_f"))

    qgd = p.sb("qgd_s", [128, 4 * R], BF16)
    q2f = p.sb("q2f_s", [128, 4 * R], BF16)
    gkhi = p.sb("gkhi_s", [128, 4 * R], BF16)
    gklo = p.sb("gklo_s", [128, 4 * R], BF16)
    qge = p.sb("qge", [128, 4 * R], BF16)
    o_acc = p.sb("o_acc", [128, NT * 512], F32)
    won2 = p.sb("won2_s", [128, 4 * D], BF16)
    gstile = p.sb("gstile", [128, 512], BF16)
    x3t = p.sb("x3t", [128, D], F32)
    zf = p.sb("zf", [128, R], F32)
    Gf = p.sb("Gf", [128, R], F32)
    e2h = p.sb("e2h", [128, 128], F32)
    rwbig = p.sb("rwbig", [128, R], F32)
    mbig = p.sb("mbig", [128, R], F32)
    rwrow = zf[0:1, 0:R]
    lc_sb = p.sb("lc_sb", [128, 256], BF16)
    cgr_sb = p.sb("cgr_sb", [128, 4], F32)
    dec_sb = p.sb("dec_sb", [128, 4], F32)
    Sin_sb = p.sb("Sin_sb", [128, 256], F32)
    Sc_sb = p.sb("Sc_sb", [128, 256], F32)
    Scb_sb = p.sb("Scb_sb", [128, 256], BF16)
    tc_sb = p.sb("tc_sb", [128, 256], F32)
    st1 = p.sb("st1", [128, 1], F32)
    st2 = p.sb("st2", [128, 1024], F32)
    s8 = p.sb("s8", [128, 8], F32)
    nw_bc = p.sb("nw_bc", [128, 512], F32)
    row_sb = p.sb("row_sb", [1, 512], F32)
    ob = p.sb("ob", [128, 512], BF16)
    obt = p.sb("obt", [128, 128], BF16)
    ones2d = p.sb("ones2d", [128, 128], F32)

    ps_tr = p.ps("ps_tr", [128, 128], BF16)
    ps_a = p.ps("ps_a", [128, 512], F32)
    ps_b = p.ps("ps_b", [128, 512], F32)
    ps_o = p.ps("ps_o", [128, D], F32)
    ps_bc = p.ps("ps_bc", [128, 512], F32)

    p.ve.op(lambda e: e.memset(ones2d[:], 1.0))
    t_in = {}
    for nm, dst, src in [("qgd", qgd, qgd_d), ("q2f", q2f, q2f_d),
                         ("gkhi", gkhi, gkhi_d), ("gklo", gklo, gklo_d)]:
        t_in[nm] = p.sy.op(lambda e, dst=dst, src=src: e.dma_start(
            out=dst[:].rearrange("p (r c) -> p r c", c=R),
            in_=src.rearrange("r p c -> p r c")))
    for m in range(NT):
        t_in[("op", m)] = p.sy.op(lambda e, m=m: e.dma_start(
            out=o_acc[:, m * 512:(m + 1) * 512], in_=opart_d[m * 128:(m + 1) * 128, :]))
    t_in["won2"] = p.gp.op(lambda e: e.dma_start(
        out=won2[:].rearrange("p (c d) -> p c d", d=D),
        in_=won2_d.rearrange("(c p) d -> p c d", p=128)))
    t_nw = p.sy.op(lambda e: e.dma_start(out=row_sb[:], in_=nw_d))
    t_nwb = bcast_row(p, C, row_sb, ps_bc, nw_bc, [t_nw, C["loaded"]], n=512)

    for inst in range(5):
        p.barrier()
        t_z = p.ve.op(lambda e: e.memset(Sin_sb[:], 0.0))
        for j in range(3):
            t_dc = p.sy.op(lambda e, inst=inst, j=j: e.dma_start(
                out=dec_sb[:], in_=dcomp_d[inst, j]), deps=[t_z, (p.sc, p.sc.n)])
            t_tc = p.sy.op(lambda e, inst=inst, j=j: e.dma_start(
                out=tc_sb[:], in_=tcomp_d[inst, j]))
            t_ex = p.sc.op(lambda e: e.activation(dec_sb[:], dec_sb[:], AF.Exp),
                           deps=[t_dc])
            t_sm = t_z
            for rc in range(4):
                t_sm = p.ve.op(lambda e, rc=rc: e.tensor_scalar(
                    Sin_sb[:, rc * 64:(rc + 1) * 64], Sin_sb[:, rc * 64:(rc + 1) * 64],
                    dec_sb[:, rc:rc + 1], None, ALU.mult), deps=[t_ex, t_sm])
            t_z = p.ve.op(lambda e: e.tensor_add(Sin_sb[:], Sin_sb[:], tc_sb[:]),
                          deps=[t_sm, t_tc])
        sin_done = t_z
        if inst == 0:
            qgsrc = qgd
            qg_done = t_in["qgd"]
        else:
            ex = inst - 1
            qgsrc = qge
            t_rwl = p.sy.op(lambda e, ex=ex: e.dma_start(
                out=rwrow[:], in_=rwf_d[0:1, ex * R:(ex + 1) * R]),
                deps=[(p.ve, p.ve.n), (p.te, p.te.n)])
            last = None
            for tb4 in range(4):
                mmb = p.te.op(lambda e, tb4=tb4: e.matmul(
                    ps_bc[:], C["ones1_f"][:],
                    rwrow[0:1, tb4 * 512:(tb4 + 1) * 512],
                    start=True, stop=True), deps=[t_rwl, (p.ve, p.ve.n)])
                last = p.ve.op(lambda e, tb4=tb4: e.tensor_copy(
                    rwbig[:, tb4 * 512:(tb4 + 1) * 512], ps_bc[:]), deps=[mmb])
            t_mb = p.sc.op(lambda e: e.activation(mbig[:], rwbig[:], AF.Sign),
                           deps=[last])
            qg_done = None
            for rc in range(4):
                t_sc = None
                for m in range(NT):
                    csl = slice(m * 128, (m + 1) * 128)
                    osl = slice(rc * R + m * 128, rc * R + (m + 1) * 128)
                    t_ad = p.ve.op(lambda e, osl=osl: e.tensor_add(
                        e2h[:], gkhi[:, osl], gklo[:, osl]),
                        deps=[t_in["gkhi"], t_in["gklo"]] + ([t_sc] if t_sc else []))
                    t_ad = p.ve.op(lambda e, csl=csl: e.tensor_mul(
                        e2h[:], e2h[:], mbig[:, csl]), deps=[t_ad, t_mb])
                    t_sc = p.ve.op(lambda e, csl=csl: e.tensor_tensor_scan(
                        Gf[:, csl], ones2d[:], e2h[:], 0.0, ALU.mult, ALU.add),
                        deps=[t_ad])
                t_e1 = p.sc.op(lambda e: e.activation(zf[:], Gf[:], AF.Exp),
                               deps=[t_sc, (p.ve, p.ve.n)])
                t_q1 = p.ve.op(lambda e, rc=rc: e.tensor_mul(
                    qge[:, rc * R:(rc + 1) * R], q2f[:, rc * R:(rc + 1) * R], zf[:]),
                    deps=[t_e1, t_in["q2f"]])
                qg_done = p.ve.op(lambda e, rc=rc: e.tensor_mul(
                    qge[:, rc * R:(rc + 1) * R], qge[:, rc * R:(rc + 1) * R], rwbig[:]),
                    deps=[t_q1])
        t_acc = None
        for m in range(NT):
            t_lc = p.sy.op(lambda e, inst=inst, m=m: e.dma_start(
                out=lc_sb[:], in_=lcs_d[inst, m]),
                deps=[(p.ve, p.ve.n)])
            t_cg = p.sy.op(lambda e, inst=inst, m=m: e.dma_start(
                out=cgr_sb[:], in_=cgc_d[inst, m]), deps=[(p.sc, p.sc.n)])
            t_dx = p.sc.op(lambda e: e.activation(dec_sb[:], cgr_sb[:], AF.Exp),
                           deps=[t_cg, sin_done])
            t_s = None
            for rc in range(4):
                t_s = p.ve.op(lambda e, rc=rc: e.tensor_scalar(
                    Sc_sb[:, rc * 64:(rc + 1) * 64], Sin_sb[:, rc * 64:(rc + 1) * 64],
                    dec_sb[:, rc:rc + 1], None, ALU.mult), deps=[t_dx, t_s])
            t_s = p.ve.op(lambda e: e.tensor_add(Sc_sb[:], Sc_sb[:], lc_sb[:]),
                          deps=[t_s, t_lc])
            t_sb = p.ve.op(lambda e: e.tensor_copy(Scb_sb[:], Sc_sb[:]), deps=[t_s])
            mmlast = None
            for rc in range(4):
                mmlast = p.te.op(lambda e, rc=rc, m=m, qgsrc=qgsrc: e.matmul(
                    ps_a[:, rc * 64:(rc + 1) * 64],
                    qgsrc[0:64, rc * R + m * 128:rc * R + (m + 1) * 128],
                    Scb_sb[0:64, rc * 64:(rc + 1) * 64],
                    start=True, stop=True, tile_position=(0, 0)),
                    deps=([t_sb, qg_done, t_acc] if rc == 0 else []))
                mmlast = p.te.op(lambda e, rc=rc, m=m, qgsrc=qgsrc: e.matmul(
                    ps_b[:, rc * 64:(rc + 1) * 64],
                    qgsrc[64:128, rc * R + m * 128:rc * R + (m + 1) * 128],
                    Scb_sb[64:128, rc * 64:(rc + 1) * 64],
                    start=True, stop=True, tile_position=(64, 0)))
            t_a1 = p.ve.op(lambda e, m=m: e.tensor_add(
                o_acc[:, m * 512:m * 512 + 256], o_acc[:, m * 512:m * 512 + 256],
                ps_a[:, 0:256]), deps=[mmlast, t_in[("op", m)]])
            t_acc = p.ve.op(lambda e, m=m: e.tensor_add(
                o_acc[:, m * 512 + 256:(m + 1) * 512],
                o_acc[:, m * 512 + 256:(m + 1) * 512], ps_b[:, 0:256]),
                deps=[t_a1])

    p.barrier()
    t_prev_ln = None
    t_x3 = None
    for m in range(NT):
        p.barrier()
        osl = slice(m * 512, (m + 1) * 512)
        t_gs = p.sy.op(lambda e, m=m: e.dma_start(out=gstile[:],
                                                  in_=gs_d[m * 128:(m + 1) * 128, :]))
        t_x3 = p.sy.op(lambda e, m=m: e.dma_start(out=x3t[:],
                                                  in_=x3_d[m * 128:(m + 1) * 128, :]))
        t_sq = p.sc.op(lambda e, osl=osl: e.activation(st2[:, 0:512], o_acc[:, osl],
                                                       AF.Square),
                       deps=[])
        t_rd = p.ve.op(lambda e: e.reduce_sum(
            s8[:], st2[:, 0:512].rearrange("p (h x) -> p h x", x=64), axis=AX.X), deps=[t_sq])
        t_v = p.ve.op(lambda e: e.tensor_scalar(s8[:], s8[:], 1.0 / 64.0, 1e-5,
                                                ALU.mult, ALU.add), deps=[t_rd])
        t_sq2 = p.sc.op(lambda e: e.activation(s8[:], s8[:], AF.Sqrt), deps=[t_v])
        t_r = p.ve.op(lambda e: e.reciprocal(s8[:], s8[:]), deps=[t_sq2])
        t_h = t_r
        for h in range(8):
            t_h = p.ve.op(lambda e, h=h, m=m: e.tensor_scalar(
                o_acc[:, m * 512 + h * 64:m * 512 + (h + 1) * 64],
                o_acc[:, m * 512 + h * 64:m * 512 + (h + 1) * 64],
                s8[:, h:h + 1], None, ALU.mult), deps=[t_h])
        t_nwm = p.ve.op(lambda e, osl=osl: e.tensor_mul(o_acc[:, osl], o_acc[:, osl],
                                                        nw_bc[:]), deps=[t_h, t_nwb])
        t_gsx = p.ve.op(lambda e, osl=osl: e.tensor_mul(ob[:], o_acc[:, osl], gstile[:]),
                        deps=[t_nwm, t_gs])
        mml = None
        evac = None
        for kc4 in range(4):
            t_tr = p.te.op(lambda e, kc4=kc4: e.transpose(
                ps_tr[:], ob[:, kc4 * 128:(kc4 + 1) * 128], C["ident_bf"][:]),
                deps=[t_gsx] + ([evac] if evac else []))
            evac = p.ve.op(lambda e: e.tensor_copy(obt[:], ps_tr[:]),
                           deps=[t_tr] + ([mml] if mml else []))
            for half in range(2):
                mml = p.te.op(lambda e, kc4=kc4, half=half: e.matmul(
                    ps_o[:, half * 512:(half + 1) * 512], obt[:],
                    won2[:, kc4 * D + half * 512:kc4 * D + (half + 1) * 512],
                    start=(kc4 == 0), stop=(kc4 == 3)),
                    inc=True,
                    deps=([evac, t_in["won2"]] if half == 0 else [evac]))
        t_y = p.ve.op(lambda e: e.tensor_add(x3t[:], x3t[:], ps_o[:]),
                      deps=[mml, t_x3])
        t_ln = ln_rowwise(p, x3t[:], st1, st2, [t_y, t_prev_ln] if t_prev_ln else [t_y],
                          n=1024)
        t_prev_ln = t_ln
        p.sy.op(lambda e, m=m: e.dma_start(out=y_d[m * 128:(m + 1) * 128, :],
                                           in_=x3t[:]), deps=[t_ln])

    p.barrier()
    p.sy.op(lambda e: e.nop(), inc=False, deps=[p.sy.tok(), p.gp.tok()])
    return p.finish()


def host_inputs_D(inputs, resB, resC):
    f = f32c
    wonT = f(np.asarray(inputs["L1_attn_Wo"]).T)
    wqP = f(np.asarray(inputs["L1_ffn_Wq"])[:, PAIR8])
    wkP = f(np.asarray(inputs["L1_ffn_Wk"])[:, PAIR8])
    wv = f(inputs["L1_ffn_Wv"])
    wqa = f(inputs["L1_ffn_Wqa"])
    wqbP = f(np.asarray(inputs["L1_ffn_Wqb"])[:, PAIR8])
    wka = f(inputs["L1_ffn_Wka"])
    wkbP = f(np.asarray(inputs["L1_ffn_Wkb"])[:, PAIR8])
    wg1a = f(inputs["L1_ffn_Wg1a"])
    wg1bP = f(np.asarray(inputs["L1_ffn_Wg1b"])[:, PAIR8])
    wg2a = f(inputs["L1_ffn_Wg2a"])
    wg2bP = f(np.asarray(inputs["L1_ffn_Wg2b"])[:, PAIR8])
    bg1P = f(np.asarray(inputs["L1_ffn_bg1"])[PAIR8].reshape(4, 128).T)
    bg2P = f(np.asarray(inputs["L1_ffn_bg2"])[PAIR8].reshape(4, 128).T)
    we = f(inputs["L1_ffn_We"])
    wga = f(inputs["L1_ffn_Wga"])
    wgb = f(inputs["L1_ffn_Wgb"])
    bo = f(inputs["L1_attn_bo"]).reshape(1, D)
    maps = []
    for c in range(NCORE):
        b, sq = c // 4, c % 4
        of_global = np.concatenate([np.asarray(resC[b * 4 + sp]["of"]) for sp in range(4)], axis=0)
        pr = np.arange(sq * R, (sq + 1) * R)
        gmap = np.where(pr < 6656, pr + 1536, pr - 6656)
        ofall = np.ascontiguousarray(of_global[:, gmap])
        maps.append({
            "ofall": ofall,
            "x2": np.asarray(resB[c]["x2"], np.float32),
            "won": wonT, "bo": bo,
            "wq": wqP, "wk": wkP, "wv": wv,
            "wqa": wqa, "wqb": wqbP, "wka": wka, "wkb": wkbP,
            "wg1a": wg1a, "wg1b": wg1bP, "bg1": bg1P,
            "wg2a": wg2a, "wg2b": wg2bP, "bg2": bg2P,
            "we": we, "wga": wga, "wgb": wgb,
        })
    return maps


def run_D(inputs, resB, resC):
    nc = get_prog("D")
    res = run_bass_kernel_spmd(nc, host_inputs_D(inputs, resB, resC),
                               core_ids=list(range(NCORE)))
    return res.results


def host_inputs_E(inputs, resD):
    nw = f32c(np.tile(np.asarray(inputs["L1_ffn_norm_w"]), 8)).reshape(1, 512)
    won2 = f32c(inputs["L1_ffn_Wo"])
    maps = []
    for c in range(NCORE):
        b, sq = c // 4, c % 4
        tcomp = np.zeros((5, 3, 128, 256), np.float32)
        dcomp = np.zeros((5, 3, 128, 4), np.float32)
        for j in range(sq):
            pc = b * 4 + j
            tcomp[:, j] = np.asarray(resD[pc]["tsg"], np.float32)
            dcomp[:, j] = np.asarray(resD[pc]["dsg"], np.float32)
        maps.append({
            "x3": np.asarray(resD[c]["x3"], np.float32),
            "opart": np.asarray(resD[c]["opart"], np.float32),
            "qgd": np.asarray(resD[c]["qgd"]),
            "q2f": np.asarray(resD[c]["q2fo"]),
            "gkhi": np.asarray(resD[c]["gkhi"]),
            "gklo": np.asarray(resD[c]["gklo"]),
            "rwf": np.asarray(resD[c]["rwf"], np.float32),
            "gs": np.asarray(resD[c]["gs"]),
            "lcs": np.asarray(resD[c]["lcs"]),
            "cgc": np.asarray(resD[c]["cgc"], np.float32),
            "tcomp": tcomp, "dcomp": dcomp,
            "nw": nw, "won2": won2,
        })
    return maps


def run_E(inputs, resD):
    nc = get_prog("E")
    res = run_bass_kernel_spmd(nc, host_inputs_E(inputs, resD),
                               core_ids=list(range(NCORE)))
    return res.results


def kernel(**inputs):
    x = np.asarray(inputs["L0_attn_x"], np.float32).reshape(B, L, D)
    slices = [np.ascontiguousarray(x[c // 4, (c % 4) * R:(c % 4 + 1) * R])
              for c in range(NCORE)]
    resA = run_A(inputs, slices)
    resB = run_B(inputs, slices, resA)
    resC = run_C(resB)
    resD = run_D(inputs, resB, resC)
    resE = run_E(inputs, resD)
    out = np.zeros((B, L, D), np.float32)
    for c in range(NCORE):
        b, sq = c // 4, c % 4
        out[b, sq * R:(sq + 1) * R] = np.asarray(resE[c]["y"], np.float32)
    return out


# version 05 (final)
# speedup vs baseline: 1.1203x; 1.0084x over previous
import numpy as np
import ml_dtypes
from contextlib import ExitStack

import concourse.bass as bass
import concourse.mybir as mybir
from concourse.bass_utils import run_bass_kernel_spmd

F32 = mybir.dt.float32
BF16 = mybir.dt.bfloat16
I32 = mybir.dt.int32
AF = mybir.ActivationFunctionType
ALU = mybir.AluOpType
AX = mybir.AxisListType

NCORE = 8
B, L, D = 2, 8192, 1024
R = 2048
NT = R // 128

bfnp = ml_dtypes.bfloat16


def f32c(x):
    return np.ascontiguousarray(np.asarray(x, np.float32))


class Q:
    def __init__(self, prog, name, step=1):
        self.prog = prog
        self.name = name
        self.step = step
        self.ops = []
        self.n = 0
        self._waited = {}

    def dep(self, token):
        if token is None:
            return
        pq, val = token
        if val <= 0 or (pq is self and val <= self.n):
            pass
        if val <= 0:
            return
        if self._waited.get(pq.name, 0) >= val:
            return
        self._waited[pq.name] = val
        sems = self.prog.sems

        def w(e, pn=pq.name, val=val):
            e.wait_ge(sems[pn], val)
        self.ops.append(w)

    def op(self, fn, inc=True, k=None, deps=()):
        for t in deps:
            self.dep(t)
        kk = (k if k is not None else self.step) if inc else 0
        sems = self.prog.sems
        if kk:
            def g(e, fn=fn, kk=kk, name=self.name):
                fn(e).then_inc(sems[name], kk)
            self.ops.append(g)
            self.n += kk
        else:
            self.ops.append(lambda e, fn=fn: fn(e))
        return (self, self.n)

    def tok(self):
        return (self, self.n)


class Prog:
    def __init__(self):
        self.nc = bass.Bass("TRN2", target_bir_lowering=False)
        self.st = ExitStack()
        self.sems = {}
        self.sy = Q(self, "sy", 16)
        self.gp = Q(self, "gp", 16)
        self.te = Q(self, "te", 1)
        self.ve = Q(self, "ve", 1)
        self.sc = Q(self, "sc", 1)
        self.queues = [self.sy, self.gp, self.te, self.ve, self.sc]

    def sb(self, name, shape, dtype):
        return self.st.enter_context(self.nc.sbuf_tensor(name, shape, dtype))

    def ps(self, name, shape, dtype=F32):
        return self.st.enter_context(self.nc.psum_tensor(name, shape, dtype))

    def barrier(self):
        toks = [(q, q.n) for q in self.queues]
        for q in self.queues:
            for t in toks:
                if t[0] is not q:
                    q.dep(t)

    def finish(self):
        nc = self.nc
        for q in self.queues:
            self.sems[q.name] = self.st.enter_context(nc.semaphore("sem_" + q.name))
        blk = self.st.enter_context(nc.Block())

        def run(qs):
            def body(e):
                for f in qs.ops:
                    f(e)
            return body
        blk.sync(run(self.sy))
        blk.gpsimd(run(self.gp))
        blk.tensor(run(self.te))
        blk.vector(run(self.ve))
        blk.scalar(run(self.sc))
        self.st.close()
        return nc


CPOOL_VALS = [1.0, -1.0, 0.5, 0.25, 1e-5, 1.0 / 16.0, 1.0 / 64.0, 1.0 / 1024.0,
              -1e30, 2.0, 1.0 / 512.0, 64.0, 1024.0, 0.125, -0.125, 1536.0]


def setup_consts(p, skip=()):
    nc = p.nc
    srcs = {
        "ident_bf": nc.inline_tensor(np.eye(128, dtype=bfnp), "c_ident_bf"),
        "ones1_bf": nc.inline_tensor(np.ones((1, 128), dtype=bfnp), "c_ones1_bf"),
        "ones1_f": nc.inline_tensor(np.ones((1, 128), np.float32), "c_ones1_f"),
        "ones128_bf": nc.inline_tensor(np.ones((128, 1), dtype=bfnp), "c_ones128_bf"),
        "ones128_f": nc.inline_tensor(np.ones((128, 1), np.float32), "c_ones128_f"),
        "ident_f": nc.inline_tensor(np.eye(128, dtype=np.float32), "c_ident_f"),
        "triu_f": nc.inline_tensor(np.triu(np.ones((128, 128), np.float32)), "c_triu_f"),
        "triu_bf": nc.inline_tensor(np.triu(np.ones((128, 128))).astype(bfnp), "c_triu_bf"),
        "cpool": nc.inline_tensor(np.tile(np.asarray(CPOOL_VALS, np.float32), (128, 1)), "c_cpool"),
    }
    shapes = {"ident_bf": ([128, 128], BF16), "ones1_bf": ([1, 128], BF16),
              "ones1_f": ([1, 128], F32), "ones128_bf": ([128, 1], BF16),
              "ones128_f": ([128, 1], F32),
              "ident_f": ([128, 128], F32),
              "triu_f": ([128, 128], F32), "triu_bf": ([128, 128], BF16),
              "cpool": ([128, len(CPOOL_VALS)], F32)}
    t = {}
    for nm, (sh, dt) in shapes.items():
        if nm in skip:
            continue
        t[nm] = p.sb("s_" + nm, sh, dt)
    for ci, cv in enumerate(CPOOL_VALS):
        nc.const_aps.register(F32, cv, t["cpool"][:, ci:ci + 1])
    for nm in shapes:
        if nm in skip:
            continue
        p.sy.op(lambda e, tt=t[nm], dd=srcs[nm]: e.dma_start(out=tt[:], in_=dd.ap()))
    t["loaded"] = p.sy.tok()
    return t


PAIR_PERM = np.concatenate([np.concatenate([np.arange(pr * 64, (pr + 1) * 64),
                                            np.arange((pr + 8) * 64, (pr + 9) * 64)])
                            for pr in range(8)])


def pair_view(ap2d, pr):
    return ap2d[:, pr * 128:(pr + 1) * 128]


def ppos(h):
    return (h % 8) * 128 + (h // 8) * 64


def proj_tm(p, psums, frees, xT, wsb, wtok, m, deps0=()):
    out = []
    for half in range(2):
        mm = None
        for kc in range(8):
            mm = p.te.op(lambda e, kc=kc, half=half, pst=psums[half]: e.matmul(
                pst[:], xT[:, kc * R + m * 128:kc * R + (m + 1) * 128],
                wsb[:, kc * D + half * 512: kc * D + (half + 1) * 512],
                start=(kc == 0), stop=(kc == 7)),
                inc=(kc == 7),
                deps=(list(deps0) + [wtok, frees[half]] if kc == 0 else []))
        out.append(mm)
    return out


def build_A():
    p = Prog()
    nc = p.nc
    x_d = nc.dram_tensor("x", [R, D], F32, kind="ExternalInput").ap()
    w_d = {n: nc.dram_tensor("w" + n, [D, D], F32, kind="ExternalInput").ap()
           for n in ("q", "k", "v", "g")}
    wgk1_d = nc.dram_tensor("wgk1", [D, 16], F32, kind="ExternalInput").ap()
    wgk2_d = nc.dram_tensor("wgk2", [16, D], F32, kind="ExternalInput").ap()
    bgk2_d = nc.dram_tensor("bgk2", [1, D], F32, kind="ExternalInput").ap()

    qf_d = nc.dram_tensor("qf", [8, 128, R], BF16, kind="ExternalOutput").ap()
    kf_d = nc.dram_tensor("kf", [8, 128, R], BF16, kind="ExternalOutput").ap()
    vt_d = nc.dram_tensor("vt", [R, D], BF16, kind="ExternalOutput").ap()
    e1_d = nc.dram_tensor("e1", [R, D], F32, kind="ExternalOutput").ap()
    gate_d = nc.dram_tensor("gate", [R, D], BF16, kind="ExternalOutput").ap()
    lc_d = nc.dram_tensor("lc", [NT, 128, 512], BF16, kind="ExternalOutput").ap()
    tseg_d = nc.dram_tensor("tseg", [128, 512], F32, kind="ExternalOutput").ap()
    dseg_d = nc.dram_tensor("dseg", [1, D], F32, kind="ExternalOutput").ap()
    dpre_d = nc.dram_tensor("dpre", [NT, 1, D], F32, kind="ExternalOutput").ap()

    C = setup_consts(p)

    xfp = p.sb("xfp", [128, D], F32)
    xbf = p.sb("xbf", [128, D], BF16)
    xT = p.sb("xT", [128, 8 * R], BF16)
    w_sb = {n: p.sb(f"w_{n}", [128, 8 * D], BF16) for n in ("q", "k", "v", "g")}
    wgk1_sb = p.sb("wgk1_sb", [128, 8 * 16], BF16)
    wgk2_sb = p.sb("wgk2_sb", [16, D], BF16)
    bgk2_row = p.sb("bgk2_row", [1, D], F32)
    bgk2_bc = p.sb("bgk2_bc", [128, D], F32)
    uf = p.sb("uf", [16, R], BF16)
    z_sb = p.sb("z_sb", [128, D], F32)
    tmp1 = p.sb("tmp1", [128, D], F32)
    hi = p.sb("hi", [128, D], BF16)
    lo = p.sb("lo", [128, D], BF16)
    e1_sb = p.sb("e1_sb", [128, D], F32)
    em_sb = p.sb("em_sb", [128, D], F32)
    vt_sb = p.sb("vt_sb", [128, D], BF16)
    ktm_sb = p.sb("ktm_sb", [128, D], BF16)
    gat_sb = p.sb("gat_sb", [128, D], BF16)
    pf_sb = p.sb("pf_sb", [128, 512], BF16)
    trb_sb = p.sb("trb_sb", [128, 128], BF16)
    egc_sb = p.sb("egc_sb", [1, D], F32)
    gc_sb = p.sb("gc_sb", [1, D], F32)
    e3_sb = p.sb("e3_sb", [128, 512], F32)
    s_sb = p.sb("s_sb", [128, 512], F32)
    sbf_sb = p.sb("sbf_sb", [128, 512], BF16)
    dacc = p.sb("dacc", [1, D], F32)

    ps_tr = p.ps("ps_tr", [128, 128], BF16)
    ps_tr2 = p.ps("ps_tr2", [128, 128], BF16)
    ps_a = p.ps("ps_a", [128, 512], F32)
    ps_b = p.ps("ps_b", [128, 512], F32)
    ps_g = p.ps("ps_g", [128, D], F32)
    ps_m = p.ps("ps_m", [128, 512], F32)
    ps_e3 = p.ps("ps_e3", [128, 512], F32)

    wtoks = {}
    for n in ("q", "k", "v", "g"):
        wtoks[n] = p.gp.op(lambda e, n=n: e.dma_start(
            out=w_sb[n][:].rearrange("p (c d) -> p c d", d=D),
            in_=w_d[n].rearrange("(c p) d -> p c d", p=128)))
    wtoks["gk1"] = p.gp.op(lambda e: e.dma_start(
        out=wgk1_sb[:].rearrange("p (c d) -> p c d", d=16),
        in_=wgk1_d.rearrange("(c p) d -> p c d", p=128)))
    wtoks["gk2"] = p.gp.op(lambda e: e.dma_start(out=wgk2_sb[:], in_=wgk2_d))
    t_bgk2 = p.sy.op(lambda e: e.dma_start(out=bgk2_row[:], in_=bgk2_d))

    mm0 = p.te.op(lambda e: e.matmul(ps_a[:], C["ones1_f"][:], bgk2_row[0:1, 0:512],
                                     start=True, stop=True), deps=[C["loaded"], t_bgk2])
    mm1 = p.te.op(lambda e: e.matmul(ps_b[:], C["ones1_f"][:], bgk2_row[0:1, 512:1024],
                                     start=True, stop=True))
    fa = p.ve.op(lambda e: e.tensor_copy(bgk2_bc[:, 0:512], ps_a[:]), deps=[mm0])
    fb = p.ve.op(lambda e: e.tensor_copy(bgk2_bc[:, 512:1024], ps_b[:]), deps=[mm1])
    p.ve.op(lambda e: e.memset(dacc[:], 0.0))
    p.ve.op(lambda e: e.memset(s_sb[:], 0.0))
    p.ve.op(lambda e: e.memset(sbf_sb[:], 0.0))

    for m in range(NT):
        p.barrier()
        t_x = p.sy.op(lambda e, m=m: e.dma_start(out=xfp[:], in_=x_d[m * 128:(m + 1) * 128, :]))
        t_cast = p.ve.op(lambda e: e.tensor_copy(xbf[:], xfp[:]), deps=[t_x])
        evac = [None, None]
        for kc in range(8):
            pst = ps_tr if kc % 2 == 0 else ps_tr2
            t_tr = p.te.op(lambda e, kc=kc, pst=pst: e.transpose(
                pst[:], xbf[:, kc * 128:(kc + 1) * 128], C["ident_bf"][:]),
                deps=[t_cast] + ([evac[kc % 2]] if evac[kc % 2] else []))
            evac[kc % 2] = p.ve.op(lambda e, kc=kc, m=m, pst=pst: e.tensor_copy(
                xT[:, kc * R + m * 128: kc * R + (m + 1) * 128], pst[:]), deps=[t_tr])

    p.barrier()
    fa, fb = p.te.tok(), p.te.tok()
    for tb4 in range(4):
        mm = None
        for kc in range(8):
            mm = p.te.op(lambda e, kc=kc, tb4=tb4: e.matmul(
                ps_a[0:16, :], wgk1_sb[:, kc * 16:(kc + 1) * 16],
                xT[:, kc * R + tb4 * 512:kc * R + (tb4 + 1) * 512],
                start=(kc == 0), stop=(kc == 7)),
                inc=(kc == 7),
                deps=([wtoks["gk1"], fa] if kc == 0 else []))
        fa = p.ve.op(lambda e, tb4=tb4: e.tensor_copy(uf[:, tb4 * 512:(tb4 + 1) * 512],
                                                      ps_a[0:16, :]), deps=[mm])
    uf_done = fa

    pf_free = None
    for pr in range(8):
        for tb4 in range(4):
            mm = None
            for kc in range(8):
                mm = p.te.op(lambda e, kc=kc, tb4=tb4, pr=pr: e.matmul(
                    ps_b[:], pair_view(w_sb["q"][:, kc * D:(kc + 1) * D], pr),
                    xT[:, kc * R + tb4 * 512:kc * R + (tb4 + 1) * 512],
                    start=(kc == 0), stop=(kc == 7)),
                    inc=(kc == 7),
                    deps=([wtoks["q"], fb] if kc == 0 else []))
            t_e = p.ve.op(lambda e: e.tensor_copy(pf_sb[:], ps_b[:]), deps=[mm, pf_free])
            fb = t_e
            t_s = p.sy.op(lambda e, pr=pr, tb4=tb4: e.dma_start(
                out=qf_d[pr, :, tb4 * 512:(tb4 + 1) * 512], in_=pf_sb[:]), deps=[t_e])
            pf_free = t_s

    for m in range(NT):
        p.barrier()
        mmz = []
        for half in range(2):
            pst = ps_a if half == 0 else ps_b
            mmz.append(p.te.op(lambda e, half=half, m=m, pst=pst: e.matmul(
                pst[:], uf[:, m * 128:(m + 1) * 128],
                wgk2_sb[:, half * 512:(half + 1) * 512], start=True, stop=True),
                deps=[uf_done, wtoks["gk2"]]))
        t_zs0 = p.ve.op(lambda e: e.tensor_add(z_sb[:, 0:512], ps_a[:], bgk2_bc[:, 0:512]),
                        deps=[mmz[0]])
        t_zs1 = p.ve.op(lambda e: e.tensor_add(z_sb[:, 512:1024], ps_b[:], bgk2_bc[:, 512:1024]),
                        deps=[mmz[1]])

        t_abs = p.sc.op(lambda e: e.activation(tmp1[:], z_sb[:], AF.Abs),
                        deps=[t_zs0, t_zs1])
        t_ex = p.sc.op(lambda e: e.activation(tmp1[:], tmp1[:], AF.Exp, scale=-1.0), deps=[t_abs])
        t_ln = p.sc.op(lambda e: e.activation(tmp1[:], tmp1[:], AF.Ln, bias=1.0), deps=[t_ex])
        t_mn = p.ve.op(lambda e: e.tensor_scalar(z_sb[:], z_sb[:], 0.0, None, ALU.min),
                       deps=[t_abs])
        t_sub = p.ve.op(lambda e: e.tensor_sub(z_sb[:], z_sb[:], tmp1[:]), deps=[t_ln, t_mn])
        t_g16 = p.ve.op(lambda e: e.tensor_scalar(z_sb[:], z_sb[:], 1.0 / 16.0, None, ALU.mult),
                        deps=[t_sub])
        t_hi = p.ve.op(lambda e: e.tensor_copy(hi[:], z_sb[:]), deps=[t_g16])
        t_lo = p.ve.op(lambda e: e.tensor_sub(lo[:], z_sb[:], hi[:]), deps=[t_hi])

        mmg = []
        for half in range(2):
            mmlast = None
            for j, src in enumerate((hi, lo)):
                mmlast = p.te.op(lambda e, half=half, src=src, j=j: e.matmul(
                    ps_g[:, half * 512:(half + 1) * 512], C["triu_bf"][:],
                    src[:, half * 512:(half + 1) * 512], start=(j == 0), stop=(j == 1)),
                    inc=(j == 1),
                    deps=([t_hi, t_lo] if j == 0 else []))
            mmg.append(mmlast)
        mmgc = []
        for half in range(2):
            pst = ps_a if half == 0 else ps_b
            mmlast = None
            for j, src in enumerate((hi, lo)):
                mmlast = p.te.op(lambda e, half=half, src=src, j=j, pst=pst: e.matmul(
                    pst[0:1, :], C["ones128_bf"][:],
                    src[:, half * 512:(half + 1) * 512], start=(j == 0), stop=(j == 1)),
                    inc=(j == 1),
                    deps=([t_hi, t_lo, t_zs0, t_zs1] if j == 0 else []))
            mmgc.append(mmlast)
        t_dpre = p.sy.op(lambda e, m=m: e.dma_start(out=dpre_d[m, :, :], in_=dacc[:]))
        t_gc0 = p.ve.op(lambda e: e.tensor_copy(gc_sb[0:1, 0:512], ps_a[0:1, :]), deps=[mmgc[0]])
        t_gc1 = p.ve.op(lambda e: e.tensor_copy(gc_sb[0:1, 512:1024], ps_b[0:1, :]), deps=[mmgc[1]])
        t_dacc = p.ve.op(lambda e: e.tensor_add(dacc[:], dacc[:], gc_sb[:]),
                         deps=[t_gc0, t_gc1, t_dpre])
        t_egc = p.sc.op(lambda e: e.activation(egc_sb[:], gc_sb[:], AF.Exp),
                        deps=[t_gc0, t_gc1])

        t_e1 = p.sc.op(lambda e: e.activation(e1_sb[:], ps_g[:], AF.Exp), deps=[mmg[0], mmg[1]])
        t_em = p.sc.op(lambda e: e.activation(em_sb[:], ps_g[:], AF.Exp, scale=-1.0), deps=[t_e1])
        t_e1s = p.sy.op(lambda e, m=m: e.dma_start(out=e1_d[m * 128:(m + 1) * 128, :], in_=e1_sb[:]),
                        deps=[t_e1])

        mmv = proj_tm(p, (ps_a, ps_b), (t_gc0, t_gc1), xT, w_sb["v"], wtoks["v"], m)
        t_vt0 = p.ve.op(lambda e: e.tensor_mul(vt_sb[:, 0:512], ps_a[:], em_sb[:, 0:512]),
                        deps=[mmv[0], t_em])
        t_vt1 = p.ve.op(lambda e: e.tensor_mul(vt_sb[:, 512:1024], ps_b[:], em_sb[:, 512:1024]),
                        deps=[mmv[1], t_em])
        t_vts = p.sy.op(lambda e, m=m: e.dma_start(out=vt_d[m * 128:(m + 1) * 128, :], in_=vt_sb[:]),
                        deps=[t_vt0, t_vt1])

        mmk = proj_tm(p, (ps_a, ps_b), (t_vt0, t_vt1), xT, w_sb["k"], wtoks["k"], m)
        t_k0 = p.ve.op(lambda e: e.tensor_copy(ktm_sb[:, 0:512], ps_a[:]), deps=[mmk[0]])
        t_k1 = p.ve.op(lambda e: e.tensor_copy(ktm_sb[:, 512:1024], ps_b[:]), deps=[mmk[1]])
        evac = [None, None]
        st_prev = [None, None]
        for pr in range(8):
            pst = ps_tr if pr % 2 == 0 else ps_tr2
            t_tr = p.te.op(lambda e, pr=pr, pst=pst: e.transpose(
                pst[:], pair_view(ktm_sb[:], pr), C["ident_bf"][:]),
                deps=[t_k0, t_k1] + ([evac[pr % 2]] if evac[pr % 2] else []))
            t_ke = p.ve.op(lambda e, pst=pst: e.tensor_copy(trb_sb[:], pst[:]),
                           deps=[t_tr] + ([st_prev[0]] if st_prev[0] else []))
            evac[pr % 2] = t_ke
            t_ks = p.sy.op(lambda e, pr=pr, m=m: e.dma_start(
                out=kf_d[pr, :, m * 128:(m + 1) * 128], in_=trb_sb[:]), deps=[t_ke])
            st_prev[0] = t_ks

        mmgat = proj_tm(p, (ps_a, ps_b), (t_k0, t_k1), xT, w_sb["g"], wtoks["g"], m)
        t_ga0 = p.sc.op(lambda e: e.activation(gat_sb[:, 0:512], ps_a[:], AF.Silu), deps=[mmgat[0]])
        t_ga1 = p.sc.op(lambda e: e.activation(gat_sb[:, 512:1024], ps_b[:], AF.Silu), deps=[mmgat[1]])
        t_gas = p.sy.op(lambda e, m=m: e.dma_start(out=gate_d[m * 128:(m + 1) * 128, :], in_=gat_sb[:]),
                        deps=[t_ga0, t_ga1])

        mmM = None
        for h in range(16):
            pr, half = h % 8, h // 8
            tp = (0, 0) if half == 0 else (0, 64)
            mmM = p.te.op(lambda e, h=h, pr=pr, half=half, tp=tp: e.matmul(
                ps_m[64 * half:64 * (half + 1), pr * 64:(pr + 1) * 64],
                ktm_sb[:, ppos(h):ppos(h) + 64], vt_sb[:, h * 64:(h + 1) * 64],
                start=True, stop=True, tile_position=tp),
                inc=(h == 15),
                deps=([t_k0, t_k1, t_vt0, t_vt1] if h == 0 else []))

        t_e3a = p.te.op(lambda e: e.matmul(ps_e3[0:64, :], C["ones1_f"][0:1, 0:64],
                                           egc_sb[0:1, 0:512], start=True, stop=True),
                        deps=[t_egc])
        t_e3b = p.te.op(lambda e: e.matmul(ps_e3[64:128, :], C["ones1_f"][0:1, 0:64],
                                           egc_sb[0:1, 512:1024], start=True, stop=True,
                                           tile_position=(0, 64)))
        t_e3c = p.ve.op(lambda e: e.tensor_copy(e3_sb[:], ps_e3[:]), deps=[t_e3b])

        t_lcs = p.sy.op(lambda e, m=m: e.dma_start(out=lc_d[m, :, :], in_=sbf_sb[:]))
        t_sadd = p.ve.op(lambda e: e.tensor_add(s_sb[:], s_sb[:], ps_m[:]), deps=[mmM, t_e3c])
        t_smul = p.ve.op(lambda e: e.tensor_mul(s_sb[:], s_sb[:], e3_sb[:]), deps=[t_sadd])
        p.ve.op(lambda e: e.tensor_copy(sbf_sb[:], s_sb[:]), deps=[t_smul, t_lcs])

    p.barrier()
    p.sy.op(lambda e: e.dma_start(out=tseg_d, in_=s_sb[:]))
    p.sy.op(lambda e: e.dma_start(out=dseg_d, in_=dacc[:]))
    p.sy.op(lambda e: e.nop(), inc=False, deps=[p.sy.tok(), p.gp.tok()])
    return p.finish()




def bcast_row(p, C, row_ap, ps, dst_sb, deps, n=1024):
    last = None
    nh = (n + 511) // 512
    for half in range(nh):
        w = min(512, n - half * 512)
        mm = p.te.op(lambda e, half=half, w=w: e.matmul(
            ps[:, 0:w], C["ones1_f"][:],
            row_ap[0:1, half * 512:half * 512 + w], start=True, stop=True),
            deps=(list(deps) if half == 0 else []) + ([last] if last else []))
        last = p.ve.op(lambda e, half=half, w=w: e.tensor_copy(
            dst_sb[:, half * 512:half * 512 + w], ps[:, 0:w]),
            deps=[mm])
    return last


def bcast_slayout(p, C, row_ap, ps_e3, e3_sb, deps):
    mm1 = p.te.op(lambda e: e.matmul(ps_e3[0:64, :], C["ones1_f"][0:1, 0:64],
                                     row_ap[0:1, 0:512], start=True, stop=True),
                  deps=deps)
    mm2 = p.te.op(lambda e: e.matmul(ps_e3[64:128, :], C["ones1_f"][0:1, 0:64],
                                     row_ap[0:1, 512:1024], start=True, stop=True,
                                     tile_position=(0, 64)))
    return p.ve.op(lambda e: e.tensor_copy(e3_sb[:], ps_e3[:]), deps=[mm1, mm2])


def ln_rowwise(p, x_ap, st1, st2, deps, eps=1e-5, n=1024):
    t_s = p.ve.op(lambda e: e.reduce_sum(st1[:], x_ap, axis=AX.X), deps=deps)
    t_m = p.ve.op(lambda e: e.tensor_scalar(st1[:], st1[:], 1.0 / n, None, ALU.mult),
                  deps=[t_s])
    t_c = p.ve.op(lambda e: e.tensor_scalar(x_ap, x_ap, st1[:, 0:1], None, ALU.subtract),
                  deps=[t_m])
    t_sq = p.sc.op(lambda e: e.activation(st2[:, 0:n], x_ap, AF.Square,
                                          accum_out=st1[:, 0:1]), deps=[t_c])
    t_v = p.ve.op(lambda e: e.tensor_scalar(st1[:], st1[:], 1.0 / n, eps, ALU.mult, ALU.add),
                  deps=[t_sq])
    t_sq2 = p.sc.op(lambda e: e.activation(st1[:], st1[:], AF.Sqrt), deps=[t_v])
    t_r = p.ve.op(lambda e: e.reciprocal(st1[:], st1[:]), deps=[t_sq2])
    t_n = p.ve.op(lambda e: e.tensor_scalar(x_ap, x_ap, st1[:, 0:1], None, ALU.mult),
                  deps=[t_r])
    return t_n


def xt_build(p, C, src_ap_fn, xT, xbf, ps_tr, ps_tr2, m, deps):
    t_cast = p.ve.op(lambda e, m=m: e.tensor_copy(xbf[:], src_ap_fn(m)), deps=deps)
    evac = [None, None]
    last = None
    for kc in range(8):
        pst = ps_tr if kc % 2 == 0 else ps_tr2
        t_tr = p.te.op(lambda e, kc=kc, pst=pst: e.transpose(
            pst[:], xbf[:, kc * 128:(kc + 1) * 128], C["ident_bf"][:]),
            deps=[t_cast] + ([evac[kc % 2]] if evac[kc % 2] else []))
        last = p.ve.op(lambda e, kc=kc, m=m, pst=pst: e.tensor_copy(
            xT[:, kc * R + m * 128: kc * R + (m + 1) * 128], pst[:]), deps=[t_tr])
        evac[kc % 2] = last
    return last


STOP_B = "full"
ATTN_TILES = 16
ATTN_SUB = 4


def build_B():
    stop = STOP_B
    def done():
        p.barrier()
        for m_ in range(NT):
            p.sy.op(lambda e, m_=m_: e.dma_start(out=x2_d[m_ * 128:(m_ + 1) * 128, :],
                                                 in_=X_sb[:, m_ * D:(m_ + 1) * D]))
        p.barrier()
        p.sy.op(lambda e: e.nop(), inc=False, deps=[p.sy.tok(), p.gp.tok()])
        return p.finish()
    p = Prog()
    nc = p.nc
    x_d = nc.dram_tensor("x", [R, D], F32, kind="ExternalInput").ap()
    qfi_d = nc.dram_tensor("qfi", [8, 128, R], BF16, kind="ExternalInput").ap()
    kfi_d = nc.dram_tensor("kfi", [8, 128, R], BF16, kind="ExternalInput").ap()
    vti_d = nc.dram_tensor("vti", [R, D], BF16, kind="ExternalInput").ap()
    e1i_d = nc.dram_tensor("e1i", [R, D], F32, kind="ExternalInput").ap()
    gati_d = nc.dram_tensor("gati", [R, D], BF16, kind="ExternalInput").ap()
    lci_d = nc.dram_tensor("lci", [NT, 128, 512], BF16, kind="ExternalInput").ap()
    dprei_d = nc.dram_tensor("dprei", [NT, 1, D], F32, kind="ExternalInput").ap()
    tcomp_d = nc.dram_tensor("tcomp", [3, 128, 512], F32, kind="ExternalInput").ap()
    dcomp_d = nc.dram_tensor("dcomp", [3, 1, D], F32, kind="ExternalInput").ap()
    wout_d = nc.dram_tensor("wout", [D, D], F32, kind="ExternalInput").ap()
    gamma_d = nc.dram_tensor("gamma", [1, D], F32, kind="ExternalInput").ap()
    beta_d = nc.dram_tensor("beta", [1, D], F32, kind="ExternalInput").ap()
    gatew_d = nc.dram_tensor("gatew", [D, 8], F32, kind="ExternalInput").ap()
    gateb_d = nc.dram_tensor("gateb", [1, 8], F32, kind="ExternalInput").ap()
    w1n_d = nc.dram_tensor("w1n", [8, D, D], F32, kind="ExternalInput").ap()
    w3n_d = nc.dram_tensor("w3n", [8, D, D], F32, kind="ExternalInput").ap()
    w2n_d = nc.dram_tensor("w2n", [8, D, D], F32, kind="ExternalInput").ap()
    sw1n_d = nc.dram_tensor("sw1n", [D, 2048], F32, kind="ExternalInput").ap()
    sw3n_d = nc.dram_tensor("sw3n", [D, 2048], F32, kind="ExternalInput").ap()
    sw2n_d = nc.dram_tensor("sw2n", [2048, D], F32, kind="ExternalInput").ap()
    wqn_d = nc.dram_tensor("wqn", [D, D], F32, kind="ExternalInput").ap()
    wkvn_d = nc.dram_tensor("wkvn", [D, D], F32, kind="ExternalInput").ap()
    lng_d = nc.dram_tensor("lng", [1, D], F32, kind="ExternalInput").ap()
    lnb_d = nc.dram_tensor("lnb", [1, D], F32, kind="ExternalInput").ap()
    cosf_d = nc.dram_tensor("cosf", [R, D], F32, kind="ExternalInput").ap()
    sinsf_d = nc.dram_tensor("sinsf", [R, D], F32, kind="ExternalInput").ap()
    x2_d = nc.dram_tensor("x2", [R, D], F32, kind="ExternalOutput").ap()
    qtm_d = nc.dram_tensor("qtm", [R, D], BF16, kind="ExternalOutput").ap()
    lkv_d = nc.dram_tensor("lkv", [R, D], BF16, kind="ExternalOutput").ap()

    C = setup_consts(p)

    X_sb = p.sb("X_sb", [128, NT * D], F32)
    xT = p.sb("xT", [128, 8 * R], BF16)
    qfb = p.sb("qfb", [128, 8 * 128], BF16)
    kfb = p.sb("kfb", [128, 8 * 128], BF16)
    vt_sb = p.sb("vt_sb", [128, D], BF16)
    e1_sb = p.sb("e1_sb", [128, D], F32)
    gat_sb = p.sb("gat_sb", [128, D], BF16)
    lc_sb = p.sb("lc_sb", [128, 512], BF16)
    dpre_sb = p.sb("dpre_sb", [1, D], F32)
    e3_sb = p.sb("e3_sb", [128, 512], F32)
    sin_sb = p.sb("sin_sb", [128, 512], F32)
    scw_sb = p.sb("scw_sb", [128, 512], F32)
    tc_sb = scw_sb
    scb_sb = p.sb("scb_sb", [128, 512], BF16)
    am_sb = p.sb("am_sb", [128, 2 * 128], BF16)
    o_sb = p.sb("o_sb", [128, D], F32)
    st1 = p.sb("st1", [128, 1], F32)
    st2 = p.sb("st2", [128, D], F32)
    row_sb = p.sb("row_sb", [1, D], F32)
    gamma_row = beta_row = row_sb
    gamma_bc = p.sb("gamma_bc", [128, D], F32)
    beta_bc = p.sb("beta_bc", [128, D], F32)
    z_sb = p.sb("z_sb", [128, D], BF16)
    xbf = z_sb
    trb_sb = p.sb("trb_sb", [128, 128], BF16)
    gatew_sb = p.sb("gatew_sb", [128, 8 * 8], BF16)
    gateb_row = p.sb("gateb_row", [1, 8], F32)
    gateb_bc = p.sb("gateb_bc", [128, 8], F32)
    gates_sb = p.sb("gates_sb", [128, NT * 8], F32)
    sg1 = p.sb("sg1", [128, 8], F32)
    sg2 = p.sb("sg2", [128, 8], F32)
    sg3 = p.sb("sg3", [128, 8], F32)
    w1_sb = p.sb("w1_sb", [128, 8 * D], BF16)
    w3_sb = p.sb("w3_sb", [128, 8 * D], BF16)
    w2_sb = p.sb("w2_sb", [128, 8 * D], BF16)
    hf_sb = p.sb("hf_sb", [128, 8 * 512], BF16)
    sl_sb = p.sb("sl_sb", [128, 512], BF16)
    wout_sb = w2_sb
    lng_bc = gamma_bc
    lnb_bc = beta_bc
    lng_row = lnb_row = row_sb
    cos_sb = e1_sb
    sins_sb = o_sb
    rp1 = None
    rp2 = st2
    ot_sb = p.sb("ot_sb", [128, D], BF16)
    stat16a = p.sb("stat16a", [128, 16], F32)
    stat16b = p.sb("stat16b", [128, 16], F32)

    rp1 = hf_sb[:].bitcast(F32)[:, 0:D]

    ps_tr = p.ps("ps_tr", [128, 128], BF16)
    ps_tr2 = p.ps("ps_tr2", [128, 128], BF16)
    ps_a = p.ps("ps_a", [128, 512], F32)
    ps_b = p.ps("ps_b", [128, 512], F32)
    ps_o = p.ps("ps_o", [128, D], F32)
    ps_e3 = p.ps("ps_e3", [128, 512], F32)

    t_x = None
    for m in range(NT):
        t_x = p.sy.op(lambda e, m=m: e.dma_start(out=X_sb[:, m * D:(m + 1) * D],
                                                 in_=x_d[m * 128:(m + 1) * 128, :]))
    pass
    t_gam = p.sy.op(lambda e: e.dma_start(out=gamma_row[:], in_=gamma_d))
    t_gbc = bcast_row(p, C, gamma_row, ps_e3, gamma_bc, [t_gam, C["loaded"]])
    t_bet = p.sy.op(lambda e: e.dma_start(out=beta_row[:], in_=beta_d), deps=[t_gbc])
    t_bbc = bcast_row(p, C, beta_row, ps_e3, beta_bc, [t_bet])
    t_wout = p.gp.op(lambda e: e.dma_start(
        out=wout_sb[:].rearrange("p (c d) -> p c d", d=D),
        in_=wout_d.rearrange("(c p) d -> p c d", p=128)))

    t_prev = p.ve.op(lambda e: e.memset(sin_sb[:], 0.0))
    for j in range(3):
        t_dc = p.sy.op(lambda e, j=j: e.dma_start(out=dpre_sb[:], in_=dcomp_d[j]),
                       deps=[t_prev])
        t_tc = p.sy.op(lambda e, j=j: e.dma_start(out=tc_sb[:], in_=tcomp_d[j]),
                       deps=[t_prev])
        t_ex = p.sc.op(lambda e: e.activation(dpre_sb[:], dpre_sb[:], AF.Exp),
                       deps=[t_dc, C["loaded"]])
        t_e3 = bcast_slayout(p, C, dpre_sb, ps_e3, e3_sb, [t_ex, t_gbc, t_bbc])
        t_m = p.ve.op(lambda e: e.tensor_mul(sin_sb[:], sin_sb[:], e3_sb[:]),
                      deps=[t_e3, t_prev])
        t_prev = p.ve.op(lambda e: e.tensor_add(sin_sb[:], sin_sb[:], tc_sb[:]),
                         deps=[t_m, t_tc])
    sin_done = t_prev
    if stop == "compose":
        return done()

    for m in range(min(NT, ATTN_TILES)):
        p.barrier()
        t_qf = p.sy.op(lambda e, m=m: e.dma_start(
            out=qfb[:].rearrange("p (r c) -> p r c", c=128),
            in_=qfi_d[:, :, m * 128:(m + 1) * 128].rearrange("r p c -> p r c")))
        t_kf = p.sy.op(lambda e, m=m: e.dma_start(
            out=kfb[:].rearrange("p (r c) -> p r c", c=128),
            in_=kfi_d[:, :, m * 128:(m + 1) * 128].rearrange("r p c -> p r c")))
        t_vt = p.sy.op(lambda e, m=m: e.dma_start(out=vt_sb[:], in_=vti_d[m * 128:(m + 1) * 128, :]))
        t_e1 = p.sy.op(lambda e, m=m: e.dma_start(out=e1_sb[:], in_=e1i_d[m * 128:(m + 1) * 128, :]))
        t_ga = p.sy.op(lambda e, m=m: e.dma_start(out=gat_sb[:], in_=gati_d[m * 128:(m + 1) * 128, :]))
        t_lc = p.sy.op(lambda e, m=m: e.dma_start(out=lc_sb[:], in_=lci_d[m]))
        t_dp = p.sy.op(lambda e, m=m: e.dma_start(out=dpre_sb[:], in_=dprei_d[m]))

        t_ex = p.sc.op(lambda e: e.activation(dpre_sb[:], dpre_sb[:], AF.Exp),
                       deps=[t_dp, sin_done])
        t_e3 = bcast_slayout(p, C, dpre_sb, ps_e3, e3_sb, [t_ex])
        t_sc1 = p.ve.op(lambda e: e.tensor_mul(scw_sb[:], e3_sb[:], sin_sb[:]), deps=[t_e3])
        t_sc2 = p.ve.op(lambda e: e.tensor_add(scw_sb[:], scw_sb[:], lc_sb[:]),
                        deps=[t_sc1, t_lc])
        t_scb = p.ve.op(lambda e: e.tensor_copy(scb_sb[:], scw_sb[:]), deps=[t_sc2])
        if ATTN_SUB < 1:
            continue

        mask_prev = None
        mm_last = None
        for pr in range(8):
            mmA = p.te.op(lambda e, pr=pr: e.matmul(
                ps_a[:, 0:128], kfb[0:64, pr * 128:(pr + 1) * 128],
                qfb[0:64, pr * 128:(pr + 1) * 128], start=True, stop=True,
                tile_position=(0, 0)),
                deps=([t_qf, t_kf, t_vt, t_scb] if pr == 0 else [mask_prev]))
            mmA2 = p.te.op(lambda e, pr=pr: e.matmul(
                ps_b[:, 0:128], kfb[64:128, pr * 128:(pr + 1) * 128],
                qfb[64:128, pr * 128:(pr + 1) * 128], start=True, stop=True,
                tile_position=(64, 0)))
            t_m1 = p.ve.op(lambda e: e.tensor_mul(am_sb[:, 0:128], ps_a[:, 0:128],
                                                  C["triu_f"][:]), deps=[mmA, mmA2])
            t_m2 = p.ve.op(lambda e: e.tensor_mul(am_sb[:, 128:256], ps_b[:, 0:128],
                                                  C["triu_f"][:]), deps=[mmA2])
            mask_prev = t_m2
            if ATTN_SUB < 2:
                mm_last = mmA2
                continue
            p.te.op(lambda e, pr=pr: e.matmul(
                ps_o[:, pr * 64:(pr + 1) * 64], am_sb[:, 0:128],
                vt_sb[:, pr * 64:(pr + 1) * 64], start=True, stop=False),
                inc=False, deps=[t_m1, t_m2])
            p.te.op(lambda e, pr=pr: e.matmul(
                ps_o[:, pr * 64:(pr + 1) * 64], qfb[0:64, pr * 128:(pr + 1) * 128],
                scb_sb[0:64, pr * 64:(pr + 1) * 64], start=False, stop=True,
                tile_position=(0, 0)), inc=False)
            p.te.op(lambda e, pr=pr: e.matmul(
                ps_o[:, 512 + pr * 64:512 + (pr + 1) * 64], am_sb[:, 128:256],
                vt_sb[:, 512 + pr * 64:512 + (pr + 1) * 64], start=True, stop=False),
                inc=False)
            mm_last = p.te.op(lambda e, pr=pr: e.matmul(
                ps_o[:, 512 + pr * 64:512 + (pr + 1) * 64],
                qfb[64:128, pr * 128:(pr + 1) * 128],
                scb_sb[64:128, pr * 64:(pr + 1) * 64], start=False, stop=True,
                tile_position=(64, 0)))

        if ATTN_SUB < 3:
            if ATTN_SUB >= 2:
                p.ve.op(lambda e: e.tensor_copy(o_sb[:], ps_o[:]), deps=[mm_last])
            continue
        t_o = p.ve.op(lambda e: e.tensor_mul(o_sb[:], ps_o[:], e1_sb[:]),
                      deps=[mm_last, t_e1])
        t_ln = ln_rowwise(p, o_sb[:], st1, st2, [t_o])
        t_a1 = p.ve.op(lambda e: e.tensor_mul(o_sb[:], o_sb[:], gamma_bc[:]),
                       deps=[t_ln, t_gbc])
        t_a2 = p.ve.op(lambda e: e.tensor_add(o_sb[:], o_sb[:], beta_bc[:]),
                       deps=[t_a1, t_bbc])
        t_zz = p.ve.op(lambda e: e.tensor_mul(z_sb[:], o_sb[:], gat_sb[:]),
                       deps=[t_a2, t_ga])

        if ATTN_SUB < 4:
            continue
        mml = None
        evac = [None, None]
        for kc in range(8):
            pst = ps_tr if kc % 2 == 0 else ps_tr2
            t_tr = p.te.op(lambda e, kc=kc, pst=pst: e.transpose(
                pst[:], z_sb[:, kc * 128:(kc + 1) * 128], C["ident_bf"][:]),
                deps=[t_zz] + ([evac[kc % 2]] if evac[kc % 2] else []))
            t_ev = p.ve.op(lambda e, pst=pst: e.tensor_copy(trb_sb[:], pst[:]),
                           deps=[t_tr] + ([mml] if mml else [t_o]))
            evac[kc % 2] = t_ev
            for half in range(2):
                mml = p.te.op(lambda e, kc=kc, half=half: e.matmul(
                    ps_o[:, half * 512:(half + 1) * 512], trb_sb[:],
                    wout_sb[:, kc * D + half * 512:kc * D + (half + 1) * 512],
                    start=(kc == 0), stop=(kc == 7)),
                    inc=True,
                    deps=([t_ev, t_wout] if half == 0 else [t_ev]))
        t_res = p.ve.op(lambda e, m=m: e.tensor_add(X_sb[:, m * D:(m + 1) * D],
                                                    X_sb[:, m * D:(m + 1) * D], ps_o[:]),
                        deps=[mml, t_x])
        ln_rowwise(p, X_sb[:, m * D:(m + 1) * D], st1, st2, [t_res])

    if stop == "attn":
        return done()
    p.barrier()
    for m in range(NT):
        p.barrier()
        xt_build(p, C, lambda mm: X_sb[:, mm * D:(mm + 1) * D], xT, xbf, ps_tr, ps_tr2, m, [])

    if stop == "xt1":
        return done()
    p.barrier()
    t_gw = p.gp.op(lambda e: e.dma_start(
        out=gatew_sb[:].rearrange("p (c d) -> p c d", d=8),
        in_=gatew_d.rearrange("(c p) d -> p c d", p=128)))
    t_gb = p.sy.op(lambda e: e.dma_start(out=gateb_row[:], in_=gateb_d))
    t_gbb = bcast_row(p, C, gateb_row, ps_e3, gateb_bc, [t_gb], n=8)
    for m in range(NT):
        mm = None
        for kc in range(8):
            mm = p.te.op(lambda e, kc=kc, m=m: e.matmul(
                ps_a[:, 0:8], xT[:, kc * R + m * 128:kc * R + (m + 1) * 128],
                gatew_sb[:, kc * 8:(kc + 1) * 8], start=(kc == 0), stop=(kc == 7)),
                inc=(kc == 7), deps=([t_gw, (p.ve, p.ve.n)] if kc == 0 else []))
        t_mx = p.ve.op(lambda e: e.reduce_max(st1[:], ps_a[:, 0:8], axis=AX.X, negate=True),
                       deps=[mm])
        t_ep = p.sc.op(lambda e: e.activation(sg1[:], ps_a[:, 0:8], AF.Exp,
                                              bias=st1[:, 0:1], accum_out=st2[:, 0:1]),
                       deps=[t_mx])
        t_rc = p.ve.op(lambda e: e.reciprocal(st2[:, 0:1], st2[:, 0:1]), deps=[t_ep])
        t_sc = p.ve.op(lambda e, m=m: e.tensor_scalar(gates_sb[:, m * 8:(m + 1) * 8],
                                                      sg1[:], st2[:, 0:1], None, ALU.mult),
                       deps=[t_rc])
        t_sb = p.ve.op(lambda e, m=m: e.tensor_add(sg1[:], gates_sb[:, m * 8:(m + 1) * 8],
                                                   gateb_bc[:]), deps=[t_sc, t_gbb])
        t_n1 = p.ve.op(lambda e: e.reduce_max(st1[:], sg1[:], axis=AX.X, negate=True),
                       deps=[t_sb])
        t_s1 = p.sc.op(lambda e: e.activation(sg2[:], sg1[:], AF.Sign, bias=st1[:, 0:1]),
                       deps=[t_n1])
        t_p1 = p.ve.op(lambda e: e.tensor_scalar(sg2[:], sg2[:], 1.0, None, ALU.add),
                       deps=[t_s1])
        t_mk = p.ve.op(lambda e: e.scalar_tensor_tensor(sg1[:], sg2[:], -1e30, sg1[:],
                                                        ALU.mult, ALU.add), deps=[t_p1])
        t_n2 = p.ve.op(lambda e: e.reduce_max(st1[:], sg1[:], axis=AX.X, negate=True),
                       deps=[t_mk])
        t_s2 = p.sc.op(lambda e: e.activation(sg3[:], sg1[:], AF.Sign, bias=st1[:, 0:1]),
                       deps=[t_n2])
        t_p2 = p.ve.op(lambda e: e.tensor_scalar(sg3[:], sg3[:], 1.0, None, ALU.add),
                       deps=[t_s2])
        t_ms = p.ve.op(lambda e: e.tensor_add(sg2[:], sg2[:], sg3[:]), deps=[t_p2])
        p.ve.op(lambda e, m=m: e.tensor_mul(gates_sb[:, m * 8:(m + 1) * 8],
                                            gates_sb[:, m * 8:(m + 1) * 8], sg2[:]),
                deps=[t_ms])

    if stop == "gating":
        return done()
    prev_sl = None
    prev_sto = None
    prev_wuse = None
    for ex in range(8):
        if prev_wuse is None:
            p.barrier()
            d1 = d3 = d2 = []
        else:
            d1, d3, d2 = ([prev_wuse[0]], [prev_wuse[1]], [prev_wuse[2]])
        t_w1 = p.gp.op(lambda e, ex=ex: e.dma_start(
            out=w1_sb[:].rearrange("p (c d) -> p c d", d=D),
            in_=w1n_d[ex].rearrange("(c p) d -> p c d", p=128)), deps=d1)
        t_w3 = p.gp.op(lambda e, ex=ex: e.dma_start(
            out=w3_sb[:].rearrange("p (c d) -> p c d", d=D),
            in_=w3n_d[ex].rearrange("(c p) d -> p c d", p=128)), deps=d3)
        t_w2 = p.gp.op(lambda e, ex=ex: e.dma_start(
            out=w2_sb[:].rearrange("p (c d) -> p c d", d=D),
            in_=w2n_d[ex].rearrange("(c p) d -> p c d", p=128)), deps=d2)
        for tg in range(4):
            for i in range(8):
                mm1 = None
                for kc in range(8):
                    mm1 = p.te.op(lambda e, kc=kc, i=i, tg=tg: e.matmul(
                        ps_a[:], w1_sb[:, kc * D + i * 128:kc * D + (i + 1) * 128],
                        xT[:, kc * R + tg * 512:kc * R + (tg + 1) * 512],
                        start=(kc == 0), stop=(kc == 7)),
                        inc=(kc == 7),
                        deps=([t_w1] + ([prev_sl] if prev_sl else [(p.ve, p.ve.n)])
                              if kc == 0 else []))
                t_sl = p.sc.op(lambda e: e.activation(sl_sb[:], ps_a[:], AF.Silu),
                               deps=[mm1])
                prev_sl = t_sl
                mm3 = None
                for kc in range(8):
                    mm3 = p.te.op(lambda e, kc=kc, i=i, tg=tg: e.matmul(
                        ps_b[:], w3_sb[:, kc * D + i * 128:kc * D + (i + 1) * 128],
                        xT[:, kc * R + tg * 512:kc * R + (tg + 1) * 512],
                        start=(kc == 0), stop=(kc == 7)),
                        inc=(kc == 7),
                        deps=([t_w3, t_sl] if kc == 0 else []))
                p.ve.op(lambda e, i=i: e.tensor_mul(hf_sb[:, i * 512:(i + 1) * 512],
                                                    ps_b[:], sl_sb[:]),
                        deps=[mm3, t_sl])
            hf_done = (p.ve, p.ve.n)
            for tt in range(4):
                mty = m_ = tg * 4 + tt
                mm = None
                for dh in range(2):
                    for i in range(8):
                        mm = p.te.op(lambda e, i=i, dh=dh, tt=tt: e.matmul(
                            ps_o[:, dh * 512:(dh + 1) * 512],
                            hf_sb[:, i * 512 + tt * 128:i * 512 + (tt + 1) * 128],
                            w2_sb[:, i * D + dh * 512:i * D + (dh + 1) * 512],
                            start=(i == 0), stop=(i == 7)),
                            inc=(i == 7),
                            deps=([hf_done, t_w2]
                                  + ([prev_sto] if prev_sto else [(p.ve, p.ve.n)])
                                  if (i == 0 and dh == 0) else []))
                prev_sto = p.ve.op(lambda e, m_=m_, ex=ex: e.scalar_tensor_tensor(
                    X_sb[:, m_ * D:(m_ + 1) * D], ps_o[:],
                    gates_sb[:, m_ * 8 + ex:m_ * 8 + ex + 1],
                    X_sb[:, m_ * D:(m_ + 1) * D], ALU.mult, ALU.add),
                    deps=[mm])
        prev_wuse = (mm1, mm3, mm)

    if stop == "experts":
        return done()
    for ic in range(4):
        p.barrier()
        t_s1w = p.gp.op(lambda e, ic=ic: e.dma_start(
            out=w1_sb[:, 0:8 * 512].rearrange("p (c d) -> p c d", d=512),
            in_=sw1n_d[:, ic * 512:(ic + 1) * 512].rearrange("(c p) d -> p c d", p=128)))
        t_s3w = p.gp.op(lambda e, ic=ic: e.dma_start(
            out=w3_sb[:, 0:8 * 512].rearrange("p (c d) -> p c d", d=512),
            in_=sw3n_d[:, ic * 512:(ic + 1) * 512].rearrange("(c p) d -> p c d", p=128)))
        t_s2w = p.gp.op(lambda e, ic=ic: e.dma_start(
            out=w2_sb[:, 0:4 * D].rearrange("p (c d) -> p c d", d=D),
            in_=sw2n_d[ic * 512:(ic + 1) * 512, :].rearrange("(c p) d -> p c d", p=128)))
        for tg in range(4):
            for i in range(4):
                mm1 = None
                for kc in range(8):
                    mm1 = p.te.op(lambda e, kc=kc, i=i, tg=tg: e.matmul(
                        ps_a[:], w1_sb[:, kc * 512 + i * 128:kc * 512 + (i + 1) * 128],
                        xT[:, kc * R + tg * 512:kc * R + (tg + 1) * 512],
                        start=(kc == 0), stop=(kc == 7)),
                        inc=(kc == 7),
                        deps=([t_s1w] + ([prev_sl] if prev_sl else [(p.ve, p.ve.n)])
                              if kc == 0 else []))
                t_sl = p.sc.op(lambda e: e.activation(sl_sb[:], ps_a[:], AF.Silu),
                               deps=[mm1])
                prev_sl = t_sl
                mm3 = None
                for kc in range(8):
                    mm3 = p.te.op(lambda e, kc=kc, i=i, tg=tg: e.matmul(
                        ps_b[:], w3_sb[:, kc * 512 + i * 128:kc * 512 + (i + 1) * 128],
                        xT[:, kc * R + tg * 512:kc * R + (tg + 1) * 512],
                        start=(kc == 0), stop=(kc == 7)),
                        inc=(kc == 7),
                        deps=([t_s3w, t_sl] if kc == 0 else []))
                p.ve.op(lambda e, i=i: e.tensor_mul(hf_sb[:, i * 512:(i + 1) * 512],
                                                    ps_b[:], sl_sb[:]),
                        deps=[mm3, t_sl])
            hf_done = (p.ve, p.ve.n)
            for tt in range(4):
                m_ = tg * 4 + tt
                mm = None
                for dh in range(2):
                    for i in range(4):
                        mm = p.te.op(lambda e, i=i, dh=dh, tt=tt: e.matmul(
                            ps_o[:, dh * 512:(dh + 1) * 512],
                            hf_sb[:, i * 512 + tt * 128:i * 512 + (tt + 1) * 128],
                            w2_sb[:, i * D + dh * 512:i * D + (dh + 1) * 512],
                            start=(i == 0), stop=(i == 3)),
                            inc=(i == 3),
                            deps=([hf_done, t_s2w]
                                  + ([prev_sto] if prev_sto else [(p.ve, p.ve.n)])
                                  if (i == 0 and dh == 0) else []))
                prev_sto = p.ve.op(lambda e, m_=m_: e.tensor_add(
                    X_sb[:, m_ * D:(m_ + 1) * D], X_sb[:, m_ * D:(m_ + 1) * D], ps_o[:]),
                    deps=[mm])

    if stop == "shared":
        return done()
    p.barrier()
    t_prevln = None
    for m in range(NT):
        t_n = ln_rowwise(p, X_sb[:, m * D:(m + 1) * D], st1, st2,
                         [t_prevln] if t_prevln else [])
        t_prevln = t_n
        p.sy.op(lambda e, m=m: e.dma_start(out=x2_d[m * 128:(m + 1) * 128, :],
                                           in_=X_sb[:, m * D:(m + 1) * D]), deps=[t_n])

    if stop == "ln2":
        return done()
    p.barrier()
    for m in range(NT):
        p.barrier()
        xt_build(p, C, lambda mm: X_sb[:, mm * D:(mm + 1) * D], xT, xbf, ps_tr, ps_tr2, m, [])
    p.barrier()
    wq_sb, wkv_sb = w1_sb, w3_sb
    t_wq = p.gp.op(lambda e: e.dma_start(
        out=wq_sb[:].rearrange("p (c d) -> p c d", d=D),
        in_=wqn_d.rearrange("(c p) d -> p c d", p=128)))
    t_wkv = p.gp.op(lambda e: e.dma_start(
        out=wkv_sb[:].rearrange("p (c d) -> p c d", d=D),
        in_=wkvn_d.rearrange("(c p) d -> p c d", p=128)))
    t_lg = p.sy.op(lambda e: e.dma_start(out=lng_row[:], in_=lng_d))
    t_lgb = bcast_row(p, C, lng_row, ps_e3, lng_bc, [t_lg])
    t_lb = p.sy.op(lambda e: e.dma_start(out=lnb_row[:], in_=lnb_d), deps=[t_lgb])
    t_lbb = bcast_row(p, C, lnb_row, ps_e3, lnb_bc, [t_lb])

    for m in range(NT):
        p.barrier()
        t_cos = p.sy.op(lambda e, m=m: e.dma_start(out=cos_sb[:], in_=cosf_d[m * 128:(m + 1) * 128, :]))
        t_sin = p.sy.op(lambda e, m=m: e.dma_start(out=sins_sb[:], in_=sinsf_d[m * 128:(m + 1) * 128, :]))
        t_store_prev = None
        for which, wsb, wtok, outd in (("q", wq_sb, t_wq, qtm_d), ("kv", wkv_sb, t_wkv, lkv_d)):
            mmp = proj_tm(p, (ps_a, ps_b), ((p.ve, p.ve.n), (p.ve, p.ve.n)), xT, wsb, wtok, m)
            t_r1a = p.ve.op(lambda e: e.tensor_mul(rp1[:, 0:512], ps_a[:], cos_sb[:, 0:512]),
                            deps=[mmp[0], t_cos])
            t_r1b = p.ve.op(lambda e: e.tensor_mul(rp1[:, 512:1024], ps_b[:], cos_sb[:, 512:1024]),
                            deps=[mmp[1], t_cos])
            t_r2a = p.ve.op(lambda e: e.tensor_copy(
                rp2[:, 0:512].rearrange("p (h s x) -> p h s x", h=8, s=2)[:, :, 0],
                ps_a[:].rearrange("p (h s x) -> p h s x", h=8, s=2)[:, :, 1]),
                deps=[mmp[0]])
            t_r2b = p.ve.op(lambda e: e.tensor_copy(
                rp2[:, 0:512].rearrange("p (h s x) -> p h s x", h=8, s=2)[:, :, 1],
                ps_a[:].rearrange("p (h s x) -> p h s x", h=8, s=2)[:, :, 0]))
            t_r2c = p.ve.op(lambda e: e.tensor_copy(
                rp2[:, 512:1024].rearrange("p (h s x) -> p h s x", h=8, s=2)[:, :, 0],
                ps_b[:].rearrange("p (h s x) -> p h s x", h=8, s=2)[:, :, 1]))
            t_r2d = p.ve.op(lambda e: e.tensor_copy(
                rp2[:, 512:1024].rearrange("p (h s x) -> p h s x", h=8, s=2)[:, :, 1],
                ps_b[:].rearrange("p (h s x) -> p h s x", h=8, s=2)[:, :, 0]))
            t_r2m = p.ve.op(lambda e: e.tensor_mul(rp2[:], rp2[:], sins_sb[:]),
                            deps=[t_r2a, t_r2b, t_r2c, t_r2d, t_sin])
            t_rs = p.ve.op(lambda e: e.tensor_add(rp1[:], rp1[:], rp2[:]),
                           deps=[t_r1a, t_r1b, t_r2m])
            if which == "q":
                t_f = p.ve.op(lambda e: e.tensor_scalar(ot_sb[:], rp1[:], 0.125, None, ALU.mult),
                              deps=[t_rs] + ([t_store_prev] if t_store_prev else []))
            else:
                t_mu = p.ve.op(lambda e: e.reduce_sum(
                    stat16a[:], rp1[:].rearrange("p (h x) -> p h x", x=64), axis=AX.X),
                    deps=[t_rs])
                t_mu2 = p.ve.op(lambda e: e.tensor_scalar(stat16a[:], stat16a[:], 1.0 / 64.0,
                                                          None, ALU.mult), deps=[t_mu])
                tsu = t_mu2
                for h in range(16):
                    tsu = p.ve.op(lambda e, h=h: e.tensor_scalar(
                        rp1[:, h * 64:(h + 1) * 64], rp1[:, h * 64:(h + 1) * 64],
                        stat16a[:, h:h + 1], None, ALU.subtract), deps=[tsu])
                t_sq = p.sc.op(lambda e: e.activation(st2[:], rp1[:], AF.Square), deps=[tsu])
                t_vv = p.ve.op(lambda e: e.reduce_sum(
                    stat16b[:], st2[:].rearrange("p (h x) -> p h x", x=64), axis=AX.X),
                    deps=[t_sq])
                t_vv2 = p.ve.op(lambda e: e.tensor_scalar(stat16b[:], stat16b[:], 1.0 / 64.0,
                                                          1e-5, ALU.mult, ALU.add), deps=[t_vv])
                t_sq2 = p.sc.op(lambda e: e.activation(stat16b[:], stat16b[:], AF.Sqrt),
                                deps=[t_vv2])
                t_rr = p.ve.op(lambda e: e.reciprocal(stat16b[:], stat16b[:]), deps=[t_sq2])
                tsu = t_rr
                for h in range(16):
                    tsu = p.ve.op(lambda e, h=h: e.tensor_scalar(
                        rp1[:, h * 64:(h + 1) * 64], rp1[:, h * 64:(h + 1) * 64],
                        stat16b[:, h:h + 1], None, ALU.mult), deps=[tsu])
                t_af1 = p.ve.op(lambda e: e.tensor_mul(rp1[:], rp1[:], lng_bc[:]),
                                deps=[tsu, t_lgb])
                t_af2 = p.ve.op(lambda e: e.tensor_add(rp1[:], rp1[:], lnb_bc[:]),
                                deps=[t_af1, t_lbb])
                t_f = p.ve.op(lambda e: e.tensor_copy(ot_sb[:], rp1[:]),
                              deps=[t_af2] + ([t_store_prev] if t_store_prev else []))
            t_store_prev = p.sy.op(lambda e, m=m, outd=outd: e.dma_start(
                out=outd[m * 128:(m + 1) * 128, :], in_=ot_sb[:]), deps=[t_f])

    p.barrier()
    p.sy.op(lambda e: e.nop(), inc=False, deps=[p.sy.tok(), p.gp.tok()])
    return p.finish()


_progs = {}


def get_prog(name):
    if name not in _progs:
        _progs[name] = {"A": build_A, "B": build_B, "C": build_C, "D": build_D, "E": build_E}[name]()
    return _progs[name]


def host_inputs_A(inputs, core_slices):
    maps = []
    for c in range(NCORE):
        maps.append({
            "x": core_slices[c],
            "wq": f32c(np.asarray(inputs["L0_attn_Wq"])[:, PAIR_PERM]),
            "wk": f32c(np.asarray(inputs["L0_attn_Wk"])[:, PAIR_PERM]),
            "wv": f32c(inputs["L0_attn_Wv"]),
            "wg": f32c(inputs["L0_attn_Wg"]),
            "wgk1": f32c(inputs["L0_attn_Wgk1"]),
            "wgk2": f32c(inputs["L0_attn_Wgk2"]),
            "bgk2": f32c(inputs["L0_attn_bgk2"]).reshape(1, D),
        })
    return maps


def run_A(inputs, core_slices):
    nc = get_prog("A")
    res = run_bass_kernel_spmd(nc, host_inputs_A(inputs, core_slices),
                               core_ids=list(range(NCORE)))
    return res.results


def rope_tables(positions):
    dh = 64
    inv = 1.0 / (10000.0 ** (np.arange(0, dh, 2, dtype=np.float64) / dh))
    fr = positions[:, None].astype(np.float64) * inv[None, :]
    emb = np.concatenate([fr, fr], axis=-1)
    cos = np.cos(emb).astype(np.float32)
    sin = np.sin(emb).astype(np.float32)
    sinS = sin.copy()
    sinS[:, 0:32] *= -1.0
    cosF = np.tile(cos, (1, 16))
    sinSF = np.tile(sinS, (1, 16))
    return f32c(cosF), f32c(sinSF)


def host_inputs_B(inputs, core_slices, resA):
    w1 = np.asarray(inputs["L0_ffn_w1"])
    w3 = np.asarray(inputs["L0_ffn_w3"])
    w2 = np.asarray(inputs["L0_ffn_w2"])
    w1n = f32c(np.transpose(w1, (0, 2, 1)))
    w3n = f32c(np.transpose(w3, (0, 2, 1)))
    w2n = f32c(np.transpose(w2, (0, 2, 1)))
    sw1n = f32c(np.asarray(inputs["L0_ffn_sw1"]).T)
    sw3n = f32c(np.asarray(inputs["L0_ffn_sw3"]).T)
    sw2n = f32c(np.asarray(inputs["L0_ffn_sw2"]).T)
    gatew = f32c(np.asarray(inputs["L0_ffn_gate_w"]).T)
    wqn = f32c(np.asarray(inputs["L1_attn_Wq"]).T)
    wkvn = f32c(np.asarray(inputs["L1_attn_Wkv"]).T)
    lng = f32c(np.tile(np.asarray(inputs["L1_attn_ln_g"]), 16)).reshape(1, D)
    lnb = f32c(np.tile(np.asarray(inputs["L1_attn_ln_b"]), 16)).reshape(1, D)
    maps = []
    for c in range(NCORE):
        b, s = c // 4, c % 4
        pos = np.arange(s * R, (s + 1) * R)
        cosF, sinSF = rope_tables(pos)
        tcomp = np.zeros((3, 128, 512), np.float32)
        dcomp = np.zeros((3, 1, D), np.float32)
        for j in range(s):
            pc = b * 4 + j
            tcomp[j] = np.asarray(resA[pc]["tseg"], np.float32)
            dcomp[j] = np.asarray(resA[pc]["dseg"], np.float32).reshape(1, D)
        maps.append({
            "x": core_slices[c],
            "qfi": np.asarray(resA[c]["qf"]),
            "kfi": np.asarray(resA[c]["kf"]),
            "vti": np.asarray(resA[c]["vt"]),
            "e1i": np.asarray(resA[c]["e1"]),
            "gati": np.asarray(resA[c]["gate"]),
            "lci": np.asarray(resA[c]["lc"]),
            "dprei": np.asarray(resA[c]["dpre"]),
            "tcomp": tcomp,
            "dcomp": dcomp,
            "wout": f32c(inputs["L0_attn_Wout"]),
            "gamma": f32c(inputs["L0_attn_gamma"]).reshape(1, D),
            "beta": f32c(inputs["L0_attn_beta"]).reshape(1, D),
            "gatew": gatew,
            "gateb": f32c(inputs["L0_ffn_gate_bias"]).reshape(1, 8),
            "w1n": w1n, "w3n": w3n, "w2n": w2n,
            "sw1n": sw1n, "sw3n": sw3n, "sw2n": sw2n,
            "wqn": wqn, "wkvn": wkvn,
            "lng": lng, "lnb": lnb,
            "cosf": cosF, "sinsf": sinSF,
        })
    return maps


def run_B(inputs, core_slices, resA):
    nc = get_prog("B")
    res = run_bass_kernel_spmd(nc, host_inputs_B(inputs, core_slices, resA),
                               core_ids=list(range(NCORE)))
    return res.results




C_HEADS = 4
C_TBS = 16
C_SUB = 3
C_NOMASK = False
C_NSCAP = 99
C_NORMCAP = 99


def build_C():
    p = Prog()
    nc = p.nc
    qsl_d = nc.dram_tensor("qsl", [L, 256], BF16, kind="ExternalInput").ap()
    ksl_d = nc.dram_tensor("ksl", [L, 256], BF16, kind="ExternalInput").ap()
    of_d = nc.dram_tensor("of", [256, L], BF16, kind="ExternalOutput").ap()

    C = setup_consts(p)
    masks_np = np.zeros((4, 128, 512), np.float32)
    for j in range(4):
        for t in range(512):
            masks_np[j, : min(128, max(0, t - 128 * j + 1)), t] = 1.0
    masks_d = nc.inline_tensor(masks_np, "c_masks")
    mask_sb = p.sb("mask_sb", [128, 4 * 512], F32)
    t_mk = p.sy.op(lambda e: e.dma_start(
        out=mask_sb[:].rearrange("p (j t) -> p j t", t=512),
        in_=masks_d.ap().rearrange("j p t -> p j t")))

    qsl_sb = p.sb("qsl_sb", [128, 64 * 256], BF16)
    ksl_sb = p.sb("ksl_sb", [128, 64 * 256], BF16)
    qf = p.sb("qf", [128, L], BF16)
    lkvf = p.sb("lkvf", [128, L], BF16)
    av65 = p.sb("av65", [128, 64 * 65], BF16)
    sq = p.sb("sq", [64, L], F32)
    pball = p.sb("pball", [128, L], BF16)
    o_acc = p.sb("o_acc", [65, L], F32)
    rl_sb = p.sb("rl_sb", [1, 512], F32)
    rlb_sb = p.sb("rlb_sb", [64, 512], F32)
    on_sb = p.sb("on_sb", [64, 512], BF16)
    kmax = p.sb("kmax", [1, 1], F32)
    kblk = p.sb("kblk", [1, 1], F32)
    mrow = p.sb("mrow", [1, 512], F32)

    ps_tr = p.ps("ps_tr", [64, 128], BF16)
    ps_tr2 = p.ps("ps_tr2", [64, 128], BF16)
    ps_s = p.ps("ps_s", [128, 512], F32)
    ps_s2 = p.ps("ps_s2", [128, 512], F32)
    ps_av = p.ps("ps_av", [65, 512], F32)
    ps_av2 = p.ps("ps_av2", [65, 512], F32)
    ps_r = p.ps("ps_r", [64, 512], F32)

    t_q = p.sy.op(lambda e: e.dma_start(
        out=qsl_sb[:].rearrange("p (r c) -> p r c", c=256),
        in_=qsl_d.rearrange("(r p) c -> p r c", p=128)))
    t_k = p.sy.op(lambda e: e.dma_start(
        out=ksl_sb[:].rearrange("p (r c) -> p r c", c=256),
        in_=ksl_d.rearrange("(r p) c -> p r c", p=128)))

    for hh in range(C_HEADS):
        p.barrier()
        t_neg = p.ve.op(lambda e: e.memset(lkvf[64:128, :], -1.0))
        t_av1 = p.ve.op(lambda e: e.memset(av65[:], 1.0))
        evac = [None, None]
        for st in range(64):
            pst = ps_tr if st % 2 == 0 else ps_tr2
            t_tr = p.te.op(lambda e, st=st, hh=hh, pst=pst: e.transpose(
                pst[:], ksl_sb[:, st * 256 + hh * 64: st * 256 + (hh + 1) * 64],
                C["ident_bf"][:]),
                deps=[t_k, C["loaded"]] + ([evac[st % 2]] if evac[st % 2] else []))
            evac[st % 2] = p.ve.op(lambda e, st=st, pst=pst: e.tensor_copy(
                lkvf[0:64, st * 128:(st + 1) * 128], pst[:]), deps=[t_tr])
            p.ve.op(lambda e, st=st, hh=hh: e.tensor_copy(
                av65[:, st * 65: st * 65 + 64],
                ksl_sb[:, st * 256 + hh * 64: st * 256 + (hh + 1) * 64]),
                deps=[t_av1])
        for st in range(64):
            pst = ps_tr if st % 2 == 0 else ps_tr2
            t_tr = p.te.op(lambda e, st=st, hh=hh, pst=pst: e.transpose(
                pst[:], qsl_sb[:, st * 256 + hh * 64: st * 256 + (hh + 1) * 64],
                C["ident_bf"][:]),
                deps=[t_q] + ([evac[st % 2]] if evac[st % 2] else []))
            evac[st % 2] = p.ve.op(lambda e, st=st, pst=pst: e.tensor_copy(
                qf[0:64, st * 128:(st + 1) * 128], pst[:]), deps=[t_tr])
        built = p.ve.tok()

        if C_SUB < 1:
            continue
        t_ks = p.sc.op(lambda e: e.activation(sq[:], lkvf[0:64, :], AF.Square),
                       deps=[built])
        t_km = p.ve.op(lambda e: e.memset(kmax[:], 0.0))
        for blk_ in range(16):
            mmk = p.te.op(lambda e, blk_=blk_: e.matmul(
                ps_r[0:1, :], C["ones128_f"][0:64, 0:1],
                sq[:, blk_ * 512:(blk_ + 1) * 512], start=True, stop=True),
                deps=[t_ks, p.ve.tok()])
            t_bm = p.ve.op(lambda e: e.reduce_max(kblk[:], ps_r[0:1, :], axis=AX.X),
                           deps=[mmk])
            t_km = p.ve.op(lambda e: e.tensor_max(kmax[:], kmax[:], kblk[:]),
                           deps=[t_bm, t_km])

        if C_SUB < 2:
            continue
        t_qs = p.sc.op(lambda e: e.activation(sq[:], qf[0:64, :], AF.Square),
                       deps=[built, t_km])
        for blk_ in range(16):
            mmq = p.te.op(lambda e, blk_=blk_: e.matmul(
                ps_r[0:1, :], C["ones128_f"][0:64, 0:1],
                sq[:, blk_ * 512:(blk_ + 1) * 512], start=True, stop=True),
                deps=[t_qs, p.ve.tok(), p.sc.tok()])
            t_mm = p.ve.op(lambda e: e.tensor_scalar(mrow[:], ps_r[0:1, :],
                                                     kmax[0:1, 0:1], None, ALU.mult),
                           deps=[mmq, t_km])
            t_sqr = p.sc.op(lambda e, blk_=blk_: e.activation(
                qf[64:65, blk_ * 512:(blk_ + 1) * 512], mrow[:], AF.Sqrt),
                deps=[t_mm])
        mhat_done = p.sc.tok()

        if C_SUB < 3:
            continue
        def kend_of(tb):
            g0 = tb * 512
            return 1536 if g0 + 512 <= 1536 else g0 + 512

        t_z = p.ve.op(lambda e: e.memset(o_acc[:], 0.0))
        prev_exp = {}
        prev_add = {}
        last_add = {}
        for st in range(64):
            tbs = [tb for tb in range(C_TBS) if st * 128 < kend_of(tb)]
            exps = {}
            for j, tb in enumerate(tbs):
                bank = j % 2
                pst = ps_s if bank == 0 else ps_s2
                mm_s = p.te.op(lambda e, st=st, tb=tb, pst=pst: e.matmul(
                    pst[:], lkvf[0:65, st * 128:(st + 1) * 128],
                    qf[0:65, tb * 512:(tb + 1) * 512], start=True, stop=True),
                    deps=[mhat_done] + ([prev_exp[bank]] if bank in prev_exp else []))
                t_e = p.sc.op(lambda e, tb=tb, pst=pst: e.activation(
                    pball[:, tb * 512:(tb + 1) * 512], pst[:], AF.Exp),
                    deps=[mm_s])
                prev_exp[bank] = t_e
                ns = kend_of(tb) // 128
                if tb * 512 >= 1536 and st >= ns - 4 and not C_NOMASK:
                    jj = st - (ns - 4)
                    t_e = p.ve.op(lambda e, jj=jj, tb=tb: e.tensor_mul(
                        pball[:, tb * 512:(tb + 1) * 512],
                        pball[:, tb * 512:(tb + 1) * 512],
                        mask_sb[:, jj * 512:(jj + 1) * 512]),
                        deps=[t_e, t_mk])
                exps[tb] = t_e
            for j, tb in enumerate(tbs):
                bank = j % 2
                pav = ps_av if bank == 0 else ps_av2
                mm_av = p.te.op(lambda e, st=st, tb=tb, pav=pav: e.matmul(
                    pav[:], av65[:, st * 65:(st + 1) * 65],
                    pball[:, tb * 512:(tb + 1) * 512], start=True, stop=True),
                    deps=[exps[tb], built] + ([prev_add[bank]] if bank in prev_add else []))
                t_add = p.ve.op(lambda e, tb=tb, pav=pav: e.tensor_add(
                    o_acc[:, tb * 512:(tb + 1) * 512],
                    o_acc[:, tb * 512:(tb + 1) * 512], pav[:]),
                    deps=[mm_av, t_z])
                prev_add[bank] = t_add
                last_add[tb] = t_add
        st_prev = None
        for tb in range(C_TBS):
            if tb not in last_add:
                continue
            t_rl = p.ve.op(lambda e, tb=tb: e.reciprocal(
                rl_sb[:], o_acc[64:65, tb * 512:(tb + 1) * 512]),
                deps=[last_add[tb]])
            mm_rb = p.te.op(lambda e: e.matmul(
                ps_r[:], C["ones1_f"][0:1, 0:64], rl_sb[0:1, :],
                start=True, stop=True), deps=[t_rl])
            t_rc = p.ve.op(lambda e: e.tensor_copy(rlb_sb[:], ps_r[:]), deps=[mm_rb])
            t_on = p.ve.op(lambda e, tb=tb: e.tensor_mul(
                on_sb[:], o_acc[0:64, tb * 512:(tb + 1) * 512], rlb_sb[:]),
                deps=[t_rc] + ([st_prev] if st_prev else []))
            st_prev = p.sy.op(lambda e, hh=hh, tb=tb: e.dma_start(
                out=of_d[hh * 64:(hh + 1) * 64, tb * 512:(tb + 1) * 512],
                in_=on_sb[:]), deps=[t_on])

    p.barrier()
    p.sy.op(lambda e: e.nop(), inc=False, deps=[p.sy.tok(), p.gp.tok()])
    return p.finish()


def host_inputs_C(resB):
    maps = []
    for c in range(NCORE):
        b, sp = c // 4, c % 4
        q_full = np.concatenate([np.asarray(resB[b * 4 + s]["qtm"]) for s in range(4)], axis=0)
        k_full = np.concatenate([np.asarray(resB[b * 4 + s]["lkv"]) for s in range(4)], axis=0)
        maps.append({
            "qsl": np.ascontiguousarray(q_full[:, sp * 256:(sp + 1) * 256]),
            "ksl": np.ascontiguousarray(k_full[:, sp * 256:(sp + 1) * 256]),
        })
    return maps


def run_C(resB):
    nc = get_prog("C")
    res = run_bass_kernel_spmd(nc, host_inputs_C(resB), core_ids=list(range(NCORE)))
    return res.results




PAIR8 = np.concatenate([np.concatenate([np.arange(h * 64, (h + 1) * 64),
                                        np.arange((h + 4) * 64, (h + 5) * 64)])
                        for h in range(4)])


D_STOP = "full"


def build_D():
    p = Prog()
    nc = p.nc
    ofall_d = nc.dram_tensor("ofall", [1024, R], BF16, kind="ExternalInput").ap()
    x2_d = nc.dram_tensor("x2", [R, D], F32, kind="ExternalInput").ap()
    won_d = nc.dram_tensor("won", [D, D], F32, kind="ExternalInput").ap()
    bo_d = nc.dram_tensor("bo", [1, D], F32, kind="ExternalInput").ap()
    wq_d = nc.dram_tensor("wq", [D, 512], F32, kind="ExternalInput").ap()
    wk_d = nc.dram_tensor("wk", [D, 512], F32, kind="ExternalInput").ap()
    wv_d = nc.dram_tensor("wv", [D, 512], F32, kind="ExternalInput").ap()
    wqa_d = nc.dram_tensor("wqa", [D, 64], F32, kind="ExternalInput").ap()
    wqb_d = nc.dram_tensor("wqb", [64, 512], F32, kind="ExternalInput").ap()
    wka_d = nc.dram_tensor("wka", [D, 64], F32, kind="ExternalInput").ap()
    wkb_d = nc.dram_tensor("wkb", [64, 512], F32, kind="ExternalInput").ap()
    wg1a_d = nc.dram_tensor("wg1a", [D, 16], F32, kind="ExternalInput").ap()
    wg1b_d = nc.dram_tensor("wg1b", [16, 512], F32, kind="ExternalInput").ap()
    bg1_d = nc.dram_tensor("bg1", [128, 4], F32, kind="ExternalInput").ap()
    wg2a_d = nc.dram_tensor("wg2a", [D, 16], F32, kind="ExternalInput").ap()
    wg2b_d = nc.dram_tensor("wg2b", [16, 512], F32, kind="ExternalInput").ap()
    bg2_d = nc.dram_tensor("bg2", [128, 4], F32, kind="ExternalInput").ap()
    we_d = nc.dram_tensor("we", [D, 4], F32, kind="ExternalInput").ap()
    wga_d = nc.dram_tensor("wga", [D, 64], F32, kind="ExternalInput").ap()
    wgb_d = nc.dram_tensor("wgb", [64, 512], F32, kind="ExternalInput").ap()

    x3_d = nc.dram_tensor("x3", [R, D], F32, kind="ExternalOutput").ap()
    opart_d = nc.dram_tensor("opart", [R, 512], F32, kind="ExternalOutput").ap()
    qgd_d = nc.dram_tensor("qgd", [4, 128, R], BF16, kind="ExternalOutput").ap()
    q2f_d = nc.dram_tensor("q2fo", [4, 128, R], BF16, kind="ExternalOutput").ap()
    gkhi_d = nc.dram_tensor("gkhi", [4, 128, R], BF16, kind="ExternalOutput").ap()
    gklo_d = nc.dram_tensor("gklo", [4, 128, R], BF16, kind="ExternalOutput").ap()
    rwf_d = nc.dram_tensor("rwf", [1, 4 * R], F32, kind="ExternalOutput").ap()
    gs_d = nc.dram_tensor("gs", [R, 512], BF16, kind="ExternalOutput").ap()
    vtm_d = nc.dram_tensor("vtm", [R, 512], BF16, kind="ExternalOutput").ap()
    lcs_d = nc.dram_tensor("lcs", [5, NT, 128, 256], BF16, kind="ExternalOutput").ap()
    tsg_d = nc.dram_tensor("tsg", [5, 128, 256], F32, kind="ExternalOutput").ap()
    dsg_d = nc.dram_tensor("dsg", [5, 128, 4], F32, kind="ExternalOutput").ap()
    cgc_d = nc.dram_tensor("cgc", [5, NT, 128, 4], F32, kind="ExternalOutput").ap()

    C = setup_consts(p, skip=("triu_bf", "ones1_bf"))

    X_sb = p.sb("X_sb", [128, NT * D], F32)
    xT = p.sb("xT", [128, 8 * R], BF16)
    st1 = p.sb("st1", [128, 1], F32)
    st16 = p.sb("st16", [128, 16], F32)
    w_small = {}
    wbig = p.sb("w_big", [128, 8 * 512], BF16)
    w_small["wq"] = w_small["wk"] = w_small["wv"] = wbig
    for nm, sh in [("wqa", [128, 8 * 64]), ("wqb", [64, 512]), ("wka", [128, 8 * 64]),
                   ("wkb", [64, 512]), ("wg1a", [128, 8 * 16]), ("wg1b", [16, 512]),
                   ("wg2a", [128, 8 * 16]), ("wg2b", [16, 512]), ("we", [128, 8 * 4]),
                   ("wga", [128, 8 * 64]), ("wgb", [64, 512])]:
        w_small[nm] = p.sb("w_" + nm, sh, BF16)
    bg1_sb = p.sb("bg1_sb", [128, 4], F32)
    bg2_sb = p.sb("bg2_sb", [128, 4], F32)
    q1f = p.sb("q1f", [128, 4 * R], BF16)
    won_sb = q1f
    v_tm = p.sb("v_tm", [128, NT * 512], BF16)
    k1f = p.sb("k1f", [128, 4 * R], BF16)
    q2f = p.sb("q2f", [128, 4 * R], BF16)
    k2f = p.sb("k2f", [128, 4 * R], BF16)
    _vv = k2f[:].bitcast(F32)
    st2 = _vv[:, 0:D]
    bo_bc = _vv[:, D:2 * D]
    ofb = k1f[:, 0:8 * 128]
    xbf = q2f[:, 0:D]
    row_sb = q2f[:].bitcast(F32)[0:1, 0:D]
    _xv = X_sb[:].bitcast(BF16)
    hi8 = _xv[:, 0 * 4 * R:1 * 4 * R]
    lo8 = _xv[:, 1 * 4 * R:2 * 4 * R]
    gk2hi = _xv[:, 2 * 4 * R:3 * 4 * R]
    gk2lo = _xv[:, 3 * 4 * R:4 * 4 * R]
    _tv = xT[:]
    qg = _tv[:, 0 * R:1 * R]
    kg = _tv[:, 1 * R:2 * R]
    kout = _tv[:, 2 * R:3 * R]
    rwbig = _tv[:, 3 * R:5 * R].bitcast(F32)
    mbig = _tv[:, 5 * R:7 * R].bitcast(F32)
    xaf = p.sb("xaf", [128, R], BF16)
    _af = xaf[:]
    zf = p.sb("zf", [128, R], F32)
    Gf = wbig[:].bitcast(F32)[:, 0:R]
    e2h = _af[:, 0:256].bitcast(F32)
    kout_tm = _af[:, 256:384]
    am2 = _af[:, 384:640]
    ob2 = _af[:, 640:896].bitcast(F32)
    ktm2 = _af[:, 896:1408]
    xab = p.sb("xab", [128, 64], BF16)
    S_sb = _af[:, 1408:1920].bitcast(F32)
    Sb_sb = _af[:, 1920:2048 + 128] if False else p.sb("Sb_sb", [128, 256], BF16)
    Stmp = e2h[:, 0:64]
    dec_sb = p.sb("dec_sb", [128, 4], F32)
    cgc_sb = p.sb("cgc_sb", [128, 4], F32)
    rw_sb = zf[:, 64:128]
    rwp = zf[0:1, 128:256]
    rwrow = None
    sg1 = p.sb("sg1", [128, 4], F32)
    sg2 = p.sb("sg2", [128, 4], F32)
    gsb = zf[:, 0:512]
    rwrow = zf[0:1, 0:R]
    ones128x128_f = p.sb("ones2d_f", [128, 128], F32)

    ps_tr = p.ps("ps_tr", [128, 128], BF16)
    ps_tr2 = p.ps("ps_tr2", [128, 128], BF16)
    ps_a = p.ps("ps_a", [128, 512], F32)
    ps_b = p.ps("ps_b", [128, 512], F32)
    ps_c = p.ps("ps_c", [128, 512], F32)
    ps_m = p.ps("ps_m", [128, 256], F32)
    p.ve.op(lambda e: e.memset(ones128x128_f[:], 1.0))

    t_x2l = None
    for m in range(NT):
        t_x2l = p.sy.op(lambda e, m=m: e.dma_start(out=X_sb[:, m * D:(m + 1) * D],
                                                   in_=x2_d[m * 128:(m + 1) * 128, :]))
    t_won = p.gp.op(lambda e: e.dma_start(
        out=won_sb[:].rearrange("p (c d) -> p c d", d=D),
        in_=won_d.rearrange("(c p) d -> p c d", p=128)))
    t_bo = p.sy.op(lambda e: e.dma_start(out=row_sb[:], in_=bo_d))
    t_bob = bcast_row(p, C, row_sb, ps_c, bo_bc, [t_bo, C["loaded"]])
    for m in range(NT):
        p.barrier()
        t_of = p.sy.op(lambda e, m=m: e.dma_start(
            out=ofb[:].rearrange("p (r c) -> p r c", c=128),
            in_=ofall_d[:, m * 128:(m + 1) * 128].rearrange("(r p) c -> p r c", p=128)))
        mml = None
        for kc in range(8):
            for half in range(2):
                mml = p.te.op(lambda e, kc=kc, half=half: e.matmul(
                    (ps_a if half == 0 else ps_b)[:], ofb[:, kc * 128:(kc + 1) * 128],
                    won_sb[:, kc * D + half * 512:kc * D + (half + 1) * 512],
                    start=(kc == 0), stop=(kc == 7)),
                    inc=(kc == 7),
                    deps=([t_of, t_won] if kc == 0 and half == 0 else []))
        t_r0 = p.ve.op(lambda e, m=m: e.tensor_add(X_sb[:, m * D:m * D + 512],
                                                   X_sb[:, m * D:m * D + 512], ps_a[:]),
                       deps=[mml, t_x2l, t_bob])
        t_r1 = p.ve.op(lambda e, m=m: e.tensor_add(X_sb[:, m * D + 512:(m + 1) * D],
                                                   X_sb[:, m * D + 512:(m + 1) * D], ps_b[:]),
                       deps=[t_r0])
        t_r2 = p.ve.op(lambda e, m=m: e.tensor_add(X_sb[:, m * D:(m + 1) * D],
                                                   X_sb[:, m * D:(m + 1) * D], bo_bc[:]),
                       deps=[t_r1])
        t_ln = ln_rowwise(p, X_sb[:, m * D:(m + 1) * D], st1, st2, [t_r2])
        p.sy.op(lambda e, m=m: e.dma_start(out=x3_d[m * 128:(m + 1) * 128, :],
                                           in_=X_sb[:, m * D:(m + 1) * D]), deps=[t_ln])

    for m in range(NT):
        p.barrier()
        xt_build(p, C, lambda mm: X_sb[:, mm * D:(mm + 1) * D], xT, xbf, ps_tr, ps_tr2, m, [])
    p.barrier()

    wt = {}
    for nm, dd, dlast in [("wqa", wqa_d, 64), ("wka", wka_d, 64),
                          ("wg1a", wg1a_d, 16), ("wg2a", wg2a_d, 16),
                          ("we", we_d, 4), ("wga", wga_d, 64)]:
        wt[nm] = p.gp.op(lambda e, nm=nm, dd=dd, dlast=dlast: e.dma_start(
            out=w_small[nm][:].rearrange("p (c d) -> p c d", d=dlast),
            in_=dd.rearrange("(c p) d -> p c d", p=128)))
    for nm, dd in [("wqb", wqb_d), ("wkb", wkb_d), ("wg1b", wg1b_d),
                   ("wg2b", wg2b_d), ("wgb", wgb_d)]:
        wt[nm] = p.gp.op(lambda e, nm=nm, dd=dd: e.dma_start(out=w_small[nm][:], in_=dd))
    t_bg1 = p.sy.op(lambda e: e.dma_start(out=bg1_sb[:], in_=bg1_d))
    t_bg2 = p.sy.op(lambda e: e.dma_start(out=bg2_sb[:], in_=bg2_d))

    def pair_proj(dst, wname, scale=None):
        last = None
        for rc in range(4):
            for tb4 in range(4):
                mm = None
                for kc in range(8):
                    mm = p.te.op(lambda e, rc=rc, tb4=tb4, kc=kc, wname=wname: e.matmul(
                        ps_c[:], w_small[wname][:, kc * 512 + rc * 128:kc * 512 + (rc + 1) * 128],
                        xT[:, kc * R + tb4 * 512:kc * R + (tb4 + 1) * 512],
                        start=(kc == 0), stop=(kc == 7)),
                        inc=(kc == 7),
                        deps=([wt[wname], (p.ve, p.ve.n)] if kc == 0 else []))
                if scale is None:
                    last = p.ve.op(lambda e, rc=rc, tb4=tb4, dst=dst: e.tensor_copy(
                        dst[:, rc * R + tb4 * 512:rc * R + (tb4 + 1) * 512], ps_c[:]),
                        deps=[mm])
                else:
                    last = p.ve.op(lambda e, rc=rc, tb4=tb4, dst=dst, scale=scale: e.tensor_scalar(
                        dst[:, rc * R + tb4 * 512:rc * R + (tb4 + 1) * 512], ps_c[:],
                        scale, None, ALU.mult), deps=[mm])
        return last

    p.barrier()
    wt["wq"] = p.gp.op(lambda e: e.dma_start(
        out=wbig[:].rearrange("p (c d) -> p c d", d=512),
        in_=wq_d.rearrange("(c p) d -> p c d", p=128)))
    pair_proj(q1f, "wq", scale=0.125)
    p.barrier()
    wt["wk"] = p.gp.op(lambda e: e.dma_start(
        out=wbig[:].rearrange("p (c d) -> p c d", d=512),
        in_=wk_d.rearrange("(c p) d -> p c d", p=128)))
    pair_proj(k1f, "wk")
    p.barrier()
    wt["wv"] = p.gp.op(lambda e: e.dma_start(
        out=wbig[:].rearrange("p (c d) -> p c d", d=512),
        in_=wv_d.rearrange("(c p) d -> p c d", p=128)))
    for m in range(NT):
        mm = None
        for kc in range(8):
            mm = p.te.op(lambda e, m=m, kc=kc: e.matmul(
                ps_c[:], xT[:, kc * R + m * 128:kc * R + (m + 1) * 128],
                w_small["wv"][:, kc * 512:(kc + 1) * 512],
                start=(kc == 0), stop=(kc == 7)),
                inc=(kc == 7), deps=([wt["wv"], (p.ve, p.ve.n)] if kc == 0 else []))
        t_v = p.ve.op(lambda e, m=m: e.tensor_copy(v_tm[:, m * 512:(m + 1) * 512], ps_c[:]),
                      deps=[mm])
        p.sy.op(lambda e, m=m: e.dma_start(out=vtm_d[m * 128:(m + 1) * 128, :],
                                           in_=v_tm[:, m * 512:(m + 1) * 512]), deps=[t_v])

    def logsig_chunk(sl, osl, hi_t, lo_t, dep):
        sc2 = st2[:, 0:512]
        t_ab = p.sc.op(lambda e: e.activation(sc2[:, 0:sl.stop - sl.start], zf[:, sl], AF.Abs), deps=[dep])
        t_ex = p.sc.op(lambda e: e.activation(sc2[:, 0:sl.stop - sl.start], sc2[:, 0:sl.stop - sl.start], AF.Exp, scale=-1.0),
                       deps=[t_ab])
        t_ln = p.sc.op(lambda e: e.activation(sc2[:, 0:sl.stop - sl.start], sc2[:, 0:sl.stop - sl.start], AF.Ln, bias=1.0),
                       deps=[t_ex])
        t_mn = p.ve.op(lambda e: e.tensor_scalar(zf[:, sl], zf[:, sl], 0.0, None, ALU.min),
                       deps=[t_ab])
        t_su = p.ve.op(lambda e: e.tensor_sub(zf[:, sl], zf[:, sl], sc2[:, 0:sl.stop - sl.start]),
                       deps=[t_ln, t_mn])
        t_sc = p.ve.op(lambda e: e.tensor_scalar(zf[:, sl], zf[:, sl], 1.0 / 16.0, None,
                                                 ALU.mult), deps=[t_su])
        t_hi = p.ve.op(lambda e: e.tensor_copy(hi_t[:, osl], zf[:, sl]), deps=[t_sc])
        return p.ve.op(lambda e: e.tensor_sub(lo_t[:, osl], zf[:, sl], hi_t[:, osl]),
                       deps=[t_hi])

    def lora_f(dst, wa, wb, ka, bias_sb=None, scale=None):
        for tb4 in range(4):
            mm = None
            for kc in range(8):
                mm = p.te.op(lambda e, tb4=tb4, kc=kc, wa=wa, ka=ka: e.matmul(
                    ps_c[0:ka, :], w_small[wa][:, kc * ka:(kc + 1) * ka],
                    xT[:, kc * R + tb4 * 512:kc * R + (tb4 + 1) * 512],
                    start=(kc == 0), stop=(kc == 7)),
                    inc=(kc == 7), deps=([wt[wa], (p.ve, p.ve.n)] if kc == 0 else []))
            p.ve.op(lambda e, tb4=tb4, ka=ka: e.tensor_copy(
                xaf[0:ka, tb4 * 512:(tb4 + 1) * 512], ps_c[0:ka, :]), deps=[mm])
        xaf_done = p.ve.tok()
        last = None
        for rc in range(4):
            for tb4 in range(4):
                mm = p.te.op(lambda e, rc=rc, tb4=tb4, wb=wb, ka=ka: e.matmul(
                    ps_c[:], w_small[wb][0:ka, rc * 128:(rc + 1) * 128],
                    xaf[0:ka, tb4 * 512:(tb4 + 1) * 512], start=True, stop=True),
                    deps=[xaf_done, wt[wb], (p.ve, p.ve.n), (p.sc, p.sc.n)])
                if bias_sb is None:
                    if scale is None:
                        last = p.ve.op(lambda e, rc=rc, tb4=tb4, dst=dst: e.tensor_copy(
                            dst[:, rc * R + tb4 * 512:rc * R + (tb4 + 1) * 512], ps_c[:]),
                            deps=[mm])
                    else:
                        last = p.ve.op(lambda e, rc=rc, tb4=tb4, dst=dst, scale=scale:
                                       e.tensor_scalar(
                            dst[:, rc * R + tb4 * 512:rc * R + (tb4 + 1) * 512], ps_c[:],
                            scale, None, ALU.mult), deps=[mm])
                else:
                    t_z = p.ve.op(lambda e, rc=rc, tb4=tb4, bias_sb=bias_sb: e.tensor_scalar(
                        zf[:, tb4 * 512:(tb4 + 1) * 512], ps_c[:],
                        bias_sb[:, rc:rc + 1], None, ALU.add), deps=[mm])
                    last = logsig_chunk(slice(tb4 * 512, (tb4 + 1) * 512),
                                        slice(rc * R + tb4 * 512, rc * R + (tb4 + 1) * 512),
                                        dst[0], dst[1], t_z)
        return last

    if D_STOP == "q1f":
        p.barrier()
        for rc in range(4):
            p.sy.op(lambda e, rc=rc: e.dma_start(out=qgd_d[rc],
                                                 in_=q1f[:, rc * R:(rc + 1) * R]))
        p.barrier()
        p.sy.op(lambda e: e.nop(), inc=False, deps=[p.sy.tok(), p.gp.tok()])
        return p.finish()
    p.barrier()
    lora_f(q2f, "wqa", "wqb", 64, scale=0.125)
    p.barrier()
    lora_f((hi8, lo8), "wg1a", "wg1b", 16, bias_sb=bg1_sb)
    p.barrier()
    t_gk2 = lora_f((gk2hi, gk2lo), "wg2a", "wg2b", 16, bias_sb=bg2_sb)
    p.barrier()
    for rc in range(4):
        p.sy.op(lambda e, rc=rc: e.dma_start(out=q2f_d[rc], in_=q2f[:, rc * R:(rc + 1) * R]))
        p.sy.op(lambda e, rc=rc: e.dma_start(out=gkhi_d[rc], in_=gk2hi[:, rc * R:(rc + 1) * R]))
        p.sy.op(lambda e, rc=rc: e.dma_start(out=gklo_d[rc], in_=gk2lo[:, rc * R:(rc + 1) * R]))

    for m in range(NT):
        p.barrier()
        mm = None
        for kc in range(8):
            mm = p.te.op(lambda e, m=m, kc=kc: e.matmul(
                ps_a[:, 0:64], xT[:, kc * R + m * 128:kc * R + (m + 1) * 128],
                w_small["wka"][:, kc * 64:(kc + 1) * 64],
                start=(kc == 0), stop=(kc == 7)),
                inc=(kc == 7), deps=([wt["wka"]] if kc == 0 else []))
        t_xc = p.ve.op(lambda e: e.tensor_copy(xab[:], ps_a[:, 0:64]), deps=[mm])
        t_tr = p.te.op(lambda e: e.transpose(ps_tr[0:64, :], xab[:], C["ident_bf"][:]),
                       deps=[t_xc])
        t_xf = p.ve.op(lambda e: e.tensor_copy(xaf[0:64, 0:128], ps_tr[0:64, 0:128]),
                       deps=[t_tr])
        mm2 = p.te.op(lambda e: e.matmul(ps_b[:], xaf[0:64, 0:128],
                                         w_small["wkb"][0:64, :], start=True, stop=True),
                      deps=[t_xf, wt["wkb"]])
        t_mx = p.ve.op(lambda e: e.reduce_max(
            st16[:, 0:8], ps_b[:].rearrange("p (h x) -> p h x", x=64), axis=AX.X,
            negate=True), deps=[mm2])
        t_eh = None
        for h in range(8):
            t_eh = p.sc.op(lambda e, h=h: e.activation(
                gsb[:, h * 64:(h + 1) * 64], ps_b[:, h * 64:(h + 1) * 64], AF.Exp,
                bias=st16[:, h:h + 1]), deps=[t_mx])
        t_sm = p.ve.op(lambda e: e.reduce_sum(
            st16[:, 8:16], gsb[:].rearrange("p (h x) -> p h x", x=64), axis=AX.X),
            deps=[t_eh])
        t_rc = p.ve.op(lambda e: e.reciprocal(st16[:, 8:16], st16[:, 8:16]), deps=[t_sm])
        t_k2 = None
        for h in range(8):
            t_k2 = p.ve.op(lambda e, h=h: e.tensor_scalar(
                ktm2[:, h * 64:(h + 1) * 64], gsb[:, h * 64:(h + 1) * 64],
                st16[:, 8 + h:9 + h], None, ALU.mult), deps=[t_rc])
        for rc in range(4):
            pst = ps_tr if rc % 2 == 0 else ps_tr2
            t_t2 = p.te.op(lambda e, rc=rc, pst=pst: e.transpose(
                pst[:], ktm2[:, rc * 128:(rc + 1) * 128], C["ident_bf"][:]),
                deps=[t_k2, (p.ve, p.ve.n)])
            p.ve.op(lambda e, rc=rc, m=m, pst=pst: e.tensor_copy(
                k2f[:, rc * R + m * 128:rc * R + (m + 1) * 128], pst[:]), deps=[t_t2])

    for m in range(NT):
        p.barrier()
        mm = None
        for kc in range(8):
            mm = p.te.op(lambda e, m=m, kc=kc: e.matmul(
                ps_a[:, 0:4], xT[:, kc * R + m * 128:kc * R + (m + 1) * 128],
                w_small["we"][:, kc * 4:(kc + 1) * 4],
                start=(kc == 0), stop=(kc == 7)),
                inc=(kc == 7), deps=([wt["we"]] if kc == 0 else []))
        t_mx = p.ve.op(lambda e: e.reduce_max(st1[:], ps_a[:, 0:4], axis=AX.X, negate=True),
                       deps=[mm])
        t_ep = p.sc.op(lambda e: e.activation(sg1[:], ps_a[:, 0:4], AF.Exp,
                                              bias=st1[:, 0:1], accum_out=st16[:, 0:1]),
                       deps=[t_mx])
        t_r = p.ve.op(lambda e: e.reciprocal(st16[:, 0:1], st16[:, 0:1]), deps=[t_ep])
        t_e = p.ve.op(lambda e, m=m: e.tensor_scalar(rw_sb[:, m * 4:(m + 1) * 4],
                                                     sg1[:], st16[:, 0:1], None, ALU.mult),
                      deps=[t_r])
        t_n1 = p.ve.op(lambda e, m=m: e.reduce_max(st1[:], rw_sb[:, m * 4:(m + 1) * 4],
                                                   axis=AX.X, negate=True), deps=[t_e])
        t_s1 = p.sc.op(lambda e, m=m: e.activation(sg2[:], rw_sb[:, m * 4:(m + 1) * 4],
                                                   AF.Sign, bias=st1[:, 0:1]), deps=[t_n1])
        t_p1 = p.ve.op(lambda e: e.tensor_scalar(sg2[:], sg2[:], 1.0, None, ALU.add),
                       deps=[t_s1])
        t_rw = p.ve.op(lambda e, m=m: e.tensor_mul(rw_sb[:, m * 4:(m + 1) * 4],
                                                   rw_sb[:, m * 4:(m + 1) * 4], sg2[:]),
                       deps=[t_p1])
        t_prevst = None
        for ex in range(4):
            mmt = p.te.op(lambda e, m=m, ex=ex: e.matmul(
                ps_m[0:1, 0:128], rw_sb[:, m * 4 + ex:m * 4 + ex + 1],
                C["ident_f"][:], start=True, stop=True),
                deps=[t_rw, (p.ve, p.ve.n)])
            t_cp = p.ve.op(lambda e: e.tensor_copy(rwp[:], ps_m[0:1, 0:128]),
                           deps=[mmt] + ([t_prevst] if t_prevst else []))
            t_prevst = p.sy.op(lambda e, m=m, ex=ex: e.dma_start(
                out=rwf_d[0:1, ex * R + m * 128:ex * R + (m + 1) * 128], in_=rwp[:]),
                deps=[t_cp])

    for m in range(NT):
        p.barrier()
        mm = None
        for kc in range(8):
            mm = p.te.op(lambda e, m=m, kc=kc: e.matmul(
                ps_a[:, 0:64], xT[:, kc * R + m * 128:kc * R + (m + 1) * 128],
                w_small["wga"][:, kc * 64:(kc + 1) * 64],
                start=(kc == 0), stop=(kc == 7)),
                inc=(kc == 7), deps=([wt["wga"]] if kc == 0 else []))
        t_xc = p.ve.op(lambda e: e.tensor_copy(xab[:], ps_a[:, 0:64]), deps=[mm])
        t_tr = p.te.op(lambda e: e.transpose(ps_tr[0:64, :], xab[:], C["ident_bf"][:]),
                       deps=[t_xc])
        t_xf = p.ve.op(lambda e: e.tensor_copy(xaf[0:64, 0:128], ps_tr[0:64, 0:128]),
                       deps=[t_tr])
        mm2 = p.te.op(lambda e: e.matmul(ps_b[:], xaf[0:64, 0:128],
                                         w_small["wgb"][0:64, :], start=True, stop=True),
                      deps=[t_xf, wt["wgb"]])
        t_sg = p.sc.op(lambda e: e.activation(gsb[:], ps_b[:], AF.Sigmoid), deps=[mm2])
        t_gs = p.ve.op(lambda e: e.tensor_mul(ktm2[:], ps_b[:], gsb[:]),
                       deps=[t_sg])
        p.sy.op(lambda e, m=m: e.dma_start(out=gs_d[m * 128:(m + 1) * 128, :],
                                           in_=ktm2[:]), deps=[t_gs])

    if D_STOP == "q1f2":
        p.barrier()
        for rc in range(4):
            p.sy.op(lambda e, rc=rc: e.dma_start(out=qgd_d[rc],
                                                 in_=q1f[:, rc * R:(rc + 1) * R]))
        p.barrier()
        p.sy.op(lambda e: e.nop(), inc=False, deps=[p.sy.tok(), p.gp.tok()])
        return p.finish()
    opart_tok = {}
    NINST = 2 if D_STOP == "inst2" else (1 if D_STOP == "inst1" else 5)
    for inst in range(NINST):
        p.barrier()
        if inst == 0:
            qsrc, ksrc, hsrc, lsrc = q1f, k1f, hi8, lo8
        else:
            qsrc, ksrc, hsrc, lsrc = q2f, k2f, gk2hi, gk2lo
            ex = inst - 1
            t_rwl = p.sy.op(lambda e, ex=ex: e.dma_start(
                out=rwrow[:], in_=rwf_d[0:1, ex * R:(ex + 1) * R]))
            last = None
            for tb4 in range(4):
                mmb = p.te.op(lambda e, tb4=tb4, ex=ex: e.matmul(
                    ps_c[:], C["ones1_f"][:],
                    rwrow[0:1, tb4 * 512:(tb4 + 1) * 512],
                    start=True, stop=True), deps=[(p.ve, p.ve.n), t_rwl])
                last = p.ve.op(lambda e, tb4=tb4: e.tensor_copy(
                    rwbig[:, tb4 * 512:(tb4 + 1) * 512], ps_c[:]), deps=[mmb])
            t_mb = p.sc.op(lambda e: e.activation(mbig[:], rwbig[:], AF.Sign), deps=[last])
            p.barrier()
        t_z0 = p.ve.op(lambda e: e.memset(S_sb[:], 0.0))
        t_z1 = p.ve.op(lambda e: e.memset(Sb_sb[:], 0.0))
        t_z2 = p.ve.op(lambda e: e.memset(cgc_sb[:], 0.0))
        for rc in range(4):
            p.barrier()
            sl = slice(rc * R, (rc + 1) * R)
            t_sc = None
            for m in range(NT):
                csl = slice(m * 128, (m + 1) * 128)
                osl = slice(rc * R + m * 128, rc * R + (m + 1) * 128)
                t_ad = p.ve.op(lambda e, osl=osl, hsrc=hsrc, lsrc=lsrc: e.tensor_add(
                    e2h[:], hsrc[:, osl], lsrc[:, osl]),
                    deps=[t_sc] if t_sc else [])
                if inst > 0:
                    t_ad = p.ve.op(lambda e, csl=csl: e.tensor_mul(e2h[:], e2h[:],
                                                                   mbig[:, csl]),
                                   deps=[t_ad])
                t_sc = p.ve.op(lambda e, csl=csl: e.tensor_tensor_scan(
                    Gf[:, csl], ones128x128_f[:], e2h[:], 0.0, ALU.mult, ALU.add),
                    deps=[t_ad])
            t_e1 = p.sc.op(lambda e: e.activation(zf[:], Gf[:], AF.Exp), deps=[t_sc])
            t_qg = p.ve.op(lambda e, sl=sl, qsrc=qsrc: e.tensor_mul(qg[:], qsrc[:, sl], zf[:]),
                           deps=[t_e1])
            if inst > 0:
                t_qg = p.ve.op(lambda e: e.tensor_mul(qg[:], qg[:], rwbig[:]),
                               deps=[t_qg])
            t_em = p.sc.op(lambda e: e.activation(zf[:], Gf[:], AF.Exp, scale=-1.0),
                           deps=[t_qg, t_e1])
            t_kg = p.ve.op(lambda e, sl=sl, ksrc=ksrc: e.tensor_mul(kg[:], ksrc[:, sl], zf[:]),
                           deps=[t_em])
            if inst > 0:
                t_kg = p.ve.op(lambda e: e.tensor_mul(kg[:], kg[:], rwbig[:]),
                               deps=[t_kg])
            t_ko = t_kg
            for m in range(NT):
                csl = slice(m * 128, (m + 1) * 128)
                t_ek = p.sc.op(lambda e, csl=csl, m=m: e.activation(
                    zf[:, csl], Gf[:, csl], AF.Exp, scale=-1.0,
                    bias=Gf[:, m * 128 + 127:m * 128 + 128]), deps=[t_ko, t_em])
                t_ko = p.ve.op(lambda e, csl=csl, m=m, rc=rc, ksrc=ksrc: e.tensor_mul(
                    kout[:, csl], ksrc[:, rc * R + m * 128:rc * R + (m + 1) * 128],
                    zf[:, csl]), deps=[t_ek])
            if inst > 0:
                t_ko = p.ve.op(lambda e: e.tensor_mul(kout[:], kout[:], rwbig[:]),
                               deps=[t_ko])
            if inst == 0:
                p.sy.op(lambda e, rc=rc: e.dma_start(out=qgd_d[rc], in_=qg[:]),
                        deps=[t_qg])
            if D_STOP == "qg0" and inst == 0 and rc == 0:
                p.sy.op(lambda e: e.dma_start(out=opart_d[0:128, :],
                                              in_=zf[:, 0:512]), deps=[t_e1, t_qg])
                p.sy.op(lambda e: e.dma_start(out=opart_d[128:256, :],
                                              in_=Gf[:, 0:512]), deps=[t_e1, t_qg])
                p.barrier()
                p.sy.op(lambda e: e.nop(), inc=False, deps=[p.sy.tok(), p.gp.tok()])
                return p.finish()
            t_m2 = None
            t_sm = None
            t_ob_prev = None
            t_os_prev = None
            for m in range(NT):
                csl = slice(m * 128, (m + 1) * 128)
                h0, h1 = rc, rc + 4
                t_ls = p.sy.op(lambda e, inst=inst, m=m, rc=rc: e.dma_start(
                    out=lcs_d[inst, m, :, rc * 64:(rc + 1) * 64],
                    in_=Sb_sb[:, rc * 64:(rc + 1) * 64]),
                    deps=[(p.ve, p.ve.n)])
                t_cs = p.sy.op(lambda e, inst=inst, m=m, rc=rc: e.dma_start(
                    out=cgc_d[inst, m, :, rc:rc + 1], in_=cgc_sb[:, rc:rc + 1],
                    allow_slow_non_contiguous=True))
                mmA = p.te.op(lambda e, csl=csl: e.matmul(
                    ps_a[:, 0:128], kg[0:64, csl], qg[0:64, csl],
                    start=True, stop=True, tile_position=(0, 0)),
                    deps=[t_ko] + ([t_m2] if t_m2 else []))
                mmA2 = p.te.op(lambda e, csl=csl: e.matmul(
                    ps_b[:, 0:128], kg[64:128, csl], qg[64:128, csl],
                    start=True, stop=True, tile_position=(64, 0)))
                t_m1 = p.ve.op(lambda e: e.tensor_mul(am2[:, 0:128], ps_a[:, 0:128],
                                                      C["triu_f"][:]), deps=[mmA, mmA2])
                t_m2 = p.ve.op(lambda e: e.tensor_mul(am2[:, 128:256], ps_b[:, 0:128],
                                                      C["triu_f"][:]), deps=[mmA2])
                mi0 = p.te.op(lambda e, m=m, h0=h0: e.matmul(
                    ps_c[:, 0:64], am2[:, 0:128],
                    v_tm[:, m * 512 + h0 * 64:m * 512 + (h0 + 1) * 64],
                    start=True, stop=True), inc=False,
                    deps=[t_m1, t_m2] + ([t_ob_prev] if t_ob_prev else []))
                mi1 = p.te.op(lambda e, m=m, h1=h1: e.matmul(
                    ps_c[:, 64:128], am2[:, 128:256],
                    v_tm[:, m * 512 + h1 * 64:m * 512 + (h1 + 1) * 64],
                    start=True, stop=True))
                t_ob = p.ve.op(lambda e: e.tensor_copy(ob2[:], ps_c[:, 0:128]),
                               deps=[mi1] + ([t_os_prev] if t_os_prev else []))
                t_ob_prev = t_ob
                key = (rc, m)
                t_os = p.gp.op(lambda e, m=m, rc=rc: e.dma_start(
                    out=opart_d[m * 128:(m + 1) * 128, :].rearrange(
                        "t (g h x) -> t g h x", g=2, h=4, x=64)[:, :, rc],
                    in_=ob2[:].rearrange("t (g x) -> t g x", x=64),
                    accum_op=ALU.add),
                    deps=[t_ob] + ([opart_tok[key]] if key in opart_tok else []))
                opart_tok[key] = t_os
                t_os_prev = t_os
                t_tr = p.te.op(lambda e, csl=csl: e.transpose(
                    ps_tr[:], kout[:, csl], C["ident_bf"][:]),
                    deps=[t_ob, (p.ve, p.ve.n)])
                t_ke = p.ve.op(lambda e: e.tensor_copy(kout_tm[:], ps_tr[:]),
                               deps=[t_tr])
                mmM = None
                for half in range(2):
                    h = rc + 4 * half
                    tp = (0, 0) if half == 0 else (0, 64)
                    mmM = p.te.op(lambda e, half=half, h=h, tp=tp, m=m, rc=rc: e.matmul(
                        ps_m[64 * half:64 * (half + 1), rc * 64:(rc + 1) * 64],
                        kout_tm[:, half * 64:(half + 1) * 64],
                        v_tm[:, m * 512 + h * 64:m * 512 + (h + 1) * 64],
                        start=True, stop=True, tile_position=tp),
                        inc=(half == 1), deps=([t_ke, t_sm] if half == 0 else []))
                t_dc = p.sc.op(lambda e, m=m, rc=rc: e.activation(
                    dec_sb[:, rc:rc + 1], Gf[:, m * 128 + 127:m * 128 + 128], AF.Exp),
                    deps=[mmM])
                t_cg = p.ve.op(lambda e, m=m, rc=rc: e.tensor_add(
                    cgc_sb[:, rc:rc + 1], cgc_sb[:, rc:rc + 1],
                    Gf[:, m * 128 + 127:m * 128 + 128]),
                    deps=[t_cs, t_dc])
                t_sa = p.ve.op(lambda e, rc=rc: e.tensor_scalar(
                    Stmp[:], S_sb[:, rc * 64:(rc + 1) * 64],
                    dec_sb[:, rc:rc + 1], None, ALU.mult), deps=[mmM, t_z0, t_cg, t_dc])
                t_sm = p.ve.op(lambda e, rc=rc: e.tensor_add(
                    S_sb[:, rc * 64:(rc + 1) * 64], Stmp[:],
                    ps_m[:, rc * 64:(rc + 1) * 64]), deps=[t_sa])
                p.ve.op(lambda e, rc=rc: e.tensor_copy(
                    Sb_sb[:, rc * 64:(rc + 1) * 64], S_sb[:, rc * 64:(rc + 1) * 64]),
                    deps=[t_sm, t_ls])
        p.barrier()
        p.sy.op(lambda e, inst=inst: e.dma_start(out=tsg_d[inst], in_=S_sb[:]))
        p.sy.op(lambda e, inst=inst: e.dma_start(out=dsg_d[inst], in_=cgc_sb[:]))

    p.barrier()
    p.sy.op(lambda e: e.nop(), inc=False, deps=[p.sy.tok(), p.gp.tok()])
    return p.finish()




def build_E():
    p = Prog()
    nc = p.nc
    x3_d = nc.dram_tensor("x3", [R, D], F32, kind="ExternalInput").ap()
    opart_d = nc.dram_tensor("opart", [R, 512], F32, kind="ExternalInput").ap()
    qgd_d = nc.dram_tensor("qgd", [4, 128, R], BF16, kind="ExternalInput").ap()
    q2f_d = nc.dram_tensor("q2f", [4, 128, R], BF16, kind="ExternalInput").ap()
    gkhi_d = nc.dram_tensor("gkhi", [4, 128, R], BF16, kind="ExternalInput").ap()
    gklo_d = nc.dram_tensor("gklo", [4, 128, R], BF16, kind="ExternalInput").ap()
    rwf_d = nc.dram_tensor("rwf", [1, 4 * R], F32, kind="ExternalInput").ap()
    gs_d = nc.dram_tensor("gs", [R, 512], BF16, kind="ExternalInput").ap()
    lcs_d = nc.dram_tensor("lcs", [5, NT, 128, 256], BF16, kind="ExternalInput").ap()
    cgc_d = nc.dram_tensor("cgc", [5, NT, 128, 4], F32, kind="ExternalInput").ap()
    tcomp_d = nc.dram_tensor("tcomp", [5, 3, 128, 256], F32, kind="ExternalInput").ap()
    dcomp_d = nc.dram_tensor("dcomp", [5, 3, 128, 4], F32, kind="ExternalInput").ap()
    nw_d = nc.dram_tensor("nw", [1, 512], F32, kind="ExternalInput").ap()
    won2_d = nc.dram_tensor("won2", [512, D], F32, kind="ExternalInput").ap()
    y_d = nc.dram_tensor("y", [R, D], F32, kind="ExternalOutput").ap()

    C = setup_consts(p, skip=("triu_bf", "triu_f"))

    qgd = p.sb("qgd_s", [128, 4 * R], BF16)
    q2f = p.sb("q2f_s", [128, 4 * R], BF16)
    gkhi = p.sb("gkhi_s", [128, 4 * R], BF16)
    gklo = p.sb("gklo_s", [128, 4 * R], BF16)
    qge = p.sb("qge", [128, 4 * R], BF16)
    o_acc = p.sb("o_acc", [128, NT * 512], F32)
    won2 = p.sb("won2_s", [128, 4 * D], BF16)
    gstile = p.sb("gstile", [128, 512], BF16)
    x3t = p.sb("x3t", [128, D], F32)
    zf = p.sb("zf", [128, R], F32)
    Gf = p.sb("Gf", [128, R], F32)
    e2h = p.sb("e2h", [128, 128], F32)
    rwbig = p.sb("rwbig", [128, R], F32)
    mbig = p.sb("mbig", [128, R], F32)
    rwrow = zf[0:1, 0:R]
    lc_sb = p.sb("lc_sb", [128, 256], BF16)
    cgr_sb = p.sb("cgr_sb", [128, 4], F32)
    dec_sb = p.sb("dec_sb", [128, 4], F32)
    Sin_sb = p.sb("Sin_sb", [128, 256], F32)
    Sc_sb = p.sb("Sc_sb", [128, 256], F32)
    Scb_sb = p.sb("Scb_sb", [128, 256], BF16)
    tc_sb = p.sb("tc_sb", [128, 256], F32)
    st1 = p.sb("st1", [128, 1], F32)
    st2 = p.sb("st2", [128, 1024], F32)
    s8 = p.sb("s8", [128, 8], F32)
    nw_bc = p.sb("nw_bc", [128, 512], F32)
    row_sb = p.sb("row_sb", [1, 512], F32)
    ob = p.sb("ob", [128, 512], BF16)
    obt = p.sb("obt", [128, 128], BF16)
    ones2d = p.sb("ones2d", [128, 128], F32)

    ps_tr = p.ps("ps_tr", [128, 128], BF16)
    ps_a = p.ps("ps_a", [128, 512], F32)
    ps_b = p.ps("ps_b", [128, 512], F32)
    ps_o = p.ps("ps_o", [128, D], F32)
    ps_bc = p.ps("ps_bc", [128, 512], F32)

    p.ve.op(lambda e: e.memset(ones2d[:], 1.0))
    t_in = {}
    for nm, dst, src in [("qgd", qgd, qgd_d), ("q2f", q2f, q2f_d),
                         ("gkhi", gkhi, gkhi_d), ("gklo", gklo, gklo_d)]:
        t_in[nm] = p.sy.op(lambda e, dst=dst, src=src: e.dma_start(
            out=dst[:].rearrange("p (r c) -> p r c", c=R),
            in_=src.rearrange("r p c -> p r c")))
    for m in range(NT):
        t_in[("op", m)] = p.sy.op(lambda e, m=m: e.dma_start(
            out=o_acc[:, m * 512:(m + 1) * 512], in_=opart_d[m * 128:(m + 1) * 128, :]))
    t_in["won2"] = p.gp.op(lambda e: e.dma_start(
        out=won2[:].rearrange("p (c d) -> p c d", d=D),
        in_=won2_d.rearrange("(c p) d -> p c d", p=128)))
    t_nw = p.sy.op(lambda e: e.dma_start(out=row_sb[:], in_=nw_d))
    t_nwb = bcast_row(p, C, row_sb, ps_bc, nw_bc, [t_nw, C["loaded"]], n=512)

    for inst in range(5):
        p.barrier()
        t_z = p.ve.op(lambda e: e.memset(Sin_sb[:], 0.0))
        for j in range(3):
            t_dc = p.sy.op(lambda e, inst=inst, j=j: e.dma_start(
                out=dec_sb[:], in_=dcomp_d[inst, j]), deps=[t_z, (p.sc, p.sc.n)])
            t_tc = p.sy.op(lambda e, inst=inst, j=j: e.dma_start(
                out=tc_sb[:], in_=tcomp_d[inst, j]))
            t_ex = p.sc.op(lambda e: e.activation(dec_sb[:], dec_sb[:], AF.Exp),
                           deps=[t_dc])
            t_sm = t_z
            for rc in range(4):
                t_sm = p.ve.op(lambda e, rc=rc: e.tensor_scalar(
                    Sin_sb[:, rc * 64:(rc + 1) * 64], Sin_sb[:, rc * 64:(rc + 1) * 64],
                    dec_sb[:, rc:rc + 1], None, ALU.mult), deps=[t_ex, t_sm])
            t_z = p.ve.op(lambda e: e.tensor_add(Sin_sb[:], Sin_sb[:], tc_sb[:]),
                          deps=[t_sm, t_tc])
        sin_done = t_z
        if inst == 0:
            qgsrc = qgd
            qg_done = t_in["qgd"]
        else:
            ex = inst - 1
            qgsrc = qge
            t_rwl = p.sy.op(lambda e, ex=ex: e.dma_start(
                out=rwrow[:], in_=rwf_d[0:1, ex * R:(ex + 1) * R]),
                deps=[(p.ve, p.ve.n), (p.te, p.te.n)])
            last = None
            for tb4 in range(4):
                mmb = p.te.op(lambda e, tb4=tb4: e.matmul(
                    ps_bc[:], C["ones1_f"][:],
                    rwrow[0:1, tb4 * 512:(tb4 + 1) * 512],
                    start=True, stop=True), deps=[t_rwl, (p.ve, p.ve.n)])
                last = p.ve.op(lambda e, tb4=tb4: e.tensor_copy(
                    rwbig[:, tb4 * 512:(tb4 + 1) * 512], ps_bc[:]), deps=[mmb])
            t_mb = p.sc.op(lambda e: e.activation(mbig[:], rwbig[:], AF.Sign),
                           deps=[last])
            qg_done = None
            for rc in range(4):
                t_sc = None
                for m in range(NT):
                    csl = slice(m * 128, (m + 1) * 128)
                    osl = slice(rc * R + m * 128, rc * R + (m + 1) * 128)
                    t_ad = p.ve.op(lambda e, osl=osl: e.tensor_add(
                        e2h[:], gkhi[:, osl], gklo[:, osl]),
                        deps=[t_in["gkhi"], t_in["gklo"]] + ([t_sc] if t_sc else []))
                    t_ad = p.ve.op(lambda e, csl=csl: e.tensor_mul(
                        e2h[:], e2h[:], mbig[:, csl]), deps=[t_ad, t_mb])
                    t_sc = p.ve.op(lambda e, csl=csl: e.tensor_tensor_scan(
                        Gf[:, csl], ones2d[:], e2h[:], 0.0, ALU.mult, ALU.add),
                        deps=[t_ad])
                t_e1 = p.sc.op(lambda e: e.activation(zf[:], Gf[:], AF.Exp),
                               deps=[t_sc, (p.ve, p.ve.n)])
                t_q1 = p.ve.op(lambda e, rc=rc: e.tensor_mul(
                    qge[:, rc * R:(rc + 1) * R], q2f[:, rc * R:(rc + 1) * R], zf[:]),
                    deps=[t_e1, t_in["q2f"]])
                qg_done = p.ve.op(lambda e, rc=rc: e.tensor_mul(
                    qge[:, rc * R:(rc + 1) * R], qge[:, rc * R:(rc + 1) * R], rwbig[:]),
                    deps=[t_q1])
        t_acc = None
        for m in range(NT):
            t_lc = p.sy.op(lambda e, inst=inst, m=m: e.dma_start(
                out=lc_sb[:], in_=lcs_d[inst, m]),
                deps=[(p.ve, p.ve.n)])
            t_cg = p.sy.op(lambda e, inst=inst, m=m: e.dma_start(
                out=cgr_sb[:], in_=cgc_d[inst, m]), deps=[(p.sc, p.sc.n)])
            t_dx = p.sc.op(lambda e: e.activation(dec_sb[:], cgr_sb[:], AF.Exp),
                           deps=[t_cg, sin_done])
            t_s = None
            for rc in range(4):
                t_s = p.ve.op(lambda e, rc=rc: e.tensor_scalar(
                    Sc_sb[:, rc * 64:(rc + 1) * 64], Sin_sb[:, rc * 64:(rc + 1) * 64],
                    dec_sb[:, rc:rc + 1], None, ALU.mult), deps=[t_dx, t_s])
            t_s = p.ve.op(lambda e: e.tensor_add(Sc_sb[:], Sc_sb[:], lc_sb[:]),
                          deps=[t_s, t_lc])
            t_sb = p.ve.op(lambda e: e.tensor_copy(Scb_sb[:], Sc_sb[:]), deps=[t_s])
            mmlast = None
            for rc in range(4):
                mmlast = p.te.op(lambda e, rc=rc, m=m, qgsrc=qgsrc: e.matmul(
                    ps_a[:, rc * 64:(rc + 1) * 64],
                    qgsrc[0:64, rc * R + m * 128:rc * R + (m + 1) * 128],
                    Scb_sb[0:64, rc * 64:(rc + 1) * 64],
                    start=True, stop=True, tile_position=(0, 0)),
                    deps=([t_sb, qg_done, t_acc] if rc == 0 else []))
                mmlast = p.te.op(lambda e, rc=rc, m=m, qgsrc=qgsrc: e.matmul(
                    ps_b[:, rc * 64:(rc + 1) * 64],
                    qgsrc[64:128, rc * R + m * 128:rc * R + (m + 1) * 128],
                    Scb_sb[64:128, rc * 64:(rc + 1) * 64],
                    start=True, stop=True, tile_position=(64, 0)))
            t_a1 = p.ve.op(lambda e, m=m: e.tensor_add(
                o_acc[:, m * 512:m * 512 + 256], o_acc[:, m * 512:m * 512 + 256],
                ps_a[:, 0:256]), deps=[mmlast, t_in[("op", m)]])
            t_acc = p.ve.op(lambda e, m=m: e.tensor_add(
                o_acc[:, m * 512 + 256:(m + 1) * 512],
                o_acc[:, m * 512 + 256:(m + 1) * 512], ps_b[:, 0:256]),
                deps=[t_a1])

    p.barrier()
    t_prev_ln = None
    t_x3 = None
    for m in range(NT):
        p.barrier()
        osl = slice(m * 512, (m + 1) * 512)
        t_gs = p.sy.op(lambda e, m=m: e.dma_start(out=gstile[:],
                                                  in_=gs_d[m * 128:(m + 1) * 128, :]))
        t_x3 = p.sy.op(lambda e, m=m: e.dma_start(out=x3t[:],
                                                  in_=x3_d[m * 128:(m + 1) * 128, :]))
        t_sq = p.sc.op(lambda e, osl=osl: e.activation(st2[:, 0:512], o_acc[:, osl],
                                                       AF.Square),
                       deps=[])
        t_rd = p.ve.op(lambda e: e.reduce_sum(
            s8[:], st2[:, 0:512].rearrange("p (h x) -> p h x", x=64), axis=AX.X), deps=[t_sq])
        t_v = p.ve.op(lambda e: e.tensor_scalar(s8[:], s8[:], 1.0 / 64.0, 1e-5,
                                                ALU.mult, ALU.add), deps=[t_rd])
        t_sq2 = p.sc.op(lambda e: e.activation(s8[:], s8[:], AF.Sqrt), deps=[t_v])
        t_r = p.ve.op(lambda e: e.reciprocal(s8[:], s8[:]), deps=[t_sq2])
        t_h = t_r
        for h in range(8):
            t_h = p.ve.op(lambda e, h=h, m=m: e.tensor_scalar(
                o_acc[:, m * 512 + h * 64:m * 512 + (h + 1) * 64],
                o_acc[:, m * 512 + h * 64:m * 512 + (h + 1) * 64],
                s8[:, h:h + 1], None, ALU.mult), deps=[t_h])
        t_nwm = p.ve.op(lambda e, osl=osl: e.tensor_mul(o_acc[:, osl], o_acc[:, osl],
                                                        nw_bc[:]), deps=[t_h, t_nwb])
        t_gsx = p.ve.op(lambda e, osl=osl: e.tensor_mul(ob[:], o_acc[:, osl], gstile[:]),
                        deps=[t_nwm, t_gs])
        mml = None
        evac = None
        for kc4 in range(4):
            t_tr = p.te.op(lambda e, kc4=kc4: e.transpose(
                ps_tr[:], ob[:, kc4 * 128:(kc4 + 1) * 128], C["ident_bf"][:]),
                deps=[t_gsx] + ([evac] if evac else []))
            evac = p.ve.op(lambda e: e.tensor_copy(obt[:], ps_tr[:]),
                           deps=[t_tr] + ([mml] if mml else []))
            for half in range(2):
                mml = p.te.op(lambda e, kc4=kc4, half=half: e.matmul(
                    ps_o[:, half * 512:(half + 1) * 512], obt[:],
                    won2[:, kc4 * D + half * 512:kc4 * D + (half + 1) * 512],
                    start=(kc4 == 0), stop=(kc4 == 3)),
                    inc=True,
                    deps=([evac, t_in["won2"]] if half == 0 else [evac]))
        t_y = p.ve.op(lambda e: e.tensor_add(x3t[:], x3t[:], ps_o[:]),
                      deps=[mml, t_x3])
        t_ln = ln_rowwise(p, x3t[:], st1, st2, [t_y, t_prev_ln] if t_prev_ln else [t_y],
                          n=1024)
        t_prev_ln = t_ln
        p.sy.op(lambda e, m=m: e.dma_start(out=y_d[m * 128:(m + 1) * 128, :],
                                           in_=x3t[:]), deps=[t_ln])

    p.barrier()
    p.sy.op(lambda e: e.nop(), inc=False, deps=[p.sy.tok(), p.gp.tok()])
    return p.finish()


def host_inputs_D(inputs, resB, resC):
    f = f32c
    wonT = f(np.asarray(inputs["L1_attn_Wo"]).T)
    wqP = f(np.asarray(inputs["L1_ffn_Wq"])[:, PAIR8])
    wkP = f(np.asarray(inputs["L1_ffn_Wk"])[:, PAIR8])
    wv = f(inputs["L1_ffn_Wv"])
    wqa = f(inputs["L1_ffn_Wqa"])
    wqbP = f(np.asarray(inputs["L1_ffn_Wqb"])[:, PAIR8])
    wka = f(inputs["L1_ffn_Wka"])
    wkbP = f(np.asarray(inputs["L1_ffn_Wkb"])[:, PAIR8])
    wg1a = f(inputs["L1_ffn_Wg1a"])
    wg1bP = f(np.asarray(inputs["L1_ffn_Wg1b"])[:, PAIR8])
    wg2a = f(inputs["L1_ffn_Wg2a"])
    wg2bP = f(np.asarray(inputs["L1_ffn_Wg2b"])[:, PAIR8])
    bg1P = f(np.asarray(inputs["L1_ffn_bg1"])[PAIR8].reshape(4, 128).T)
    bg2P = f(np.asarray(inputs["L1_ffn_bg2"])[PAIR8].reshape(4, 128).T)
    we = f(inputs["L1_ffn_We"])
    wga = f(inputs["L1_ffn_Wga"])
    wgb = f(inputs["L1_ffn_Wgb"])
    bo = f(inputs["L1_attn_bo"]).reshape(1, D)
    maps = []
    for c in range(NCORE):
        b, sq = c // 4, c % 4
        of_global = np.concatenate([np.asarray(resC[b * 4 + sp]["of"]) for sp in range(4)], axis=0)
        pr = np.arange(sq * R, (sq + 1) * R)
        gmap = np.where(pr < 6656, pr + 1536, pr - 6656)
        ofall = np.ascontiguousarray(of_global[:, gmap])
        maps.append({
            "ofall": ofall,
            "x2": np.asarray(resB[c]["x2"], np.float32),
            "won": wonT, "bo": bo,
            "wq": wqP, "wk": wkP, "wv": wv,
            "wqa": wqa, "wqb": wqbP, "wka": wka, "wkb": wkbP,
            "wg1a": wg1a, "wg1b": wg1bP, "bg1": bg1P,
            "wg2a": wg2a, "wg2b": wg2bP, "bg2": bg2P,
            "we": we, "wga": wga, "wgb": wgb,
        })
    return maps


def run_D(inputs, resB, resC):
    nc = get_prog("D")
    res = run_bass_kernel_spmd(nc, host_inputs_D(inputs, resB, resC),
                               core_ids=list(range(NCORE)))
    return res.results


def host_inputs_E(inputs, resD):
    nw = f32c(np.tile(np.asarray(inputs["L1_ffn_norm_w"]), 8)).reshape(1, 512)
    won2 = f32c(inputs["L1_ffn_Wo"])
    maps = []
    for c in range(NCORE):
        b, sq = c // 4, c % 4
        tcomp = np.zeros((5, 3, 128, 256), np.float32)
        dcomp = np.zeros((5, 3, 128, 4), np.float32)
        for j in range(sq):
            pc = b * 4 + j
            tcomp[:, j] = np.asarray(resD[pc]["tsg"], np.float32)
            dcomp[:, j] = np.asarray(resD[pc]["dsg"], np.float32)
        maps.append({
            "x3": np.asarray(resD[c]["x3"], np.float32),
            "opart": np.asarray(resD[c]["opart"], np.float32),
            "qgd": np.asarray(resD[c]["qgd"]),
            "q2f": np.asarray(resD[c]["q2fo"]),
            "gkhi": np.asarray(resD[c]["gkhi"]),
            "gklo": np.asarray(resD[c]["gklo"]),
            "rwf": np.asarray(resD[c]["rwf"], np.float32),
            "gs": np.asarray(resD[c]["gs"]),
            "lcs": np.asarray(resD[c]["lcs"]),
            "cgc": np.asarray(resD[c]["cgc"], np.float32),
            "tcomp": tcomp, "dcomp": dcomp,
            "nw": nw, "won2": won2,
        })
    return maps


def run_E(inputs, resD):
    nc = get_prog("E")
    res = run_bass_kernel_spmd(nc, host_inputs_E(inputs, resD),
                               core_ids=list(range(NCORE)))
    return res.results


def kernel(**inputs):
    x = np.asarray(inputs["L0_attn_x"], np.float32).reshape(B, L, D)
    slices = [np.ascontiguousarray(x[c // 4, (c % 4) * R:(c % 4 + 1) * R])
              for c in range(NCORE)]
    resA = run_A(inputs, slices)
    resB = run_B(inputs, slices, resA)
    resC = run_C(resB)
    resD = run_D(inputs, resB, resC)
    resE = run_E(inputs, resD)
    out = np.zeros((B, L, D), np.float32)
    for c in range(NCORE):
        b, sq = c // 4, c % 4
        out[b, sq * R:(sq + 1) * R] = np.asarray(resE[c]["y"], np.float32)
    return out
